# version 03
# baseline (speedup 1.0000x reference)
import contextlib
import math

import numpy as np
import ml_dtypes

import concourse.bass as bass
import concourse.mybir as mybir
from concourse.bass_utils import run_bass_kernel_spmd

F32 = mybir.dt.float32
BF16 = mybir.dt.bfloat16
AF = mybir.ActivationFunctionType
ALU = mybir.AluOpType
AX = mybir.AxisListType

D = 1024
NCORES = 8


class PB:
    STREAMS = ("tensor", "vector", "scalar", "gpsimd", "sync")
    DEFSEM = {"tensor": "t", "vector": "v", "scalar": "s", "gpsimd": "p", "sync": "qH"}

    def __init__(self, nc, es):
        self.nc = nc
        self.es = es
        self.streams = {e: [] for e in self.STREAMS}
        self.NROT = 4
        semkeys = ["t", "v", "s", "p"]
        for q in ("qH", "cH", "pG"):
            semkeys += [f"{q}{i}" for i in range(self.NROT)]
        self.sems = {k: es.enter_context(nc.semaphore("S" + k)) for k in semkeys}
        self.cnt = {k: 0 for k in semkeys}
        self.dma_i = {"qH": 0, "cH": 0, "pG": 0}
        self.waited = {}
        self.writer = {}
        self.readers = {}
        self.sb = {}
        self.dr = {}
        self.sbuf_bytes = 0

    def sbuf(self, name, shape, dtype):
        if name not in self.sb:
            self.sb[name] = self.es.enter_context(self.nc.sbuf_tensor(name, shape, dtype))
            self.sbuf_bytes += shape[1] * mybir.dt.size(dtype)
        return self.sb[name]

    def psum(self, name, shape, dtype):
        if name not in self.sb:
            self.sb[name] = self.es.enter_context(self.nc.psum_tensor(name, shape, dtype))
        return self.sb[name]

    def dram(self, name, shape, dtype, kind=None):
        if name not in self.dr:
            if kind is None:
                self.dr[name] = self.nc.dram_tensor(name, shape, dtype).ap()
            else:
                self.dr[name] = self.nc.dram_tensor(name, shape, dtype, kind=kind).ap()
        return self.dr[name]

    def _wait(self, stream, sk, val):
        if val <= 0:
            return
        if sk[-1].isdigit() and sk[:-1] in ("qH", "pG", "cH"):
            val = self.cnt[sk]
        key = (stream, sk)
        if self.waited.get(key, 0) >= val:
            return
        self.waited[key] = val
        sem = self.sems[sk]
        self.streams[stream].append(("w", sem, val))

    def op(self, stream, fn, reads=(), writes=(), sem=None, dma=False):
        if dma:
            q = sem
            sk = f"{q}{self.dma_i[q] % self.NROT}"
            self.dma_i[q] += 1
            self._wait(stream, sk, self.cnt[sk])
        else:
            sk = sem if sem is not None else self.DEFSEM[stream]
        for b in reads:
            w = self.writer.get(b)
            if w:
                self._wait(stream, w[0], w[1])
        for b in writes:
            w = self.writer.get(b)
            if w:
                self._wait(stream, w[0], w[1])
            for skr, c in self.readers.get(b, {}).items():
                self._wait(stream, skr, c)
        amt = 16 if dma else 1
        self.cnt[sk] += amt
        self.streams[stream].append(("i", fn, self.sems[sk], amt))
        pos = (sk, self.cnt[sk])
        for b in reads:
            d = self.readers.setdefault(b, {})
            d[sk] = max(d.get(sk, 0), pos[1])
        for b in writes:
            self.writer[b] = pos
            self.readers[b] = {}

    def q_dma(self, out, in_, reads, writes, transpose=False):
        if transpose:
            self.op("sync", lambda em: em.dma_start_transpose(out, in_),
                    reads=reads, writes=writes, sem="qH", dma=True)
        else:
            self.op("sync", lambda em: em.dma_start(out=out, in_=in_),
                    reads=reads, writes=writes, sem="qH", dma=True)

    def c_dma(self, out, in_, reads, writes):
        self.op("scalar", lambda em: em.dma_start(out=out, in_=in_),
                reads=reads, writes=writes, sem="cH", dma=True)

    def g_dma(self, out, in_, reads, writes, accum=False):
        if accum:
            self.op("gpsimd", lambda em: em.dma_start(out=out, in_=in_, accum_op=ALU.add),
                    reads=reads, writes=writes, sem="pG", dma=True)
        else:
            self.op("gpsimd", lambda em: em.dma_start(out=out, in_=in_),
                    reads=reads, writes=writes, sem="pG", dma=True)

    def mm(self, out, lhsT, rhs, start, stop, reads, writes):
        self.op("tensor",
                lambda em: em.matmul(out, lhsT, rhs, start=start, stop=stop),
                reads=reads, writes=writes)

    def transp(self, out, in_, reads, writes, np_=128, f32=False):
        nm = "identf" if f32 else "ident"
        ident = self.sb[nm]
        self.op("tensor", lambda em: em.transpose(out, in_, ident[0:np_, 0:np_]),
                reads=list(reads) + [nm], writes=writes)

    def finalize(self):
        block = self.es.enter_context(self.nc.Block())
        for stream in self.STREAMS:
            items = self.streams[stream]
            if not items:
                continue

            def make(items):
                def emit(em):
                    for it in items:
                        if it[0] == "w":
                            em.wait_ge(it[1], it[2])
                        else:
                            it[1](em).then_inc(it[2], it[3])
                return emit

            getattr(block, stream)(make(items))

    def drain(self):
        for sk, c in self.cnt.items():
            if c > 0:
                sem = self.sems[sk]
                self._wait("sync", sk, c)


def emit_prelude(pb):
    nc = pb.nc
    pb.sbuf("ident", [128, 128], BF16)
    pb.sbuf("identf", [128, 128], F32)
    pb.sbuf("onescol", [1, 128], BF16)
    pb.op("gpsimd", lambda em: em.iota(pb.sb["identf"][:, :], [[1, 128]], base=0,
                                       channel_multiplier=-1,
                                       allow_small_or_imprecise_dtypes=True),
          writes=["identf"])
    pb.op("vector", lambda em: em.tensor_single_scalar(
        pb.sb["identf"][:, :], pb.sb["identf"][:, :], 0.0, ALU.is_equal),
        reads=["identf"], writes=["identf"])
    pb.op("vector", lambda em: em.tensor_copy(pb.sb["ident"][:, :], pb.sb["identf"][:, :]),
          reads=["identf"], writes=["ident"])
    pb.op("vector", lambda em: em.memset(pb.sb["onescol"][:, :], 1.0), writes=["onescol"])


def emit_causal_masks(pb):
    for ds in range(4):
        t = pb.sbuf(f"mask{ds}", [128, 512], BF16)
        pb.op("gpsimd", lambda em, t=t, ds=ds: em.iota(
            t[:, :], [[1, 512]], base=-128 * ds, channel_multiplier=-1,
            allow_small_or_imprecise_dtypes=True), writes=[f"mask{ds}"])
        pb.op("vector", lambda em, t=t: em.tensor_single_scalar(t[:, :], t[:, :], 0.0, ALU.is_ge),
              reads=[f"mask{ds}"], writes=[f"mask{ds}"])
        pb.op("vector", lambda em, t=t: em.tensor_single_scalar(t[:, :], t[:, :], -1e30, ALU.mult),
              reads=[f"mask{ds}"], writes=[f"mask{ds}"])



def emit_ln(pb, tag, src_name, src_ap, T, eps=1e-5):
    xn_f = pb.dram(f"xnf_{tag}", [T, D], F32)
    xn_b = pb.dram(f"xnb_{tag}", [T, D], BF16)
    nt = T // 128
    for sl in range(2):
        pb.sbuf(f"FP{sl}", [128, D], F32)
        pb.sbuf(f"FP{2+sl}", [128, D], F32)
        pb.sbuf(f"FP{4+sl}", [128, D], F32)
        for cn in ("csum", "css", "cm", "cv", "cr"):
            pb.sbuf(f"{cn}{sl}", [128, 1], F32)
    for i in range(nt):
        sl = i % 2
        xt, sq, xo = pb.sb[f"FP{sl}"], pb.sb[f"FP{2+sl}"], pb.sb[f"FP{4+sl}"]
        csum, css = pb.sb[f"csum{sl}"], pb.sb[f"css{sl}"]
        cm, cv, cr = pb.sb[f"cm{sl}"], pb.sb[f"cv{sl}"], pb.sb[f"cr{sl}"]
        kx, ks, ko = f"FP{sl}", f"FP{2+sl}", f"FP{4+sl}"
        kc = [f"csum{sl}", f"css{sl}", f"cm{sl}", f"cv{sl}", f"cr{sl}"]
        pb.q_dma(xt[:, :], src_ap[i * 128:(i + 1) * 128, :], reads=[src_name], writes=[kx])
        pb.op("vector", lambda em, xt=xt, c=csum: em.reduce_sum(c[:, :], xt[:, :], AX.X),
              reads=[kx], writes=[kc[0]])
        pb.op("scalar", lambda em, xt=xt, sq=sq, c=css: em.activation(
            sq[:, :], xt[:, :], AF.Square, accum_out=c[:, 0:1]),
            reads=[kx], writes=[ks, kc[1]])
        pb.op("vector", lambda em, c=csum, m=cm: em.tensor_scalar_mul(m[:, :], c[:, :], 1.0 / D),
              reads=[kc[0]], writes=[kc[2]])
        pb.op("vector", lambda em, m=cm, v=cv: em.tensor_mul(v[:, :], m[:, :], m[:, :]),
              reads=[kc[2]], writes=[kc[3]])
        pb.op("vector", lambda em, c=css, v=cv: em.scalar_tensor_tensor(
            v[:, :], c[:, :], 1.0 / D, v[:, :], ALU.mult, ALU.subtract),
            reads=[kc[1], kc[3]], writes=[kc[3]])
        pb.op("vector", lambda em, v=cv: em.tensor_single_scalar(v[:, :], v[:, :], eps, ALU.add),
              reads=[kc[3]], writes=[kc[3]])
        pb.op("vector", lambda em, v=cv, r=cr: em.reciprocal(r[:, :], v[:, :]),
              reads=[kc[3]], writes=[kc[4]])
        pb.op("scalar", lambda em, r=cr: em.activation(r[:, :], r[:, :], AF.Sqrt),
              reads=[kc[4]], writes=[kc[4]])
        pb.op("vector", lambda em, xt=xt, xo=xo, m=cm, r=cr: em.tensor_scalar(
            xo[:, :], xt[:, :], m[:, 0:1], r[:, 0:1], ALU.subtract, ALU.mult),
            reads=[kx, kc[2], kc[4]], writes=[ko])
        pb.q_dma(xn_f[i * 128:(i + 1) * 128, :], xo[:, :], reads=[ko], writes=[f"xnf_{tag}@{i}"])
        pb.g_dma(xn_b[i * 128:(i + 1) * 128, :], xo[:, :], reads=[ko], writes=[f"xnb_{tag}@{i}"])
    return xn_f, xn_b


def rd_all(name, nt):
    return [f"{name}@{i}" for i in range(nt)]


def emit_load_xnT(pb, tag, xn_b, seq, S, T):
    nt = T // 128
    for d in range(8):
        xp = pb.sbuf(f"XP{d}", [128, 2048], BF16)
        pb.q_dma(xp[:, 0:S], xn_b[seq * S:(seq + 1) * S, d * 128:(d + 1) * 128],
                 reads=rd_all(f"xnb_{tag}", nt), writes=[f"XP{d}"], transpose=True)


def emit_combine(pb, tag, xres, s_scr, g_scr, T):
    nt = T // 128
    for i in range(nt):
        sl = i % 2
        xt = pb.sb[f"FP{sl}"]
        st = pb.sb[f"FP{2+sl}"]
        gt = pb.sb[f"FP{4+sl}"]
        kx, ks, kg = f"FP{sl}", f"FP{2+sl}", f"FP{4+sl}"
        r = slice(i * 128, (i + 1) * 128)
        pb.q_dma(xt[:, :], xres[r, :], reads=["xres"], writes=[kx])
        pb.q_dma(st[:, :], s_scr[r, :], reads=rd_all(f"s_{tag}", nt), writes=[ks])
        pb.q_dma(gt[:, :], g_scr[r, :], reads=rd_all(f"g_{tag}", nt), writes=[kg])
        pb.op("scalar", lambda em, g=gt: em.activation(g[:, :], g[:, :], AF.Silu),
              reads=[kg], writes=[kg])
        pb.op("vector", lambda em, s=st, g=gt: em.tensor_mul(s[:, :], s[:, :], g[:, :]),
              reads=[ks, kg], writes=[ks])
        pb.op("vector", lambda em, x=xt, s=st: em.tensor_add(x[:, :], x[:, :], s[:, :]),
              reads=[kx, ks], writes=[kx])
        pb.q_dma(xres[r, :], xt[:, :], reads=[kx], writes=["xres"])


def emit_L0(pb, inp, n_seq, S):
    T = n_seq * S
    nt = T // 128
    nib = S // 128
    xres = pb.dram("xres", [T, D], F32)
    xn_f, xn_b = emit_ln(pb, "l0", "xres", xres, T)

    qT = pb.dram("l0_qT", [D, T], BF16)
    Y2 = pb.dram("l0_Y2", [T, D], BF16)
    hT = pb.dram("l0_hT", [32, 128, T], BF16)
    s_scr = pb.dram("s_l0", [T, D], F32)
    g_scr = pb.dram("g_l0", [T, D], F32)

    PS = [pb.psum(f"PS{i}", [128, 512], F32) for i in range(6)]
    PT = pb.psum("PTB0", [128, 1024], BF16)

    for seq in range(n_seq):
        emit_load_xnT(pb, "l0", xn_b, seq, S, T)
        xp = [pb.sb[f"XP{d}"] for d in range(8)]
        xpk = [f"XP{d}" for d in range(8)]

        for d in range(8):
            g0 = pb.sbuf(f"G{d}", [128, D], BF16)
            pb.g_dma(g0[:, :], inp["L0_attn_w_dec_attn"][d * 128:(d + 1) * 128, :],
                     reads=["in_wdec"], writes=[f"G{d}"])
            g1 = pb.sbuf(f"G{8+d}", [128, D], BF16)
            pb.g_dma(g1[:, :], inp["l0_w2t"][d * 128:(d + 1) * 128, :],
                     reads=["in_w2t"], writes=[f"G{8+d}"])

        e0 = pb.sbuf("E0", [128, 512], BF16)
        e1 = pb.sbuf("QOUT", [128, 512], BF16)
        for tg in range(S // 512):
            tsl = slice(tg * 512, (tg + 1) * 512)
            for do in range(8):
                ps = PS[do % 2]
                psk = f"PS{do % 2}"
                for d in range(8):
                    pb.mm(ps[:, :], pb.sb[f"G{d}"][:, do * 128:(do + 1) * 128],
                          xp[d][:, tsl], start=(d == 0), stop=(d == 7),
                          reads=[f"G{d}", xpk[d]], writes=[psk])
                ev = (e0, "E0") if do % 2 == 0 else (e1, "QOUT")
                pb.op("vector", lambda em, ev=ev[0], ps=ps: em.tensor_copy(ev[:, :], ps[:, :]),
                      reads=[psk], writes=[ev[1]])
                pb.q_dma(qT[do * 128:(do + 1) * 128, seq * S + tg * 512: seq * S + (tg + 1) * 512],
                         ev[0][:, :], reads=[ev[1]], writes=[f"l0_qT@{seq}"])

        for tb in range(S // 128):
            t0 = tb * 128
            for oh in range(2):
                ps = PS[2 + oh]
                psk = f"PS{2+oh}"
                for d in range(8):
                    pb.mm(ps[:, :], xp[d][:, t0:t0 + 128],
                          pb.sb[f"G{8+d}"][:, oh * 512:(oh + 1) * 512],
                          start=(d == 0), stop=(d == 7),
                          reads=[xpk[d], f"G{8+d}"], writes=[psk])
            y2t = pb.sbuf("Y2T", [128, D], BF16)
            pb.op("vector", lambda em, ps=PS[2]: em.tensor_copy(y2t[:, 0:512], ps[:, :]),
                  reads=["PS2"], writes=["Y2T"])
            pb.op("vector", lambda em, ps=PS[3]: em.tensor_copy(y2t[:, 512:1024], ps[:, :]),
                  reads=["PS3"], writes=["Y2T"])
            pb.q_dma(Y2[seq * S + t0: seq * S + t0 + 128, :], y2t[:, :],
                     reads=["Y2T"], writes=[f"l0_Y2@{seq}"])

        for jb in range(nib):
            gi = pb.sbuf(f"G{16 + jb}", [128, D], BF16)
            pb.q_dma(gi[:, :], Y2[seq * S + jb * 128: seq * S + (jb + 1) * 128, :],
                     reads=[f"l0_Y2@{seq}"], writes=[f"G{16+jb}"])
        for d in range(8):
            pb.g_dma(pb.sb[f"G{8+d}"][:, :], inp["l0_w1t"][d * 128:(d + 1) * 128, :],
                     reads=["in_w1t"], writes=[f"G{8+d}"])
        brow = pb.sbuf("BROW", [1, D], BF16)
        pb.g_dma(brow[:, :], inp["L0_attn_b_out"][0:1, :], reads=["in_bout"], writes=["BROW"])

        scores = pb.sbuf("SCORES", [128, 2048], F32)
        pbuf = pb.sbuf("PBUF", [128, 2048], BF16)
        nmx = pb.sbuf("NMX", [128, 1], F32)
        ssum = pb.sbuf("SSUM", [128, 1], F32)
        srcp = pb.sbuf("SRCP", [128, 1], F32)
        qi = [pb.sbuf(f"QI{d}", [128, 128], BF16) for d in range(8)]
        ptile = [pb.sbuf(f"PTS{k}", [128, 128], BF16) for k in range(2)]
        stile = pb.sbuf("STILE", [128, 1024], F32)

        for ib in range(nib):
            Lj = (ib + 1) * 128
            for d in range(8):
                pb.q_dma(qi[d][:, :],
                         qT[d * 128:(d + 1) * 128, seq * S + ib * 128: seq * S + (ib + 1) * 128],
                         reads=[f"l0_qT@{seq}"], writes=[f"QI{d}"])
            ngroups = (Lj + 511) // 512
            for jg in range(ngroups):
                w = min(512, Lj - jg * 512)
                ps = PS[4]
                for d in range(8):
                    pb.mm(ps[:, 0:w], qi[d][:, :], xp[d][:, jg * 512: jg * 512 + w],
                          start=(d == 0), stop=(d == 7),
                          reads=[f"QI{d}", xpk[d]], writes=["PS4"])
                last = (jg == ngroups - 1)
                if last:
                    ds = ib % 4
                    mk = f"mask{ds}"
                    pb.op("vector", lambda em, w=w, jg=jg, mk=mk, ps=ps: em.tensor_add(
                        scores[:, jg * 512: jg * 512 + w], ps[:, 0:w], pb.sb[mk][:, 0:w]),
                        reads=["PS4", mk], writes=["SCORES"])
                    if ib == 0:
                        pb.op("vector", lambda em, ps=ps: em.tensor_copy(
                            scores[0:1, 0:1], ps[0:1, 0:1]),
                            reads=["PS4"], writes=["SCORES"])
                else:
                    pb.op("vector", lambda em, w=w, jg=jg, ps=ps: em.tensor_copy(
                        scores[:, jg * 512: jg * 512 + w], ps[:, 0:w]),
                        reads=["PS4"], writes=["SCORES"])
            pb.op("vector", lambda em, Lj=Lj: em.reduce_max(nmx[:, :], scores[:, 0:Lj], AX.X,
                                                            negate=True),
                  reads=["SCORES"], writes=["NMX"])
            pb.op("scalar", lambda em, Lj=Lj: em.activation(
                pbuf[:, 0:Lj], scores[:, 0:Lj], AF.Exp, bias=nmx[:, 0:1],
                accum_out=ssum[:, 0:1]),
                reads=["SCORES", "NMX"], writes=["PBUF", "SSUM"])
            pb.op("vector", lambda em: em.reciprocal(srcp[:, :], ssum[:, :]),
                  reads=["SSUM"], writes=["SRCP"])
            pb.op("vector", lambda em, Lj=Lj: em.tensor_scalar_mul(
                pbuf[:, 0:Lj], pbuf[:, 0:Lj], srcp[:, 0:1]),
                reads=["PBUF", "SRCP"], writes=["PBUF"])

            for oh in range(2):
                ps = PS[oh]
                psk = f"PS{oh}"
                osl = slice(oh * 512, (oh + 1) * 512)
                pb.mm(ps[:, :], pb.sb["onescol"][:, :], brow[:, osl], start=True, stop=False,
                      reads=["onescol", "BROW"], writes=[psk])
                for d in range(8):
                    pb.mm(ps[:, :], xp[d][:, ib * 128:(ib + 1) * 128],
                          pb.sb[f"G{8+d}"][:, osl], start=False, stop=False,
                          reads=[xpk[d], f"G{8+d}"], writes=[psk])
            for jb in range(ib + 1):
                pt = ptile[jb % 2]
                ptk = f"PTS{jb % 2}"
                pb.transp(PT[:, 0:128], pbuf[:, jb * 128:(jb + 1) * 128],
                          reads=["PBUF"], writes=["PTB0"])
                pb.op("vector", lambda em, pt=pt: em.tensor_copy(pt[:, :], PT[:, 0:128]),
                      reads=["PTB0"], writes=[ptk])
                for oh in range(2):
                    ps = PS[oh]
                    psk = f"PS{oh}"
                    osl = slice(oh * 512, (oh + 1) * 512)
                    lastmm = (jb == ib)
                    pb.mm(ps[:, :], pt[:, :], pb.sb[f"G{16+jb}"][:, osl],
                          start=False, stop=lastmm,
                          reads=[ptk, f"G{16+jb}"], writes=[psk])
            for oh in range(2):
                ps = PS[oh]
                psk = f"PS{oh}"
                ssl = slice(oh * 512, (oh + 1) * 512)
                pb.op("scalar", lambda em, ps=ps, ssl=ssl: em.activation(
                    stile[:, ssl], ps[:, :], AF.Exp, scale=2.0),
                    reads=[psk], writes=["STILE"])
                pb.op("vector", lambda em, ssl=ssl: em.tensor_single_scalar(
                    stile[:, ssl], stile[:, ssl], 1.0, ALU.add),
                    reads=["STILE"], writes=["STILE"])
                pb.op("vector", lambda em, ssl=ssl: em.reciprocal(
                    stile[:, ssl], stile[:, ssl]),
                    reads=["STILE"], writes=["STILE"])
                pb.op("vector", lambda em, ssl=ssl: em.tensor_scalar(
                    stile[:, ssl], stile[:, ssl], -2.0, 1.0, ALU.mult, ALU.add),
                    reads=["STILE"], writes=["STILE"])
            pb.q_dma(s_scr[seq * S + ib * 128: seq * S + (ib + 1) * 128, :], stile[:, :],
                     reads=["STILE"], writes=[f"s_l0@{seq * nib + ib}"])

        hf32 = pb.sbuf("HF32", [128, 512], F32)
        hbf = pb.sbuf("HBF", [128, 512], BF16)
        for fc in range(32):
            for d in range(8):
                pb.g_dma(pb.sb[f"G{d}"][:, 0:128],
                         inp["l0_wfct"][d * 128:(d + 1) * 128, fc * 128:(fc + 1) * 128],
                         reads=["in_wfct"], writes=[f"G{d}"])
            for tg in range(S // 512):
                tsl = slice(tg * 512, (tg + 1) * 512)
                ps = PS[4 + tg % 2]
                psk = f"PS{4 + tg % 2}"
                for d in range(8):
                    pb.mm(ps[:, :], pb.sb[f"G{d}"][:, 0:128], xp[d][:, tsl],
                          start=(d == 0), stop=(d == 7),
                          reads=[f"G{d}", xpk[d]], writes=[psk])
                pb.op("scalar", lambda em, ps=ps: em.activation(hf32[:, :], ps[:, :], AF.Relu),
                      reads=[psk], writes=["HF32"])
                pb.op("scalar", lambda em: em.activation(hbf[:, :], hf32[:, :], AF.Square),
                      reads=["HF32"], writes=["HBF"])
                pb.q_dma(hT[fc, :, seq * S + tg * 512: seq * S + (tg + 1) * 512],
                         hbf[:, :], reads=["HBF"], writes=[f"l0_hT@{fc}"])

        ybuf = pb.dram("l0_y", [T, D], F32)
        for fc in range(32):
            g = pb.sbuf(f"G{fc}", [128, D], BF16)
            pb.g_dma(g[:, :], inp["l0_wprojt"][fc * 128:(fc + 1) * 128, :],
                     reads=["in_wprojt"], writes=[f"G{fc}"])
        ht0 = pb.sbuf("HT0", [128, 128], BF16)
        ht1 = pb.sbuf("HT1", [128, 128], BF16)
        yev = pb.sbuf("STILE", [128, 1024], F32)
        for tb in range(S // 128):
            t0 = seq * S + tb * 128
            for fc in range(32):
                ht = (ht0, "HT0") if fc % 2 == 0 else (ht1, "HT1")
                pb.q_dma(ht[0][:, :], hT[fc, :, t0:t0 + 128], reads=[f"l0_hT@{fc}"],
                         writes=[ht[1]])
                for oh in range(2):
                    ps = PS[oh]
                    psk = f"PS{oh}"
                    pb.mm(ps[:, :], ht[0][:, :], pb.sb[f"G{fc}"][:, oh * 512:(oh + 1) * 512],
                          start=(fc == 0), stop=(fc == 31),
                          reads=[ht[1], f"G{fc}"], writes=[psk])
            pb.op("vector", lambda em: em.tensor_copy(yev[:, 0:512], PS[0][:, :]),
                  reads=["PS0"], writes=["STILE"])
            pb.op("vector", lambda em: em.tensor_copy(yev[:, 512:1024], PS[1][:, :]),
                  reads=["PS1"], writes=["STILE"])
            pb.q_dma(ybuf[t0:t0 + 128, :], yev[:, :], reads=["STILE"], writes=[f"l0_y@{seq}"])

    ybuf = pb.dr["l0_y"]
    cn1 = pb.sbuf("CN1", [128, 1], F32)
    pb.op("gpsimd", lambda em: em.memset(cn1[:, :], -1.0), writes=["CN1"])

    for i in range(nt):
        sl = i % 2
        yt = pb.sb[f"FP{sl}"]
        mem = pb.sb[f"FP{2+sl}"]
        acc = pb.sb[f"FP{4+sl}"]
        if sl == 0:
            spk = pb.sbuf("STILE", [128, 1024], F32)
            kp = "STILE"
        else:
            spk = pb.sbuf("CONVA", [128, 2048], F32)[:, 0:1024]
            kp = "CONVA"
        ky, km, ka = f"FP{sl}", f"FP{2+sl}", f"FP{4+sl}"
        r = slice(i * 128, (i + 1) * 128)
        eng = "vector" if i % 2 == 0 else "gpsimd"
        pb.q_dma(yt[:, :], ybuf[r, :], reads=[f"l0_y@{0}", f"l0_y@{n_seq-1}"], writes=[ky])

        def sgt(dst, src, kd, ks_):
            if eng == "vector":
                pb.op(eng, lambda em, d=dst, s=src: em.tensor_single_scalar(
                    d[:, :], s[:, :], 1.0, ALU.is_gt), reads=[ks_], writes=[kd])
            else:
                pb.op("scalar", lambda em, d=dst, s=src: em.sign(
                    d[:, :], s[:, :], bias=cn1[:, 0:1]), reads=[ks_, "CN1"], writes=[kd])
                pb.op("scalar", lambda em, d=dst: em.activation(
                    d[:, :], d[:, :], AF.Relu), reads=[kd], writes=[kd])

        def scl(dst_t, src_t, kd, ks_, cval):
            pb.op("scalar", lambda em, d=dst_t, s=src_t, cval=cval: em.activation(
                d[:, :], s[:, :], AF.Identity, scale=cval), reads=[ks_], writes=[kd])

        sgt(spk, yt, kp, ky)
        pb.op(eng, lambda em, a=acc, s=spk: em.tensor_copy(a[:, :], s[:, :]),
              reads=[kp], writes=[ka])
        pb.op(eng, lambda em, m=mem, y=yt: em.tensor_copy(m[:, :], y[:, :]),
              reads=[ky], writes=[km])
        for step in range(4):
            if eng == "vector":
                pb.op(eng, lambda em, m=mem, y=yt: em.scalar_tensor_tensor(
                    m[:, :], m[:, :], 0.9, y[:, :], ALU.mult, ALU.add),
                    reads=[km, ky], writes=[km])
            else:
                scl(mem, mem, km, km, 0.9)
                pb.op(eng, lambda em, m=mem, y=yt: em.tensor_add(m[:, :], m[:, :], y[:, :]),
                      reads=[km, ky], writes=[km])
            pb.op(eng, lambda em, m=mem, s=spk: em.tensor_sub(m[:, :], m[:, :], s[:, :]),
                  reads=[km, kp], writes=[km])
            sgt(spk, mem, kp, km)
            pb.op(eng, lambda em, a=acc, s=spk: em.tensor_add(a[:, :], a[:, :], s[:, :]),
                  reads=[ka, kp], writes=[ka])
        if eng == "vector":
            pb.op(eng, lambda em, a=acc: em.tensor_single_scalar(
                a[:, :], a[:, :], 0.2, ALU.mult), reads=[ka], writes=[ka])
        else:
            scl(acc, acc, ka, ka, 0.2)
        pb.q_dma(pb.dr["g_l0"][r, :], acc[:, :], reads=[ka], writes=[f"g_l0@{i}"])

    emit_combine(pb, "l0", xres, pb.dr["s_l0"], pb.dr["g_l0"], T)


QMIDS = [(0.75, 0.5), (1.25, 0.5), (1.75, 0.5), (2.5, 1.0), (3.5, 1.0), (5.0, 2.0)]


def emit_quant(pb, src, dst, W, rk, wk, uniq):
    nb = W // 16
    mx = pb.sbuf(f"QMX{uniq}", [128, 32], F32)
    _cvr = pb.sbuf("CONVR", [128, 2112], BF16)
    ub = _cvr[:, 0:512]
    ab = _cvr[:, 512:1024]
    ac = pb.sbuf("E0", [128, 512], BF16)
    cb = pb.sbuf("HBF", [128, 512], BF16)
    kmx, ku, ka, kac, kc = (f"QMX{uniq}", "CONVR", "CONVR", "E0", "HBF")
    assert nb <= 32 and W <= 512
    pb.op("vector", lambda em: em.tensor_reduce(
        mx[:, 0:nb], src[:, 0:W].rearrange("p (b c) -> p b c", c=16), AX.X, ALU.max,
        apply_absolute_value=True), reads=[rk], writes=[kmx])
    pb.op("vector", lambda em: em.tensor_scalar(
        mx[:, 0:nb], mx[:, 0:nb], 1.0 / 6.0, 1e-12, ALU.mult, ALU.max),
        reads=[kmx], writes=[kmx])
    sb_ = lambda t: bass.AP(t, 0, [[32, 128], [1, nb], [0, 16]])
    r3 = lambda t: t[:, 0:W].rearrange("p (b c) -> p b c", c=16)
    pb.op("vector", lambda em: em.reciprocal(mx[:, 0:nb], mx[:, 0:nb]),
          reads=[kmx], writes=[kmx])
    pb.op("vector", lambda em: em.tensor_tensor(
        r3(ub), r3(src), sb_(mx), ALU.mult), reads=[rk, kmx], writes=[ku])
    pb.op("scalar", lambda em: em.activation(ab[:, 0:W], ub[:, 0:W], AF.Abs),
          reads=[ku], writes=[ka])
    pb.op("vector", lambda em: em.tensor_scalar(
        ac[:, 0:W], ab[:, 0:W], 0.25, 0.5, ALU.is_gt, ALU.mult),
        reads=[ka], writes=[kac])
    for m, dl in QMIDS:
        pb.op("vector", lambda em, m=m, dl=dl: em.tensor_scalar(
            cb[:, 0:W], ab[:, 0:W], m, dl, ALU.is_gt, ALU.mult),
            reads=[ka], writes=[kc])
        pb.op("vector", lambda em: em.tensor_add(ac[:, 0:W], ac[:, 0:W], cb[:, 0:W]),
              reads=[kac, kc], writes=[kac])
    pb.op("scalar", lambda em: em.activation(ab[:, 0:W], ub[:, 0:W], AF.Sign),
          reads=[ku], writes=[ka])
    pb.op("vector", lambda em: em.tensor_mul(ac[:, 0:W], ac[:, 0:W], ab[:, 0:W]),
          reads=[kac, ka], writes=[kac])
    pb.op("vector", lambda em: em.reciprocal(mx[:, 0:nb], mx[:, 0:nb]),
          reads=[kmx], writes=[kmx])
    pb.op("vector", lambda em: em.tensor_tensor(
        r3(dst), r3(ac), sb_(mx), ALU.mult),
        reads=[kac, kmx], writes=[wk])


def emit_quant_dram(pb, src_dram, dst_dram, rows, W, rk, wk):
    qin = pb.sbuf("HF32", [128, 512], F32)
    qout = pb.sbuf("QOUT", [128, 512], BF16)
    for i in range(rows // 128):
        r = slice(i * 128, (i + 1) * 128)
        for c0 in range(0, W, 512):
            cw = min(512, W - c0)
            pb.q_dma(qin[:, 0:cw], src_dram[r, c0:c0 + cw], reads=[rk], writes=["HF32"])
            emit_quant(pb, qin, qout, cw, "HF32", "QOUT", "q")
            pb.q_dma(dst_dram[r, c0:c0 + cw], qout[:, 0:cw], reads=["QOUT"], writes=[wk])


def emit_incl_masks(pb):
    for ds in range(4):
        t = pb.sbuf(f"mi{ds}", [128, 512], BF16)
        pb.op("gpsimd", lambda em, t=t, ds=ds: em.iota(
            t[:, :], [[1, 512]], base=-128 * ds, channel_multiplier=-1,
            allow_small_or_imprecise_dtypes=True), writes=[f"mi{ds}"])
        pb.op("vector", lambda em, t=t: em.tensor_single_scalar(t[:, :], t[:, :], 0.0, ALU.is_gt),
              reads=[f"mi{ds}"], writes=[f"mi{ds}"])
        pb.op("vector", lambda em, t=t: em.tensor_single_scalar(t[:, :], t[:, :], -1e30, ALU.mult),
              reads=[f"mi{ds}"], writes=[f"mi{ds}"])


def emit_L3(pb, inp, n_seq, S):
    T = n_seq * S
    nt = T // 128
    nib = S // 128
    xres = pb.dr["xres"]
    xn_f, xn_b = emit_ln(pb, "l3", "xres", xres, T)

    vT = pb.dram("l3_vT", [D, T], BF16)
    xq = pb.dram("l3_xq", [T, D], BF16)
    hq = pb.dram("l3_hq", [T, 2816], BF16)
    wgq = pb.dram("l3_wgq", [2816, D], BF16)
    wuq = pb.dram("l3_wuq", [2816, D], BF16)
    wdq = pb.dram("l3_wdq", [D, 2816], BF16)
    s_scr = pb.dram("s_l3", [T, D], F32)
    g_scr = pb.dram("g_l3", [T, D], F32)
    PS = [pb.psum(f"PS{i}", [128, 512], F32) for i in range(6)]
    PT = pb.psum("PTB0", [128, 1024], BF16)

    emit_quant_dram(pb, inp["L3_ffn_w_gate"], wgq, 2816, D, "in_wg", "l3_wgq")
    emit_quant_dram(pb, inp["L3_ffn_w_up"], wuq, 2816, D, "in_wu", "l3_wuq")
    emit_quant_dram(pb, inp["L3_ffn_w_down"], wdq, D, 2816, "in_wd", "l3_wdq")

    qin = pb.sbuf("HF32", [128, 512], F32)
    qout = pb.sbuf("QOUT", [128, 512], BF16)
    for i in range(nt):
        r = slice(i * 128, (i + 1) * 128)
        for c0 in range(0, D, 512):
            pb.q_dma(qin[:, :], xn_f[r, c0:c0 + 512], reads=[f"xnf_l3@{i}"], writes=["HF32"])
            emit_quant(pb, qin, qout, 512, "HF32", "QOUT", "q")
            pb.q_dma(xq[r, c0:c0 + 512], qout[:, :], reads=["QOUT"], writes=[f"l3_xq@{i}"])

    LT = pb.sbuf("LT", [64, 2048], BF16)
    brow = pb.sbuf("BROW", [1, D], BF16)
    bvT = pb.sbuf("BVT", [128, 8], F32)
    pb.q_dma(bvT[:, :], inp["l3_bvT"][:, :], reads=["in_bvT"], writes=["BVT"])

    for seq in range(n_seq):
        emit_load_xnT(pb, "l3", xn_b, seq, S, T)
        xp = [pb.sb[f"XP{d}"] for d in range(8)]
        xpk = [f"XP{d}" for d in range(8)]

        wb_t = pb.sbuf("CONVR", [128, 2112], BF16)[:, 1024:1536]
        for d in range(8):
            pb.g_dma(wb_t[:, d * 64:(d + 1) * 64],
                     inp["l3_wbasisT"][d * 128:(d + 1) * 128, :],
                     reads=["in_wbasisT"], writes=["CONVR"])
            g = pb.sbuf(f"G{24+d}", [128, D], BF16)
            pb.g_dma(g[:, :], inp["l3_wvT"][d * 128:(d + 1) * 128, :],
                     reads=["in_wvT"], writes=[f"G{24+d}"])

        ev = pb.sbuf("E0", [128, 512], BF16)
        for tg in range(S // 512):
            tsl = slice(tg * 512, (tg + 1) * 512)
            ps = PS[4]
            for d in range(8):
                pb.mm(ps[0:64, :], wb_t[:, d * 64:(d + 1) * 64], xp[d][:, tsl],
                      start=(d == 0), stop=(d == 7), reads=["CONVR", xpk[d]], writes=["PS4"])
            pb.op("vector", lambda em, tsl=tsl, ps=ps: em.tensor_copy(LT[:, tsl], ps[0:64, :]),
                  reads=["PS4"], writes=["LT"])

        for do in range(8):
            for tg in range(S // 512):
                tsl = slice(tg * 512, (tg + 1) * 512)
                ps = PS[4 + tg % 2]
                psk = f"PS{4 + tg % 2}"
                for d in range(8):
                    pb.mm(ps[:, :], pb.sb[f"G{24+d}"][:, do * 128:(do + 1) * 128],
                          xp[d][:, tsl], start=(d == 0), stop=(d == 7),
                          reads=[f"G{24+d}", xpk[d]], writes=[psk])
                pb.op("scalar", lambda em, ps=ps, do=do: em.activation(
                    ev[:, :], ps[:, :], AF.Identity, bias=bvT[:, do:do + 1]),
                    reads=[psk, "BVT"], writes=["E0"])
                pb.q_dma(vT[do * 128:(do + 1) * 128, seq * S + tg * 512:seq * S + (tg + 1) * 512],
                         ev[:, :], reads=["E0"], writes=[f"l3_vT@{seq}"])

        for hp in range(8):
            for jb in range(nib):
                pb.q_dma(xp[hp][:, jb * 128:(jb + 1) * 128],
                         vT[hp * 128:(hp + 1) * 128,
                            seq * S + jb * 128:seq * S + (jb + 1) * 128],
                         reads=[f"l3_vT@{seq}"], writes=[xpk[hp]], transpose=True)

        for d in range(8):
            pb.g_dma(pb.sb[f"G{24+d}"][:, :], inp["l3_woT"][d * 128:(d + 1) * 128, :],
                     reads=["in_woT"], writes=[f"G{24+d}"])
        pb.g_dma(brow[:, :], inp["L3_attn_b_o"][0:1, :], reads=["in_bo"], writes=["BROW"])

        scores = pb.sbuf("SCORES", [128, 2048], F32)
        pbuf = pb.sbuf("PBUF", [128, 2048], BF16)
        nmx = pb.sbuf("NMX", [128, 1], F32)
        ssum = pb.sbuf("SSUM", [128, 1], F32)
        srcp = pb.sbuf("SRCP", [128, 1], F32)
        pts = [pb.sbuf(f"QI{k}", [128, 128], BF16) for k in range(8)] + \
              [pb.sbuf(f"PTS{k}", [128, 128], BF16) for k in range(8, 16)]
        ptsk = [f"QI{k}" if k < 8 else f"PTS{k}" for k in range(16)]
        _e0t = pb.sbuf("E0", [128, 512], BF16)
        ots = [pb.sbuf(f"OTS{c}", [128, 128], BF16) for c in range(6)] + \
              [_e0t[:, 0:128], _e0t[:, 128:256]]
        otsk = [f"OTS{c}" for c in range(6)] + ["E0", "E0"]
        stile = pb.sbuf("STILE", [128, 1024], F32)

        for ib in range(nib):
            Lj = (ib + 1) * 128
            isl = slice(ib * 128, (ib + 1) * 128)
            ngroups = (Lj + 511) // 512
            for jg in range(ngroups):
                w = min(512, Lj - jg * 512)
                ps = PS[4]
                pb.mm(ps[:, 0:w], LT[:, isl], LT[:, jg * 512:jg * 512 + w],
                      start=True, stop=True, reads=["LT"], writes=["PS4"])
                last = (jg == ngroups - 1)
                mk = f"mi{ib % 4}" if last else None
                if last:
                    pb.op("vector", lambda em, w=w, jg=jg, mk=mk, ps=ps: em.scalar_tensor_tensor(
                        scores[:, jg * 512:jg * 512 + w], ps[:, 0:w], 0.125,
                        pb.sb[mk][:, 0:w], ALU.mult, ALU.add),
                        reads=["PS4", mk], writes=["SCORES"])
                else:
                    pb.op("vector", lambda em, w=w, jg=jg, ps=ps: em.tensor_scalar_mul(
                        scores[:, jg * 512:jg * 512 + w], ps[:, 0:w], 0.125),
                        reads=["PS4"], writes=["SCORES"])
            pb.op("vector", lambda em, Lj=Lj: em.reduce_max(nmx[:, :], scores[:, 0:Lj], AX.X,
                                                            negate=True),
                  reads=["SCORES"], writes=["NMX"])
            pb.op("scalar", lambda em, Lj=Lj: em.activation(
                pbuf[:, 0:Lj], scores[:, 0:Lj], AF.Exp, bias=nmx[:, 0:1],
                accum_out=ssum[:, 0:1]),
                reads=["SCORES", "NMX"], writes=["PBUF", "SSUM"])
            pb.op("vector", lambda em: em.reciprocal(srcp[:, :], ssum[:, :]),
                  reads=["SSUM"], writes=["SRCP"])
            pb.op("vector", lambda em, Lj=Lj: em.tensor_scalar_mul(
                pbuf[:, 0:Lj], pbuf[:, 0:Lj], srcp[:, 0:1]),
                reads=["PBUF", "SRCP"], writes=["PBUF"])
            for jb in range(ib + 1):
                pb.transp(PT[:, 0:128], pbuf[:, jb * 128:(jb + 1) * 128],
                          reads=["PBUF"], writes=["PTB0"])
                pb.op("vector", lambda em, jb=jb: em.tensor_copy(pts[jb][:, :], PT[:, 0:128]),
                      reads=["PTB0"], writes=[ptsk[jb]])
            for h in range(16):
                hp, hh = h // 2, h % 2
                ps = PS[5]
                for jb in range(ib + 1):
                    pb.mm(ps[0:64, 0:128],
                          xp[hp][:, jb * 128 + hh * 64:jb * 128 + hh * 64 + 64],
                          pts[jb][:, :], start=(jb == 0), stop=(jb == ib),
                          reads=[xpk[hp], ptsk[jb]], writes=["PS5"])
                pb.op("vector", lambda em, h=h, ps=ps: em.tensor_copy(
                    ots[h // 2][(h % 2) * 64:(h % 2) * 64 + 64, :], ps[0:64, 0:128]),
                    reads=["PS5"], writes=[otsk[h // 2]])
            for oh in range(2):
                ps = PS[oh]
                psk = f"PS{oh}"
                osl = slice(oh * 512, (oh + 1) * 512)
                pb.mm(ps[:, :], pb.sb["onescol"][:, :], brow[:, osl], start=True, stop=False,
                      reads=["onescol", "BROW"], writes=[psk])
                for c in range(8):
                    pb.mm(ps[:, :], ots[c][:, :], pb.sb[f"G{24+c}"][:, osl],
                          start=False, stop=(c == 7),
                          reads=[otsk[c], f"G{24+c}"], writes=[psk])
                pb.op("scalar", lambda em, ps=ps, oh=oh: em.copy(
                    stile[:, oh * 512:(oh + 1) * 512], ps[:, :]),
                    reads=[psk], writes=["STILE"])
            pb.q_dma(s_scr[seq * S + ib * 128:seq * S + (ib + 1) * 128, :], stile[:, :],
                     reads=["STILE"], writes=[f"s_l3@{seq * nib + ib}"])

        for d in range(8):
            pb.q_dma(xp[d][:, 0:S], xq[seq * S:(seq + 1) * S, d * 128:(d + 1) * 128],
                     reads=rd_all("l3_xq", nt), writes=[xpk[d]], transpose=True)
        eg = pb.sbuf("EG", [128, 512], F32)
        eu = pb.sbuf("HF32", [128, 512], F32)
        hf = pb.sbuf("CONVA", [128, 2048], F32)[:, 0:512]
        hqb = pb.sbuf("CONVR", [128, 2112], BF16)[:, 1536:2048]
        fslices = [(k * 512, min(512, 2816 - k * 512)) for k in range(6)]
        for f0, fw in fslices:
            for d in range(8):
                g = pb.sbuf(f"G{d}", [128, D], BF16)
                pb.q_dma(g[:, 0:fw], wgq[f0:f0 + fw, d * 128:(d + 1) * 128],
                         reads=["l3_wgq"], writes=[f"G{d}"], transpose=True)
                g2 = pb.sbuf(f"G{8+d}", [128, D], BF16)
                pb.q_dma(g2[:, 0:fw], wuq[f0:f0 + fw, d * 128:(d + 1) * 128],
                         reads=["l3_wuq"], writes=[f"G{8+d}"], transpose=True)
            for tb in range(nib):
                t0 = seq * S + tb * 128
                tloc = slice(tb * 128, (tb + 1) * 128)
                for d in range(8):
                    pb.mm(PS[0][:, 0:fw], xp[d][:, tloc], pb.sb[f"G{d}"][:, 0:fw],
                          start=(d == 0), stop=(d == 7), reads=[xpk[d], f"G{d}"],
                          writes=["PS0"])
                    pb.mm(PS[1][:, 0:fw], xp[d][:, tloc], pb.sb[f"G{8+d}"][:, 0:fw],
                          start=(d == 0), stop=(d == 7), reads=[xpk[d], f"G{8+d}"],
                          writes=["PS1"])
                pb.op("scalar", lambda em, fw=fw: em.activation(eg[:, 0:fw], PS[0][:, 0:fw],
                                                                AF.Silu),
                      reads=["PS0"], writes=["EG"])
                pb.op("vector", lambda em, fw=fw: em.tensor_copy(eu[:, 0:fw], PS[1][:, 0:fw]),
                      reads=["PS1"], writes=["HF32"])
                pb.op("vector", lambda em, fw=fw: em.tensor_mul(hf[:, 0:fw], eg[:, 0:fw],
                                                                eu[:, 0:fw]),
                      reads=["EG", "HF32"], writes=["CONVA"])
                emit_quant(pb, hf, hqb, fw, "CONVA", "CONVR", "q")
                pb.q_dma(hq[t0:t0 + 128, f0:f0 + fw], hqb[:, 0:fw],
                         reads=["CONVR"], writes=[f"l3_hq@{seq}"])
        for fc in range(22):
            g = pb.sbuf(f"G{fc}", [128, D], BF16)
            pb.q_dma(g[:, :], wdq[0:D, fc * 128:(fc + 1) * 128],
                     reads=["l3_wdq"], writes=[f"G{fc}"], transpose=True)
        _e0b = pb.sbuf("E0", [128, 512], BF16)
        hti = [_e0b[:, 256:384], _e0b[:, 384:512]]
        gev = pb.sbuf("STILE", [128, 1024], F32)
        for tb in range(nib):
            t0 = seq * S + tb * 128
            for fc in range(22):
                ht = hti[fc % 2]
                htk = "E0"
                pb.q_dma(ht[:, :], hq[t0:t0 + 128, fc * 128:(fc + 1) * 128],
                         reads=[f"l3_hq@{seq}"], writes=[htk], transpose=True)
                for oh in range(2):
                    pb.mm(PS[oh][:, :], ht[:, :], pb.sb[f"G{fc}"][:, oh * 512:(oh + 1) * 512],
                          start=(fc == 0), stop=(fc == 21),
                          reads=[htk, f"G{fc}"], writes=[f"PS{oh}"])
            pb.op("vector", lambda em: em.tensor_copy(gev[:, 0:512], PS[0][:, :]),
                  reads=["PS0"], writes=["STILE"])
            pb.op("vector", lambda em: em.tensor_copy(gev[:, 512:1024], PS[1][:, :]),
                  reads=["PS1"], writes=["STILE"])
            pb.q_dma(g_scr[t0:t0 + 128, :], gev[:, :], reads=["STILE"],
                     writes=[f"g_l3@{(t0) // 128}"])

    emit_combine(pb, "l3", xres, s_scr, g_scr, T)


def prep_weights_L3(w):
    out = {}
    out["l3_wbasisT"] = np.ascontiguousarray(np.asarray(w["L3_attn_W_basis"]).T)
    out["l3_wvT"] = np.ascontiguousarray(np.asarray(w["L3_attn_W_v"]).T)
    out["l3_bvT"] = np.ascontiguousarray(np.asarray(w["L3_attn_b_v"]).reshape(8, 128).T)
    out["l3_woT"] = np.ascontiguousarray(np.asarray(w["L3_attn_W_o"]).T)
    out["L3_attn_b_o"] = np.ascontiguousarray(np.asarray(w["L3_attn_b_o"]).reshape(1, D))
    out["L3_ffn_w_gate"] = np.ascontiguousarray(np.asarray(w["L3_ffn_w_gate"]))
    out["L3_ffn_w_up"] = np.ascontiguousarray(np.asarray(w["L3_ffn_w_up"]))
    out["L3_ffn_w_down"] = np.ascontiguousarray(np.asarray(w["L3_ffn_w_down"]))
    return out


def declare_inputs_L3(pb):
    inp = {}
    inp["l3_wbasisT"] = pb.dram("l3_wbasisT", [D, 64], F32, kind="ExternalInput")
    inp["l3_wvT"] = pb.dram("l3_wvT", [D, D], F32, kind="ExternalInput")
    inp["l3_bvT"] = pb.dram("l3_bvT", [128, 8], F32, kind="ExternalInput")
    inp["l3_woT"] = pb.dram("l3_woT", [D, D], F32, kind="ExternalInput")
    inp["L3_attn_b_o"] = pb.dram("L3_attn_b_o", [1, D], F32, kind="ExternalInput")
    inp["L3_ffn_w_gate"] = pb.dram("L3_ffn_w_gate", [2816, D], F32, kind="ExternalInput")
    inp["L3_ffn_w_up"] = pb.dram("L3_ffn_w_up", [2816, D], F32, kind="ExternalInput")
    inp["L3_ffn_w_down"] = pb.dram("L3_ffn_w_down", [D, 2816], F32, kind="ExternalInput")
    return inp




SHIFT_ID_MAP = {}


def emit_shift_idents(pb):
    hosts = ([f"PTS{k}" for k in range(8, 16)] + [f"OTS{c}" for c in range(6)])
    for c in range(6):
        pb.sbuf(f"OTS{c}", [128, 128], BF16)
    for k in range(8, 16):
        pb.sbuf(f"PTS{k}", [128, 128], BF16)
    hi = 0
    for sz in [1, 2, 4, 8, 16, 32, 64]:
        for kind in ("IU", "ID"):
            SHIFT_ID_MAP[f"{kind}{sz}"] = hosts[hi]
            hi += 1
    for sz in [1, 2, 4, 8, 16, 32, 64]:
        up = pb.sb[SHIFT_ID_MAP[f"IU{sz}"]]
        dn = pb.sb[SHIFT_ID_MAP[f"ID{sz}"]]
        ku, kd = SHIFT_ID_MAP[f"IU{sz}"], SHIFT_ID_MAP[f"ID{sz}"]
        pb.op("vector", lambda em, up=up: em.memset(up[:, :], 0.0), writes=[ku])
        pb.op("vector", lambda em, dn=dn: em.memset(dn[:, :], 0.0), writes=[kd])
        pb.op("vector", lambda em, up=up, sz=sz: em.tensor_copy(
            up[:, sz:128], pb.sb["ident"][:, 0:128 - sz]),
            reads=["ident", ku], writes=[ku])
        pb.op("vector", lambda em, dn=dn, sz=sz: em.tensor_copy(
            dn[:, 0:128 - sz], pb.sb["ident"][:, sz:128]),
            reads=["ident", kd], writes=[kd])


def emit_butterfly_fold(pb, tag, loc_cols, cross_ap, w_src_ap, w_dst, transposed,
                        rk_loc, rk_cross, rk_w, wk_dst, bias_srcs=None,
                        bias_dst=None, bias_rks=None):
    xpn = [f"XP{d}" for d in range(8)]
    xp = [pb.sbuf(n, [128, 2048], BF16) for n in xpn]

    def blk(pair, b):
        return xp[pair * 2 + b // 16][:, (b % 16) * 128:(b % 16 + 1) * 128]

    def blkk(pair, b):
        return xpn[pair * 2 + b // 16]

    PT = pb.psum("PTB0", [128, 1024], BF16)
    PS0 = pb.psum("PS0", [128, 512], F32)
    PS1 = pb.psum("PS1", [128, 512], F32)
    lcbuf = pb.sbuf("SCORES", [128, 2048], F32)
    lc = lcbuf[:, 0:1120]
    pb.q_dma(lc, loc_cols[:, :], reads=[rk_loc], writes=["SCORES"])

    def build_stage(s, into_pair, transp):
        sz = 2 ** s
        kiu, kid = SHIFT_ID_MAP[f"IU{sz}"], SHIFT_ID_MAP[f"ID{sz}"]
        iu, idn = pb.sb[kiu], pb.sb[kid]
        kd, ku, kn = (0, 3, 4) if transp else (0, 1, 2)
        for b in range(32):
            t = blk(into_pair, b)
            tk = blkk(into_pair, b)
            cd = lc[:, s * 160 + kd * 32 + b:s * 160 + kd * 32 + b + 1]
            cu = lc[:, s * 160 + ku * 32 + b:s * 160 + ku * 32 + b + 1]
            cn = lc[:, s * 160 + kn * 32 + b:s * 160 + kn * 32 + b + 1]
            pb.op("vector", lambda em, t=t, cd=cd: em.tensor_scalar_mul(
                t, pb.sb["ident"][:, :], cd), reads=["ident", "SCORES"], writes=[tk])
            pb.op("vector", lambda em, t=t, cu=cu, iu=iu: em.scalar_tensor_tensor(
                t, iu[:, :], cu, t, ALU.mult, ALU.add),
                reads=[kiu, "SCORES", tk], writes=[tk])
            pb.op("vector", lambda em, t=t, cn=cn, idn=idn: em.scalar_tensor_tensor(
                t, idn[:, :], cn, t, ALU.mult, ALU.add),
                reads=[kid, "SCORES", tk], writes=[tk])

    build_stage(0, 1, transp=False)
    for k in range(1, 7):
        build_stage(k, 0, transp=True)
        for b in range(32):
            pb.mm(PS0[:, 0:128], blk(0, b), blk(1, b), start=True, stop=True,
                  reads=[blkk(0, b), blkk(1, b)], writes=["PS0"])
            pb.op("vector", lambda em, b=b: em.tensor_copy(blk(2, b), PS0[:, 0:128]),
                  reads=["PS0"], writes=[blkk(2, b)])
        for t in range(2):
            pb.op("vector", lambda em, t=t: em.tensor_copy(xp[2 + t][:, :], xp[4 + t][:, :]),
                  reads=[xpn[4 + t]], writes=[xpn[2 + t]])
    if not transposed:
        for b in range(32):
            pb.transp(PT[:, 0:128], blk(1, b), reads=[blkk(1, b)], writes=["PTB0"])
            pb.op("vector", lambda em, b=b: em.tensor_copy(blk(3, b), PT[:, 0:128]),
                  reads=["PTB0"], writes=[blkk(3, b)])
        lhs_pair = 3
    else:
        lhs_pair = 1

    G = [pb.sbuf(f"G{i}", [128, D], BF16) for i in range(32)]
    wtmp = pb.sbuf("WTMP", [128, D], BF16)
    for b in range(32):
        pb.g_dma(wtmp[:, :], w_src_ap[b * 128:(b + 1) * 128, :], reads=[rk_w],
                 writes=["WTMP"])
        if transposed:
            pb.op("vector", lambda em, b=b: em.tensor_copy(G[b][:, :], wtmp[:, :]),
                  reads=["WTMP"], writes=[f"G{b}"])
        else:
            for oh in range(2):
                ps = (PS0, "PS0") if oh == 0 else (PS1, "PS1")
                pb.mm(ps[0][:, :], blk(lhs_pair, b), wtmp[:, oh * 512:(oh + 1) * 512],
                      start=True, stop=True,
                      reads=[blkk(lhs_pair, b), "WTMP"], writes=[ps[1]])
                pb.op("vector", lambda em, b=b, oh=oh, ps=ps: em.tensor_copy(
                    G[b][:, oh * 512:(oh + 1) * 512], ps[0][:, :]),
                    reads=[ps[1]], writes=[f"G{b}"])

    cc = lcbuf[:, 1120:1440]
    pb.q_dma(cc, cross_ap[:, :], reads=[rk_cross], writes=["SCORES"])
    ctmp1 = pb.sbuf("CT1", [128, D], BF16)
    ctmp2 = pb.sbuf("CT2", [128, D], BF16)
    stages = [11, 10, 9, 8, 7] if transposed else [7, 8, 9, 10, 11]
    for s in stages:
        cs = 2 ** (s - 7)
        pairs = [c for c in range(32) if not (c & cs)]
        for pi, cA in enumerate(pairs):
            cB = cA | cs
            col = (s - 7) * 64 + pi * 4
            i01, i10 = (2, 1) if transposed else (1, 2)
            w00 = cc[:, col + 0:col + 1]
            w01 = cc[:, col + i01:col + i01 + 1]
            w10 = cc[:, col + i10:col + i10 + 1]
            w11 = cc[:, col + 3:col + 4]
            pb.op("vector", lambda em, cB=cB, w01=w01: em.tensor_scalar_mul(
                ctmp1[:, :], G[cB][:, :], w01), reads=[f"G{cB}", "SCORES"], writes=["CT1"])
            pb.op("vector", lambda em, cA=cA, w10=w10: em.tensor_scalar_mul(
                ctmp2[:, :], G[cA][:, :], w10), reads=[f"G{cA}", "SCORES"], writes=["CT2"])
            pb.op("vector", lambda em, cA=cA, w00=w00: em.scalar_tensor_tensor(
                G[cA][:, :], G[cA][:, :], w00, ctmp1[:, :], ALU.mult, ALU.add),
                reads=[f"G{cA}", "SCORES", "CT1"], writes=[f"G{cA}"])
            pb.op("vector", lambda em, cB=cB, w11=w11: em.scalar_tensor_tensor(
                G[cB][:, :], G[cB][:, :], w11, ctmp2[:, :], ALU.mult, ALU.add),
                reads=[f"G{cB}", "SCORES", "CT2"], writes=[f"G{cB}"])

    if transposed:
        for b in range(32):
            for oh in range(2):
                ps = (PS0, "PS0") if oh == 0 else (PS1, "PS1")
                pb.mm(ps[0][:, :], blk(lhs_pair, b), G[b][:, oh * 512:(oh + 1) * 512],
                      start=True, stop=True,
                      reads=[blkk(lhs_pair, b), f"G{b}"], writes=[ps[1]])
                pb.op("vector", lambda em, b=b, oh=oh, ps=ps: em.tensor_copy(
                    G[b][:, oh * 512:(oh + 1) * 512], ps[0][:, :]),
                    reads=[ps[1]], writes=[f"G{b}"])

    for b in range(32):
        pb.q_dma(w_dst[b * 128:(b + 1) * 128, :], G[b][:, :], reads=[f"G{b}"],
                 writes=[wk_dst])

    if bias_srcs is not None:
        b0, bup = bias_srcs
        rb0, rbup = bias_rks
        bcol = pb.sbuf("BCOL", [128, 32], F32)
        bcu = pb.sbuf("BCU", [128, 32], F32)
        pb.q_dma(bcol[:, :], b0[:, :], reads=[rb0], writes=["BCOL"])
        pb.q_dma(bcu[:, :], bup[:, :], reads=[rbup], writes=["BCU"])
        bb = pb.sbuf("BB", [128, 32], BF16)
        pb.op("vector", lambda em: em.tensor_copy(bb[:, :], bcol[:, :]),
              reads=["BCOL"], writes=["BB"])
        for b in range(32):
            pb.mm(PS0[:, 0:1], blk(3, b), bb[:, b:b + 1], start=True, stop=True,
                  reads=[blkk(3, b), "BB"], writes=["PS0"])
            pb.op("vector", lambda em, b=b: em.tensor_copy(bcol[:, b:b + 1], PS0[:, 0:1]),
                  reads=["PS0"], writes=["BCOL"])
        for s in [7, 8, 9, 10, 11]:
            cs = 2 ** (s - 7)
            pairs = [c for c in range(32) if not (c & cs)]
            for pi, cA in enumerate(pairs):
                cB = cA | cs
                col = (s - 7) * 64 + pi * 4
                pb.op("vector", lambda em, cB=cB, col=col: em.tensor_scalar_mul(
                    bb[:, 0:1], bcol[:, cB:cB + 1], cc[:, col + 1:col + 2]),
                    reads=["BCOL", "SCORES"], writes=["BB"])
                pb.op("vector", lambda em, cA=cA, col=col: em.tensor_scalar_mul(
                    bb[:, 1:2], bcol[:, cA:cA + 1], cc[:, col + 2:col + 3]),
                    reads=["BCOL", "SCORES"], writes=["BB"])
                pb.op("vector", lambda em, cA=cA, col=col: em.scalar_tensor_tensor(
                    bcol[:, cA:cA + 1], bcol[:, cA:cA + 1], cc[:, col:col + 1],
                    bb[:, 0:1], ALU.mult, ALU.add),
                    reads=["BCOL", "SCORES", "BB"], writes=["BCOL"])
                pb.op("vector", lambda em, cB=cB, col=col: em.scalar_tensor_tensor(
                    bcol[:, cB:cB + 1], bcol[:, cB:cB + 1], cc[:, col + 3:col + 4],
                    bb[:, 1:2], ALU.mult, ALU.add),
                    reads=["BCOL", "SCORES", "BB"], writes=["BCOL"])
        pb.op("vector", lambda em: em.tensor_add(bias_dst[:, :], bcol[:, :], bcu[:, :]),
              reads=["BCOL", "BCU"], writes=["BT32"])


def emit_L2(pb, inp, n_seq, S):
    T = n_seq * S
    nt = T // 128
    nib = S // 128
    xres = pb.dr["xres"]

    wexpF = pb.dram("l2_wexpF", [4 * D, D], BF16)
    wconF = pb.dram("l2_wconF", [4 * D, D], BF16)
    bt32 = pb.sbuf("BT32", [128, 32], F32)
    emit_shift_idents(pb)
    emit_butterfly_fold(pb, "u", inp["l2_bfu_loc"], inp["l2_bfu_cross"],
                        inp["l2_wexpT"], wexpF, False,
                        "in_bfu_loc", "in_bfu_cross", "in_wexpT", "l2_wexpF",
                        bias_srcs=(inp["l2_bexpT"], inp["l2_bupT"]),
                        bias_dst=bt32, bias_rks=("in_bexpT", "in_bupT"))
    emit_butterfly_fold(pb, "d", inp["l2_bfd_loc"], inp["l2_bfd_cross"],
                        inp["L2_ffn_W_con"], wconF, True,
                        "in_bfd_loc", "in_bfd_cross", "in_wcon", "l2_wconF")

    wpre = pb.sbuf("WPRE", [128, 128], BF16)
    wpost = pb.sbuf("WPOST", [128, 128], BF16)
    for nm, src_ap, rk in (("WPRE", inp["l2_wpreT"], "in_wpreT"),
                           ("WPOST", inp["l2_wpostT"], "in_wpostT")):
        t = pb.sb[nm]
        pb.op("vector", lambda em, t=t: em.memset(t[:, :], 0.0), writes=[nm])
        for b in range(8):
            pb.g_dma(t[b * 16:(b + 1) * 16, b * 16:(b + 1) * 16],
                     src_ap[:, :], reads=[rk], writes=[nm])

    xn_f, xn_b = emit_ln(pb, "l2", "xres", xres, T)

    qhm = pb.dram("l2_qhm", [D, T], BF16)
    khm = pb.dram("l2_khm", [D, T], BF16)
    vT = pb.dram("l2_vT", [D, T], BF16)
    qraw = pb.dram("l2_qraw", [D, T], BF16)
    kraw = pb.dram("l2_kraw", [D, T], BF16)
    Sm = pb.dram("l2_sm", [T // 8 * 128, 2048], BF16)
    S3 = pb.dram("l2_s3", [16, 2048, 2048], BF16)
    hT = pb.dram("l2_hT", [32, 128, T], BF16)
    s_scr = pb.dram("s_l2", [T, D], F32)
    g_scr = pb.dram("g_l2", [T, D], F32)
    PS = [pb.psum(f"PS{i}", [128, 512], F32) for i in range(6)]
    PT = pb.sb["PTB0"]

    invc = pb.sbuf("INVC", [128, 1], F32)
    pb.op("gpsimd", lambda em: em.iota(invc[0:32, :], [[1, 1]], base=0,
                                       channel_multiplier=1,
                                       allow_small_or_imprecise_dtypes=True),
          writes=["INVC"])
    pb.op("scalar", lambda em: em.activation(invc[0:32, :], invc[0:32, :], AF.Exp,
                                             scale=-math.log(10000.0) / 32.0),
          reads=["INVC"], writes=["INVC"])
    for rep in range(1, 4):
        pb.g_dma(invc[rep * 32:(rep + 1) * 32, :], invc[0:32, :], reads=["INVC"],
                 writes=["INVC"])
    TWO_PI = 2.0 * math.pi

    def emit_rope_tables(cosb, sinb):
        fr = pb.sbuf("FP0", [128, D], F32)
        ftmp = pb.sbuf("FP2", [128, D], F32)

        def wrap_pi():
            pb.op("vector", lambda em: em.tensor_scalar(
                ftmp[:, :], fr[:, :], math.pi, -TWO_PI, ALU.is_ge, ALU.mult),
                reads=["FP0"], writes=["FP2"])
            pb.op("vector", lambda em: em.tensor_add(fr[:, :], fr[:, :], ftmp[:, :]),
                  reads=["FP0", "FP2"], writes=["FP0"])

        for hh_ in range(max(1, S // 512)):
            w_ = min(S, 512)
            csl = slice(hh_ * 512, hh_ * 512 + w_)
            pb.op("gpsimd", lambda em, hh_=hh_, w_=w_: em.iota(
                fr[:, 0:w_], [[1, w_]], base=hh_ * 512, channel_multiplier=0,
                allow_small_or_imprecise_dtypes=True), writes=["FP0"])
            pb.op("vector", lambda em, w_=w_: em.tensor_scalar_mul(
                fr[:, 0:w_], fr[:, 0:w_], invc[:, 0:1]),
                reads=["FP0", "INVC"], writes=["FP0"])
            pb.op("vector", lambda em, w_=w_: em.tensor_scalar(
                ftmp[:, 0:w_], fr[:, 0:w_], 1.0 / TWO_PI, 8388608.0 - 0.5,
                ALU.mult, ALU.add), reads=["FP0"], writes=["FP2"])
            pb.op("vector", lambda em, w_=w_: em.tensor_single_scalar(
                ftmp[:, 0:w_], ftmp[:, 0:w_], -8388608.0, ALU.add),
                reads=["FP2"], writes=["FP2"])
            pb.op("vector", lambda em, w_=w_: em.scalar_tensor_tensor(
                fr[:, 0:w_], ftmp[:, 0:w_], -TWO_PI, fr[:, 0:w_], ALU.mult, ALU.add),
                reads=["FP2", "FP0"], writes=["FP0"])
            wrap_pi()
            pb.op("scalar", lambda em, csl=csl, w_=w_: em.activation(
                sinb[:, csl], fr[:, 0:w_], AF.Sin), reads=["FP0"], writes=["XP1"])
            pb.op("vector", lambda em, w_=w_: em.tensor_single_scalar(
                fr[:, 0:w_], fr[:, 0:w_], math.pi / 2.0, ALU.add),
                reads=["FP0"], writes=["FP0"])
            wrap_pi()
            pb.op("scalar", lambda em, csl=csl, w_=w_: em.activation(
                cosb[:, csl], fr[:, 0:w_], AF.Sin), reads=["FP0"], writes=["XP0"])

    bqkvT = pb.sbuf("BQKVT", [128, 24], F32)
    pb.q_dma(bqkvT[:, :], inp["l2_bqkvT"][:, :], reads=["in_bqkvT"], writes=["BQKVT"])

    for seq in range(n_seq):
        emit_load_xnT(pb, "l2", xn_b, seq, S, T)
        xp = [pb.sb[f"XP{d}"] for d in range(8)]
        xpk = [f"XP{d}" for d in range(8)]
        ssl = slice(seq * S, (seq + 1) * S)

        wq = [pb.sbuf(f"G{24+i}", [128, D], BF16) for i in range(2)]
        ev = pb.sbuf("E0", [128, 512], BF16)
        for m in range(24):
            wqt = wq[m % 2]
            wqk = f"G{24 + m % 2}"
            for d in range(8):
                pb.g_dma(wqt[:, d * 128:(d + 1) * 128],
                         inp["l2_wqkvT"][d * 128:(d + 1) * 128, m * 128:(m + 1) * 128],
                         reads=["in_wqkvT"], writes=[wqk])
            for tg in range(S // 512):
                tsl = slice(tg * 512, (tg + 1) * 512)
                ps = PS[4 + tg % 2]
                psk = f"PS{4 + tg % 2}"
                for d in range(8):
                    pb.mm(ps[:, :], wqt[:, d * 128:(d + 1) * 128], xp[d][:, tsl],
                          start=(d == 0), stop=(d == 7), reads=[wqk, xpk[d]],
                          writes=[psk])
                pb.op("scalar", lambda em, ps=ps, m=m: em.activation(
                    ev[:, :], ps[:, :], AF.Identity, bias=bqkvT[:, m:m + 1]),
                    reads=[psk, "BQKVT"], writes=["E0"])
                gcol = seq * S + tg * 512
                if m < 16:
                    dst = qraw if m < 8 else kraw
                    dk = "l2_qraw" if m < 8 else "l2_kraw"
                    mm = m % 8
                    for half in range(2):
                        for hh in range(2):
                            rbase = hh * 64 + half * 32
                            drow = half * 512 + 64 * mm + hh * 32
                            pb.q_dma(dst[drow:drow + 32, gcol:gcol + 512],
                                     ev[rbase:rbase + 32, :],
                                     reads=["E0"], writes=[f"{dk}@{seq}"])
                else:
                    pb.q_dma(vT[(m - 16) * 128:(m - 15) * 128, gcol:gcol + 512],
                             ev[:, :], reads=["E0"], writes=[f"l2_vT@{seq}"])

        cosb = pb.sb["XP0"]
        sinb = pb.sb["XP1"]
        emit_rope_tables(cosb, sinb)
        rw = [pb.sbuf(f"XP{4+i}", [128, 2048], BF16) for i in range(4)]
        rwk = [f"XP{4+i}" for i in range(4)]
        for (raw, hm, rkk, wkk, scaled) in ((qraw, qhm, "l2_qraw", "l2_qhm", True),
                                            (kraw, khm, "l2_kraw", "l2_khm", False)):
            for rc in range(4):
                pb.q_dma(rw[0][:, 0:S], raw[rc * 128:(rc + 1) * 128, ssl],
                         reads=[f"{rkk}@{seq}"], writes=[rwk[0]])
                pb.q_dma(rw[1][:, 0:S], raw[512 + rc * 128:512 + (rc + 1) * 128, ssl],
                         reads=[f"{rkk}@{seq}"], writes=[rwk[1]])
                pb.op("vector", lambda em: em.tensor_mul(rw[2][:, 0:S], rw[0][:, 0:S],
                                                         cosb[:, 0:S]),
                      reads=[rwk[0], "XP0"], writes=[rwk[2]])
                pb.op("vector", lambda em: em.tensor_mul(rw[3][:, 0:S], rw[1][:, 0:S],
                                                         sinb[:, 0:S]),
                      reads=[rwk[1], "XP1"], writes=[rwk[3]])
                pb.op("vector", lambda em: em.tensor_sub(rw[2][:, 0:S], rw[2][:, 0:S],
                                                         rw[3][:, 0:S]),
                      reads=[rwk[2], rwk[3]], writes=[rwk[2]])
                if scaled:
                    pb.op("vector", lambda em: em.tensor_single_scalar(
                        rw[2][:, 0:S], rw[2][:, 0:S], 0.125, ALU.mult),
                        reads=[rwk[2]], writes=[rwk[2]])
                pb.op("vector", lambda em: em.tensor_mul(rw[3][:, 0:S], rw[0][:, 0:S],
                                                         sinb[:, 0:S]),
                      reads=[rwk[0], "XP1"], writes=[rwk[3]])
                pb.op("vector", lambda em: em.tensor_mul(rw[0][:, 0:S], rw[1][:, 0:S],
                                                         cosb[:, 0:S]),
                      reads=[rwk[1], "XP0"], writes=[rwk[0]])
                pb.op("vector", lambda em: em.tensor_add(rw[3][:, 0:S], rw[3][:, 0:S],
                                                         rw[0][:, 0:S]),
                      reads=[rwk[3], rwk[0]], writes=[rwk[3]])
                if scaled:
                    pb.op("vector", lambda em: em.tensor_single_scalar(
                        rw[3][:, 0:S], rw[3][:, 0:S], 0.125, ALU.mult),
                        reads=[rwk[3]], writes=[rwk[3]])
                for half, til, tk in ((0, rw[2], rwk[2]), (1, rw[3], rwk[3])):
                    for a4 in range(4):
                        base = (256 * rc + 32 * half + 64 * a4) * T + seq * S
                        pb.q_dma(bass.AP(hm.tensor, base, [[T, 32], [1, S]]),
                                 til[a4 * 32:(a4 + 1) * 32, 0:S],
                                 reads=[tk], writes=[f"{wkk}@{seq}"])

        khm_sb = [pb.sbuf(f"G{i}", [128, D], BF16) for i in range(16)]
        nhalf = max(1, S // 1024)
        hw = min(S, 1024)
        for c in range(8):
            for jh in range(nhalf):
                pb.q_dma(khm_sb[2 * c + jh][:, 0:hw],
                         khm[c * 128:(c + 1) * 128,
                             seq * S + jh * 1024:seq * S + jh * 1024 + hw],
                         reads=[f"l2_khm@{seq}"], writes=[f"G{2*c+jh}"])
        qi = [pb.sbuf(f"QI{d}", [128, 128], BF16) for d in range(8)]
        lb = [pb.sbuf("E0", [128, 512], BF16), pb.sbuf("QOUT", [128, 512], BF16)]
        for ib in range(nib):
            for d in range(8):
                pb.q_dma(qi[d][:, :],
                         qhm[d * 128:(d + 1) * 128,
                             seq * S + ib * 128:seq * S + (ib + 1) * 128],
                         reads=[f"l2_qhm@{seq}"], writes=[f"QI{d}"])
            for h in range(16):
                lhs = qi[h // 2][(h % 2) * 64:(h % 2) * 64 + 64, :]
                for jg in range(S // 512):
                    ps = PS[4 + jg % 2]
                    psk = f"PS{4 + jg % 2}"
                    jh = (jg * 512) // 1024
                    rhs = khm_sb[2 * (h // 2) + jh][
                        (h % 2) * 64:(h % 2) * 64 + 64,
                        jg * 512 - jh * 1024:jg * 512 - jh * 1024 + 512]
                    pb.mm(ps[:, :], lhs, rhs, start=True, stop=True,
                          reads=[f"QI{h//2}", f"G{2*(h//2)+jh}"], writes=[psk])
                    e = lb[jg % 2]
                    ek = "E0" if jg % 2 == 0 else "QOUT"
                    eng = "vector" if jg % 2 == 0 else "scalar"
                    if eng == "vector":
                        pb.op(eng, lambda em, e=e, ps=ps: em.tensor_copy(e[:, :], ps[:, :]),
                              reads=[psk], writes=[ek])
                    else:
                        pb.op(eng, lambda em, e=e, ps=ps: em.copy(e[:, :], ps[:, :]),
                              reads=[psk], writes=[ek])
                    base = seq * 16 * S * S + h * S * S + (ib * 128) * S + jg * 512
                    pb.q_dma(bass.AP(Sm.tensor, base, [[S, 128], [1, 512]]),
                             e[:, :], reads=[ek], writes=[f"l2_sm@{seq}"])

        strip = pb.sbuf("PBUF", [128, 2048], BF16)
        pstr = pb.sbuf("PBUF2", [128, 2048], BF16)
        scol = pb.sbuf("SCOL", [128, 4], F32)
        ssum = pb.sbuf("SSUM", [128, 1], F32)
        srcp = pb.sbuf("SRCP", [128, 1], F32)
        for gi in range(S // 8):
            sbase = seq * 16 * S * S
            for is_ in range(8):
                pb.q_dma(strip[is_ * 16:(is_ + 1) * 16, 0:S],
                         bass.AP(Sm.tensor, sbase + (gi * 8 + is_) * S,
                                 [[S * S, 16], [1, S]]),
                         reads=[f"l2_sm@{seq}"], writes=["PBUF"])
            for jg in range(S // 512):
                ps = PS[jg % 2]
                psk = f"PS{jg % 2}"
                pb.mm(ps[:, :], wpre[:, :], strip[:, jg * 512:(jg + 1) * 512],
                      start=True, stop=True, reads=["WPRE", "PBUF"], writes=[psk])
                pb.op("scalar", lambda em, jg=jg, ps=ps: em.activation(
                    pstr[:, jg * 512:(jg + 1) * 512], ps[:, :], AF.Exp,
                    accum_out=scol[:, jg:jg + 1]),
                    reads=[psk], writes=["PBUF2", "SCOL"])
            pb.op("vector", lambda em: em.reduce_sum(ssum[:, :], scol[:, 0:S // 512], AX.X),
                  reads=["SCOL"], writes=["SSUM"])
            pb.op("vector", lambda em: em.reciprocal(srcp[:, :], ssum[:, :]),
                  reads=["SSUM"], writes=["SRCP"])
            pb.op("vector", lambda em: em.tensor_scalar_mul(pstr[:, 0:S], pstr[:, 0:S],
                                                            srcp[:, 0:1]),
                  reads=["PBUF2", "SRCP"], writes=["PBUF2"])
            for jg in range(S // 512):
                ps = PS[2 + jg % 2]
                psk = f"PS{2 + jg % 2}"
                pb.mm(ps[:, :], wpost[:, :], pstr[:, jg * 512:(jg + 1) * 512],
                      start=True, stop=True, reads=["WPOST", "PBUF2"], writes=[psk])
                eng = "vector" if jg % 2 == 0 else "scalar"
                if eng == "vector":
                    pb.op(eng, lambda em, jg=jg, ps=ps: em.tensor_copy(
                        strip[:, jg * 512:(jg + 1) * 512], ps[:, :]),
                        reads=[psk], writes=["PBUF"])
                else:
                    pb.op(eng, lambda em, jg=jg, ps=ps: em.copy(
                        strip[:, jg * 512:(jg + 1) * 512], ps[:, :]),
                        reads=[psk], writes=["PBUF"])
            for is_ in range(8):
                base = (gi * 8 + is_) * 2048
                pb.q_dma(bass.AP(S3.tensor, base, [[2048 * 2048, 16], [1, S]]),
                         strip[is_ * 16:(is_ + 1) * 16, 0:S],
                         reads=["PBUF"], writes=[f"l2_s3@{seq}"])

        for hp in range(8):
            for jb in range(nib):
                pb.q_dma(xp[hp][:, jb * 128:(jb + 1) * 128],
                         vT[hp * 128:(hp + 1) * 128,
                            seq * S + jb * 128:seq * S + (jb + 1) * 128],
                         reads=[f"l2_vT@{seq}"], writes=[xpk[hp]], transpose=True)
        for d in range(8):
            g = pb.sbuf(f"G{16+d}", [128, D], BF16)
            pb.g_dma(g[:, :], inp["l2_woutT"][d * 128:(d + 1) * 128, :],
                     reads=["in_woutT"], writes=[f"G{16+d}"])
        brow = pb.sbuf("BROW", [1, D], BF16)
        pb.g_dma(brow[:, :], inp["L2_attn_bout"][0:1, :], reads=["in_bout2"],
                 writes=["BROW"])

        rt = [pb.sbuf("E0", [128, 512], BF16), pb.sbuf("QOUT", [128, 512], BF16)]
        _pb1 = pb.sbuf("PBUF", [128, 2048], BF16)
        _pb2 = pb.sbuf("PBUF2", [128, 2048], BF16)
        ot = [(_pb1 if c < 4 else _pb2)[:, (c % 4) * 512:(c % 4 + 1) * 512]
              for c in range(8)]
        otk = ["PBUF" if c < 4 else "PBUF2" for c in range(8)]
        stile = pb.sbuf("STILE", [128, 1024], F32)
        for ig in range(S // 512):
            for g in range(16):
                ps = PS[5]
                for jb in range(nib):
                    r = rt[jb % 2]
                    rk_ = "E0" if jb % 2 == 0 else "QOUT"
                    pb.q_dma(r[:, :],
                             S3[g, ig * 512:(ig + 1) * 512, jb * 128:(jb + 1) * 128],
                             reads=[f"l2_s3@{seq}"], writes=[rk_], transpose=True)
                    pb.mm(ps[0:64, :],
                          xp[g // 2][:, jb * 128 + (g % 2) * 64:jb * 128 + (g % 2) * 64 + 64],
                          r[:, :], start=(jb == 0), stop=(jb == nib - 1),
                          reads=[xpk[g // 2], rk_], writes=["PS5"])
                pb.op("vector", lambda em, g=g, ps=ps: em.tensor_copy(
                    ot[g // 2][(g % 2) * 64:(g % 2) * 64 + 64, :], ps[0:64, :]),
                    reads=["PS5"], writes=[otk[g // 2]])
            for isub in range(4):
                for oh in range(2):
                    ps = PS[oh]
                    psk = f"PS{oh}"
                    osl = slice(oh * 512, (oh + 1) * 512)
                    pb.mm(ps[:, :], pb.sb["onescol"][:, :], brow[:, osl],
                          start=True, stop=False, reads=["onescol", "BROW"], writes=[psk])
                    for c in range(8):
                        pb.mm(ps[:, :], ot[c][:, isub * 128:(isub + 1) * 128],
                              pb.sb[f"G{16+c}"][:, osl], start=False, stop=(c == 7),
                              reads=[otk[c], f"G{16+c}"], writes=[psk])
                    if oh == 0:
                        pb.op("vector", lambda em, ps=ps: em.tensor_copy(
                            stile[:, 0:512], ps[:, :]), reads=[psk], writes=["STILE"])
                    else:
                        pb.op("scalar", lambda em, ps=ps: em.copy(
                            stile[:, 512:1024], ps[:, :]), reads=[psk], writes=["STILE"])
                t0 = seq * S + ig * 512 + isub * 128
                pb.q_dma(s_scr[t0:t0 + 128, :], stile[:, :], reads=["STILE"],
                         writes=[f"s_l2@{t0 // 128}"])

        emit_load_xnT(pb, "l2", xn_b, seq, S, T)
        hf32 = pb.sbuf("HF32", [128, 512], F32)
        hbf = pb.sbuf("HBF", [128, 512], BF16)
        for fc in range(32):
            for d in range(8):
                pb.q_dma(qi[d][:, :],
                         wexpF[fc * 128:(fc + 1) * 128, d * 128:(d + 1) * 128],
                         reads=["l2_wexpF"], writes=[f"QI{d}"], transpose=True)
            for tg in range(S // 512):
                tsl = slice(tg * 512, (tg + 1) * 512)
                ps = PS[4 + tg % 2]
                psk = f"PS{4 + tg % 2}"
                for d in range(8):
                    pb.mm(ps[:, :], qi[d][:, :], xp[d][:, tsl],
                          start=(d == 0), stop=(d == 7), reads=[f"QI{d}", xpk[d]],
                          writes=[psk])
                pb.op("scalar", lambda em, ps=ps, fc=fc: em.activation(
                    hbf[:, :], ps[:, :], AF.Gelu, bias=bt32[:, fc:fc + 1]),
                    reads=[psk, "BT32"], writes=["HBF"])
                pb.q_dma(hT[fc, :, seq * S + tg * 512:seq * S + (tg + 1) * 512],
                         hbf[:, :], reads=["HBF"], writes=[f"l2_hT@{fc}"])
        for fc in range(32):
            g = pb.sbuf(f"G{fc}", [128, D], BF16)
            pb.q_dma(g[:, :], wconF[fc * 128:(fc + 1) * 128, :], reads=["l2_wconF"],
                     writes=[f"G{fc}"])
        brow2 = pb.sbuf("PBUF2", [128, 2048], BF16)[0:1, 0:D]
        brow3 = pb.sb["PBUF2"][0:1, D:2 * D]
        pb.g_dma(brow2, inp["l2_bcon"][0:1, :], reads=["in_bcon"], writes=["PBUF2"])
        pb.g_dma(brow3, inp["l2_dbias"][0:1, :], reads=["in_dbias"], writes=["PBUF2"])
        ht0 = pb.sbuf("HT0", [128, 128], BF16)
        ht1 = pb.sbuf("HT1", [128, 128], BF16)
        yev = pb.sbuf("STILE", [128, 1024], F32)
        for tb in range(nib):
            t0 = seq * S + tb * 128
            for oh in range(2):
                osl = slice(oh * 512, (oh + 1) * 512)
                pb.mm(PS[oh][:, :], pb.sb["onescol"][:, :], brow2[:, osl],
                      start=True, stop=False, reads=["onescol", "PBUF2"], writes=[f"PS{oh}"])
                pb.mm(PS[oh][:, :], pb.sb["onescol"][:, :], brow3[:, osl],
                      start=False, stop=False, reads=["onescol", "PBUF2"], writes=[f"PS{oh}"])
            for fc in range(32):
                ht = (ht0, "HT0") if fc % 2 == 0 else (ht1, "HT1")
                pb.q_dma(ht[0][:, :], hT[fc, :, t0:t0 + 128], reads=[f"l2_hT@{fc}"],
                         writes=[ht[1]])
                for oh in range(2):
                    pb.mm(PS[oh][:, :], ht[0][:, :],
                          pb.sb[f"G{fc}"][:, oh * 512:(oh + 1) * 512],
                          start=False, stop=(fc == 31),
                          reads=[ht[1], f"G{fc}"], writes=[f"PS{oh}"])
            pb.op("vector", lambda em: em.tensor_copy(yev[:, 0:512], PS[0][:, :]),
                  reads=["PS0"], writes=["STILE"])
            pb.op("vector", lambda em: em.tensor_copy(yev[:, 512:1024], PS[1][:, :]),
                  reads=["PS1"], writes=["STILE"])
            pb.q_dma(g_scr[t0:t0 + 128, :], yev[:, :], reads=["STILE"],
                     writes=[f"g_l2@{t0 // 128}"])

    emit_combine(pb, "l2", xres, s_scr, g_scr, T)


def prep_weights_L2(w):
    out = {}
    out["l2_wqkvT"] = np.ascontiguousarray(np.asarray(w["L2_attn_Wqkv"]).T)
    out["l2_bqkvT"] = np.ascontiguousarray(np.asarray(w["L2_attn_bqkv"]).reshape(24, 128).T)
    out["l2_wpreT"] = np.ascontiguousarray(np.asarray(w["L2_attn_Wpre"]).T)
    out["l2_wpostT"] = np.ascontiguousarray(np.asarray(w["L2_attn_Wpost"]).T)
    out["l2_woutT"] = np.ascontiguousarray(np.asarray(w["L2_attn_Wout"]).T)
    out["L2_attn_bout"] = np.ascontiguousarray(np.asarray(w["L2_attn_bout"]).reshape(1, D))
    out["l2_wexpT"] = np.ascontiguousarray(np.asarray(w["L2_ffn_W_exp"]).T)
    out["L2_ffn_W_con"] = np.ascontiguousarray(np.asarray(w["L2_ffn_W_con"]))
    out["l2_bexpT"] = np.ascontiguousarray(np.asarray(w["L2_ffn_b_exp"]).reshape(32, 128).T)
    out["l2_bupT"] = np.ascontiguousarray(np.asarray(w["L2_ffn_up_bias"]).reshape(32, 128).T)
    out["l2_bcon"] = np.ascontiguousarray(np.asarray(w["L2_ffn_b_con"]).reshape(1, D))
    out["l2_dbias"] = np.ascontiguousarray(np.asarray(w["L2_ffn_down_bias"]).reshape(1, D))
    for tag, wname in (("u", "L2_ffn_up_weights"), ("d", "L2_ffn_down_weights")):
        bw = np.asarray(w[wname])
        loc = np.zeros((128, 1120), np.float32)
        for s in range(7):
            sz = 2 ** s
            ww = bw[s]
            for blk in range(32):
                cd = np.zeros(128, np.float32)
                cu = np.zeros(128, np.float32)
                cn = np.zeros(128, np.float32)
                cuT = np.zeros(128, np.float32)
                cnT = np.zeros(128, np.float32)
                for p in range(128):
                    f = blk * 128 + p
                    q_, rr = f // (2 * sz), f % (2 * sz)
                    if rr < sz:
                        k = q_ * sz + rr
                        cd[p] = ww[k, 0, 0]
                        cu[p] = ww[k, 0, 1]
                        cuT[p] = ww[k, 1, 0]
                    else:
                        k = q_ * sz + (rr - sz)
                        cd[p] = ww[k, 1, 1]
                        cn[p] = ww[k, 1, 0]
                        cnT[p] = ww[k, 0, 1]
                loc[:, s * 160 + 0 * 32 + blk] = cd
                loc[:, s * 160 + 1 * 32 + blk] = cu
                loc[:, s * 160 + 2 * 32 + blk] = cn
                loc[:, s * 160 + 3 * 32 + blk] = cuT
                loc[:, s * 160 + 4 * 32 + blk] = cnT
        out[f"l2_bf{tag}_loc"] = loc
        cross = np.zeros((128, 320), np.float32)
        for s in range(7, 12):
            sz = 2 ** s
            cs = 2 ** (s - 7)
            ww = bw[s]
            pairs = [c for c in range(32) if not (c & cs)]
            for pi, cA in enumerate(pairs):
                for e in range(128):
                    a = cA * 128 + e
                    k = (a // (2 * sz)) * sz + a % sz
                    col = (s - 7) * 64 + pi * 4
                    cross[e, col + 0] = ww[k, 0, 0]
                    cross[e, col + 1] = ww[k, 0, 1]
                    cross[e, col + 2] = ww[k, 1, 0]
                    cross[e, col + 3] = ww[k, 1, 1]
        out[f"l2_bf{tag}_cross"] = cross
    return out


def declare_inputs_L2(pb):
    inp = {}
    inp["l2_wqkvT"] = pb.dram("l2_wqkvT", [D, 3 * D], F32, kind="ExternalInput")
    inp["l2_bqkvT"] = pb.dram("l2_bqkvT", [128, 24], F32, kind="ExternalInput")
    inp["l2_wpreT"] = pb.dram("l2_wpreT", [16, 16], F32, kind="ExternalInput")
    inp["l2_wpostT"] = pb.dram("l2_wpostT", [16, 16], F32, kind="ExternalInput")
    inp["l2_woutT"] = pb.dram("l2_woutT", [D, D], F32, kind="ExternalInput")
    inp["L2_attn_bout"] = pb.dram("L2_attn_bout", [1, D], F32, kind="ExternalInput")
    inp["l2_wexpT"] = pb.dram("l2_wexpT", [4 * D, D], F32, kind="ExternalInput")
    inp["L2_ffn_W_con"] = pb.dram("L2_ffn_W_con", [4 * D, D], F32, kind="ExternalInput")
    inp["l2_bexpT"] = pb.dram("l2_bexpT", [128, 32], F32, kind="ExternalInput")
    inp["l2_bupT"] = pb.dram("l2_bupT", [128, 32], F32, kind="ExternalInput")
    inp["l2_bcon"] = pb.dram("l2_bcon", [1, D], F32, kind="ExternalInput")
    inp["l2_dbias"] = pb.dram("l2_dbias", [1, D], F32, kind="ExternalInput")
    inp["l2_bfu_loc"] = pb.dram("l2_bfu_loc", [128, 1120], F32, kind="ExternalInput")
    inp["l2_bfd_loc"] = pb.dram("l2_bfd_loc", [128, 1120], F32, kind="ExternalInput")
    inp["l2_bfu_cross"] = pb.dram("l2_bfu_cross", [128, 320], F32, kind="ExternalInput")
    inp["l2_bfd_cross"] = pb.dram("l2_bfd_cross", [128, 320], F32, kind="ExternalInput")
    return inp




def emit_L1(pb, inp, n_seq, S):
    T = n_seq * S
    nt = T // 128
    nib = S // 128
    H, DK = 4, 256
    xres = pb.dr["xres"]
    xn_f, xn_b = emit_ln(pb, "l1", "xres", xres, T)

    qT = pb.dram("l1_qT", [D, T], BF16)
    kT = pb.dram("l1_kT", [D, T], BF16)
    vT = pb.dram("l1_vT", [D, T], BF16)
    beta_scr = pb.dram("l1_beta", [T, 4], F32)
    od_scr = pb.dram("l1_od", [T, D], BF16)
    odf_scr = pb.dram("l1_odf", [D, T], BF16)
    ps_scr = pb.dram("l1_ps", [D, T], BF16)
    plg_scr = pb.dram("l1_pl", [D, T], BF16)
    gin_scr = pb.dram("l1_gin", [64, T], F32)
    o_scr = pb.dram("l1_o", [T, D], BF16)
    xnT_scr = pb.dram("l1_xnT", [D, T], BF16)
    acc_scr = pb.dram("l1_acc", [T, D], F32)
    s_scr = pb.dram("s_l1", [T, D], F32)
    g_scr = pb.dram("g_l1", [T, D], F32)
    PS = [pb.psum(f"PS{i}", [128, 512], F32) for i in range(6)]
    PT = pb.psum("PTB0", [128, 1024], BF16)
    ident = pb.sb["ident"]
    pb.sbuf("STILE", [128, 1024], F32)
    pb.sbuf("HF32", [128, 512], F32)

    for nm, cm, pat, thr in (("M01SL", 1, -1, 1.0),
                             ("M01SU", -1, 1, 1.0),
                             ("M01IU", -1, 1, 0.0)):
        t = pb.sbuf(nm, [128, 128], BF16)
        tf = pb.sbuf("identf", [128, 128], F32)
        pb.op("gpsimd", lambda em, tf=tf, cm=cm, pat=pat: em.iota(
            tf[:, :], [[pat, 128]], base=0, channel_multiplier=cm,
            allow_small_or_imprecise_dtypes=True), writes=["identf"])
        pb.op("vector", lambda em, t=t, tf=tf, thr=thr: em.tensor_single_scalar(
            t[:, :], tf[:, :], thr, ALU.is_ge), reads=["identf"], writes=[nm])
    pb.op("gpsimd", lambda em: em.iota(pb.sb["identf"][:, :], [[1, 128]], base=0,
                                       channel_multiplier=-1,
                                       allow_small_or_imprecise_dtypes=True),
          writes=["identf"])
    pb.op("vector", lambda em: em.tensor_single_scalar(
        pb.sb["identf"][:, :], pb.sb["identf"][:, :], 0.0, ALU.is_equal),
        reads=["identf"], writes=["identf"])

    cq = pb.sbuf("CQW", [128, 96], F32)
    pb.q_dma(cq[:, 0:32], inp["l1_cqT"][:, :], reads=["in_cq"], writes=["CQW"])
    pb.q_dma(cq[:, 32:64], inp["l1_ckT"][:, :], reads=["in_ck"], writes=["CQW"])
    pb.q_dma(cq[:, 64:96], inp["l1_cvT"][:, :], reads=["in_cv"], writes=["CQW"])
    firs = pb.sbuf("FIRS", [128, 40], F32)
    firl = pb.sbuf("FIRL", [128, 504], F32)
    pb.q_dma(firs[:, :], inp["l1_firS"][:, :], reads=["in_firS"], writes=["FIRS"])
    pb.q_dma(firl[:, :], inp["l1_firL"][:, :], reads=["in_firL"], writes=["FIRL"])

    convr = pb.sbuf("CONVR", [128, 2112], BF16)
    conva = pb.sbuf("CONVA", [128, 2048], F32)
    ev = pb.sbuf("E0", [128, 512], BF16)

    for seq in range(n_seq):
        emit_load_xnT(pb, "l1", xn_b, seq, S, T)
        xp = [pb.sb[f"XP{d}"] for d in range(8)]
        xpk = [f"XP{d}" for d in range(8)]
        for d in range(8):
            pb.q_dma(xnT_scr[d * 128:(d + 1) * 128, seq * S:(seq + 1) * S],
                     xp[d][:, 0:S], reads=[xpk[d]], writes=["l1_xnT"])

        for pi, (wkey, rkw, dst, dk_, coff) in enumerate(
                (("l1_wqT", "in_wq", qT, "l1_qT", 0),
                 ("l1_wkT", "in_wk", kT, "l1_kT", 32),
                 ("l1_wvT", "in_wv", vT, "l1_vT", 64))):
            for d in range(8):
                g = pb.sbuf(f"G{d}", [128, D], BF16)
                pb.g_dma(g[:, :], inp[wkey][d * 128:(d + 1) * 128, :],
                         reads=[rkw], writes=[f"G{d}"])
            for c in range(8):
                pb.op("vector", lambda em: em.memset(convr[:, 0:3], 0.0), writes=["CONVR"])
                for tg in range(S // 512):
                    ps = PS[4 + tg % 2]
                    psk = f"PS{4 + tg % 2}"
                    for d in range(8):
                        pb.mm(ps[:, :], pb.sb[f"G{d}"][:, c * 128:(c + 1) * 128],
                              xp[d][:, tg * 512:(tg + 1) * 512],
                              start=(d == 0), stop=(d == 7),
                              reads=[f"G{d}", xpk[d]], writes=[psk])
                    pb.op("vector", lambda em, tg=tg, ps=ps: em.tensor_copy(
                        convr[:, 3 + tg * 512:3 + (tg + 1) * 512], ps[:, :]),
                        reads=[psk], writes=["CONVR"])
                pb.op("vector", lambda em, c=c, coff=coff: em.tensor_scalar_mul(
                    conva[:, 0:S], convr[:, 0:S], cq[:, coff + c * 4:coff + c * 4 + 1]),
                    reads=["CONVR", "CQW"], writes=["CONVA"])
                for k_ in range(1, 4):
                    pb.op("vector", lambda em, c=c, k_=k_, coff=coff: em.scalar_tensor_tensor(
                        conva[:, 0:S], convr[:, k_:k_ + S],
                        cq[:, coff + c * 4 + k_:coff + c * 4 + k_ + 1],
                        conva[:, 0:S], ALU.mult, ALU.add),
                        reads=["CONVR", "CQW", "CONVA"], writes=["CONVA"])
                for tg in range(S // 512):
                    pb.op("scalar", lambda em, tg=tg: em.activation(
                        ev[:, :], conva[:, tg * 512:(tg + 1) * 512], AF.Silu),
                        reads=["CONVA"], writes=["E0"])
                    pb.q_dma(dst[c * 128:(c + 1) * 128,
                                  seq * S + tg * 512:seq * S + (tg + 1) * 512],
                             ev[:, :], reads=["E0"], writes=[f"{dk_}@{seq}"])

        wb = pb.sbuf("WBT", [128, 32], BF16)
        for d in range(8):
            pb.g_dma(wb[:, d * 4:(d + 1) * 4], inp["l1_wbT"][d * 128:(d + 1) * 128, :],
                     reads=["in_wb"], writes=["WBT"])
        bt4 = pb.sbuf("BT4", [128, 8], F32)
        for tb in range(nib):
            ps = PS[0]
            for d in range(8):
                pb.mm(ps[:, 0:4], xp[d][:, tb * 128:(tb + 1) * 128],
                      wb[:, d * 4:(d + 1) * 4], start=(d == 0), stop=(d == 7),
                      reads=[xpk[d], "WBT"], writes=["PS0"])
            pb.op("scalar", lambda em: em.activation(bt4[:, 0:4], PS[0][:, 0:4],
                                                     AF.Exp, scale=-1.0),
                  reads=["PS0"], writes=["BT4"])
            pb.op("vector", lambda em: em.tensor_single_scalar(
                bt4[:, 0:4], bt4[:, 0:4], 1.0, ALU.add),
                reads=["BT4"], writes=["BT4"])
            pb.op("vector", lambda em: em.reciprocal(bt4[:, 0:4], bt4[:, 0:4]),
                  reads=["BT4"], writes=["BT4"])
            pb.q_dma(beta_scr[seq * S + tb * 128:seq * S + (tb + 1) * 128, 0:4],
                     bt4[:, 0:4], reads=["BT4"], writes=["l1_beta"])

        rws = pb.sbuf("RWS", [128, 96], BF16)
        for c in range(8):
            pb.g_dma(rws[:, c * 9:(c + 1) * 9],
                     inp["L1_ffn_router_w"][c * 128:(c + 1) * 128, :],
                     reads=["in_rw"], writes=["RWS"])
        pb.g_dma(rws[0:1, 72:81], inp["l1_rb"][0:1, :], reads=["in_rb"], writes=["RWS"])
        bb9s = pb.sbuf("BB9", [128, 9], F32)
        pb.mm(PS[3][:, 0:9], pb.sb["onescol"][:, :], rws[0:1, 72:81],
              start=True, stop=True, reads=["onescol", "RWS"], writes=["PS3"])
        pb.op("vector", lambda em: em.tensor_copy(bb9s[:, :], PS[3][:, 0:9]),
              reads=["PS3"], writes=["BB9"])
        gt32 = pb.sbuf("GT32", [128, 288], F32)
        pr9 = pb.sbuf("PR9", [128, 16], F32)
        ncol9 = pb.sbuf("DC0", [128, 1], F32)
        scol9 = pb.sbuf("DC1", [128, 1], F32)
        rcol9 = pb.sbuf("DC2", [128, 1], F32)
        logt9 = pb.sbuf("LOGT", [128, 16], F32)
        for tb in range(nib):
            tbg = seq * nib + tb
            ps = PS[2]
            for d in range(8):
                pb.mm(ps[:, 0:9], xp[d][:, tb * 128:(tb + 1) * 128],
                      rws[:, d * 9:(d + 1) * 9], start=(d == 0), stop=(d == 7),
                      reads=[xpk[d], "RWS"], writes=["PS2"])
            pb.op("vector", lambda em, ps=ps: em.tensor_add(pr9[:, 0:9], bb9s[:, :],
                                                            ps[:, 0:9]),
                  reads=["PS2", "BB9"], writes=["PR9"])
            pb.op("vector", lambda em: em.reduce_max(ncol9[:, :], pr9[:, 0:9], AX.X,
                                                     negate=True),
                  reads=["PR9"], writes=["DC0"])
            pb.op("scalar", lambda em: em.activation(pr9[:, 0:9], pr9[:, 0:9], AF.Exp,
                                                     bias=ncol9[:, 0:1],
                                                     accum_out=scol9[:, 0:1]),
                  reads=["PR9", "DC0"], writes=["PR9", "DC1"])
            pb.op("vector", lambda em: em.reciprocal(rcol9[:, :], scol9[:, :]),
                  reads=["DC1"], writes=["DC2"])
            pb.op("vector", lambda em: em.tensor_scalar_mul(pr9[:, 0:9], pr9[:, 0:9],
                                                            rcol9[:, 0:1]),
                  reads=["PR9", "DC2"], writes=["PR9"])
            pb.op("vector", lambda em: em.reduce_max(ncol9[:, :], pr9[:, 0:9], AX.X),
                  reads=["PR9"], writes=["DC0"])
            pb.op("vector", lambda em: em.tensor_scalar(
                logt9[:, 0:9], pr9[:, 0:9], ncol9[:, 0:1], None, ALU.is_equal),
                reads=["PR9", "DC0"], writes=["LOGT"])
            pb.op("vector", lambda em: em.scalar_tensor_tensor(
                logt9[:, 0:9], logt9[:, 0:9], -10.0, pr9[:, 0:9], ALU.mult, ALU.add),
                reads=["LOGT", "PR9"], writes=["LOGT"])
            pb.op("vector", lambda em: em.reduce_max(scol9[:, :], logt9[:, 0:9], AX.X),
                  reads=["LOGT"], writes=["DC1"])
            pb.op("vector", lambda em: em.tensor_scalar(
                logt9[:, 0:9], pr9[:, 0:9], scol9[:, 0:1], None, ALU.is_ge),
                reads=["PR9", "DC1"], writes=["LOGT"])
            pb.op("vector", lambda em, tbg=tbg: em.tensor_mul(
                gt32[:, tbg * 9:tbg * 9 + 9], pr9[:, 0:9], logt9[:, 0:9]),
                reads=["PR9", "LOGT"], writes=["GT32"])
            xt9 = pb.sb["FP0"]
            t0g = tbg * 128
            pb.q_dma(xt9[:, :], pb.dr["xnf_l1"][t0g:t0g + 128, :],
                     reads=[f"xnf_l1@{tbg}"], writes=["FP0"])
            pb.op("vector", lambda em, tbg=tbg: em.tensor_scalar_mul(
                xt9[:, :], xt9[:, :], gt32[:, tbg * 9 + 8:tbg * 9 + 9]),
                reads=["FP0", "GT32"], writes=["FP0"])
            pb.q_dma(acc_scr[t0g:t0g + 128, :], xt9[:, :], reads=["FP0"],
                     writes=[f"g_l1@{tbg}"])

        firc = pb.sbuf("CONVR", [128, 2112], BF16)
        for c in range(8):
            pb.op("vector", lambda em: em.memset(firc[:, 0:62], 0.0), writes=["CONVR"])
            pb.q_dma(firc[:, 62:62 + S],
                     vT[c * 128:(c + 1) * 128, seq * S:(seq + 1) * S],
                     reads=[f"l1_vT@{seq}"], writes=["CONVR"])
            for (pk, ntap, wof, wt, dst, dk_) in ((4, 5, 58, firs, ps_scr, "l1_ps"),
                                                  (62, 63, 0, firl, plg_scr, "l1_pl")):
                pb.op("vector", lambda em, c=c, wt=wt, ntap=ntap, pk=pk: em.tensor_scalar_mul(
                    conva[:, 0:S], firc[:, 62 - (ntap - 1):62 - (ntap - 1) + S],
                    wt[:, c * ntap:c * ntap + 1]),
                    reads=["CONVR", "FIRS", "FIRL"], writes=["CONVA"])
                for k_ in range(1, ntap):
                    pb.op("vector", lambda em, c=c, k_=k_, wt=wt, ntap=ntap: em.scalar_tensor_tensor(
                        conva[:, 0:S], firc[:, 62 - (ntap - 1) + k_:62 - (ntap - 1) + k_ + S],
                        wt[:, c * ntap + k_:c * ntap + k_ + 1],
                        conva[:, 0:S], ALU.mult, ALU.add),
                        reads=["CONVR", "FIRS", "FIRL", "CONVA"], writes=["CONVA"])
                for tg in range(S // 512):
                    pb.op("scalar", lambda em, tg=tg: em.copy(
                        ev[:, :], conva[:, tg * 512:(tg + 1) * 512]),
                        reads=["CONVA"], writes=["E0"])
                    pb.q_dma(dst[c * 128:(c + 1) * 128,
                                  seq * S + tg * 512:seq * S + (tg + 1) * 512],
                             ev[:, :], reads=["E0"], writes=[f"{dk_}@{seq}"])

        qtok = pb.sbuf("QTOK", [128, 256], BF16)
        ktok = pb.sbuf("KTOK", [128, 256], BF16)
        vtok = pb.sbuf("VTOK", [128, 256], BF16)
        kbt = pb.sbuf("KBT", [128, 256], BF16)
        vbt = pb.sbuf("VBT", [128, 256], BF16)
        sqt = pb.sbuf("HF32", [128, 512], F32)[:, 0:256]
        kfm = pb.sbuf("KFM", [128, 256], BF16)
        kbfm = pb.sbuf("KBFM", [128, 256], BF16)
        qfm = pb.sbuf("QFM", [128, 256], BF16)
        wfm = pb.sbuf("WFM", [128, 256], BF16)
        asb = pb.sbuf("ASB", [128, 128], BF16)
        bsb = pb.sbuf("BSB", [128, 128], BF16)
        a2sb = pb.sbuf("A2SB", [128, 128], BF16)
        b2sb = pb.sbuf("B2SB", [128, 128], BF16)
        mtsb = pb.sbuf("MTSB", [128, 128], BF16)
        atsb = pb.sbuf("ATSB", [128, 128], BF16)
        usb = pb.sbuf("USB", [128, 256], BF16)
        wsb = pb.sbuf("WSB", [128, 256], BF16)
        uadj = pb.sbuf("UADJ", [128, 256], BF16)
        otok = pb.sbuf("OTOK", [128, 256], BF16)
        ssb = [pb.sbuf(f"SSB{i}", [128, 256], F32) for i in range(2)]
        sbf = [pb.sbuf(f"SBF{i}", [128, 256], BF16) for i in range(2)]
        c1, c2, c3 = (pb.sbuf(f"DC{i}", [128, 1], F32) for i in range(3))

        for h in range(H):
            hr = slice(h * 256, (h + 1) * 256)
            for i in range(2):
                pb.op("vector", lambda em, i=i: em.memset(ssb[i][:, :], 0.0),
                      writes=[f"SSB{i}"])
                pb.op("vector", lambda em, i=i: em.memset(sbf[i][:, :], 0.0),
                      writes=[f"SBF{i}"])
            for ci in range(nib):
                t0 = seq * S + ci * 128
                csl = slice(t0, t0 + 128)
                pb.q_dma(qtok[:, :], qT[hr, csl], reads=[f"l1_qT@{seq}"],
                         writes=["QTOK"], transpose=True)
                pb.q_dma(ktok[:, :], kT[hr, csl], reads=[f"l1_kT@{seq}"],
                         writes=["KTOK"], transpose=True)
                pb.q_dma(vtok[:, :], vT[hr, csl], reads=[f"l1_vT@{seq}"],
                         writes=["VTOK"], transpose=True)
                pb.q_dma(bt4[:, 0:4], beta_scr[csl, 0:4], reads=["l1_beta"],
                         writes=["BT4"])
                bcol = bt4[:, h:h + 1]
                for (tsrc, tk, dst2, dk2) in ((qtok, "QTOK", qfm, None),
                                              (ktok, "KTOK", None, None)):
                    pb.op("scalar", lambda em, t=tsrc: em.activation(
                        sqt[:, :], t[:, :], AF.Square, accum_out=c1[:, 0:1]),
                        reads=[tk], writes=["HF32", "DC0"])
                    pb.op("vector", lambda em: em.tensor_single_scalar(
                        c1[:, :], c1[:, :], 1e-6, ALU.add), reads=["DC0"], writes=["DC0"])
                    pb.op("vector", lambda em: em.reciprocal(c2[:, :], c1[:, :]),
                          reads=["DC0"], writes=["DC1"])
                    pb.op("scalar", lambda em: em.activation(c2[:, :], c2[:, :], AF.Sqrt),
                          reads=["DC1"], writes=["DC1"])
                    pb.op("vector", lambda em, t=tsrc: em.tensor_scalar_mul(
                        t[:, :], t[:, :], c2[:, 0:1]), reads=[tk, "DC1"], writes=[tk])
                pb.op("vector", lambda em, bcol=bcol: em.tensor_scalar_mul(
                    kbt[:, :], ktok[:, :], bcol), reads=["KTOK", "BT4"], writes=["KBT"])
                pb.op("vector", lambda em, bcol=bcol: em.tensor_scalar_mul(
                    vbt[:, :], vtok[:, :], bcol), reads=["VTOK", "BT4"], writes=["VBT"])
                for (src2, sk2, dfm, dfk) in ((ktok, "KTOK", kfm, "KFM"),
                                              (kbt, "KBT", kbfm, "KBFM"),
                                              (qtok, "QTOK", qfm, "QFM")):
                    for half in range(2):
                        pb.transp(PT[:, 0:128], src2[:, half * 128:(half + 1) * 128],
                                  reads=[sk2], writes=["PTB0"])
                        pb.op("vector", lambda em, dfm=dfm, half=half: em.tensor_copy(
                            dfm[:, half * 128:(half + 1) * 128], PT[:, 0:128]),
                            reads=["PTB0"], writes=[dfk])
                for half in range(2):
                    pb.mm(PS[0][:, 0:128], kbfm[:, half * 128:(half + 1) * 128],
                          kfm[:, half * 128:(half + 1) * 128],
                          start=(half == 0), stop=(half == 1),
                          reads=["KBFM", "KFM"], writes=["PS0"])
                pb.op("vector", lambda em: em.scalar_tensor_tensor(
                    asb[:, :], PS[0][:, 0:128], -1.0, pb.sb["M01SL"][:, :],
                    ALU.mult, ALU.mult), reads=["PS0", "M01SL"], writes=["ASB"])
                for half in range(2):
                    pb.mm(PS[1][:, 0:128], kfm[:, half * 128:(half + 1) * 128],
                          kbfm[:, half * 128:(half + 1) * 128],
                          start=(half == 0), stop=(half == 1),
                          reads=["KFM", "KBFM"], writes=["PS1"])
                pb.op("vector", lambda em: em.scalar_tensor_tensor(
                    bsb[:, :], PS[1][:, 0:128], -1.0, pb.sb["M01SU"][:, :],
                    ALU.mult, ALU.mult), reads=["PS1", "M01SU"], writes=["BSB"])
                pb.op("vector", lambda em: em.tensor_add(mtsb[:, :], bsb[:, :],
                                                         ident[:, :]),
                      reads=["BSB", "ident"], writes=["MTSB"])
                srcA, srcAk, srcB, srcBk = asb, "ASB", bsb, "BSB"
                for lev in range(6):
                    pb.mm(PS[0][:, 0:128], srcB[:, :], srcA[:, :], start=True, stop=True,
                          reads=[srcBk, srcAk], writes=["PS0"])
                    pb.mm(PS[1][:, 0:128], srcA[:, :], srcB[:, :], start=True, stop=True,
                          reads=[srcAk, srcBk], writes=["PS1"])
                    dstA, dstAk = (a2sb, "A2SB") if srcA is asb else (asb, "ASB")
                    dstB, dstBk = (b2sb, "B2SB") if srcB is bsb else (bsb, "BSB")
                    pb.op("vector", lambda em, dstA=dstA: em.tensor_copy(dstA[:, :],
                                                                         PS[0][:, 0:128]),
                          reads=["PS0"], writes=[dstAk])
                    pb.op("vector", lambda em, dstB=dstB: em.tensor_copy(dstB[:, :],
                                                                         PS[1][:, 0:128]),
                          reads=["PS1"], writes=[dstBk])
                    pb.mm(PS[2][:, 0:128], dstA[:, :], mtsb[:, :], start=True, stop=True,
                          reads=[dstAk, "MTSB"], writes=["PS2"])
                    pb.op("vector", lambda em: em.scalar_tensor_tensor(
                        mtsb[:, :], PS[2][:, 0:128], 1.0, mtsb[:, :], ALU.mult, ALU.add),
                        reads=["PS2", "MTSB"], writes=["MTSB"])
                    srcA, srcAk = dstA, dstAk
                    srcB, srcBk = dstB, dstBk
                pb.mm(PS[1][:, 0:256], mtsb[:, :], vbt[:, :], start=True, stop=True,
                      reads=["MTSB", "VBT"], writes=["PS1"])
                pb.op("vector", lambda em: em.tensor_copy(usb[:, :], PS[1][:, 0:256]),
                      reads=["PS1"], writes=["USB"])
                pb.mm(PS[1][:, 0:256], mtsb[:, :], kbt[:, :], start=True, stop=True,
                      reads=["MTSB", "KBT"], writes=["PS1"])
                pb.op("vector", lambda em: em.tensor_copy(wsb[:, :], PS[1][:, 0:256]),
                      reads=["PS1"], writes=["WSB"])
                for half in range(2):
                    pb.transp(PT[:, 0:128], wsb[:, half * 128:(half + 1) * 128],
                              reads=["WSB"], writes=["PTB0"])
                    pb.op("vector", lambda em, half=half: em.tensor_copy(
                        wfm[:, half * 128:(half + 1) * 128], PT[:, 0:128]),
                        reads=["PTB0"], writes=["WFM"])
                for half in range(2):
                    pb.mm(PS[1][:, 0:256], wfm[:, half * 128:(half + 1) * 128],
                          sbf[half][:, :], start=(half == 0), stop=(half == 1),
                          reads=["WFM", f"SBF{half}"], writes=["PS1"])
                pb.op("vector", lambda em: em.tensor_sub(uadj[:, :], usb[:, :],
                                                         PS[1][:, 0:256]),
                      reads=["USB", "PS1"], writes=["UADJ"])
                for half in range(2):
                    pb.mm(PS[0][:, 0:128],
                          kfm[:, half * 128:(half + 1) * 128],
                          qfm[:, half * 128:(half + 1) * 128],
                          start=(half == 0), stop=(half == 1),
                          reads=["KFM", "QFM"], writes=["PS0"])
                pb.op("vector", lambda em: em.tensor_mul(
                    atsb[:, :], PS[0][:, 0:128], pb.sb["M01IU"][:, :]),
                    reads=["PS0", "M01IU"], writes=["ATSB"])
                for half in range(2):
                    pb.mm(PS[3][:, 0:256], qfm[:, half * 128:(half + 1) * 128],
                          sbf[half][:, :], start=(half == 0), stop=False,
                          reads=["QFM", f"SBF{half}"], writes=["PS3"])
                pb.mm(PS[3][:, 0:256], atsb[:, :], uadj[:, :], start=False, stop=True,
                      reads=["ATSB", "UADJ"], writes=["PS3"])
                pb.op("vector", lambda em: em.tensor_copy(otok[:, :], PS[3][:, 0:256]),
                      reads=["PS3"], writes=["OTOK"])
                pb.q_dma(od_scr[csl, hr], otok[:, :], reads=["OTOK"],
                         writes=[f"l1_od@{seq}"])
                for half in range(2):
                    pb.transp(PT[:, 0:128], otok[:, half * 128:(half + 1) * 128],
                              reads=["OTOK"], writes=["PTB0"])
                    pb.op("vector", lambda em, half=half: em.tensor_copy(
                        kfm[:, half * 128:(half + 1) * 128], PT[:, 0:128]),
                        reads=["PTB0"], writes=["KFM"])
                for half in range(2):
                    pb.q_dma(odf_scr[h * 256 + half * 128:h * 256 + (half + 1) * 128, csl],
                             kfm[:, half * 128:(half + 1) * 128],
                             reads=["KFM"], writes=[f"l1_odf@{seq}"])
                for half in range(2):
                    pb.mm(PS[4][:, 0:256], ktok[:, half * 128:(half + 1) * 128],
                          uadj[:, :], start=True, stop=True,
                          reads=["KTOK", "UADJ"], writes=["PS4"])
                    pb.op("vector", lambda em, half=half: em.tensor_add(
                        ssb[half][:, :], ssb[half][:, :], PS[4][:, 0:256]),
                        reads=[f"SSB{half}", "PS4"], writes=[f"SSB{half}"])
                    pb.op("vector", lambda em, half=half: em.tensor_copy(
                        sbf[half][:, :], ssb[half][:, :]),
                        reads=[f"SSB{half}"], writes=[f"SBF{half}"])

        ones128 = pb.sbuf("ONES128", [128, 1], BF16)
        pb.op("vector", lambda em: em.memset(ones128[:, :], 1.0), writes=["ONES128"])
        pfm = pb.sbuf("PBUF", [128, 2048], BF16)
        pfm2 = pb.sbuf("PBUF2", [128, 2048], BF16)
        nts = S // 512
        for pi, (scr, rkp) in enumerate(((ps_scr, f"l1_ps@{seq}"),
                                         (plg_scr, f"l1_pl@{seq}"),
                                         (odf_scr, f"l1_odf@{seq}"),
                                         (vT, f"l1_vT@{seq}"))):
            for h in range(H):
                sc_ = pb.sbuf("SCORES", [128, 2048], F32)
                for kind in range(2):
                    for ch in range(2):
                        row0 = h * 256 + ch * 128
                        pb.q_dma(pfm[:, 0:S], scr[row0:row0 + 128, seq * S:(seq + 1) * S],
                                 reads=[rkp], writes=["PBUF"])
                        if kind == 0:
                            til, tkk = pfm, "PBUF"
                        else:
                            fn = AF.Square if kind == 1 else AF.Abs
                            pb.op("scalar", lambda em, fn=fn: em.activation(
                                pfm2[:, 0:S], pfm[:, 0:S], fn),
                                reads=["PBUF"], writes=["PBUF2"])
                            til, tkk = pfm2, "PBUF2"
                        for ts_ in range(nts):
                            pb.mm(PS[ts_][0:1, :], ones128[:, :],
                                  til[:, ts_ * 512:(ts_ + 1) * 512],
                                  start=(ch == 0), stop=(ch == 1),
                                  reads=["ONES128", tkk], writes=[f"PS{ts_}"])
                    if kind == 0:
                        for ts_ in range(nts):
                            pb.op("vector", lambda em, ts_=ts_: em.tensor_scalar_mul(
                                conva[0:1, ts_ * 512:(ts_ + 1) * 512],
                                PS[ts_][0:1, :], 1.0 / 256.0),
                                reads=[f"PS{ts_}"], writes=["CONVA"])
                    else:
                        for ts_ in range(nts):
                            pb.op("vector", lambda em, ts_=ts_: em.tensor_copy(
                                sc_[0:1, ts_ * 512:(ts_ + 1) * 512], PS[ts_][0:1, :]),
                                reads=[f"PS{ts_}"], writes=["SCORES"])
                r0 = pi * 16 + h * 4
                pb.q_dma(gin_scr[r0:r0 + 1, seq * S:(seq + 1) * S], conva[0:1, 0:S],
                         reads=["CONVA"], writes=["l1_gin"])
                pb.op("scalar", lambda em: em.activation(pfm[0:1, 0:S], sc_[0:1, 0:S],
                                                         AF.Sqrt),
                      reads=["SCORES"], writes=["PBUF"])
                pb.g_dma(gin_scr[r0 + 3:r0 + 4, seq * S:(seq + 1) * S], pfm[0:1, 0:S],
                         reads=["PBUF"], writes=["l1_gin"])
                pb.op("vector", lambda em: em.tensor_mul(conva[0:1, 0:S],
                                                         conva[0:1, 0:S],
                                                         conva[0:1, 0:S]),
                      reads=["CONVA"], writes=["CONVA"])
                pb.op("vector", lambda em: em.scalar_tensor_tensor(
                    sc_[0:1, 0:S], sc_[0:1, 0:S], 1.0 / 256.0, conva[0:1, 0:S],
                    ALU.mult, ALU.subtract), reads=["SCORES", "CONVA"], writes=["SCORES"])
                pb.q_dma(gin_scr[r0 + 1:r0 + 2, seq * S:(seq + 1) * S], sc_[0:1, 0:S],
                         reads=["SCORES"], writes=["l1_gin"])
                for ch in range(2):
                    row0_ = h * 256 + ch * 128
                    pb.q_dma(pfm[:, 0:S], scr[row0_:row0_ + 128, seq * S:(seq + 1) * S],
                             reads=[rkp], writes=["PBUF"])
                    pb.op("scalar", lambda em: em.activation(pfm2[:, 0:S], pfm[:, 0:S],
                                                             AF.Abs),
                          reads=["PBUF"], writes=["PBUF2"])
                    for ts_ in range(nts):
                        pb.mm(PS[ts_][0:1, :], ones128[:, :],
                              pfm2[:, ts_ * 512:(ts_ + 1) * 512],
                              start=(ch == 0), stop=(ch == 1),
                              reads=["ONES128", "PBUF2"], writes=[f"PS{ts_}"])
                for ts_ in range(nts):
                    pb.op("vector", lambda em, ts_=ts_: em.tensor_scalar_mul(
                        conva[0:1, ts_ * 512:(ts_ + 1) * 512], PS[ts_][0:1, :],
                        1.0 / 256.0), reads=[f"PS{ts_}"], writes=["CONVA"])
                pb.q_dma(gin_scr[r0 + 2:r0 + 3, seq * S:(seq + 1) * S], conva[0:1, 0:S],
                         reads=["CONVA"], writes=["l1_gin"])

        tcon = pb.sbuf("TCON", [1, 64], F32)
        tinvb = pb.sbuf("TINVB", [128, 16], F32)
        nfeb = pb.sbuf("NFEB", [128, 16], F32)
        ltr = tcon[0:1, 0:16]
        fpr = tcon[0:1, 16:32]
        t1r = tcon[0:1, 32:48]
        t2r = tcon[0:1, 48:64]
        pb.q_dma(ltr, inp["l1_ltemp"][:, :], reads=["in_ltemp"], writes=["TCON"])
        pb.q_dma(fpr, inp["l1_fpar"][:, :], reads=["in_fpar"], writes=["TCON"])
        pb.op("scalar", lambda em: em.activation(t1r, ltr, AF.Abs),
              reads=["TCON"], writes=["TCON"])
        pb.op("scalar", lambda em: em.activation(t1r, t1r, AF.Exp, scale=-1.0),
              reads=["TCON"], writes=["TCON"])
        pb.op("scalar", lambda em: em.activation(t1r, t1r, AF.Ln, bias=1.0),
              reads=["TCON"], writes=["TCON"])
        pb.op("vector", lambda em: em.tensor_single_scalar(t2r, ltr, 0.0, ALU.max),
              reads=["TCON"], writes=["TCON"])
        pb.op("vector", lambda em: em.tensor_add(t1r, t1r, t2r),
              reads=["TCON"], writes=["TCON"])
        pb.op("vector", lambda em: em.tensor_single_scalar(t1r, t1r, 0.25, ALU.add),
              reads=["TCON"], writes=["TCON"])
        pb.op("vector", lambda em: em.reciprocal(t1r, t1r), reads=["TCON"], writes=["TCON"])
        pb.op("scalar", lambda em: em.activation(t2r, fpr, AF.Exp, scale=-1.0),
              reads=["TCON"], writes=["TCON"])
        pb.op("vector", lambda em: em.tensor_single_scalar(t2r, t2r, 1.0, ALU.add),
              reads=["TCON"], writes=["TCON"])
        pb.op("vector", lambda em: em.reciprocal(t2r, t2r),
              reads=["TCON"], writes=["TCON"])
        pb.op("vector", lambda em: em.tensor_scalar(t2r, t2r, 0.07, 0.005,
                                                    ALU.mult, ALU.add),
              reads=["TCON"], writes=["TCON"])
        pb.op("vector", lambda em: em.tensor_single_scalar(t2r, t2r, 0.25, ALU.min),
              reads=["TCON"], writes=["TCON"])
        for h in range(4):
            pb.op("vector", lambda em, h=h: em.reduce_sum(
                fpr[0:1, h:h + 1], tcon[0:1, 48 + h * 4:48 + (h + 1) * 4], AX.X),
                reads=["TCON"], writes=["TCON"])
        pb.op("vector", lambda em: em.tensor_single_scalar(
            fpr[0:1, 0:4], fpr[0:1, 0:4], 0.99, ALU.min), reads=["TCON"], writes=["TCON"])
        pb.op("vector", lambda em: em.reciprocal(fpr[0:1, 0:4], fpr[0:1, 0:4]),
              reads=["TCON"], writes=["TCON"])
        for h in range(4):
            pb.op("vector", lambda em, h=h: em.tensor_scalar_mul(
                tcon[0:1, 48 + h * 4:48 + (h + 1) * 4],
                tcon[0:1, 48 + h * 4:48 + (h + 1) * 4],
                fpr[0:1, h:h + 1]), reads=["TCON"], writes=["TCON"])
        pb.op("vector", lambda em: em.tensor_scalar(
            t2r, t2r, 0.99, 1e-9, ALU.mult, ALU.add), reads=["TCON"], writes=["TCON"])
        evb = pb.sb["E0"]
        for (rsrc, dst2, dk2) in ((t1r, tinvb, "TINVB"), (t2r, nfeb, "NFEB")):
            pb.op("vector", lambda em, rsrc=rsrc: em.tensor_copy(evb[0:1, 0:16], rsrc),
                  reads=["TCON"], writes=["E0"])
            pb.mm(PS[0][:, 0:16], pb.sb["onescol"][:, :], evb[0:1, 0:16],
                  start=True, stop=True, reads=["onescol", "E0"], writes=["PS0"])
            pb.op("vector", lambda em, dst2=dst2: em.tensor_copy(dst2[:, :], PS[0][:, 0:16]),
                  reads=["PS0"], writes=[dk2])

        for d in range(8):
            g = pb.sbuf(f"G{d}", [128, D], BF16)
            pb.g_dma(g[:, :], inp["l1_w1T"][d * 128:(d + 1) * 128, :],
                     reads=["in_w1T"], writes=[f"G{d}"])
        g8t = pb.sbuf("G8", [128, D], BF16)
        pb.g_dma(g8t[0:64, :], inp["l1_w1T"][1024:1088, :], reads=["in_w1T"],
                 writes=["G8"])
        gins = pb.sbuf("LT", [64, 2048], BF16)
        pb.g_dma(gins[:, 0:S], gin_scr[:, seq * S:(seq + 1) * S], reads=["l1_gin"],
                 writes=["LT"])
        gb1 = pb.sbuf("GB1", [128, 8], F32)
        pb.q_dma(gb1[:, :], inp["l1_gb1T"][:, :], reads=["in_gb1"], writes=["GB1"])
        w2ts = pb.sbuf("W2TS", [128, 128], BF16)
        for mc in range(8):
            pb.g_dma(w2ts[:, mc * 16:(mc + 1) * 16],
                     inp["l1_w2T"][mc * 128:(mc + 1) * 128, :],
                     reads=["in_w2T"], writes=["W2TS"])
        gb2 = pb.sbuf("GB2", [16, 1], F32)
        pb.q_dma(gb2[:, :], inp["l1_gb2c"][:, :], reads=["in_gb2"], writes=["GB2"])
        wgts = pb.sbuf("EG", [128, 512], F32)
        logt = pb.sbuf("LOGT", [128, 16], F32)
        ncol = pb.sbuf("DC0", [128, 1], F32)
        scol = pb.sbuf("DC1", [128, 1], F32)
        rcol = pb.sbuf("DC2", [128, 1], F32)

        for ts_ in range(nts):
            tsl = slice(ts_ * 512, (ts_ + 1) * 512)
            for mc in range(8):
                ps = PS[4 + mc % 2]
                psk = f"PS{4 + mc % 2}"
                for d in range(8):
                    pb.mm(ps[:, :], pb.sb[f"G{d}"][:, mc * 128:(mc + 1) * 128],
                          xp[d][:, tsl], start=(d == 0), stop=False,
                          reads=[f"G{d}", xpk[d]], writes=[psk])
                pb.mm(ps[:, :], g8t[0:64, mc * 128:(mc + 1) * 128], gins[:, tsl],
                      start=False, stop=True, reads=["G8", "LT"], writes=[psk])
                tgt = pfm if mc < 4 else pfm2
                tgtk = "PBUF" if mc < 4 else "PBUF2"
                pb.op("scalar", lambda em, ps=ps, mc=mc, tgt=tgt: em.activation(
                    tgt[:, (mc % 4) * 512:(mc % 4 + 1) * 512], ps[:, :], AF.Gelu,
                    bias=gb1[:, mc:mc + 1]),
                    reads=[psk, "GB1"], writes=[tgtk])
            for mc in range(8):
                tgt = pfm if mc < 4 else pfm2
                tgtk = "PBUF" if mc < 4 else "PBUF2"
                pb.mm(PS[0][0:16, :], w2ts[:, mc * 16:(mc + 1) * 16],
                      tgt[:, (mc % 4) * 512:(mc % 4 + 1) * 512],
                      start=(mc == 0), stop=(mc == 7),
                      reads=["W2TS", tgtk], writes=["PS0"])
            pb.op("scalar", lambda em: em.activation(
                pb.sb["STILE"][0:16, 0:512], PS[0][0:16, :], AF.Identity,
                bias=gb2[:, 0:1]), reads=["PS0", "GB2"], writes=["STILE"])
            for sub in range(4):
                tb = ts_ * 4 + sub
                pb.transp(PS[1][:, 0:16],
                          pb.sb["STILE"][0:16, sub * 128:(sub + 1) * 128],
                          reads=["STILE"], writes=["PS1"], np_=16, f32=True)
                pb.op("vector", lambda em: em.tensor_copy(logt[:, :], PS[1][:, 0:16]),
                      reads=["PS1"], writes=["LOGT"])
                pb.op("vector", lambda em: em.tensor_mul(logt[:, :], logt[:, :],
                                                         tinvb[:, :]),
                      reads=["LOGT", "TINVB"], writes=["LOGT"])
                for h in range(4):
                    hs = slice(h * 4, (h + 1) * 4)
                    pb.op("vector", lambda em, hs=hs: em.reduce_max(
                        ncol[:, :], logt[:, hs], AX.X, negate=True),
                        reads=["LOGT"], writes=["DC0"])
                    pb.op("scalar", lambda em, hs=hs: em.activation(
                        logt[:, hs], logt[:, hs], AF.Exp, bias=ncol[:, 0:1],
                        accum_out=scol[:, 0:1]),
                        reads=["LOGT", "DC0"], writes=["LOGT", "DC1"])
                    pb.op("vector", lambda em: em.reciprocal(rcol[:, :], scol[:, :]),
                          reads=["DC1"], writes=["DC2"])
                    pb.op("vector", lambda em, hs=hs: em.tensor_scalar_mul(
                        logt[:, hs], logt[:, hs], rcol[:, 0:1]),
                        reads=["LOGT", "DC2"], writes=["LOGT"])
                pb.op("vector", lambda em: em.tensor_max(logt[:, :], logt[:, :],
                                                         nfeb[:, :]),
                      reads=["LOGT", "NFEB"], writes=["LOGT"])
                for h in range(4):
                    hs = slice(h * 4, (h + 1) * 4)
                    pb.op("vector", lambda em, hs=hs: em.reduce_sum(
                        scol[:, :], logt[:, hs], AX.X), reads=["LOGT"], writes=["DC1"])
                    pb.op("vector", lambda em: em.tensor_single_scalar(
                        scol[:, :], scol[:, :], 1e-8, ALU.add),
                        reads=["DC1"], writes=["DC1"])
                    pb.op("vector", lambda em: em.reciprocal(rcol[:, :], scol[:, :]),
                          reads=["DC1"], writes=["DC2"])
                    pb.op("vector", lambda em, hs=hs, tb=tb, h=h: em.tensor_single_scalar(
                        wgts[:, tb * 16 + h * 4:tb * 16 + (h + 1) * 4],
                        logt[:, hs], rcol[:, 0:1], ALU.mult),
                        reads=["LOGT", "DC2"], writes=["EG"])

        gamb = pb.sbuf("CONVR", [128, 2112], BF16)[:, 0:1024]
        cp = [pb.sb["PBUF"][:, 0:1024], pb.sb["PBUF"][:, 1024:2048],
              pb.sb["PBUF2"][:, 0:1024], pb.sb["PBUF2"][:, 1024:2048]]
        cpk = ["PBUF", "PBUF", "PBUF2", "PBUF2"]
        pb.g_dma(cp[0][0:1, :], inp["l1_onw"][0:1, :], reads=["in_onw"], writes=["PBUF"])
        pb.mm(PS[0][:, 0:512], pb.sb["onescol"][:, :], cp[0][0:1, 0:512],
              start=True, stop=True, reads=["onescol", "PBUF"], writes=["PS0"])
        pb.mm(PS[1][:, 0:512], pb.sb["onescol"][:, :], cp[0][0:1, 512:1024],
              start=True, stop=True, reads=["onescol", "PBUF"], writes=["PS1"])
        pb.op("vector", lambda em: em.tensor_copy(gamb[:, 0:512], PS[0][:, 0:512]),
              reads=["PS0"], writes=["CONVR"])
        pb.op("vector", lambda em: em.tensor_copy(gamb[:, 512:1024], PS[1][:, 0:512]),
              reads=["PS1"], writes=["CONVR"])
        accm = pb.sb["STILE"]
        ob16 = pb.sb["CONVR"][:, 1024:2048]
        for tb in range(nib):
            t0 = seq * S + tb * 128
            for pi2, (scr, rkp) in enumerate(((ps_scr, f"l1_ps@{seq}"),
                                              (plg_scr, f"l1_pl@{seq}"),
                                              (od_scr, f"l1_od@{seq}"),
                                              (vT, f"l1_vT@{seq}"))):
                if pi2 == 2:
                    pb.q_dma(cp[pi2][:, :], scr[t0:t0 + 128, :], reads=[rkp],
                             writes=[cpk[pi2]])
                else:
                    for c in range(8):
                        pb.q_dma(cp[pi2][:, c * 128:(c + 1) * 128],
                                 scr[c * 128:(c + 1) * 128, t0:t0 + 128],
                                 reads=[rkp], writes=[cpk[pi2]], transpose=True)
            for h in range(4):
                hs = slice(h * 256, (h + 1) * 256)
                for pi2 in range(4):
                    wcol = wgts[:, (tb + 0) * 16 + h * 4 + pi2:
                                (tb + 0) * 16 + h * 4 + pi2 + 1]
                    if pi2 == 0:
                        pb.op("vector", lambda em, hs=hs, wcol=wcol: em.tensor_scalar_mul(
                            accm[:, hs], cp[0][:, hs], wcol),
                            reads=[cpk[0], "EG"], writes=["STILE"])
                    else:
                        pb.op("vector", lambda em, hs=hs, wcol=wcol, pi2=pi2:
                              em.scalar_tensor_tensor(
                                  accm[:, hs], cp[pi2][:, hs], wcol, accm[:, hs],
                                  ALU.mult, ALU.add),
                              reads=[cpk[pi2], "EG", "STILE"], writes=["STILE"])
                pb.op("scalar", lambda em, hs=hs: em.activation(
                    pb.sb["HF32"][:, 0:256], accm[:, hs], AF.Square,
                    accum_out=ncol[:, 0:1]),
                    reads=["STILE"], writes=["HF32", "DC0"])
                pb.op("vector", lambda em: em.tensor_scalar(
                    ncol[:, :], ncol[:, :], 1.0 / 256.0, 1e-5, ALU.mult, ALU.add),
                    reads=["DC0"], writes=["DC0"])
                pb.op("vector", lambda em: em.reciprocal(scol[:, :], ncol[:, :]),
                      reads=["DC0"], writes=["DC1"])
                pb.op("scalar", lambda em: em.activation(scol[:, :], scol[:, :], AF.Sqrt),
                      reads=["DC1"], writes=["DC1"])
                pb.op("vector", lambda em, hs=hs: em.tensor_scalar_mul(
                    accm[:, hs], accm[:, hs], scol[:, 0:1]),
                    reads=["STILE", "DC1"], writes=["STILE"])
            pb.op("vector", lambda em: em.tensor_mul(ob16[:, :], accm[:, 0:1024],
                                                     gamb[:, :]),
                  reads=["STILE", "CONVR"], writes=["CONVR"])
            pb.q_dma(o_scr[t0:t0 + 128, :], ob16[:, :], reads=["CONVR"],
                     writes=[f"l1_o@{seq}"])

        for d in range(8):
            g = pb.sbuf(f"G{16+d}", [128, D], BF16)
            pb.g_dma(g[:, :], inp["l1_woT"][d * 128:(d + 1) * 128, :],
                     reads=["in_wo1"], writes=[f"G{16+d}"])
        qi = [pb.sbuf(f"QI{d}", [128, 128], BF16) for d in range(8)]
        for tb in range(nib):
            t0 = seq * S + tb * 128
            for c in range(8):
                pb.q_dma(qi[c][:, :], o_scr[t0:t0 + 128, c * 128:(c + 1) * 128],
                         reads=[f"l1_o@{seq}"], writes=[f"QI{c}"], transpose=True)
            for oh in range(2):
                ps = PS[oh]
                psk = f"PS{oh}"
                for c in range(8):
                    pb.mm(ps[:, :], qi[c][:, :],
                          pb.sb[f"G{16+c}"][:, oh * 512:(oh + 1) * 512],
                          start=(c == 0), stop=(c == 7),
                          reads=[f"QI{c}", f"G{16+c}"], writes=[psk])
                eng = "vector" if oh == 0 else "scalar"
                if eng == "vector":
                    pb.op(eng, lambda em, ps=ps, oh=oh: em.tensor_copy(
                        pb.sb["HF32"][:, 0:512], ps[:, :]), reads=[psk], writes=["HF32"])
                else:
                    pb.op(eng, lambda em, ps=ps, oh=oh: em.copy(
                        pb.sb["HF32"][:, 0:512], ps[:, :]), reads=[psk], writes=["HF32"])
                pb.q_dma(s_scr[t0:t0 + 128, oh * 512:(oh + 1) * 512],
                         pb.sb["HF32"][:, 0:512], reads=["HF32"],
                         writes=[f"s_l1@{t0 // 128}"])

    gt32 = pb.sbuf("GT32", [128, 288], F32)
    xp = [pb.sb[f"XP{d}"] for d in range(8)]
    xpk = [f"XP{d}" for d in range(8)]
    ehalf = pb.sb["EG"] if "EG" in pb.sb else pb.sbuf("EG", [128, 512], F32)
    for e in range(8):
        for fs in range(2):
            for c in range(8):
                for j in range(2):
                    g = pb.sbuf(f"G{2*c+j}", [128, D], BF16)
                    pb.g_dma(g[:, :],
                             inp["L1_ffn_w1"][e, c * 128:(c + 1) * 128,
                                              fs * 2048 + j * 1024:fs * 2048 + (j + 1) * 1024],
                             reads=["in_moe1"], writes=[f"G{2*c+j}"])
            for fc in range(16):
                g = pb.sbuf(f"G{16+fc}", [128, D], BF16)
                pb.g_dma(g[:, :],
                         inp["L1_ffn_w2"][e, fs * 2048 + fc * 128:fs * 2048 + (fc + 1) * 128, :],
                         reads=["in_moe2"], writes=[f"G{16+fc}"])
            for tg in range(T // 512):
                for d in range(8):
                    xsl = xp[4 + d // 4][:, (d % 4) * 512:(d % 4 + 1) * 512]
                    pb.q_dma(xsl, xnT_scr[d * 128:(d + 1) * 128,
                                          tg * 512:(tg + 1) * 512],
                             reads=["l1_xnT"], writes=[xpk[4 + d // 4]])
                for fc in range(16):
                    ps = PS[4 + fc % 2]
                    psk = f"PS{4 + fc % 2}"
                    j = fc // 8
                    off = (fc % 8) * 128
                    for d in range(8):
                        xsl = xp[4 + d // 4][:, (d % 4) * 512:(d % 4 + 1) * 512]
                        pb.mm(ps[:, :], pb.sb[f"G{2*d+j}"][:, off:off + 128],
                              xsl, start=(d == 0), stop=(d == 7),
                              reads=[f"G{2*d+j}", xpk[4 + d // 4]], writes=[psk])
                    hsl = xp[fc // 4][:, (fc % 4) * 512:(fc % 4 + 1) * 512]
                    pb.op("scalar", lambda em, ps=ps, hsl=hsl: em.activation(
                        hsl, ps[:, :], AF.Gelu), reads=[psk],
                        writes=[xpk[fc // 4]])
                for tsub in range(4):
                    tbg = tg * 4 + tsub
                    for oh in range(2):
                        ps = PS[oh]
                        psk = f"PS{oh}"
                        for fc in range(16):
                            hsl = xp[fc // 4][:, (fc % 4) * 512 + tsub * 128:
                                              (fc % 4) * 512 + (tsub + 1) * 128]
                            pb.mm(ps[:, :], hsl,
                                  pb.sb[f"G{16+fc}"][:, oh * 512:(oh + 1) * 512],
                                  start=(fc == 0), stop=(fc == 15),
                                  reads=[xpk[fc // 4], f"G{16+fc}"], writes=[psk])
                        gcol = gt32[:, tbg * 9 + e:tbg * 9 + e + 1]
                        pb.op("vector", lambda em, ps=ps, gcol=gcol: em.tensor_scalar_mul(
                            ehalf[:, :], ps[:, :], gcol),
                            reads=[psk, "GT32"], writes=["EG"])
                        pb.g_dma(acc_scr[tbg * 128:(tbg + 1) * 128,
                                         oh * 512:(oh + 1) * 512],
                                 ehalf[:, :], reads=["EG"], writes=[f"g_l1@{tbg}"],
                                 accum=True)

    emit_combine(pb, "l1", xres, s_scr, pb.dr["l1_acc"], T)



def prep_weights_L1(w):
    out = {}
    out["l1_wqT"] = np.ascontiguousarray(np.asarray(w["L1_attn_Wq"]).T)
    out["l1_wkT"] = np.ascontiguousarray(np.asarray(w["L1_attn_Wk"]).T)
    out["l1_wvT"] = np.ascontiguousarray(np.asarray(w["L1_attn_Wv"]).T)
    out["l1_wbT"] = np.ascontiguousarray(np.asarray(w["L1_attn_Wb"]).T)
    for nm, key in (("l1_cqT", "L1_attn_conv_q_w"), ("l1_ckT", "L1_attn_conv_k_w"),
                    ("l1_cvT", "L1_attn_conv_v_w")):
        cw = np.asarray(w[key])
        out[nm] = np.ascontiguousarray(cw.reshape(8, 128, 4).transpose(1, 0, 2).reshape(128, 32))
    fs = np.asarray(w["L1_attn_fir_short_w"]).reshape(1024, 5)
    fl = np.asarray(w["L1_attn_fir_long_w"]).reshape(1024, 63)
    out["l1_firS"] = np.ascontiguousarray(fs.reshape(8, 128, 5).transpose(1, 0, 2).reshape(128, 40))
    out["l1_firL"] = np.ascontiguousarray(fl.reshape(8, 128, 63).transpose(1, 0, 2).reshape(128, 504))
    out["l1_ltemp"] = np.ascontiguousarray(np.asarray(w["L1_attn_log_temp"]).reshape(1, 16))
    out["l1_fpar"] = np.ascontiguousarray(np.asarray(w["L1_attn_floor_param"]).reshape(1, 16))
    out["l1_w1T"] = np.ascontiguousarray(np.asarray(w["L1_attn_gate_W1"]).T)
    out["l1_gb1T"] = np.ascontiguousarray(np.asarray(w["L1_attn_gate_b1"]).reshape(8, 128).T)
    out["l1_w2T"] = np.ascontiguousarray(np.asarray(w["L1_attn_gate_W2"]).T)
    out["l1_gb2c"] = np.ascontiguousarray(np.asarray(w["L1_attn_gate_b2"]).reshape(16, 1))
    out["l1_onw"] = np.ascontiguousarray(np.tile(np.asarray(w["L1_attn_o_norm_w"]), 4).reshape(1, 1024))
    out["l1_woT"] = np.ascontiguousarray(np.asarray(w["L1_attn_Wo"]).T)
    out["L1_ffn_router_w"] = np.ascontiguousarray(np.asarray(w["L1_ffn_router_w"]))
    out["l1_rb"] = np.ascontiguousarray(np.asarray(w["L1_ffn_router_b"]).reshape(1, 9))
    out["L1_ffn_w1"] = np.ascontiguousarray(np.asarray(w["L1_ffn_w1"]))
    out["L1_ffn_w2"] = np.ascontiguousarray(np.asarray(w["L1_ffn_w2"]))
    return out


def declare_inputs_L1(pb):
    inp = {}
    inp["l1_wqT"] = pb.dram("l1_wqT", [D, D], F32, kind="ExternalInput")
    inp["l1_wkT"] = pb.dram("l1_wkT", [D, D], F32, kind="ExternalInput")
    inp["l1_wvT"] = pb.dram("l1_wvT", [D, D], F32, kind="ExternalInput")
    inp["l1_wbT"] = pb.dram("l1_wbT", [D, 4], F32, kind="ExternalInput")
    inp["l1_cqT"] = pb.dram("l1_cqT", [128, 32], F32, kind="ExternalInput")
    inp["l1_ckT"] = pb.dram("l1_ckT", [128, 32], F32, kind="ExternalInput")
    inp["l1_cvT"] = pb.dram("l1_cvT", [128, 32], F32, kind="ExternalInput")
    inp["l1_firS"] = pb.dram("l1_firS", [128, 40], F32, kind="ExternalInput")
    inp["l1_firL"] = pb.dram("l1_firL", [128, 504], F32, kind="ExternalInput")
    inp["l1_ltemp"] = pb.dram("l1_ltemp", [1, 16], F32, kind="ExternalInput")
    inp["l1_fpar"] = pb.dram("l1_fpar", [1, 16], F32, kind="ExternalInput")
    inp["l1_w1T"] = pb.dram("l1_w1T", [1088, D], F32, kind="ExternalInput")
    inp["l1_gb1T"] = pb.dram("l1_gb1T", [128, 8], F32, kind="ExternalInput")
    inp["l1_w2T"] = pb.dram("l1_w2T", [D, 16], F32, kind="ExternalInput")
    inp["l1_gb2c"] = pb.dram("l1_gb2c", [16, 1], F32, kind="ExternalInput")
    inp["l1_onw"] = pb.dram("l1_onw", [1, D], F32, kind="ExternalInput")
    inp["l1_woT"] = pb.dram("l1_woT", [D, D], F32, kind="ExternalInput")
    inp["L1_ffn_router_w"] = pb.dram("L1_ffn_router_w", [D, 9], F32, kind="ExternalInput")
    inp["l1_rb"] = pb.dram("l1_rb", [1, 9], F32, kind="ExternalInput")
    inp["L1_ffn_w1"] = pb.dram("L1_ffn_w1", [8, D, 4 * D], F32, kind="ExternalInput")
    inp["L1_ffn_w2"] = pb.dram("L1_ffn_w2", [8, 4 * D, D], F32, kind="ExternalInput")
    return inp


def prep_weights_L0(w):
    out = {}
    out["L0_attn_w_dec_attn"] = np.ascontiguousarray(w["L0_attn_w_dec_attn"])
    wo = np.asarray(w["L0_attn_w_out"])
    out["l0_w1t"] = np.ascontiguousarray(wo[:, :D].T)
    out["l0_w2t"] = np.ascontiguousarray(wo[:, D:].T)
    out["L0_attn_b_out"] = np.ascontiguousarray(np.asarray(w["L0_attn_b_out"]).reshape(1, D))
    out["l0_wfct"] = np.ascontiguousarray(np.asarray(w["L0_ffn_W_fc"]).T)
    out["l0_wprojt"] = np.ascontiguousarray(np.asarray(w["L0_ffn_W_proj"]).T)
    return out


def declare_inputs_L0(pb):
    inp = {}
    inp["L0_attn_w_dec_attn"] = pb.dram("L0_attn_w_dec_attn", [D, D], F32, kind="ExternalInput")
    inp["l0_w1t"] = pb.dram("l0_w1t", [D, D], F32, kind="ExternalInput")
    inp["l0_w2t"] = pb.dram("l0_w2t", [D, D], F32, kind="ExternalInput")
    inp["L0_attn_b_out"] = pb.dram("L0_attn_b_out", [1, D], F32, kind="ExternalInput")
    inp["l0_wfct"] = pb.dram("l0_wfct", [D, 4 * D], F32, kind="ExternalInput")
    inp["l0_wprojt"] = pb.dram("l0_wprojt", [4 * D, D], F32, kind="ExternalInput")
    return inp


def build_program(n_seq=2, S=2048, layers=(0, 1, 2, 3), taps=()):
    nc = bass.Bass("TRN2", target_bir_lowering=False)
    es = contextlib.ExitStack()
    pb = PB(nc, es)
    T = n_seq * S

    x0 = pb.dram("x0", [T, D], F32, kind="ExternalInput")
    y = pb.dram("y", [T, D], F32, kind="ExternalOutput")

    emit_prelude(pb)
    emit_causal_masks(pb)
    emit_incl_masks(pb)
    xres = pb.dram("xres", [T, D], F32)
    pb.q_dma(xres[:, :], x0[:, :], reads=["in_x0"], writes=["xres"])

    if 0 in layers:
        inp = declare_inputs_L0(pb)
        emit_L0(pb, inp, n_seq, S)
    if 1 in layers:
        inp1 = declare_inputs_L1(pb)
        emit_L1(pb, inp1, n_seq, S)
    if 2 in layers:
        inp2 = declare_inputs_L2(pb)
        emit_L2(pb, inp2, n_seq, S)
    if 3 in layers:
        inp3 = declare_inputs_L3(pb)
        emit_L3(pb, inp3, n_seq, S)

    pb.q_dma(y[:, :], xres[:, :], reads=["xres"], writes=["out_y"])
    for tname in taps:
        tap = pb.dram(f"tap_{tname}", list(pb.dr[tname].shape), pb.dr[tname].dtype,
                      kind="ExternalOutput")
        nt = T // 128
        pb.q_dma(tap[:], pb.dr[tname][:],
                 reads=[tname] + rd_all(tname, nt) + [f"{tname}@{s}" for s in range(n_seq)],
                 writes=[f"tap_{tname}"])
    pb.drain()
    pb.finalize()
    es.close()
    return nc, pb


def kernel(**inputs):
    n_seq, S = 2, 2048
    T = n_seq * S
    nc, pb = build_program(n_seq=n_seq, S=S)
    w = prep_weights_L0(inputs)
    w.update(prep_weights_L1(inputs))
    w.update(prep_weights_L2(inputs))
    w.update(prep_weights_L3(inputs))
    x = np.ascontiguousarray(np.asarray(inputs["L0_attn_hidden_states"], dtype=np.float32))
    in_maps = []
    for c in range(NCORES):
        m = dict(w)
        m["x0"] = np.ascontiguousarray(x[c * n_seq:(c + 1) * n_seq].reshape(T, D))
        in_maps.append(m)
    res = run_bass_kernel_spmd(nc, in_maps, core_ids=list(range(NCORES)))
    out = np.concatenate([np.asarray(r["y"]).reshape(n_seq, S, D) for r in res.results], axis=0)
    return out


# version 04
# speedup vs baseline: 1.0409x; 1.0409x over previous
import contextlib
import math

import numpy as np
import ml_dtypes

import concourse.bass as bass
import concourse.mybir as mybir
from concourse.bass_utils import run_bass_kernel_spmd

F32 = mybir.dt.float32
BF16 = mybir.dt.bfloat16
AF = mybir.ActivationFunctionType
ALU = mybir.AluOpType
AX = mybir.AxisListType

D = 1024
NCORES = 8


class PB:
    STREAMS = ("tensor", "vector", "scalar", "gpsimd", "sync")
    DEFSEM = {"tensor": "t", "vector": "v", "scalar": "s", "gpsimd": "p", "sync": "qH"}

    def __init__(self, nc, es):
        self.nc = nc
        self.es = es
        self.streams = {e: [] for e in self.STREAMS}
        self.NROT = 8
        semkeys = ["t", "v", "s", "p"]
        for q in ("qH", "cH", "pG"):
            semkeys += [f"{q}{i}" for i in range(self.NROT)]
        self.sems = {k: es.enter_context(nc.semaphore("S" + k)) for k in semkeys}
        self.cnt = {k: 0 for k in semkeys}
        self.dma_i = {"qH": 0, "cH": 0, "pG": 0}
        self.waited = {}
        self.writer = {}
        self.readers = {}
        self.sb = {}
        self.dr = {}
        self.sbuf_bytes = 0

    def sbuf(self, name, shape, dtype):
        if name not in self.sb:
            self.sb[name] = self.es.enter_context(self.nc.sbuf_tensor(name, shape, dtype))
            self.sbuf_bytes += shape[1] * mybir.dt.size(dtype)
        return self.sb[name]

    def psum(self, name, shape, dtype):
        if name not in self.sb:
            self.sb[name] = self.es.enter_context(self.nc.psum_tensor(name, shape, dtype))
        return self.sb[name]

    def dram(self, name, shape, dtype, kind=None):
        if name not in self.dr:
            if kind is None:
                self.dr[name] = self.nc.dram_tensor(name, shape, dtype).ap()
            else:
                self.dr[name] = self.nc.dram_tensor(name, shape, dtype, kind=kind).ap()
        return self.dr[name]

    def _wait(self, stream, sk, val):
        if val <= 0:
            return
        if sk[-1].isdigit() and sk[:-1] in ("qH", "pG", "cH"):
            val = self.cnt[sk]
        key = (stream, sk)
        if self.waited.get(key, 0) >= val:
            return
        self.waited[key] = val
        sem = self.sems[sk]
        self.streams[stream].append(("w", sem, val))

    def op(self, stream, fn, reads=(), writes=(), sem=None, dma=False):
        if dma:
            q = sem
            sk = f"{q}{self.dma_i[q] % self.NROT}"
            self.dma_i[q] += 1
            self._wait(stream, sk, self.cnt[sk])
        else:
            sk = sem if sem is not None else self.DEFSEM[stream]
        for b in reads:
            w = self.writer.get(b)
            if w:
                self._wait(stream, w[0], w[1])
        for b in writes:
            w = self.writer.get(b)
            if w:
                self._wait(stream, w[0], w[1])
            for skr, c in self.readers.get(b, {}).items():
                self._wait(stream, skr, c)
        amt = 16 if dma else 1
        self.cnt[sk] += amt
        self.streams[stream].append(("i", fn, self.sems[sk], amt))
        pos = (sk, self.cnt[sk])
        for b in reads:
            d = self.readers.setdefault(b, {})
            d[sk] = max(d.get(sk, 0), pos[1])
        for b in writes:
            self.writer[b] = pos
            self.readers[b] = {}

    def q_dma(self, out, in_, reads, writes, transpose=False):
        if transpose:
            self.op("sync", lambda em: em.dma_start_transpose(out, in_),
                    reads=reads, writes=writes, sem="qH", dma=True)
        else:
            self.op("sync", lambda em: em.dma_start(out=out, in_=in_),
                    reads=reads, writes=writes, sem="qH", dma=True)

    def c_dma(self, out, in_, reads, writes):
        self.op("scalar", lambda em: em.dma_start(out=out, in_=in_),
                reads=reads, writes=writes, sem="cH", dma=True)

    def g_dma(self, out, in_, reads, writes, accum=False):
        if accum:
            self.op("gpsimd", lambda em: em.dma_start(out=out, in_=in_, accum_op=ALU.add),
                    reads=reads, writes=writes, sem="pG", dma=True)
        else:
            self.op("gpsimd", lambda em: em.dma_start(out=out, in_=in_),
                    reads=reads, writes=writes, sem="pG", dma=True)

    def mm(self, out, lhsT, rhs, start, stop, reads, writes):
        self.op("tensor",
                lambda em: em.matmul(out, lhsT, rhs, start=start, stop=stop),
                reads=reads, writes=writes)

    def transp(self, out, in_, reads, writes, np_=128, f32=False):
        nm = "identf" if f32 else "ident"
        ident = self.sb[nm]
        self.op("tensor", lambda em: em.transpose(out, in_, ident[0:np_, 0:np_]),
                reads=list(reads) + [nm], writes=writes)

    def finalize(self):
        block = self.es.enter_context(self.nc.Block())
        for stream in self.STREAMS:
            items = self.streams[stream]
            if not items:
                continue

            def make(items):
                def emit(em):
                    for it in items:
                        if it[0] == "w":
                            em.wait_ge(it[1], it[2])
                        else:
                            it[1](em).then_inc(it[2], it[3])
                return emit

            getattr(block, stream)(make(items))

    def drain(self):
        for sk, c in self.cnt.items():
            if c > 0:
                sem = self.sems[sk]
                self._wait("sync", sk, c)


def emit_prelude(pb):
    nc = pb.nc
    pb.sbuf("ident", [128, 128], BF16)
    pb.sbuf("identf", [128, 128], F32)
    pb.sbuf("onescol", [1, 128], BF16)
    pb.op("gpsimd", lambda em: em.iota(pb.sb["identf"][:, :], [[1, 128]], base=0,
                                       channel_multiplier=-1,
                                       allow_small_or_imprecise_dtypes=True),
          writes=["identf"])
    pb.op("vector", lambda em: em.tensor_single_scalar(
        pb.sb["identf"][:, :], pb.sb["identf"][:, :], 0.0, ALU.is_equal),
        reads=["identf"], writes=["identf"])
    pb.op("vector", lambda em: em.tensor_copy(pb.sb["ident"][:, :], pb.sb["identf"][:, :]),
          reads=["identf"], writes=["ident"])
    pb.op("vector", lambda em: em.memset(pb.sb["onescol"][:, :], 1.0), writes=["onescol"])


def emit_causal_masks(pb):
    for ds in range(4):
        t = pb.sbuf(f"mask{ds}", [128, 512], BF16)
        pb.op("gpsimd", lambda em, t=t, ds=ds: em.iota(
            t[:, :], [[1, 512]], base=-128 * ds, channel_multiplier=-1,
            allow_small_or_imprecise_dtypes=True), writes=[f"mask{ds}"])
        pb.op("vector", lambda em, t=t: em.tensor_single_scalar(t[:, :], t[:, :], 0.0, ALU.is_ge),
              reads=[f"mask{ds}"], writes=[f"mask{ds}"])
        pb.op("vector", lambda em, t=t: em.tensor_single_scalar(t[:, :], t[:, :], -1e30, ALU.mult),
              reads=[f"mask{ds}"], writes=[f"mask{ds}"])



def emit_ln(pb, tag, src_name, src_ap, T, eps=1e-5):
    xn_f = pb.dram(f"xnf_{tag}", [T, D], F32)
    xn_b = pb.dram(f"xnb_{tag}", [T, D], BF16)
    nt = T // 128
    for sl in range(2):
        pb.sbuf(f"FP{sl}", [128, D], F32)
        pb.sbuf(f"FP{2+sl}", [128, D], F32)
        pb.sbuf(f"FP{4+sl}", [128, D], F32)
        for cn in ("csum", "css", "cm", "cv", "cr"):
            pb.sbuf(f"{cn}{sl}", [128, 1], F32)
    for i in range(nt):
        sl = i % 2
        xt, sq, xo = pb.sb[f"FP{sl}"], pb.sb[f"FP{2+sl}"], pb.sb[f"FP{4+sl}"]
        csum, css = pb.sb[f"csum{sl}"], pb.sb[f"css{sl}"]
        cm, cv, cr = pb.sb[f"cm{sl}"], pb.sb[f"cv{sl}"], pb.sb[f"cr{sl}"]
        kx, ks, ko = f"FP{sl}", f"FP{2+sl}", f"FP{4+sl}"
        kc = [f"csum{sl}", f"css{sl}", f"cm{sl}", f"cv{sl}", f"cr{sl}"]
        pb.q_dma(xt[:, :], src_ap[i * 128:(i + 1) * 128, :], reads=[src_name], writes=[kx])
        pb.op("vector", lambda em, xt=xt, c=csum: em.reduce_sum(c[:, :], xt[:, :], AX.X),
              reads=[kx], writes=[kc[0]])
        pb.op("scalar", lambda em, xt=xt, sq=sq, c=css: em.activation(
            sq[:, :], xt[:, :], AF.Square, accum_out=c[:, 0:1]),
            reads=[kx], writes=[ks, kc[1]])
        pb.op("vector", lambda em, c=csum, m=cm: em.tensor_scalar_mul(m[:, :], c[:, :], 1.0 / D),
              reads=[kc[0]], writes=[kc[2]])
        pb.op("vector", lambda em, m=cm, v=cv: em.tensor_mul(v[:, :], m[:, :], m[:, :]),
              reads=[kc[2]], writes=[kc[3]])
        pb.op("vector", lambda em, c=css, v=cv: em.scalar_tensor_tensor(
            v[:, :], c[:, :], 1.0 / D, v[:, :], ALU.mult, ALU.subtract),
            reads=[kc[1], kc[3]], writes=[kc[3]])
        pb.op("vector", lambda em, v=cv: em.tensor_single_scalar(v[:, :], v[:, :], eps, ALU.add),
              reads=[kc[3]], writes=[kc[3]])
        pb.op("vector", lambda em, v=cv, r=cr: em.reciprocal(r[:, :], v[:, :]),
              reads=[kc[3]], writes=[kc[4]])
        pb.op("scalar", lambda em, r=cr: em.activation(r[:, :], r[:, :], AF.Sqrt),
              reads=[kc[4]], writes=[kc[4]])
        pb.op("vector", lambda em, xt=xt, xo=xo, m=cm, r=cr: em.tensor_scalar(
            xo[:, :], xt[:, :], m[:, 0:1], r[:, 0:1], ALU.subtract, ALU.mult),
            reads=[kx, kc[2], kc[4]], writes=[ko])
        pb.q_dma(xn_f[i * 128:(i + 1) * 128, :], xo[:, :], reads=[ko], writes=[f"xnf_{tag}@{i}"])
        pb.g_dma(xn_b[i * 128:(i + 1) * 128, :], xo[:, :], reads=[ko], writes=[f"xnb_{tag}@{i}"])
    return xn_f, xn_b


def rd_all(name, nt):
    return [f"{name}@{i}" for i in range(nt)]


def emit_load_xnT(pb, tag, xn_b, seq, S, T):
    nt = T // 128
    for d in range(8):
        xp = pb.sbuf(f"XP{d}", [128, 2048], BF16)
        pb.q_dma(xp[:, 0:S], xn_b[seq * S:(seq + 1) * S, d * 128:(d + 1) * 128],
                 reads=rd_all(f"xnb_{tag}", nt), writes=[f"XP{d}"], transpose=True)


def emit_combine(pb, tag, xres, s_scr, g_scr, T):
    nt = T // 128
    for i in range(nt):
        sl = i % 2
        xt = pb.sb[f"FP{sl}"]
        st = pb.sb[f"FP{2+sl}"]
        gt = pb.sb[f"FP{4+sl}"]
        kx, ks, kg = f"FP{sl}", f"FP{2+sl}", f"FP{4+sl}"
        r = slice(i * 128, (i + 1) * 128)
        pb.q_dma(xt[:, :], xres[r, :], reads=["xres"], writes=[kx])
        pb.q_dma(st[:, :], s_scr[r, :], reads=rd_all(f"s_{tag}", nt), writes=[ks])
        pb.q_dma(gt[:, :], g_scr[r, :], reads=rd_all(f"g_{tag}", nt), writes=[kg])
        pb.op("scalar", lambda em, g=gt: em.activation(g[:, :], g[:, :], AF.Silu),
              reads=[kg], writes=[kg])
        pb.op("vector", lambda em, s=st, g=gt: em.tensor_mul(s[:, :], s[:, :], g[:, :]),
              reads=[ks, kg], writes=[ks])
        pb.op("vector", lambda em, x=xt, s=st: em.tensor_add(x[:, :], x[:, :], s[:, :]),
              reads=[kx, ks], writes=[kx])
        pb.q_dma(xres[r, :], xt[:, :], reads=[kx], writes=["xres"])


def emit_L0(pb, inp, n_seq, S):
    T = n_seq * S
    nt = T // 128
    nib = S // 128
    xres = pb.dram("xres", [T, D], F32)
    xn_f, xn_b = emit_ln(pb, "l0", "xres", xres, T)

    qT = pb.dram("l0_qT", [D, T], BF16)
    Y2 = pb.dram("l0_Y2", [T, D], BF16)
    hT = pb.dram("l0_hT", [32, 128, T], BF16)
    s_scr = pb.dram("s_l0", [T, D], F32)
    g_scr = pb.dram("g_l0", [T, D], F32)

    PS = [pb.psum(f"PS{i}", [128, 512], F32) for i in range(6)]
    PT = pb.psum("PTB0", [128, 1024], BF16)

    for seq in range(n_seq):
        emit_load_xnT(pb, "l0", xn_b, seq, S, T)
        xp = [pb.sb[f"XP{d}"] for d in range(8)]
        xpk = [f"XP{d}" for d in range(8)]

        for d in range(8):
            g0 = pb.sbuf(f"G{d}", [128, D], BF16)
            pb.g_dma(g0[:, :], inp["L0_attn_w_dec_attn"][d * 128:(d + 1) * 128, :],
                     reads=["in_wdec"], writes=[f"G{d}"])
            g1 = pb.sbuf(f"G{8+d}", [128, D], BF16)
            pb.g_dma(g1[:, :], inp["l0_w2t"][d * 128:(d + 1) * 128, :],
                     reads=["in_w2t"], writes=[f"G{8+d}"])

        e0 = pb.sbuf("E0", [128, 512], BF16)
        e1 = pb.sbuf("QOUT", [128, 512], BF16)
        for tg in range(S // 512):
            tsl = slice(tg * 512, (tg + 1) * 512)
            for do in range(8):
                ps = PS[do % 2]
                psk = f"PS{do % 2}"
                for d in range(8):
                    pb.mm(ps[:, :], pb.sb[f"G{d}"][:, do * 128:(do + 1) * 128],
                          xp[d][:, tsl], start=(d == 0), stop=(d == 7),
                          reads=[f"G{d}", xpk[d]], writes=[psk])
                ev = (e0, "E0") if do % 2 == 0 else (e1, "QOUT")
                pb.op("vector", lambda em, ev=ev[0], ps=ps: em.tensor_copy(ev[:, :], ps[:, :]),
                      reads=[psk], writes=[ev[1]])
                pb.q_dma(qT[do * 128:(do + 1) * 128, seq * S + tg * 512: seq * S + (tg + 1) * 512],
                         ev[0][:, :], reads=[ev[1]], writes=[f"l0_qT@{seq}"])

        for tb in range(S // 128):
            t0 = tb * 128
            for oh in range(2):
                ps = PS[2 + oh]
                psk = f"PS{2+oh}"
                for d in range(8):
                    pb.mm(ps[:, :], xp[d][:, t0:t0 + 128],
                          pb.sb[f"G{8+d}"][:, oh * 512:(oh + 1) * 512],
                          start=(d == 0), stop=(d == 7),
                          reads=[xpk[d], f"G{8+d}"], writes=[psk])
            y2t = pb.sbuf("Y2T", [128, D], BF16)
            pb.op("vector", lambda em, ps=PS[2]: em.tensor_copy(y2t[:, 0:512], ps[:, :]),
                  reads=["PS2"], writes=["Y2T"])
            pb.op("vector", lambda em, ps=PS[3]: em.tensor_copy(y2t[:, 512:1024], ps[:, :]),
                  reads=["PS3"], writes=["Y2T"])
            pb.q_dma(Y2[seq * S + t0: seq * S + t0 + 128, :], y2t[:, :],
                     reads=["Y2T"], writes=[f"l0_Y2@{seq}"])

        for jb in range(nib):
            gi = pb.sbuf(f"G{16 + jb}", [128, D], BF16)
            pb.q_dma(gi[:, :], Y2[seq * S + jb * 128: seq * S + (jb + 1) * 128, :],
                     reads=[f"l0_Y2@{seq}"], writes=[f"G{16+jb}"])
        for d in range(8):
            pb.g_dma(pb.sb[f"G{8+d}"][:, :], inp["l0_w1t"][d * 128:(d + 1) * 128, :],
                     reads=["in_w1t"], writes=[f"G{8+d}"])
        brow = pb.sbuf("BROW", [1, D], BF16)
        pb.g_dma(brow[:, :], inp["L0_attn_b_out"][0:1, :], reads=["in_bout"], writes=["BROW"])

        scores = pb.sbuf("SCORES", [128, 2048], F32)
        pbuf = pb.sbuf("PBUF", [128, 2048], BF16)
        nmx = pb.sbuf("NMX", [128, 1], F32)
        ssum = pb.sbuf("SSUM", [128, 1], F32)
        srcp = pb.sbuf("SRCP", [128, 1], F32)
        qi = [pb.sbuf(f"QI{d}", [128, 128], BF16) for d in range(8)]
        ptile = [pb.sbuf(f"PTS{k}", [128, 128], BF16) for k in range(2)]
        stile = pb.sbuf("STILE", [128, 1024], F32)

        for ib in range(nib):
            Lj = (ib + 1) * 128
            for d in range(8):
                pb.q_dma(qi[d][:, :],
                         qT[d * 128:(d + 1) * 128, seq * S + ib * 128: seq * S + (ib + 1) * 128],
                         reads=[f"l0_qT@{seq}"], writes=[f"QI{d}"])
            ngroups = (Lj + 511) // 512
            for jg in range(ngroups):
                w = min(512, Lj - jg * 512)
                ps = PS[4]
                for d in range(8):
                    pb.mm(ps[:, 0:w], qi[d][:, :], xp[d][:, jg * 512: jg * 512 + w],
                          start=(d == 0), stop=(d == 7),
                          reads=[f"QI{d}", xpk[d]], writes=["PS4"])
                last = (jg == ngroups - 1)
                if last:
                    ds = ib % 4
                    mk = f"mask{ds}"
                    pb.op("vector", lambda em, w=w, jg=jg, mk=mk, ps=ps: em.tensor_add(
                        scores[:, jg * 512: jg * 512 + w], ps[:, 0:w], pb.sb[mk][:, 0:w]),
                        reads=["PS4", mk], writes=["SCORES"])
                    if ib == 0:
                        pb.op("vector", lambda em, ps=ps: em.tensor_copy(
                            scores[0:1, 0:1], ps[0:1, 0:1]),
                            reads=["PS4"], writes=["SCORES"])
                else:
                    pb.op("vector", lambda em, w=w, jg=jg, ps=ps: em.tensor_copy(
                        scores[:, jg * 512: jg * 512 + w], ps[:, 0:w]),
                        reads=["PS4"], writes=["SCORES"])
            pb.op("vector", lambda em, Lj=Lj: em.reduce_max(nmx[:, :], scores[:, 0:Lj], AX.X,
                                                            negate=True),
                  reads=["SCORES"], writes=["NMX"])
            pb.op("scalar", lambda em, Lj=Lj: em.activation(
                pbuf[:, 0:Lj], scores[:, 0:Lj], AF.Exp, bias=nmx[:, 0:1],
                accum_out=ssum[:, 0:1]),
                reads=["SCORES", "NMX"], writes=["PBUF", "SSUM"])
            pb.op("vector", lambda em: em.reciprocal(srcp[:, :], ssum[:, :]),
                  reads=["SSUM"], writes=["SRCP"])
            pb.op("vector", lambda em, Lj=Lj: em.tensor_scalar_mul(
                pbuf[:, 0:Lj], pbuf[:, 0:Lj], srcp[:, 0:1]),
                reads=["PBUF", "SRCP"], writes=["PBUF"])

            for oh in range(2):
                ps = PS[oh]
                psk = f"PS{oh}"
                osl = slice(oh * 512, (oh + 1) * 512)
                pb.mm(ps[:, :], pb.sb["onescol"][:, :], brow[:, osl], start=True, stop=False,
                      reads=["onescol", "BROW"], writes=[psk])
                for d in range(8):
                    pb.mm(ps[:, :], xp[d][:, ib * 128:(ib + 1) * 128],
                          pb.sb[f"G{8+d}"][:, osl], start=False, stop=False,
                          reads=[xpk[d], f"G{8+d}"], writes=[psk])
            for jb in range(ib + 1):
                pt = ptile[jb % 2]
                ptk = f"PTS{jb % 2}"
                pb.transp(PT[:, 0:128], pbuf[:, jb * 128:(jb + 1) * 128],
                          reads=["PBUF"], writes=["PTB0"])
                pb.op("vector", lambda em, pt=pt: em.tensor_copy(pt[:, :], PT[:, 0:128]),
                      reads=["PTB0"], writes=[ptk])
                for oh in range(2):
                    ps = PS[oh]
                    psk = f"PS{oh}"
                    osl = slice(oh * 512, (oh + 1) * 512)
                    lastmm = (jb == ib)
                    pb.mm(ps[:, :], pt[:, :], pb.sb[f"G{16+jb}"][:, osl],
                          start=False, stop=lastmm,
                          reads=[ptk, f"G{16+jb}"], writes=[psk])
            for oh in range(2):
                ps = PS[oh]
                psk = f"PS{oh}"
                ssl = slice(oh * 512, (oh + 1) * 512)
                pb.op("scalar", lambda em, ps=ps, ssl=ssl: em.activation(
                    stile[:, ssl], ps[:, :], AF.Exp, scale=2.0),
                    reads=[psk], writes=["STILE"])
                pb.op("vector", lambda em, ssl=ssl: em.tensor_single_scalar(
                    stile[:, ssl], stile[:, ssl], 1.0, ALU.add),
                    reads=["STILE"], writes=["STILE"])
                pb.op("vector", lambda em, ssl=ssl: em.reciprocal(
                    stile[:, ssl], stile[:, ssl]),
                    reads=["STILE"], writes=["STILE"])
                pb.op("vector", lambda em, ssl=ssl: em.tensor_scalar(
                    stile[:, ssl], stile[:, ssl], -2.0, 1.0, ALU.mult, ALU.add),
                    reads=["STILE"], writes=["STILE"])
            pb.q_dma(s_scr[seq * S + ib * 128: seq * S + (ib + 1) * 128, :], stile[:, :],
                     reads=["STILE"], writes=[f"s_l0@{seq * nib + ib}"])

        hf32 = pb.sbuf("HF32", [128, 512], F32)
        hbf = pb.sbuf("HBF", [128, 512], BF16)
        for fc in range(32):
            for d in range(8):
                pb.g_dma(pb.sb[f"G{d}"][:, 0:128],
                         inp["l0_wfct"][d * 128:(d + 1) * 128, fc * 128:(fc + 1) * 128],
                         reads=["in_wfct"], writes=[f"G{d}"])
            for tg in range(S // 512):
                tsl = slice(tg * 512, (tg + 1) * 512)
                ps = PS[4 + tg % 2]
                psk = f"PS{4 + tg % 2}"
                for d in range(8):
                    pb.mm(ps[:, :], pb.sb[f"G{d}"][:, 0:128], xp[d][:, tsl],
                          start=(d == 0), stop=(d == 7),
                          reads=[f"G{d}", xpk[d]], writes=[psk])
                pb.op("scalar", lambda em, ps=ps: em.activation(hf32[:, :], ps[:, :], AF.Relu),
                      reads=[psk], writes=["HF32"])
                pb.op("scalar", lambda em: em.activation(hbf[:, :], hf32[:, :], AF.Square),
                      reads=["HF32"], writes=["HBF"])
                pb.q_dma(hT[fc, :, seq * S + tg * 512: seq * S + (tg + 1) * 512],
                         hbf[:, :], reads=["HBF"], writes=[f"l0_hT@{fc}"])

        ybuf = pb.dram("l0_y", [T, D], F32)
        for fc in range(32):
            g = pb.sbuf(f"G{fc}", [128, D], BF16)
            pb.g_dma(g[:, :], inp["l0_wprojt"][fc * 128:(fc + 1) * 128, :],
                     reads=["in_wprojt"], writes=[f"G{fc}"])
        ht0 = pb.sbuf("HT0", [128, 128], BF16)
        ht1 = pb.sbuf("HT1", [128, 128], BF16)
        yev = pb.sbuf("STILE", [128, 1024], F32)
        for tb in range(S // 128):
            t0 = seq * S + tb * 128
            for fc in range(32):
                ht = (ht0, "HT0") if fc % 2 == 0 else (ht1, "HT1")
                pb.q_dma(ht[0][:, :], hT[fc, :, t0:t0 + 128], reads=[f"l0_hT@{fc}"],
                         writes=[ht[1]])
                for oh in range(2):
                    ps = PS[oh]
                    psk = f"PS{oh}"
                    pb.mm(ps[:, :], ht[0][:, :], pb.sb[f"G{fc}"][:, oh * 512:(oh + 1) * 512],
                          start=(fc == 0), stop=(fc == 31),
                          reads=[ht[1], f"G{fc}"], writes=[psk])
            pb.op("vector", lambda em: em.tensor_copy(yev[:, 0:512], PS[0][:, :]),
                  reads=["PS0"], writes=["STILE"])
            pb.op("vector", lambda em: em.tensor_copy(yev[:, 512:1024], PS[1][:, :]),
                  reads=["PS1"], writes=["STILE"])
            pb.q_dma(ybuf[t0:t0 + 128, :], yev[:, :], reads=["STILE"], writes=[f"l0_y@{seq}"])

    ybuf = pb.dr["l0_y"]
    cn1 = pb.sbuf("CN1", [128, 1], F32)
    pb.op("gpsimd", lambda em: em.memset(cn1[:, :], -1.0), writes=["CN1"])

    for i in range(nt):
        sl = i % 2
        yt = pb.sb[f"FP{sl}"]
        mem = pb.sb[f"FP{2+sl}"]
        acc = pb.sb[f"FP{4+sl}"]
        if sl == 0:
            spk = pb.sbuf("STILE", [128, 1024], F32)
            kp = "STILE"
        else:
            spk = pb.sbuf("CONVA", [128, 2048], F32)[:, 0:1024]
            kp = "CONVA"
        ky, km, ka = f"FP{sl}", f"FP{2+sl}", f"FP{4+sl}"
        r = slice(i * 128, (i + 1) * 128)
        eng = "vector" if i % 2 == 0 else "gpsimd"
        pb.q_dma(yt[:, :], ybuf[r, :], reads=[f"l0_y@{0}", f"l0_y@{n_seq-1}"], writes=[ky])

        def sgt(dst, src, kd, ks_):
            if eng == "vector":
                pb.op(eng, lambda em, d=dst, s=src: em.tensor_single_scalar(
                    d[:, :], s[:, :], 1.0, ALU.is_gt), reads=[ks_], writes=[kd])
            else:
                pb.op("scalar", lambda em, d=dst, s=src: em.sign(
                    d[:, :], s[:, :], bias=cn1[:, 0:1]), reads=[ks_, "CN1"], writes=[kd])
                pb.op("scalar", lambda em, d=dst: em.activation(
                    d[:, :], d[:, :], AF.Relu), reads=[kd], writes=[kd])

        def scl(dst_t, src_t, kd, ks_, cval):
            pb.op("scalar", lambda em, d=dst_t, s=src_t, cval=cval: em.activation(
                d[:, :], s[:, :], AF.Identity, scale=cval), reads=[ks_], writes=[kd])

        sgt(spk, yt, kp, ky)
        pb.op(eng, lambda em, a=acc, s=spk: em.tensor_copy(a[:, :], s[:, :]),
              reads=[kp], writes=[ka])
        pb.op(eng, lambda em, m=mem, y=yt: em.tensor_copy(m[:, :], y[:, :]),
              reads=[ky], writes=[km])
        for step in range(4):
            if eng == "vector":
                pb.op(eng, lambda em, m=mem, y=yt: em.scalar_tensor_tensor(
                    m[:, :], m[:, :], 0.9, y[:, :], ALU.mult, ALU.add),
                    reads=[km, ky], writes=[km])
            else:
                scl(mem, mem, km, km, 0.9)
                pb.op(eng, lambda em, m=mem, y=yt: em.tensor_add(m[:, :], m[:, :], y[:, :]),
                      reads=[km, ky], writes=[km])
            pb.op(eng, lambda em, m=mem, s=spk: em.tensor_sub(m[:, :], m[:, :], s[:, :]),
                  reads=[km, kp], writes=[km])
            sgt(spk, mem, kp, km)
            pb.op(eng, lambda em, a=acc, s=spk: em.tensor_add(a[:, :], a[:, :], s[:, :]),
                  reads=[ka, kp], writes=[ka])
        if eng == "vector":
            pb.op(eng, lambda em, a=acc: em.tensor_single_scalar(
                a[:, :], a[:, :], 0.2, ALU.mult), reads=[ka], writes=[ka])
        else:
            scl(acc, acc, ka, ka, 0.2)
        pb.q_dma(pb.dr["g_l0"][r, :], acc[:, :], reads=[ka], writes=[f"g_l0@{i}"])

    emit_combine(pb, "l0", xres, pb.dr["s_l0"], pb.dr["g_l0"], T)


QMIDS = [(0.75, 0.5), (1.25, 0.5), (1.75, 0.5), (2.5, 1.0), (3.5, 1.0), (5.0, 2.0)]


def emit_quant(pb, src, dst, W, rk, wk, uniq):
    nb = W // 16
    mx = pb.sbuf(f"QMX{uniq}", [128, 32], F32)
    _cvr = pb.sbuf("CONVR", [128, 2112], BF16)
    ub = _cvr[:, 0:512]
    ab = _cvr[:, 512:1024]
    ac = pb.sbuf("E0", [128, 512], BF16)
    cb = pb.sbuf("HBF", [128, 512], BF16)
    kmx, ku, ka, kac, kc = (f"QMX{uniq}", "CONVR", "CONVR", "E0", "HBF")
    assert nb <= 32 and W <= 512
    pb.op("vector", lambda em: em.tensor_reduce(
        mx[:, 0:nb], src[:, 0:W].rearrange("p (b c) -> p b c", c=16), AX.X, ALU.max,
        apply_absolute_value=True), reads=[rk], writes=[kmx])
    pb.op("vector", lambda em: em.tensor_scalar(
        mx[:, 0:nb], mx[:, 0:nb], 1.0 / 6.0, 1e-12, ALU.mult, ALU.max),
        reads=[kmx], writes=[kmx])
    sb_ = lambda t: bass.AP(t, 0, [[32, 128], [1, nb], [0, 16]])
    r3 = lambda t: t[:, 0:W].rearrange("p (b c) -> p b c", c=16)
    pb.op("vector", lambda em: em.reciprocal(mx[:, 0:nb], mx[:, 0:nb]),
          reads=[kmx], writes=[kmx])
    pb.op("vector", lambda em: em.tensor_tensor(
        r3(ub), r3(src), sb_(mx), ALU.mult), reads=[rk, kmx], writes=[ku])
    pb.op("scalar", lambda em: em.activation(ab[:, 0:W], ub[:, 0:W], AF.Abs),
          reads=[ku], writes=[ka])
    pb.op("vector", lambda em: em.tensor_scalar(
        ac[:, 0:W], ab[:, 0:W], 0.25, 0.5, ALU.is_gt, ALU.mult),
        reads=[ka], writes=[kac])
    for m, dl in QMIDS:
        pb.op("vector", lambda em, m=m, dl=dl: em.tensor_scalar(
            cb[:, 0:W], ab[:, 0:W], m, dl, ALU.is_gt, ALU.mult),
            reads=[ka], writes=[kc])
        pb.op("vector", lambda em: em.tensor_add(ac[:, 0:W], ac[:, 0:W], cb[:, 0:W]),
              reads=[kac, kc], writes=[kac])
    pb.op("scalar", lambda em: em.activation(ab[:, 0:W], ub[:, 0:W], AF.Sign),
          reads=[ku], writes=[ka])
    pb.op("vector", lambda em: em.tensor_mul(ac[:, 0:W], ac[:, 0:W], ab[:, 0:W]),
          reads=[kac, ka], writes=[kac])
    pb.op("vector", lambda em: em.reciprocal(mx[:, 0:nb], mx[:, 0:nb]),
          reads=[kmx], writes=[kmx])
    pb.op("vector", lambda em: em.tensor_tensor(
        r3(dst), r3(ac), sb_(mx), ALU.mult),
        reads=[kac, kmx], writes=[wk])


def emit_quant_dram(pb, src_dram, dst_dram, rows, W, rk, wk):
    qin = pb.sbuf("HF32", [128, 512], F32)
    qout = pb.sbuf("QOUT", [128, 512], BF16)
    for i in range(rows // 128):
        r = slice(i * 128, (i + 1) * 128)
        for c0 in range(0, W, 512):
            cw = min(512, W - c0)
            pb.q_dma(qin[:, 0:cw], src_dram[r, c0:c0 + cw], reads=[rk], writes=["HF32"])
            emit_quant(pb, qin, qout, cw, "HF32", "QOUT", "q")
            pb.q_dma(dst_dram[r, c0:c0 + cw], qout[:, 0:cw], reads=["QOUT"], writes=[wk])


def emit_incl_masks(pb):
    for ds in range(4):
        t = pb.sbuf(f"mi{ds}", [128, 512], BF16)
        pb.op("gpsimd", lambda em, t=t, ds=ds: em.iota(
            t[:, :], [[1, 512]], base=-128 * ds, channel_multiplier=-1,
            allow_small_or_imprecise_dtypes=True), writes=[f"mi{ds}"])
        pb.op("vector", lambda em, t=t: em.tensor_single_scalar(t[:, :], t[:, :], 0.0, ALU.is_gt),
              reads=[f"mi{ds}"], writes=[f"mi{ds}"])
        pb.op("vector", lambda em, t=t: em.tensor_single_scalar(t[:, :], t[:, :], -1e30, ALU.mult),
              reads=[f"mi{ds}"], writes=[f"mi{ds}"])


def emit_L3(pb, inp, n_seq, S):
    T = n_seq * S
    nt = T // 128
    nib = S // 128
    xres = pb.dr["xres"]
    xn_f, xn_b = emit_ln(pb, "l3", "xres", xres, T)

    vT = pb.dram("l3_vT", [D, T], BF16)
    xq = pb.dram("l3_xq", [T, D], BF16)
    hq = pb.dram("l3_hq", [T, 2816], BF16)
    wgq = pb.dram("l3_wgq", [2816, D], BF16)
    wuq = pb.dram("l3_wuq", [2816, D], BF16)
    wdq = pb.dram("l3_wdq", [D, 2816], BF16)
    s_scr = pb.dram("s_l3", [T, D], F32)
    g_scr = pb.dram("g_l3", [T, D], F32)
    PS = [pb.psum(f"PS{i}", [128, 512], F32) for i in range(6)]
    PT = pb.psum("PTB0", [128, 1024], BF16)

    emit_quant_dram(pb, inp["L3_ffn_w_gate"], wgq, 2816, D, "in_wg", "l3_wgq")
    emit_quant_dram(pb, inp["L3_ffn_w_up"], wuq, 2816, D, "in_wu", "l3_wuq")
    emit_quant_dram(pb, inp["L3_ffn_w_down"], wdq, D, 2816, "in_wd", "l3_wdq")

    qin = pb.sbuf("HF32", [128, 512], F32)
    qout = pb.sbuf("QOUT", [128, 512], BF16)
    for i in range(nt):
        r = slice(i * 128, (i + 1) * 128)
        for c0 in range(0, D, 512):
            pb.q_dma(qin[:, :], xn_f[r, c0:c0 + 512], reads=[f"xnf_l3@{i}"], writes=["HF32"])
            emit_quant(pb, qin, qout, 512, "HF32", "QOUT", "q")
            pb.q_dma(xq[r, c0:c0 + 512], qout[:, :], reads=["QOUT"], writes=[f"l3_xq@{i}"])

    LT = pb.sbuf("LT", [64, 2048], BF16)
    brow = pb.sbuf("BROW", [1, D], BF16)
    bvT = pb.sbuf("BVT", [128, 8], F32)
    pb.q_dma(bvT[:, :], inp["l3_bvT"][:, :], reads=["in_bvT"], writes=["BVT"])

    for seq in range(n_seq):
        emit_load_xnT(pb, "l3", xn_b, seq, S, T)
        xp = [pb.sb[f"XP{d}"] for d in range(8)]
        xpk = [f"XP{d}" for d in range(8)]

        wb_t = pb.sbuf("CONVR", [128, 2112], BF16)[:, 1024:1536]
        for d in range(8):
            pb.g_dma(wb_t[:, d * 64:(d + 1) * 64],
                     inp["l3_wbasisT"][d * 128:(d + 1) * 128, :],
                     reads=["in_wbasisT"], writes=["CONVR"])
            g = pb.sbuf(f"G{24+d}", [128, D], BF16)
            pb.g_dma(g[:, :], inp["l3_wvT"][d * 128:(d + 1) * 128, :],
                     reads=["in_wvT"], writes=[f"G{24+d}"])

        ev = pb.sbuf("E0", [128, 512], BF16)
        for tg in range(S // 512):
            tsl = slice(tg * 512, (tg + 1) * 512)
            ps = PS[4]
            for d in range(8):
                pb.mm(ps[0:64, :], wb_t[:, d * 64:(d + 1) * 64], xp[d][:, tsl],
                      start=(d == 0), stop=(d == 7), reads=["CONVR", xpk[d]], writes=["PS4"])
            pb.op("vector", lambda em, tsl=tsl, ps=ps: em.tensor_copy(LT[:, tsl], ps[0:64, :]),
                  reads=["PS4"], writes=["LT"])

        for do in range(8):
            for tg in range(S // 512):
                tsl = slice(tg * 512, (tg + 1) * 512)
                ps = PS[4 + tg % 2]
                psk = f"PS{4 + tg % 2}"
                for d in range(8):
                    pb.mm(ps[:, :], pb.sb[f"G{24+d}"][:, do * 128:(do + 1) * 128],
                          xp[d][:, tsl], start=(d == 0), stop=(d == 7),
                          reads=[f"G{24+d}", xpk[d]], writes=[psk])
                pb.op("scalar", lambda em, ps=ps, do=do: em.activation(
                    ev[:, :], ps[:, :], AF.Identity, bias=bvT[:, do:do + 1]),
                    reads=[psk, "BVT"], writes=["E0"])
                pb.q_dma(vT[do * 128:(do + 1) * 128, seq * S + tg * 512:seq * S + (tg + 1) * 512],
                         ev[:, :], reads=["E0"], writes=[f"l3_vT@{seq}"])

        for hp in range(8):
            for jb in range(nib):
                pb.q_dma(xp[hp][:, jb * 128:(jb + 1) * 128],
                         vT[hp * 128:(hp + 1) * 128,
                            seq * S + jb * 128:seq * S + (jb + 1) * 128],
                         reads=[f"l3_vT@{seq}"], writes=[xpk[hp]], transpose=True)

        for d in range(8):
            pb.g_dma(pb.sb[f"G{24+d}"][:, :], inp["l3_woT"][d * 128:(d + 1) * 128, :],
                     reads=["in_woT"], writes=[f"G{24+d}"])
        pb.g_dma(brow[:, :], inp["L3_attn_b_o"][0:1, :], reads=["in_bo"], writes=["BROW"])

        scores = pb.sbuf("SCORES", [128, 2048], F32)
        pbuf = pb.sbuf("PBUF", [128, 2048], BF16)
        nmx = pb.sbuf("NMX", [128, 1], F32)
        ssum = pb.sbuf("SSUM", [128, 1], F32)
        srcp = pb.sbuf("SRCP", [128, 1], F32)
        pts = [pb.sbuf(f"QI{k}", [128, 128], BF16) for k in range(8)] + \
              [pb.sbuf(f"PTS{k}", [128, 128], BF16) for k in range(8, 16)]
        ptsk = [f"QI{k}" if k < 8 else f"PTS{k}" for k in range(16)]
        _e0t = pb.sbuf("E0", [128, 512], BF16)
        ots = [pb.sbuf(f"OTS{c}", [128, 128], BF16) for c in range(6)] + \
              [_e0t[:, 0:128], _e0t[:, 128:256]]
        otsk = [f"OTS{c}" for c in range(6)] + ["E0", "E0"]
        stile = pb.sbuf("STILE", [128, 1024], F32)

        for ib in range(nib):
            Lj = (ib + 1) * 128
            isl = slice(ib * 128, (ib + 1) * 128)
            ngroups = (Lj + 511) // 512
            for jg in range(ngroups):
                w = min(512, Lj - jg * 512)
                ps = PS[4]
                pb.mm(ps[:, 0:w], LT[:, isl], LT[:, jg * 512:jg * 512 + w],
                      start=True, stop=True, reads=["LT"], writes=["PS4"])
                last = (jg == ngroups - 1)
                mk = f"mi{ib % 4}" if last else None
                if last:
                    pb.op("vector", lambda em, w=w, jg=jg, mk=mk, ps=ps: em.scalar_tensor_tensor(
                        scores[:, jg * 512:jg * 512 + w], ps[:, 0:w], 0.125,
                        pb.sb[mk][:, 0:w], ALU.mult, ALU.add),
                        reads=["PS4", mk], writes=["SCORES"])
                else:
                    pb.op("vector", lambda em, w=w, jg=jg, ps=ps: em.tensor_scalar_mul(
                        scores[:, jg * 512:jg * 512 + w], ps[:, 0:w], 0.125),
                        reads=["PS4"], writes=["SCORES"])
            pb.op("vector", lambda em, Lj=Lj: em.reduce_max(nmx[:, :], scores[:, 0:Lj], AX.X,
                                                            negate=True),
                  reads=["SCORES"], writes=["NMX"])
            pb.op("scalar", lambda em, Lj=Lj: em.activation(
                pbuf[:, 0:Lj], scores[:, 0:Lj], AF.Exp, bias=nmx[:, 0:1],
                accum_out=ssum[:, 0:1]),
                reads=["SCORES", "NMX"], writes=["PBUF", "SSUM"])
            pb.op("vector", lambda em: em.reciprocal(srcp[:, :], ssum[:, :]),
                  reads=["SSUM"], writes=["SRCP"])
            pb.op("vector", lambda em, Lj=Lj: em.tensor_scalar_mul(
                pbuf[:, 0:Lj], pbuf[:, 0:Lj], srcp[:, 0:1]),
                reads=["PBUF", "SRCP"], writes=["PBUF"])
            for jb in range(ib + 1):
                pb.transp(PT[:, 0:128], pbuf[:, jb * 128:(jb + 1) * 128],
                          reads=["PBUF"], writes=["PTB0"])
                pb.op("vector", lambda em, jb=jb: em.tensor_copy(pts[jb][:, :], PT[:, 0:128]),
                      reads=["PTB0"], writes=[ptsk[jb]])
            for h in range(16):
                hp, hh = h // 2, h % 2
                ps = PS[5]
                for jb in range(ib + 1):
                    pb.mm(ps[0:64, 0:128],
                          xp[hp][:, jb * 128 + hh * 64:jb * 128 + hh * 64 + 64],
                          pts[jb][:, :], start=(jb == 0), stop=(jb == ib),
                          reads=[xpk[hp], ptsk[jb]], writes=["PS5"])
                pb.op("vector", lambda em, h=h, ps=ps: em.tensor_copy(
                    ots[h // 2][(h % 2) * 64:(h % 2) * 64 + 64, :], ps[0:64, 0:128]),
                    reads=["PS5"], writes=[otsk[h // 2]])
            for oh in range(2):
                ps = PS[oh]
                psk = f"PS{oh}"
                osl = slice(oh * 512, (oh + 1) * 512)
                pb.mm(ps[:, :], pb.sb["onescol"][:, :], brow[:, osl], start=True, stop=False,
                      reads=["onescol", "BROW"], writes=[psk])
                for c in range(8):
                    pb.mm(ps[:, :], ots[c][:, :], pb.sb[f"G{24+c}"][:, osl],
                          start=False, stop=(c == 7),
                          reads=[otsk[c], f"G{24+c}"], writes=[psk])
                pb.op("scalar", lambda em, ps=ps, oh=oh: em.copy(
                    stile[:, oh * 512:(oh + 1) * 512], ps[:, :]),
                    reads=[psk], writes=["STILE"])
            pb.q_dma(s_scr[seq * S + ib * 128:seq * S + (ib + 1) * 128, :], stile[:, :],
                     reads=["STILE"], writes=[f"s_l3@{seq * nib + ib}"])

        for d in range(8):
            pb.q_dma(xp[d][:, 0:S], xq[seq * S:(seq + 1) * S, d * 128:(d + 1) * 128],
                     reads=rd_all("l3_xq", nt), writes=[xpk[d]], transpose=True)
        eg = pb.sbuf("EG", [128, 512], F32)
        eu = pb.sbuf("HF32", [128, 512], F32)
        hf = pb.sbuf("CONVA", [128, 2048], F32)[:, 0:512]
        hqb = pb.sbuf("CONVR", [128, 2112], BF16)[:, 1536:2048]
        fslices = [(k * 512, min(512, 2816 - k * 512)) for k in range(6)]
        for f0, fw in fslices:
            for d in range(8):
                g = pb.sbuf(f"G{d}", [128, D], BF16)
                pb.q_dma(g[:, 0:fw], wgq[f0:f0 + fw, d * 128:(d + 1) * 128],
                         reads=["l3_wgq"], writes=[f"G{d}"], transpose=True)
                g2 = pb.sbuf(f"G{8+d}", [128, D], BF16)
                pb.q_dma(g2[:, 0:fw], wuq[f0:f0 + fw, d * 128:(d + 1) * 128],
                         reads=["l3_wuq"], writes=[f"G{8+d}"], transpose=True)
            for tb in range(nib):
                t0 = seq * S + tb * 128
                tloc = slice(tb * 128, (tb + 1) * 128)
                for d in range(8):
                    pb.mm(PS[0][:, 0:fw], xp[d][:, tloc], pb.sb[f"G{d}"][:, 0:fw],
                          start=(d == 0), stop=(d == 7), reads=[xpk[d], f"G{d}"],
                          writes=["PS0"])
                    pb.mm(PS[1][:, 0:fw], xp[d][:, tloc], pb.sb[f"G{8+d}"][:, 0:fw],
                          start=(d == 0), stop=(d == 7), reads=[xpk[d], f"G{8+d}"],
                          writes=["PS1"])
                pb.op("scalar", lambda em, fw=fw: em.activation(eg[:, 0:fw], PS[0][:, 0:fw],
                                                                AF.Silu),
                      reads=["PS0"], writes=["EG"])
                pb.op("vector", lambda em, fw=fw: em.tensor_copy(eu[:, 0:fw], PS[1][:, 0:fw]),
                      reads=["PS1"], writes=["HF32"])
                pb.op("vector", lambda em, fw=fw: em.tensor_mul(hf[:, 0:fw], eg[:, 0:fw],
                                                                eu[:, 0:fw]),
                      reads=["EG", "HF32"], writes=["CONVA"])
                emit_quant(pb, hf, hqb, fw, "CONVA", "CONVR", "q")
                pb.q_dma(hq[t0:t0 + 128, f0:f0 + fw], hqb[:, 0:fw],
                         reads=["CONVR"], writes=[f"l3_hq@{seq}"])
        for fc in range(22):
            g = pb.sbuf(f"G{fc}", [128, D], BF16)
            pb.q_dma(g[:, :], wdq[0:D, fc * 128:(fc + 1) * 128],
                     reads=["l3_wdq"], writes=[f"G{fc}"], transpose=True)
        _e0b = pb.sbuf("E0", [128, 512], BF16)
        hti = [_e0b[:, 256:384], _e0b[:, 384:512]]
        gev = pb.sbuf("STILE", [128, 1024], F32)
        for tb in range(nib):
            t0 = seq * S + tb * 128
            for fc in range(22):
                ht = hti[fc % 2]
                htk = "E0"
                pb.q_dma(ht[:, :], hq[t0:t0 + 128, fc * 128:(fc + 1) * 128],
                         reads=[f"l3_hq@{seq}"], writes=[htk], transpose=True)
                for oh in range(2):
                    pb.mm(PS[oh][:, :], ht[:, :], pb.sb[f"G{fc}"][:, oh * 512:(oh + 1) * 512],
                          start=(fc == 0), stop=(fc == 21),
                          reads=[htk, f"G{fc}"], writes=[f"PS{oh}"])
            pb.op("vector", lambda em: em.tensor_copy(gev[:, 0:512], PS[0][:, :]),
                  reads=["PS0"], writes=["STILE"])
            pb.op("vector", lambda em: em.tensor_copy(gev[:, 512:1024], PS[1][:, :]),
                  reads=["PS1"], writes=["STILE"])
            pb.q_dma(g_scr[t0:t0 + 128, :], gev[:, :], reads=["STILE"],
                     writes=[f"g_l3@{(t0) // 128}"])

    emit_combine(pb, "l3", xres, s_scr, g_scr, T)


def prep_weights_L3(w):
    out = {}
    out["l3_wbasisT"] = np.ascontiguousarray(np.asarray(w["L3_attn_W_basis"]).T)
    out["l3_wvT"] = np.ascontiguousarray(np.asarray(w["L3_attn_W_v"]).T)
    out["l3_bvT"] = np.ascontiguousarray(np.asarray(w["L3_attn_b_v"]).reshape(8, 128).T)
    out["l3_woT"] = np.ascontiguousarray(np.asarray(w["L3_attn_W_o"]).T)
    out["L3_attn_b_o"] = np.ascontiguousarray(np.asarray(w["L3_attn_b_o"]).reshape(1, D))
    out["L3_ffn_w_gate"] = np.ascontiguousarray(np.asarray(w["L3_ffn_w_gate"]))
    out["L3_ffn_w_up"] = np.ascontiguousarray(np.asarray(w["L3_ffn_w_up"]))
    out["L3_ffn_w_down"] = np.ascontiguousarray(np.asarray(w["L3_ffn_w_down"]))
    return out


def declare_inputs_L3(pb):
    inp = {}
    inp["l3_wbasisT"] = pb.dram("l3_wbasisT", [D, 64], F32, kind="ExternalInput")
    inp["l3_wvT"] = pb.dram("l3_wvT", [D, D], F32, kind="ExternalInput")
    inp["l3_bvT"] = pb.dram("l3_bvT", [128, 8], F32, kind="ExternalInput")
    inp["l3_woT"] = pb.dram("l3_woT", [D, D], F32, kind="ExternalInput")
    inp["L3_attn_b_o"] = pb.dram("L3_attn_b_o", [1, D], F32, kind="ExternalInput")
    inp["L3_ffn_w_gate"] = pb.dram("L3_ffn_w_gate", [2816, D], F32, kind="ExternalInput")
    inp["L3_ffn_w_up"] = pb.dram("L3_ffn_w_up", [2816, D], F32, kind="ExternalInput")
    inp["L3_ffn_w_down"] = pb.dram("L3_ffn_w_down", [D, 2816], F32, kind="ExternalInput")
    return inp




SHIFT_ID_MAP = {}


def emit_shift_idents(pb):
    hosts = ([f"PTS{k}" for k in range(8, 16)] + [f"OTS{c}" for c in range(6)])
    for c in range(6):
        pb.sbuf(f"OTS{c}", [128, 128], BF16)
    for k in range(8, 16):
        pb.sbuf(f"PTS{k}", [128, 128], BF16)
    hi = 0
    for sz in [1, 2, 4, 8, 16, 32, 64]:
        for kind in ("IU", "ID"):
            SHIFT_ID_MAP[f"{kind}{sz}"] = hosts[hi]
            hi += 1
    for sz in [1, 2, 4, 8, 16, 32, 64]:
        up = pb.sb[SHIFT_ID_MAP[f"IU{sz}"]]
        dn = pb.sb[SHIFT_ID_MAP[f"ID{sz}"]]
        ku, kd = SHIFT_ID_MAP[f"IU{sz}"], SHIFT_ID_MAP[f"ID{sz}"]
        pb.op("vector", lambda em, up=up: em.memset(up[:, :], 0.0), writes=[ku])
        pb.op("vector", lambda em, dn=dn: em.memset(dn[:, :], 0.0), writes=[kd])
        pb.op("vector", lambda em, up=up, sz=sz: em.tensor_copy(
            up[:, sz:128], pb.sb["ident"][:, 0:128 - sz]),
            reads=["ident", ku], writes=[ku])
        pb.op("vector", lambda em, dn=dn, sz=sz: em.tensor_copy(
            dn[:, 0:128 - sz], pb.sb["ident"][:, sz:128]),
            reads=["ident", kd], writes=[kd])


def emit_butterfly_fold(pb, tag, loc_cols, cross_ap, w_src_ap, w_dst, transposed,
                        rk_loc, rk_cross, rk_w, wk_dst, bias_srcs=None,
                        bias_dst=None, bias_rks=None):
    xpn = [f"XP{d}" for d in range(8)]
    xp = [pb.sbuf(n, [128, 2048], BF16) for n in xpn]

    def blk(pair, b):
        return xp[pair * 2 + b // 16][:, (b % 16) * 128:(b % 16 + 1) * 128]

    def blkk(pair, b):
        return xpn[pair * 2 + b // 16]

    PT = pb.psum("PTB0", [128, 1024], BF16)
    PS0 = pb.psum("PS0", [128, 512], F32)
    PS1 = pb.psum("PS1", [128, 512], F32)
    lcbuf = pb.sbuf("SCORES", [128, 2048], F32)
    lc = lcbuf[:, 0:1120]
    pb.q_dma(lc, loc_cols[:, :], reads=[rk_loc], writes=["SCORES"])

    def build_stage(s, into_pair, transp):
        sz = 2 ** s
        kiu, kid = SHIFT_ID_MAP[f"IU{sz}"], SHIFT_ID_MAP[f"ID{sz}"]
        iu, idn = pb.sb[kiu], pb.sb[kid]
        kd, ku, kn = (0, 3, 4) if transp else (0, 1, 2)
        for b in range(32):
            t = blk(into_pair, b)
            tk = blkk(into_pair, b)
            cd = lc[:, s * 160 + kd * 32 + b:s * 160 + kd * 32 + b + 1]
            cu = lc[:, s * 160 + ku * 32 + b:s * 160 + ku * 32 + b + 1]
            cn = lc[:, s * 160 + kn * 32 + b:s * 160 + kn * 32 + b + 1]
            pb.op("vector", lambda em, t=t, cd=cd: em.tensor_scalar_mul(
                t, pb.sb["ident"][:, :], cd), reads=["ident", "SCORES"], writes=[tk])
            pb.op("vector", lambda em, t=t, cu=cu, iu=iu: em.scalar_tensor_tensor(
                t, iu[:, :], cu, t, ALU.mult, ALU.add),
                reads=[kiu, "SCORES", tk], writes=[tk])
            pb.op("vector", lambda em, t=t, cn=cn, idn=idn: em.scalar_tensor_tensor(
                t, idn[:, :], cn, t, ALU.mult, ALU.add),
                reads=[kid, "SCORES", tk], writes=[tk])

    build_stage(0, 1, transp=False)
    for k in range(1, 7):
        build_stage(k, 0, transp=True)
        for b in range(32):
            pb.mm(PS0[:, 0:128], blk(0, b), blk(1, b), start=True, stop=True,
                  reads=[blkk(0, b), blkk(1, b)], writes=["PS0"])
            pb.op("vector", lambda em, b=b: em.tensor_copy(blk(2, b), PS0[:, 0:128]),
                  reads=["PS0"], writes=[blkk(2, b)])
        for t in range(2):
            pb.op("vector", lambda em, t=t: em.tensor_copy(xp[2 + t][:, :], xp[4 + t][:, :]),
                  reads=[xpn[4 + t]], writes=[xpn[2 + t]])
    if not transposed:
        for b in range(32):
            pb.transp(PT[:, 0:128], blk(1, b), reads=[blkk(1, b)], writes=["PTB0"])
            pb.op("vector", lambda em, b=b: em.tensor_copy(blk(3, b), PT[:, 0:128]),
                  reads=["PTB0"], writes=[blkk(3, b)])
        lhs_pair = 3
    else:
        lhs_pair = 1

    G = [pb.sbuf(f"G{i}", [128, D], BF16) for i in range(32)]
    wtmp = pb.sbuf("WTMP", [128, D], BF16)
    for b in range(32):
        pb.g_dma(wtmp[:, :], w_src_ap[b * 128:(b + 1) * 128, :], reads=[rk_w],
                 writes=["WTMP"])
        if transposed:
            pb.op("vector", lambda em, b=b: em.tensor_copy(G[b][:, :], wtmp[:, :]),
                  reads=["WTMP"], writes=[f"G{b}"])
        else:
            for oh in range(2):
                ps = (PS0, "PS0") if oh == 0 else (PS1, "PS1")
                pb.mm(ps[0][:, :], blk(lhs_pair, b), wtmp[:, oh * 512:(oh + 1) * 512],
                      start=True, stop=True,
                      reads=[blkk(lhs_pair, b), "WTMP"], writes=[ps[1]])
                pb.op("vector", lambda em, b=b, oh=oh, ps=ps: em.tensor_copy(
                    G[b][:, oh * 512:(oh + 1) * 512], ps[0][:, :]),
                    reads=[ps[1]], writes=[f"G{b}"])

    cc = lcbuf[:, 1120:1440]
    pb.q_dma(cc, cross_ap[:, :], reads=[rk_cross], writes=["SCORES"])
    ctmp1 = pb.sbuf("CT1", [128, D], BF16)
    ctmp2 = pb.sbuf("CT2", [128, D], BF16)
    stages = [11, 10, 9, 8, 7] if transposed else [7, 8, 9, 10, 11]
    for s in stages:
        cs = 2 ** (s - 7)
        pairs = [c for c in range(32) if not (c & cs)]
        for pi, cA in enumerate(pairs):
            cB = cA | cs
            col = (s - 7) * 64 + pi * 4
            i01, i10 = (2, 1) if transposed else (1, 2)
            w00 = cc[:, col + 0:col + 1]
            w01 = cc[:, col + i01:col + i01 + 1]
            w10 = cc[:, col + i10:col + i10 + 1]
            w11 = cc[:, col + 3:col + 4]
            pb.op("vector", lambda em, cB=cB, w01=w01: em.tensor_scalar_mul(
                ctmp1[:, :], G[cB][:, :], w01), reads=[f"G{cB}", "SCORES"], writes=["CT1"])
            pb.op("vector", lambda em, cA=cA, w10=w10: em.tensor_scalar_mul(
                ctmp2[:, :], G[cA][:, :], w10), reads=[f"G{cA}", "SCORES"], writes=["CT2"])
            pb.op("vector", lambda em, cA=cA, w00=w00: em.scalar_tensor_tensor(
                G[cA][:, :], G[cA][:, :], w00, ctmp1[:, :], ALU.mult, ALU.add),
                reads=[f"G{cA}", "SCORES", "CT1"], writes=[f"G{cA}"])
            pb.op("vector", lambda em, cB=cB, w11=w11: em.scalar_tensor_tensor(
                G[cB][:, :], G[cB][:, :], w11, ctmp2[:, :], ALU.mult, ALU.add),
                reads=[f"G{cB}", "SCORES", "CT2"], writes=[f"G{cB}"])

    if transposed:
        for b in range(32):
            for oh in range(2):
                ps = (PS0, "PS0") if oh == 0 else (PS1, "PS1")
                pb.mm(ps[0][:, :], blk(lhs_pair, b), G[b][:, oh * 512:(oh + 1) * 512],
                      start=True, stop=True,
                      reads=[blkk(lhs_pair, b), f"G{b}"], writes=[ps[1]])
                pb.op("vector", lambda em, b=b, oh=oh, ps=ps: em.tensor_copy(
                    G[b][:, oh * 512:(oh + 1) * 512], ps[0][:, :]),
                    reads=[ps[1]], writes=[f"G{b}"])

    for b in range(32):
        pb.q_dma(w_dst[b * 128:(b + 1) * 128, :], G[b][:, :], reads=[f"G{b}"],
                 writes=[wk_dst])

    if bias_srcs is not None:
        b0, bup = bias_srcs
        rb0, rbup = bias_rks
        bcol = pb.sbuf("BCOL", [128, 32], F32)
        bcu = pb.sbuf("BCU", [128, 32], F32)
        pb.q_dma(bcol[:, :], b0[:, :], reads=[rb0], writes=["BCOL"])
        pb.q_dma(bcu[:, :], bup[:, :], reads=[rbup], writes=["BCU"])
        bb = pb.sbuf("BB", [128, 32], BF16)
        pb.op("vector", lambda em: em.tensor_copy(bb[:, :], bcol[:, :]),
              reads=["BCOL"], writes=["BB"])
        for b in range(32):
            pb.mm(PS0[:, 0:1], blk(3, b), bb[:, b:b + 1], start=True, stop=True,
                  reads=[blkk(3, b), "BB"], writes=["PS0"])
            pb.op("vector", lambda em, b=b: em.tensor_copy(bcol[:, b:b + 1], PS0[:, 0:1]),
                  reads=["PS0"], writes=["BCOL"])
        for s in [7, 8, 9, 10, 11]:
            cs = 2 ** (s - 7)
            pairs = [c for c in range(32) if not (c & cs)]
            for pi, cA in enumerate(pairs):
                cB = cA | cs
                col = (s - 7) * 64 + pi * 4
                pb.op("vector", lambda em, cB=cB, col=col: em.tensor_scalar_mul(
                    bb[:, 0:1], bcol[:, cB:cB + 1], cc[:, col + 1:col + 2]),
                    reads=["BCOL", "SCORES"], writes=["BB"])
                pb.op("vector", lambda em, cA=cA, col=col: em.tensor_scalar_mul(
                    bb[:, 1:2], bcol[:, cA:cA + 1], cc[:, col + 2:col + 3]),
                    reads=["BCOL", "SCORES"], writes=["BB"])
                pb.op("vector", lambda em, cA=cA, col=col: em.scalar_tensor_tensor(
                    bcol[:, cA:cA + 1], bcol[:, cA:cA + 1], cc[:, col:col + 1],
                    bb[:, 0:1], ALU.mult, ALU.add),
                    reads=["BCOL", "SCORES", "BB"], writes=["BCOL"])
                pb.op("vector", lambda em, cB=cB, col=col: em.scalar_tensor_tensor(
                    bcol[:, cB:cB + 1], bcol[:, cB:cB + 1], cc[:, col + 3:col + 4],
                    bb[:, 1:2], ALU.mult, ALU.add),
                    reads=["BCOL", "SCORES", "BB"], writes=["BCOL"])
        pb.op("vector", lambda em: em.tensor_add(bias_dst[:, :], bcol[:, :], bcu[:, :]),
              reads=["BCOL", "BCU"], writes=["BT32"])


def emit_L2(pb, inp, n_seq, S):
    T = n_seq * S
    nt = T // 128
    nib = S // 128
    xres = pb.dr["xres"]

    wexpF = pb.dram("l2_wexpF", [4 * D, D], BF16)
    wconF = pb.dram("l2_wconF", [4 * D, D], BF16)
    bt32 = pb.sbuf("BT32", [128, 32], F32)
    emit_shift_idents(pb)
    emit_butterfly_fold(pb, "u", inp["l2_bfu_loc"], inp["l2_bfu_cross"],
                        inp["l2_wexpT"], wexpF, False,
                        "in_bfu_loc", "in_bfu_cross", "in_wexpT", "l2_wexpF",
                        bias_srcs=(inp["l2_bexpT"], inp["l2_bupT"]),
                        bias_dst=bt32, bias_rks=("in_bexpT", "in_bupT"))
    emit_butterfly_fold(pb, "d", inp["l2_bfd_loc"], inp["l2_bfd_cross"],
                        inp["L2_ffn_W_con"], wconF, True,
                        "in_bfd_loc", "in_bfd_cross", "in_wcon", "l2_wconF")

    wpre = pb.sbuf("WPRE", [128, 128], BF16)
    wpost = pb.sbuf("WPOST", [128, 128], BF16)
    for nm, src_ap, rk in (("WPRE", inp["l2_wpreT"], "in_wpreT"),
                           ("WPOST", inp["l2_wpostT"], "in_wpostT")):
        t = pb.sb[nm]
        pb.op("vector", lambda em, t=t: em.memset(t[:, :], 0.0), writes=[nm])
        for b in range(8):
            pb.g_dma(t[b * 16:(b + 1) * 16, b * 16:(b + 1) * 16],
                     src_ap[:, :], reads=[rk], writes=[nm])

    xn_f, xn_b = emit_ln(pb, "l2", "xres", xres, T)

    qhm = pb.dram("l2_qhm", [D, T], BF16)
    khm = pb.dram("l2_khm", [D, T], BF16)
    vT = pb.dram("l2_vT", [D, T], BF16)
    qraw = pb.dram("l2_qraw", [D, T], BF16)
    kraw = pb.dram("l2_kraw", [D, T], BF16)
    Sm = pb.dram("l2_sm", [T // 8 * 128, 2048], BF16)
    S3 = pb.dram("l2_s3", [16, 2048, 2048], BF16)
    hT = pb.dram("l2_hT", [32, 128, T], BF16)
    s_scr = pb.dram("s_l2", [T, D], F32)
    g_scr = pb.dram("g_l2", [T, D], F32)
    PS = [pb.psum(f"PS{i}", [128, 512], F32) for i in range(6)]
    PT = pb.sb["PTB0"]

    invc = pb.sbuf("INVC", [128, 1], F32)
    pb.op("gpsimd", lambda em: em.iota(invc[0:32, :], [[1, 1]], base=0,
                                       channel_multiplier=1,
                                       allow_small_or_imprecise_dtypes=True),
          writes=["INVC"])
    pb.op("scalar", lambda em: em.activation(invc[0:32, :], invc[0:32, :], AF.Exp,
                                             scale=-math.log(10000.0) / 32.0),
          reads=["INVC"], writes=["INVC"])
    for rep in range(1, 4):
        pb.g_dma(invc[rep * 32:(rep + 1) * 32, :], invc[0:32, :], reads=["INVC"],
                 writes=["INVC"])
    TWO_PI = 2.0 * math.pi

    def emit_rope_tables(cosb, sinb):
        fr = pb.sbuf("FP0", [128, D], F32)
        ftmp = pb.sbuf("FP2", [128, D], F32)

        def wrap_pi():
            pb.op("vector", lambda em: em.tensor_scalar(
                ftmp[:, :], fr[:, :], math.pi, -TWO_PI, ALU.is_ge, ALU.mult),
                reads=["FP0"], writes=["FP2"])
            pb.op("vector", lambda em: em.tensor_add(fr[:, :], fr[:, :], ftmp[:, :]),
                  reads=["FP0", "FP2"], writes=["FP0"])

        for hh_ in range(max(1, S // 512)):
            w_ = min(S, 512)
            csl = slice(hh_ * 512, hh_ * 512 + w_)
            pb.op("gpsimd", lambda em, hh_=hh_, w_=w_: em.iota(
                fr[:, 0:w_], [[1, w_]], base=hh_ * 512, channel_multiplier=0,
                allow_small_or_imprecise_dtypes=True), writes=["FP0"])
            pb.op("vector", lambda em, w_=w_: em.tensor_scalar_mul(
                fr[:, 0:w_], fr[:, 0:w_], invc[:, 0:1]),
                reads=["FP0", "INVC"], writes=["FP0"])
            pb.op("vector", lambda em, w_=w_: em.tensor_scalar(
                ftmp[:, 0:w_], fr[:, 0:w_], 1.0 / TWO_PI, 8388608.0 - 0.5,
                ALU.mult, ALU.add), reads=["FP0"], writes=["FP2"])
            pb.op("vector", lambda em, w_=w_: em.tensor_single_scalar(
                ftmp[:, 0:w_], ftmp[:, 0:w_], -8388608.0, ALU.add),
                reads=["FP2"], writes=["FP2"])
            pb.op("vector", lambda em, w_=w_: em.scalar_tensor_tensor(
                fr[:, 0:w_], ftmp[:, 0:w_], -TWO_PI, fr[:, 0:w_], ALU.mult, ALU.add),
                reads=["FP2", "FP0"], writes=["FP0"])
            wrap_pi()
            pb.op("scalar", lambda em, csl=csl, w_=w_: em.activation(
                sinb[:, csl], fr[:, 0:w_], AF.Sin), reads=["FP0"], writes=["XP1"])
            pb.op("vector", lambda em, w_=w_: em.tensor_single_scalar(
                fr[:, 0:w_], fr[:, 0:w_], math.pi / 2.0, ALU.add),
                reads=["FP0"], writes=["FP0"])
            wrap_pi()
            pb.op("scalar", lambda em, csl=csl, w_=w_: em.activation(
                cosb[:, csl], fr[:, 0:w_], AF.Sin), reads=["FP0"], writes=["XP0"])

    bqkvT = pb.sbuf("BQKVT", [128, 24], F32)
    pb.q_dma(bqkvT[:, :], inp["l2_bqkvT"][:, :], reads=["in_bqkvT"], writes=["BQKVT"])

    for seq in range(n_seq):
        emit_load_xnT(pb, "l2", xn_b, seq, S, T)
        xp = [pb.sb[f"XP{d}"] for d in range(8)]
        xpk = [f"XP{d}" for d in range(8)]
        ssl = slice(seq * S, (seq + 1) * S)

        wq = [pb.sbuf(f"G{24+i}", [128, D], BF16) for i in range(2)]
        ev = pb.sbuf("E0", [128, 512], BF16)
        for m in range(24):
            wqt = wq[m % 2]
            wqk = f"G{24 + m % 2}"
            for d in range(8):
                pb.g_dma(wqt[:, d * 128:(d + 1) * 128],
                         inp["l2_wqkvT"][d * 128:(d + 1) * 128, m * 128:(m + 1) * 128],
                         reads=["in_wqkvT"], writes=[wqk])
            for tg in range(S // 512):
                tsl = slice(tg * 512, (tg + 1) * 512)
                ps = PS[4 + tg % 2]
                psk = f"PS{4 + tg % 2}"
                for d in range(8):
                    pb.mm(ps[:, :], wqt[:, d * 128:(d + 1) * 128], xp[d][:, tsl],
                          start=(d == 0), stop=(d == 7), reads=[wqk, xpk[d]],
                          writes=[psk])
                pb.op("scalar", lambda em, ps=ps, m=m: em.activation(
                    ev[:, :], ps[:, :], AF.Identity, bias=bqkvT[:, m:m + 1]),
                    reads=[psk, "BQKVT"], writes=["E0"])
                gcol = seq * S + tg * 512
                if m < 16:
                    dst = qraw if m < 8 else kraw
                    dk = "l2_qraw" if m < 8 else "l2_kraw"
                    mm = m % 8
                    for half in range(2):
                        for hh in range(2):
                            rbase = hh * 64 + half * 32
                            drow = half * 512 + 64 * mm + hh * 32
                            pb.q_dma(dst[drow:drow + 32, gcol:gcol + 512],
                                     ev[rbase:rbase + 32, :],
                                     reads=["E0"], writes=[f"{dk}@{seq}"])
                else:
                    pb.q_dma(vT[(m - 16) * 128:(m - 15) * 128, gcol:gcol + 512],
                             ev[:, :], reads=["E0"], writes=[f"l2_vT@{seq}"])

        cosb = pb.sb["XP0"]
        sinb = pb.sb["XP1"]
        emit_rope_tables(cosb, sinb)
        rw = [pb.sbuf(f"XP{4+i}", [128, 2048], BF16) for i in range(4)]
        rwk = [f"XP{4+i}" for i in range(4)]
        for (raw, hm, rkk, wkk, scaled) in ((qraw, qhm, "l2_qraw", "l2_qhm", True),
                                            (kraw, khm, "l2_kraw", "l2_khm", False)):
            for rc in range(4):
                pb.q_dma(rw[0][:, 0:S], raw[rc * 128:(rc + 1) * 128, ssl],
                         reads=[f"{rkk}@{seq}"], writes=[rwk[0]])
                pb.q_dma(rw[1][:, 0:S], raw[512 + rc * 128:512 + (rc + 1) * 128, ssl],
                         reads=[f"{rkk}@{seq}"], writes=[rwk[1]])
                pb.op("vector", lambda em: em.tensor_mul(rw[2][:, 0:S], rw[0][:, 0:S],
                                                         cosb[:, 0:S]),
                      reads=[rwk[0], "XP0"], writes=[rwk[2]])
                pb.op("vector", lambda em: em.tensor_mul(rw[3][:, 0:S], rw[1][:, 0:S],
                                                         sinb[:, 0:S]),
                      reads=[rwk[1], "XP1"], writes=[rwk[3]])
                pb.op("vector", lambda em: em.tensor_sub(rw[2][:, 0:S], rw[2][:, 0:S],
                                                         rw[3][:, 0:S]),
                      reads=[rwk[2], rwk[3]], writes=[rwk[2]])
                if scaled:
                    pb.op("vector", lambda em: em.tensor_single_scalar(
                        rw[2][:, 0:S], rw[2][:, 0:S], 0.125, ALU.mult),
                        reads=[rwk[2]], writes=[rwk[2]])
                pb.op("vector", lambda em: em.tensor_mul(rw[3][:, 0:S], rw[0][:, 0:S],
                                                         sinb[:, 0:S]),
                      reads=[rwk[0], "XP1"], writes=[rwk[3]])
                pb.op("vector", lambda em: em.tensor_mul(rw[0][:, 0:S], rw[1][:, 0:S],
                                                         cosb[:, 0:S]),
                      reads=[rwk[1], "XP0"], writes=[rwk[0]])
                pb.op("vector", lambda em: em.tensor_add(rw[3][:, 0:S], rw[3][:, 0:S],
                                                         rw[0][:, 0:S]),
                      reads=[rwk[3], rwk[0]], writes=[rwk[3]])
                if scaled:
                    pb.op("vector", lambda em: em.tensor_single_scalar(
                        rw[3][:, 0:S], rw[3][:, 0:S], 0.125, ALU.mult),
                        reads=[rwk[3]], writes=[rwk[3]])
                for half, til, tk in ((0, rw[2], rwk[2]), (1, rw[3], rwk[3])):
                    for a4 in range(4):
                        base = (256 * rc + 32 * half + 64 * a4) * T + seq * S
                        pb.q_dma(bass.AP(hm.tensor, base, [[T, 32], [1, S]]),
                                 til[a4 * 32:(a4 + 1) * 32, 0:S],
                                 reads=[tk], writes=[f"{wkk}@{seq}"])

        khm_sb = [pb.sbuf(f"G{i}", [128, D], BF16) for i in range(16)]
        nhalf = max(1, S // 1024)
        hw = min(S, 1024)
        for c in range(8):
            for jh in range(nhalf):
                pb.q_dma(khm_sb[2 * c + jh][:, 0:hw],
                         khm[c * 128:(c + 1) * 128,
                             seq * S + jh * 1024:seq * S + jh * 1024 + hw],
                         reads=[f"l2_khm@{seq}"], writes=[f"G{2*c+jh}"])
        qi = [pb.sbuf(f"QI{d}", [128, 128], BF16) for d in range(8)]
        lb = [pb.sbuf("E0", [128, 512], BF16), pb.sbuf("QOUT", [128, 512], BF16)]
        for ib in range(nib):
            for d in range(8):
                pb.q_dma(qi[d][:, :],
                         qhm[d * 128:(d + 1) * 128,
                             seq * S + ib * 128:seq * S + (ib + 1) * 128],
                         reads=[f"l2_qhm@{seq}"], writes=[f"QI{d}"])
            for h in range(16):
                lhs = qi[h // 2][(h % 2) * 64:(h % 2) * 64 + 64, :]
                for jg in range(S // 512):
                    ps = PS[4 + jg % 2]
                    psk = f"PS{4 + jg % 2}"
                    jh = (jg * 512) // 1024
                    rhs = khm_sb[2 * (h // 2) + jh][
                        (h % 2) * 64:(h % 2) * 64 + 64,
                        jg * 512 - jh * 1024:jg * 512 - jh * 1024 + 512]
                    pb.mm(ps[:, :], lhs, rhs, start=True, stop=True,
                          reads=[f"QI{h//2}", f"G{2*(h//2)+jh}"], writes=[psk])
                    e = lb[jg % 2]
                    ek = "E0" if jg % 2 == 0 else "QOUT"
                    eng = "vector" if jg % 2 == 0 else "scalar"
                    if eng == "vector":
                        pb.op(eng, lambda em, e=e, ps=ps: em.tensor_copy(e[:, :], ps[:, :]),
                              reads=[psk], writes=[ek])
                    else:
                        pb.op(eng, lambda em, e=e, ps=ps: em.copy(e[:, :], ps[:, :]),
                              reads=[psk], writes=[ek])
                    base = seq * 16 * S * S + h * S * S + (ib * 128) * S + jg * 512
                    pb.q_dma(bass.AP(Sm.tensor, base, [[S, 128], [1, 512]]),
                             e[:, :], reads=[ek], writes=[f"l2_sm@{seq}"])

        rot = [
            (pb.sbuf("PBUF", [128, 2048], BF16), "PBUF",
             pb.sbuf("PBUF2", [128, 2048], BF16), "PBUF2"),
            (pb.sbuf("XP0", [128, 2048], BF16), "XP0",
             pb.sbuf("XP1", [128, 2048], BF16), "XP1"),
            (pb.sbuf("XP2", [128, 2048], BF16), "XP2",
             pb.sbuf("XP3", [128, 2048], BF16), "XP3"),
        ]
        stats = [(pb.sbuf(f"SCOL{r}", [128, 4], F32),
                  pb.sbuf(f"SSUM{r}", [128, 1], F32),
                  pb.sbuf(f"SRCP{r}", [128, 1], F32)) for r in range(3)]
        NROTB = 3
        for gi in range(S // 8):
            strip, bufk, pstr, pstk = rot[gi % NROTB]
            scol, ssum, srcp = stats[gi % NROTB]
            sck, ssk, srk = f"SCOL{gi % NROTB}", f"SSUM{gi % NROTB}", f"SRCP{gi % NROTB}"
            sbase = seq * 16 * S * S
            for is_ in range(8):
                pb.q_dma(strip[is_ * 16:(is_ + 1) * 16, 0:S],
                         bass.AP(Sm.tensor, sbase + (gi * 8 + is_) * S,
                                 [[S * S, 16], [1, S]]),
                         reads=[f"l2_sm@{seq}"], writes=[bufk])
            for jg in range(S // 512):
                ps = PS[jg % 2]
                psk = f"PS{jg % 2}"
                pb.mm(ps[:, :], wpre[:, :], strip[:, jg * 512:(jg + 1) * 512],
                      start=True, stop=True, reads=["WPRE", bufk], writes=[psk])
                pb.op("scalar", lambda em, jg=jg, ps=ps, pstr=pstr, scol=scol:
                      em.activation(
                          pstr[:, jg * 512:(jg + 1) * 512], ps[:, :], AF.Exp,
                          accum_out=scol[:, jg:jg + 1]),
                      reads=[psk], writes=[pstk, sck])
            pb.op("vector", lambda em, ssum=ssum, scol=scol: em.reduce_sum(
                ssum[:, :], scol[:, 0:S // 512], AX.X),
                reads=[sck], writes=[ssk])
            pb.op("vector", lambda em, srcp=srcp, ssum=ssum: em.reciprocal(
                srcp[:, :], ssum[:, :]),
                reads=[ssk], writes=[srk])
            pb.op("vector", lambda em, pstr=pstr, srcp=srcp: em.tensor_scalar_mul(
                pstr[:, 0:S], pstr[:, 0:S], srcp[:, 0:1]),
                reads=[pstk, srk], writes=[pstk])
            for jg in range(S // 512):
                ps = PS[2 + jg % 2]
                psk = f"PS{2 + jg % 2}"
                pb.mm(ps[:, :], wpost[:, :], pstr[:, jg * 512:(jg + 1) * 512],
                      start=True, stop=True, reads=["WPOST", pstk], writes=[psk])
                eng = "vector" if jg % 2 == 0 else "scalar"
                if eng == "vector":
                    pb.op(eng, lambda em, jg=jg, ps=ps, strip=strip: em.tensor_copy(
                        strip[:, jg * 512:(jg + 1) * 512], ps[:, :]),
                        reads=[psk], writes=[bufk])
                else:
                    pb.op(eng, lambda em, jg=jg, ps=ps, strip=strip: em.copy(
                        strip[:, jg * 512:(jg + 1) * 512], ps[:, :]),
                        reads=[psk], writes=[bufk])
            for is_ in range(8):
                base = (gi * 8 + is_) * 2048
                pb.q_dma(bass.AP(S3.tensor, base, [[2048 * 2048, 16], [1, S]]),
                         strip[is_ * 16:(is_ + 1) * 16, 0:S],
                         reads=[bufk], writes=[f"l2_s3@{seq}"])

        for hp in range(8):
            for jb in range(nib):
                pb.q_dma(xp[hp][:, jb * 128:(jb + 1) * 128],
                         vT[hp * 128:(hp + 1) * 128,
                            seq * S + jb * 128:seq * S + (jb + 1) * 128],
                         reads=[f"l2_vT@{seq}"], writes=[xpk[hp]], transpose=True)
        for d in range(8):
            g = pb.sbuf(f"G{16+d}", [128, D], BF16)
            pb.g_dma(g[:, :], inp["l2_woutT"][d * 128:(d + 1) * 128, :],
                     reads=["in_woutT"], writes=[f"G{16+d}"])
        brow = pb.sbuf("BROW", [1, D], BF16)
        pb.g_dma(brow[:, :], inp["L2_attn_bout"][0:1, :], reads=["in_bout2"],
                 writes=["BROW"])

        rt = [pb.sbuf("E0", [128, 512], BF16), pb.sbuf("QOUT", [128, 512], BF16)]
        _pb1 = pb.sbuf("PBUF", [128, 2048], BF16)
        _pb2 = pb.sbuf("PBUF2", [128, 2048], BF16)
        ot = [(_pb1 if c < 4 else _pb2)[:, (c % 4) * 512:(c % 4 + 1) * 512]
              for c in range(8)]
        otk = ["PBUF" if c < 4 else "PBUF2" for c in range(8)]
        stile = pb.sbuf("STILE", [128, 1024], F32)
        for ig in range(S // 512):
            for g in range(16):
                ps = PS[5]
                for jb in range(nib):
                    r = rt[jb % 2]
                    rk_ = "E0" if jb % 2 == 0 else "QOUT"
                    pb.q_dma(r[:, :],
                             S3[g, ig * 512:(ig + 1) * 512, jb * 128:(jb + 1) * 128],
                             reads=[f"l2_s3@{seq}"], writes=[rk_], transpose=True)
                    pb.mm(ps[0:64, :],
                          xp[g // 2][:, jb * 128 + (g % 2) * 64:jb * 128 + (g % 2) * 64 + 64],
                          r[:, :], start=(jb == 0), stop=(jb == nib - 1),
                          reads=[xpk[g // 2], rk_], writes=["PS5"])
                pb.op("vector", lambda em, g=g, ps=ps: em.tensor_copy(
                    ot[g // 2][(g % 2) * 64:(g % 2) * 64 + 64, :], ps[0:64, :]),
                    reads=["PS5"], writes=[otk[g // 2]])
            for isub in range(4):
                for oh in range(2):
                    ps = PS[oh]
                    psk = f"PS{oh}"
                    osl = slice(oh * 512, (oh + 1) * 512)
                    pb.mm(ps[:, :], pb.sb["onescol"][:, :], brow[:, osl],
                          start=True, stop=False, reads=["onescol", "BROW"], writes=[psk])
                    for c in range(8):
                        pb.mm(ps[:, :], ot[c][:, isub * 128:(isub + 1) * 128],
                              pb.sb[f"G{16+c}"][:, osl], start=False, stop=(c == 7),
                              reads=[otk[c], f"G{16+c}"], writes=[psk])
                    if oh == 0:
                        pb.op("vector", lambda em, ps=ps: em.tensor_copy(
                            stile[:, 0:512], ps[:, :]), reads=[psk], writes=["STILE"])
                    else:
                        pb.op("scalar", lambda em, ps=ps: em.copy(
                            stile[:, 512:1024], ps[:, :]), reads=[psk], writes=["STILE"])
                t0 = seq * S + ig * 512 + isub * 128
                pb.q_dma(s_scr[t0:t0 + 128, :], stile[:, :], reads=["STILE"],
                         writes=[f"s_l2@{t0 // 128}"])

        emit_load_xnT(pb, "l2", xn_b, seq, S, T)
        hf32 = pb.sbuf("HF32", [128, 512], F32)
        hbf = pb.sbuf("HBF", [128, 512], BF16)
        for fc in range(32):
            for d in range(8):
                pb.q_dma(qi[d][:, :],
                         wexpF[fc * 128:(fc + 1) * 128, d * 128:(d + 1) * 128],
                         reads=["l2_wexpF"], writes=[f"QI{d}"], transpose=True)
            for tg in range(S // 512):
                tsl = slice(tg * 512, (tg + 1) * 512)
                ps = PS[4 + tg % 2]
                psk = f"PS{4 + tg % 2}"
                for d in range(8):
                    pb.mm(ps[:, :], qi[d][:, :], xp[d][:, tsl],
                          start=(d == 0), stop=(d == 7), reads=[f"QI{d}", xpk[d]],
                          writes=[psk])
                pb.op("scalar", lambda em, ps=ps, fc=fc: em.activation(
                    hbf[:, :], ps[:, :], AF.Gelu, bias=bt32[:, fc:fc + 1]),
                    reads=[psk, "BT32"], writes=["HBF"])
                pb.q_dma(hT[fc, :, seq * S + tg * 512:seq * S + (tg + 1) * 512],
                         hbf[:, :], reads=["HBF"], writes=[f"l2_hT@{fc}"])
        for fc in range(32):
            g = pb.sbuf(f"G{fc}", [128, D], BF16)
            pb.q_dma(g[:, :], wconF[fc * 128:(fc + 1) * 128, :], reads=["l2_wconF"],
                     writes=[f"G{fc}"])
        brow2 = pb.sbuf("PBUF2", [128, 2048], BF16)[0:1, 0:D]
        brow3 = pb.sb["PBUF2"][0:1, D:2 * D]
        pb.g_dma(brow2, inp["l2_bcon"][0:1, :], reads=["in_bcon"], writes=["PBUF2"])
        pb.g_dma(brow3, inp["l2_dbias"][0:1, :], reads=["in_dbias"], writes=["PBUF2"])
        ht0 = pb.sbuf("HT0", [128, 128], BF16)
        ht1 = pb.sbuf("HT1", [128, 128], BF16)
        yev = pb.sbuf("STILE", [128, 1024], F32)
        for tb in range(nib):
            t0 = seq * S + tb * 128
            for oh in range(2):
                osl = slice(oh * 512, (oh + 1) * 512)
                pb.mm(PS[oh][:, :], pb.sb["onescol"][:, :], brow2[:, osl],
                      start=True, stop=False, reads=["onescol", "PBUF2"], writes=[f"PS{oh}"])
                pb.mm(PS[oh][:, :], pb.sb["onescol"][:, :], brow3[:, osl],
                      start=False, stop=False, reads=["onescol", "PBUF2"], writes=[f"PS{oh}"])
            for fc in range(32):
                ht = (ht0, "HT0") if fc % 2 == 0 else (ht1, "HT1")
                pb.q_dma(ht[0][:, :], hT[fc, :, t0:t0 + 128], reads=[f"l2_hT@{fc}"],
                         writes=[ht[1]])
                for oh in range(2):
                    pb.mm(PS[oh][:, :], ht[0][:, :],
                          pb.sb[f"G{fc}"][:, oh * 512:(oh + 1) * 512],
                          start=False, stop=(fc == 31),
                          reads=[ht[1], f"G{fc}"], writes=[f"PS{oh}"])
            pb.op("vector", lambda em: em.tensor_copy(yev[:, 0:512], PS[0][:, :]),
                  reads=["PS0"], writes=["STILE"])
            pb.op("vector", lambda em: em.tensor_copy(yev[:, 512:1024], PS[1][:, :]),
                  reads=["PS1"], writes=["STILE"])
            pb.q_dma(g_scr[t0:t0 + 128, :], yev[:, :], reads=["STILE"],
                     writes=[f"g_l2@{t0 // 128}"])

    emit_combine(pb, "l2", xres, s_scr, g_scr, T)


def prep_weights_L2(w):
    out = {}
    out["l2_wqkvT"] = np.ascontiguousarray(np.asarray(w["L2_attn_Wqkv"]).T)
    out["l2_bqkvT"] = np.ascontiguousarray(np.asarray(w["L2_attn_bqkv"]).reshape(24, 128).T)
    out["l2_wpreT"] = np.ascontiguousarray(np.asarray(w["L2_attn_Wpre"]).T)
    out["l2_wpostT"] = np.ascontiguousarray(np.asarray(w["L2_attn_Wpost"]).T)
    out["l2_woutT"] = np.ascontiguousarray(np.asarray(w["L2_attn_Wout"]).T)
    out["L2_attn_bout"] = np.ascontiguousarray(np.asarray(w["L2_attn_bout"]).reshape(1, D))
    out["l2_wexpT"] = np.ascontiguousarray(np.asarray(w["L2_ffn_W_exp"]).T)
    out["L2_ffn_W_con"] = np.ascontiguousarray(np.asarray(w["L2_ffn_W_con"]))
    out["l2_bexpT"] = np.ascontiguousarray(np.asarray(w["L2_ffn_b_exp"]).reshape(32, 128).T)
    out["l2_bupT"] = np.ascontiguousarray(np.asarray(w["L2_ffn_up_bias"]).reshape(32, 128).T)
    out["l2_bcon"] = np.ascontiguousarray(np.asarray(w["L2_ffn_b_con"]).reshape(1, D))
    out["l2_dbias"] = np.ascontiguousarray(np.asarray(w["L2_ffn_down_bias"]).reshape(1, D))
    for tag, wname in (("u", "L2_ffn_up_weights"), ("d", "L2_ffn_down_weights")):
        bw = np.asarray(w[wname])
        loc = np.zeros((128, 1120), np.float32)
        for s in range(7):
            sz = 2 ** s
            ww = bw[s]
            for blk in range(32):
                cd = np.zeros(128, np.float32)
                cu = np.zeros(128, np.float32)
                cn = np.zeros(128, np.float32)
                cuT = np.zeros(128, np.float32)
                cnT = np.zeros(128, np.float32)
                for p in range(128):
                    f = blk * 128 + p
                    q_, rr = f // (2 * sz), f % (2 * sz)
                    if rr < sz:
                        k = q_ * sz + rr
                        cd[p] = ww[k, 0, 0]
                        cu[p] = ww[k, 0, 1]
                        cuT[p] = ww[k, 1, 0]
                    else:
                        k = q_ * sz + (rr - sz)
                        cd[p] = ww[k, 1, 1]
                        cn[p] = ww[k, 1, 0]
                        cnT[p] = ww[k, 0, 1]
                loc[:, s * 160 + 0 * 32 + blk] = cd
                loc[:, s * 160 + 1 * 32 + blk] = cu
                loc[:, s * 160 + 2 * 32 + blk] = cn
                loc[:, s * 160 + 3 * 32 + blk] = cuT
                loc[:, s * 160 + 4 * 32 + blk] = cnT
        out[f"l2_bf{tag}_loc"] = loc
        cross = np.zeros((128, 320), np.float32)
        for s in range(7, 12):
            sz = 2 ** s
            cs = 2 ** (s - 7)
            ww = bw[s]
            pairs = [c for c in range(32) if not (c & cs)]
            for pi, cA in enumerate(pairs):
                for e in range(128):
                    a = cA * 128 + e
                    k = (a // (2 * sz)) * sz + a % sz
                    col = (s - 7) * 64 + pi * 4
                    cross[e, col + 0] = ww[k, 0, 0]
                    cross[e, col + 1] = ww[k, 0, 1]
                    cross[e, col + 2] = ww[k, 1, 0]
                    cross[e, col + 3] = ww[k, 1, 1]
        out[f"l2_bf{tag}_cross"] = cross
    return out


def declare_inputs_L2(pb):
    inp = {}
    inp["l2_wqkvT"] = pb.dram("l2_wqkvT", [D, 3 * D], F32, kind="ExternalInput")
    inp["l2_bqkvT"] = pb.dram("l2_bqkvT", [128, 24], F32, kind="ExternalInput")
    inp["l2_wpreT"] = pb.dram("l2_wpreT", [16, 16], F32, kind="ExternalInput")
    inp["l2_wpostT"] = pb.dram("l2_wpostT", [16, 16], F32, kind="ExternalInput")
    inp["l2_woutT"] = pb.dram("l2_woutT", [D, D], F32, kind="ExternalInput")
    inp["L2_attn_bout"] = pb.dram("L2_attn_bout", [1, D], F32, kind="ExternalInput")
    inp["l2_wexpT"] = pb.dram("l2_wexpT", [4 * D, D], F32, kind="ExternalInput")
    inp["L2_ffn_W_con"] = pb.dram("L2_ffn_W_con", [4 * D, D], F32, kind="ExternalInput")
    inp["l2_bexpT"] = pb.dram("l2_bexpT", [128, 32], F32, kind="ExternalInput")
    inp["l2_bupT"] = pb.dram("l2_bupT", [128, 32], F32, kind="ExternalInput")
    inp["l2_bcon"] = pb.dram("l2_bcon", [1, D], F32, kind="ExternalInput")
    inp["l2_dbias"] = pb.dram("l2_dbias", [1, D], F32, kind="ExternalInput")
    inp["l2_bfu_loc"] = pb.dram("l2_bfu_loc", [128, 1120], F32, kind="ExternalInput")
    inp["l2_bfd_loc"] = pb.dram("l2_bfd_loc", [128, 1120], F32, kind="ExternalInput")
    inp["l2_bfu_cross"] = pb.dram("l2_bfu_cross", [128, 320], F32, kind="ExternalInput")
    inp["l2_bfd_cross"] = pb.dram("l2_bfd_cross", [128, 320], F32, kind="ExternalInput")
    return inp




def emit_L1(pb, inp, n_seq, S):
    T = n_seq * S
    nt = T // 128
    nib = S // 128
    H, DK = 4, 256
    xres = pb.dr["xres"]
    xn_f, xn_b = emit_ln(pb, "l1", "xres", xres, T)

    qT = pb.dram("l1_qT", [D, T], BF16)
    kT = pb.dram("l1_kT", [D, T], BF16)
    vT = pb.dram("l1_vT", [D, T], BF16)
    beta_scr = pb.dram("l1_beta", [T, 4], F32)
    od_scr = pb.dram("l1_od", [T, D], BF16)
    odf_scr = pb.dram("l1_odf", [D, T], BF16)
    ps_scr = pb.dram("l1_ps", [D, T], BF16)
    plg_scr = pb.dram("l1_pl", [D, T], BF16)
    gin_scr = pb.dram("l1_gin", [64, T], F32)
    o_scr = pb.dram("l1_o", [T, D], BF16)
    xnT_scr = pb.dram("l1_xnT", [D, T], BF16)
    acc_scr = pb.dram("l1_acc", [T, D], F32)
    s_scr = pb.dram("s_l1", [T, D], F32)
    g_scr = pb.dram("g_l1", [T, D], F32)
    PS = [pb.psum(f"PS{i}", [128, 512], F32) for i in range(6)]
    PT = pb.psum("PTB0", [128, 1024], BF16)
    ident = pb.sb["ident"]
    pb.sbuf("STILE", [128, 1024], F32)
    pb.sbuf("HF32", [128, 512], F32)

    for nm, cm, pat, thr in (("M01SL", 1, -1, 1.0),
                             ("M01SU", -1, 1, 1.0),
                             ("M01IU", -1, 1, 0.0)):
        t = pb.sbuf(nm, [128, 128], BF16)
        tf = pb.sbuf("identf", [128, 128], F32)
        pb.op("gpsimd", lambda em, tf=tf, cm=cm, pat=pat: em.iota(
            tf[:, :], [[pat, 128]], base=0, channel_multiplier=cm,
            allow_small_or_imprecise_dtypes=True), writes=["identf"])
        pb.op("vector", lambda em, t=t, tf=tf, thr=thr: em.tensor_single_scalar(
            t[:, :], tf[:, :], thr, ALU.is_ge), reads=["identf"], writes=[nm])
    pb.op("gpsimd", lambda em: em.iota(pb.sb["identf"][:, :], [[1, 128]], base=0,
                                       channel_multiplier=-1,
                                       allow_small_or_imprecise_dtypes=True),
          writes=["identf"])
    pb.op("vector", lambda em: em.tensor_single_scalar(
        pb.sb["identf"][:, :], pb.sb["identf"][:, :], 0.0, ALU.is_equal),
        reads=["identf"], writes=["identf"])

    cq = pb.sbuf("CQW", [128, 96], F32)
    pb.q_dma(cq[:, 0:32], inp["l1_cqT"][:, :], reads=["in_cq"], writes=["CQW"])
    pb.q_dma(cq[:, 32:64], inp["l1_ckT"][:, :], reads=["in_ck"], writes=["CQW"])
    pb.q_dma(cq[:, 64:96], inp["l1_cvT"][:, :], reads=["in_cv"], writes=["CQW"])
    firs = pb.sbuf("FIRS", [128, 40], F32)
    firl = pb.sbuf("FIRL", [128, 504], F32)
    pb.q_dma(firs[:, :], inp["l1_firS"][:, :], reads=["in_firS"], writes=["FIRS"])
    pb.q_dma(firl[:, :], inp["l1_firL"][:, :], reads=["in_firL"], writes=["FIRL"])

    convr = pb.sbuf("CONVR", [128, 2112], BF16)
    conva = pb.sbuf("CONVA", [128, 2048], F32)
    ev = pb.sbuf("E0", [128, 512], BF16)

    for seq in range(n_seq):
        emit_load_xnT(pb, "l1", xn_b, seq, S, T)
        xp = [pb.sb[f"XP{d}"] for d in range(8)]
        xpk = [f"XP{d}" for d in range(8)]
        for d in range(8):
            pb.q_dma(xnT_scr[d * 128:(d + 1) * 128, seq * S:(seq + 1) * S],
                     xp[d][:, 0:S], reads=[xpk[d]], writes=["l1_xnT"])

        for pi, (wkey, rkw, dst, dk_, coff) in enumerate(
                (("l1_wqT", "in_wq", qT, "l1_qT", 0),
                 ("l1_wkT", "in_wk", kT, "l1_kT", 32),
                 ("l1_wvT", "in_wv", vT, "l1_vT", 64))):
            for d in range(8):
                g = pb.sbuf(f"G{d}", [128, D], BF16)
                pb.g_dma(g[:, :], inp[wkey][d * 128:(d + 1) * 128, :],
                         reads=[rkw], writes=[f"G{d}"])
            for c in range(8):
                pb.op("vector", lambda em: em.memset(convr[:, 0:3], 0.0), writes=["CONVR"])
                for tg in range(S // 512):
                    ps = PS[4 + tg % 2]
                    psk = f"PS{4 + tg % 2}"
                    for d in range(8):
                        pb.mm(ps[:, :], pb.sb[f"G{d}"][:, c * 128:(c + 1) * 128],
                              xp[d][:, tg * 512:(tg + 1) * 512],
                              start=(d == 0), stop=(d == 7),
                              reads=[f"G{d}", xpk[d]], writes=[psk])
                    pb.op("vector", lambda em, tg=tg, ps=ps: em.tensor_copy(
                        convr[:, 3 + tg * 512:3 + (tg + 1) * 512], ps[:, :]),
                        reads=[psk], writes=["CONVR"])
                pb.op("vector", lambda em, c=c, coff=coff: em.tensor_scalar_mul(
                    conva[:, 0:S], convr[:, 0:S], cq[:, coff + c * 4:coff + c * 4 + 1]),
                    reads=["CONVR", "CQW"], writes=["CONVA"])
                for k_ in range(1, 4):
                    pb.op("vector", lambda em, c=c, k_=k_, coff=coff: em.scalar_tensor_tensor(
                        conva[:, 0:S], convr[:, k_:k_ + S],
                        cq[:, coff + c * 4 + k_:coff + c * 4 + k_ + 1],
                        conva[:, 0:S], ALU.mult, ALU.add),
                        reads=["CONVR", "CQW", "CONVA"], writes=["CONVA"])
                for tg in range(S // 512):
                    pb.op("scalar", lambda em, tg=tg: em.activation(
                        ev[:, :], conva[:, tg * 512:(tg + 1) * 512], AF.Silu),
                        reads=["CONVA"], writes=["E0"])
                    pb.q_dma(dst[c * 128:(c + 1) * 128,
                                  seq * S + tg * 512:seq * S + (tg + 1) * 512],
                             ev[:, :], reads=["E0"], writes=[f"{dk_}@{seq}"])

        wb = pb.sbuf("WBT", [128, 32], BF16)
        for d in range(8):
            pb.g_dma(wb[:, d * 4:(d + 1) * 4], inp["l1_wbT"][d * 128:(d + 1) * 128, :],
                     reads=["in_wb"], writes=["WBT"])
        bt4 = pb.sbuf("BT4", [128, 8], F32)
        for tb in range(nib):
            ps = PS[0]
            for d in range(8):
                pb.mm(ps[:, 0:4], xp[d][:, tb * 128:(tb + 1) * 128],
                      wb[:, d * 4:(d + 1) * 4], start=(d == 0), stop=(d == 7),
                      reads=[xpk[d], "WBT"], writes=["PS0"])
            pb.op("scalar", lambda em: em.activation(bt4[:, 0:4], PS[0][:, 0:4],
                                                     AF.Exp, scale=-1.0),
                  reads=["PS0"], writes=["BT4"])
            pb.op("vector", lambda em: em.tensor_single_scalar(
                bt4[:, 0:4], bt4[:, 0:4], 1.0, ALU.add),
                reads=["BT4"], writes=["BT4"])
            pb.op("vector", lambda em: em.reciprocal(bt4[:, 0:4], bt4[:, 0:4]),
                  reads=["BT4"], writes=["BT4"])
            pb.q_dma(beta_scr[seq * S + tb * 128:seq * S + (tb + 1) * 128, 0:4],
                     bt4[:, 0:4], reads=["BT4"], writes=["l1_beta"])

        rws = pb.sbuf("RWS", [128, 96], BF16)
        for c in range(8):
            pb.g_dma(rws[:, c * 9:(c + 1) * 9],
                     inp["L1_ffn_router_w"][c * 128:(c + 1) * 128, :],
                     reads=["in_rw"], writes=["RWS"])
        pb.g_dma(rws[0:1, 72:81], inp["l1_rb"][0:1, :], reads=["in_rb"], writes=["RWS"])
        bb9s = pb.sbuf("BB9", [128, 9], F32)
        pb.mm(PS[3][:, 0:9], pb.sb["onescol"][:, :], rws[0:1, 72:81],
              start=True, stop=True, reads=["onescol", "RWS"], writes=["PS3"])
        pb.op("vector", lambda em: em.tensor_copy(bb9s[:, :], PS[3][:, 0:9]),
              reads=["PS3"], writes=["BB9"])
        gt32 = pb.sbuf("GT32", [128, 288], F32)
        pr9 = pb.sbuf("PR9", [128, 16], F32)
        ncol9 = pb.sbuf("DC0", [128, 1], F32)
        scol9 = pb.sbuf("DC1", [128, 1], F32)
        rcol9 = pb.sbuf("DC2", [128, 1], F32)
        logt9 = pb.sbuf("LOGT", [128, 16], F32)
        for tb in range(nib):
            tbg = seq * nib + tb
            ps = PS[2]
            for d in range(8):
                pb.mm(ps[:, 0:9], xp[d][:, tb * 128:(tb + 1) * 128],
                      rws[:, d * 9:(d + 1) * 9], start=(d == 0), stop=(d == 7),
                      reads=[xpk[d], "RWS"], writes=["PS2"])
            pb.op("vector", lambda em, ps=ps: em.tensor_add(pr9[:, 0:9], bb9s[:, :],
                                                            ps[:, 0:9]),
                  reads=["PS2", "BB9"], writes=["PR9"])
            pb.op("vector", lambda em: em.reduce_max(ncol9[:, :], pr9[:, 0:9], AX.X,
                                                     negate=True),
                  reads=["PR9"], writes=["DC0"])
            pb.op("scalar", lambda em: em.activation(pr9[:, 0:9], pr9[:, 0:9], AF.Exp,
                                                     bias=ncol9[:, 0:1],
                                                     accum_out=scol9[:, 0:1]),
                  reads=["PR9", "DC0"], writes=["PR9", "DC1"])
            pb.op("vector", lambda em: em.reciprocal(rcol9[:, :], scol9[:, :]),
                  reads=["DC1"], writes=["DC2"])
            pb.op("vector", lambda em: em.tensor_scalar_mul(pr9[:, 0:9], pr9[:, 0:9],
                                                            rcol9[:, 0:1]),
                  reads=["PR9", "DC2"], writes=["PR9"])
            pb.op("vector", lambda em: em.reduce_max(ncol9[:, :], pr9[:, 0:9], AX.X),
                  reads=["PR9"], writes=["DC0"])
            pb.op("vector", lambda em: em.tensor_scalar(
                logt9[:, 0:9], pr9[:, 0:9], ncol9[:, 0:1], None, ALU.is_equal),
                reads=["PR9", "DC0"], writes=["LOGT"])
            pb.op("vector", lambda em: em.scalar_tensor_tensor(
                logt9[:, 0:9], logt9[:, 0:9], -10.0, pr9[:, 0:9], ALU.mult, ALU.add),
                reads=["LOGT", "PR9"], writes=["LOGT"])
            pb.op("vector", lambda em: em.reduce_max(scol9[:, :], logt9[:, 0:9], AX.X),
                  reads=["LOGT"], writes=["DC1"])
            pb.op("vector", lambda em: em.tensor_scalar(
                logt9[:, 0:9], pr9[:, 0:9], scol9[:, 0:1], None, ALU.is_ge),
                reads=["PR9", "DC1"], writes=["LOGT"])
            pb.op("vector", lambda em, tbg=tbg: em.tensor_mul(
                gt32[:, tbg * 9:tbg * 9 + 9], pr9[:, 0:9], logt9[:, 0:9]),
                reads=["PR9", "LOGT"], writes=["GT32"])
            xt9 = pb.sb["FP0"]
            t0g = tbg * 128
            pb.q_dma(xt9[:, :], pb.dr["xnf_l1"][t0g:t0g + 128, :],
                     reads=[f"xnf_l1@{tbg}"], writes=["FP0"])
            pb.op("vector", lambda em, tbg=tbg: em.tensor_scalar_mul(
                xt9[:, :], xt9[:, :], gt32[:, tbg * 9 + 8:tbg * 9 + 9]),
                reads=["FP0", "GT32"], writes=["FP0"])
            pb.q_dma(acc_scr[t0g:t0g + 128, :], xt9[:, :], reads=["FP0"],
                     writes=[f"g_l1@{tbg}"])

        firc = pb.sbuf("CONVR", [128, 2112], BF16)
        for c in range(8):
            pb.op("vector", lambda em: em.memset(firc[:, 0:62], 0.0), writes=["CONVR"])
            pb.q_dma(firc[:, 62:62 + S],
                     vT[c * 128:(c + 1) * 128, seq * S:(seq + 1) * S],
                     reads=[f"l1_vT@{seq}"], writes=["CONVR"])
            for (pk, ntap, wof, wt, dst, dk_) in ((4, 5, 58, firs, ps_scr, "l1_ps"),
                                                  (62, 63, 0, firl, plg_scr, "l1_pl")):
                pb.op("vector", lambda em, c=c, wt=wt, ntap=ntap, pk=pk: em.tensor_scalar_mul(
                    conva[:, 0:S], firc[:, 62 - (ntap - 1):62 - (ntap - 1) + S],
                    wt[:, c * ntap:c * ntap + 1]),
                    reads=["CONVR", "FIRS", "FIRL"], writes=["CONVA"])
                for k_ in range(1, ntap):
                    pb.op("vector", lambda em, c=c, k_=k_, wt=wt, ntap=ntap: em.scalar_tensor_tensor(
                        conva[:, 0:S], firc[:, 62 - (ntap - 1) + k_:62 - (ntap - 1) + k_ + S],
                        wt[:, c * ntap + k_:c * ntap + k_ + 1],
                        conva[:, 0:S], ALU.mult, ALU.add),
                        reads=["CONVR", "FIRS", "FIRL", "CONVA"], writes=["CONVA"])
                for tg in range(S // 512):
                    pb.op("scalar", lambda em, tg=tg: em.copy(
                        ev[:, :], conva[:, tg * 512:(tg + 1) * 512]),
                        reads=["CONVA"], writes=["E0"])
                    pb.q_dma(dst[c * 128:(c + 1) * 128,
                                  seq * S + tg * 512:seq * S + (tg + 1) * 512],
                             ev[:, :], reads=["E0"], writes=[f"{dk_}@{seq}"])

        qtok = pb.sbuf("QTOK", [128, 256], BF16)
        ktok = pb.sbuf("KTOK", [128, 256], BF16)
        vtok = pb.sbuf("VTOK", [128, 256], BF16)
        kbt = pb.sbuf("KBT", [128, 256], BF16)
        vbt = pb.sbuf("VBT", [128, 256], BF16)
        sqt = pb.sbuf("HF32", [128, 512], F32)[:, 0:256]
        kfm = pb.sbuf("KFM", [128, 256], BF16)
        kbfm = pb.sbuf("KBFM", [128, 256], BF16)
        qfm = pb.sbuf("QFM", [128, 256], BF16)
        wfm = pb.sbuf("WFM", [128, 256], BF16)
        asb = pb.sbuf("ASB", [128, 128], BF16)
        bsb = pb.sbuf("BSB", [128, 128], BF16)
        a2sb = pb.sbuf("A2SB", [128, 128], BF16)
        b2sb = pb.sbuf("B2SB", [128, 128], BF16)
        mtsb = pb.sbuf("MTSB", [128, 128], BF16)
        atsb = pb.sbuf("ATSB", [128, 128], BF16)
        usb = pb.sbuf("USB", [128, 256], BF16)
        wsb = pb.sbuf("WSB", [128, 256], BF16)
        uadj = pb.sbuf("UADJ", [128, 256], BF16)
        otok = pb.sbuf("OTOK", [128, 256], BF16)
        ssb = [pb.sbuf(f"SSB{i}", [128, 256], F32) for i in range(2)]
        sbf = [pb.sbuf(f"SBF{i}", [128, 256], BF16) for i in range(2)]
        c1, c2, c3 = (pb.sbuf(f"DC{i}", [128, 1], F32) for i in range(3))

        for h in range(H):
            hr = slice(h * 256, (h + 1) * 256)
            for i in range(2):
                pb.op("vector", lambda em, i=i: em.memset(ssb[i][:, :], 0.0),
                      writes=[f"SSB{i}"])
                pb.op("vector", lambda em, i=i: em.memset(sbf[i][:, :], 0.0),
                      writes=[f"SBF{i}"])
            for ci in range(nib):
                t0 = seq * S + ci * 128
                csl = slice(t0, t0 + 128)
                pb.q_dma(qtok[:, :], qT[hr, csl], reads=[f"l1_qT@{seq}"],
                         writes=["QTOK"], transpose=True)
                pb.q_dma(ktok[:, :], kT[hr, csl], reads=[f"l1_kT@{seq}"],
                         writes=["KTOK"], transpose=True)
                pb.q_dma(vtok[:, :], vT[hr, csl], reads=[f"l1_vT@{seq}"],
                         writes=["VTOK"], transpose=True)
                pb.q_dma(bt4[:, 0:4], beta_scr[csl, 0:4], reads=["l1_beta"],
                         writes=["BT4"])
                bcol = bt4[:, h:h + 1]
                for (tsrc, tk, dst2, dk2) in ((qtok, "QTOK", qfm, None),
                                              (ktok, "KTOK", None, None)):
                    pb.op("scalar", lambda em, t=tsrc: em.activation(
                        sqt[:, :], t[:, :], AF.Square, accum_out=c1[:, 0:1]),
                        reads=[tk], writes=["HF32", "DC0"])
                    pb.op("vector", lambda em: em.tensor_single_scalar(
                        c1[:, :], c1[:, :], 1e-6, ALU.add), reads=["DC0"], writes=["DC0"])
                    pb.op("vector", lambda em: em.reciprocal(c2[:, :], c1[:, :]),
                          reads=["DC0"], writes=["DC1"])
                    pb.op("scalar", lambda em: em.activation(c2[:, :], c2[:, :], AF.Sqrt),
                          reads=["DC1"], writes=["DC1"])
                    pb.op("vector", lambda em, t=tsrc: em.tensor_scalar_mul(
                        t[:, :], t[:, :], c2[:, 0:1]), reads=[tk, "DC1"], writes=[tk])
                pb.op("vector", lambda em, bcol=bcol: em.tensor_scalar_mul(
                    kbt[:, :], ktok[:, :], bcol), reads=["KTOK", "BT4"], writes=["KBT"])
                pb.op("vector", lambda em, bcol=bcol: em.tensor_scalar_mul(
                    vbt[:, :], vtok[:, :], bcol), reads=["VTOK", "BT4"], writes=["VBT"])
                for (src2, sk2, dfm, dfk) in ((ktok, "KTOK", kfm, "KFM"),
                                              (kbt, "KBT", kbfm, "KBFM"),
                                              (qtok, "QTOK", qfm, "QFM")):
                    for half in range(2):
                        pb.transp(PT[:, 0:128], src2[:, half * 128:(half + 1) * 128],
                                  reads=[sk2], writes=["PTB0"])
                        pb.op("vector", lambda em, dfm=dfm, half=half: em.tensor_copy(
                            dfm[:, half * 128:(half + 1) * 128], PT[:, 0:128]),
                            reads=["PTB0"], writes=[dfk])
                for half in range(2):
                    pb.mm(PS[0][:, 0:128], kbfm[:, half * 128:(half + 1) * 128],
                          kfm[:, half * 128:(half + 1) * 128],
                          start=(half == 0), stop=(half == 1),
                          reads=["KBFM", "KFM"], writes=["PS0"])
                pb.op("vector", lambda em: em.scalar_tensor_tensor(
                    asb[:, :], PS[0][:, 0:128], -1.0, pb.sb["M01SL"][:, :],
                    ALU.mult, ALU.mult), reads=["PS0", "M01SL"], writes=["ASB"])
                for half in range(2):
                    pb.mm(PS[1][:, 0:128], kfm[:, half * 128:(half + 1) * 128],
                          kbfm[:, half * 128:(half + 1) * 128],
                          start=(half == 0), stop=(half == 1),
                          reads=["KFM", "KBFM"], writes=["PS1"])
                pb.op("vector", lambda em: em.scalar_tensor_tensor(
                    bsb[:, :], PS[1][:, 0:128], -1.0, pb.sb["M01SU"][:, :],
                    ALU.mult, ALU.mult), reads=["PS1", "M01SU"], writes=["BSB"])
                pb.op("vector", lambda em: em.tensor_add(mtsb[:, :], bsb[:, :],
                                                         ident[:, :]),
                      reads=["BSB", "ident"], writes=["MTSB"])
                srcA, srcAk, srcB, srcBk = asb, "ASB", bsb, "BSB"
                for lev in range(6):
                    pb.mm(PS[0][:, 0:128], srcB[:, :], srcA[:, :], start=True, stop=True,
                          reads=[srcBk, srcAk], writes=["PS0"])
                    pb.mm(PS[1][:, 0:128], srcA[:, :], srcB[:, :], start=True, stop=True,
                          reads=[srcAk, srcBk], writes=["PS1"])
                    dstA, dstAk = (a2sb, "A2SB") if srcA is asb else (asb, "ASB")
                    dstB, dstBk = (b2sb, "B2SB") if srcB is bsb else (bsb, "BSB")
                    pb.op("vector", lambda em, dstA=dstA: em.tensor_copy(dstA[:, :],
                                                                         PS[0][:, 0:128]),
                          reads=["PS0"], writes=[dstAk])
                    pb.op("vector", lambda em, dstB=dstB: em.tensor_copy(dstB[:, :],
                                                                         PS[1][:, 0:128]),
                          reads=["PS1"], writes=[dstBk])
                    pb.mm(PS[2][:, 0:128], dstA[:, :], mtsb[:, :], start=True, stop=True,
                          reads=[dstAk, "MTSB"], writes=["PS2"])
                    pb.op("vector", lambda em: em.scalar_tensor_tensor(
                        mtsb[:, :], PS[2][:, 0:128], 1.0, mtsb[:, :], ALU.mult, ALU.add),
                        reads=["PS2", "MTSB"], writes=["MTSB"])
                    srcA, srcAk = dstA, dstAk
                    srcB, srcBk = dstB, dstBk
                pb.mm(PS[1][:, 0:256], mtsb[:, :], vbt[:, :], start=True, stop=True,
                      reads=["MTSB", "VBT"], writes=["PS1"])
                pb.op("vector", lambda em: em.tensor_copy(usb[:, :], PS[1][:, 0:256]),
                      reads=["PS1"], writes=["USB"])
                pb.mm(PS[1][:, 0:256], mtsb[:, :], kbt[:, :], start=True, stop=True,
                      reads=["MTSB", "KBT"], writes=["PS1"])
                pb.op("vector", lambda em: em.tensor_copy(wsb[:, :], PS[1][:, 0:256]),
                      reads=["PS1"], writes=["WSB"])
                for half in range(2):
                    pb.transp(PT[:, 0:128], wsb[:, half * 128:(half + 1) * 128],
                              reads=["WSB"], writes=["PTB0"])
                    pb.op("vector", lambda em, half=half: em.tensor_copy(
                        wfm[:, half * 128:(half + 1) * 128], PT[:, 0:128]),
                        reads=["PTB0"], writes=["WFM"])
                for half in range(2):
                    pb.mm(PS[1][:, 0:256], wfm[:, half * 128:(half + 1) * 128],
                          sbf[half][:, :], start=(half == 0), stop=(half == 1),
                          reads=["WFM", f"SBF{half}"], writes=["PS1"])
                pb.op("vector", lambda em: em.tensor_sub(uadj[:, :], usb[:, :],
                                                         PS[1][:, 0:256]),
                      reads=["USB", "PS1"], writes=["UADJ"])
                for half in range(2):
                    pb.mm(PS[0][:, 0:128],
                          kfm[:, half * 128:(half + 1) * 128],
                          qfm[:, half * 128:(half + 1) * 128],
                          start=(half == 0), stop=(half == 1),
                          reads=["KFM", "QFM"], writes=["PS0"])
                pb.op("vector", lambda em: em.tensor_mul(
                    atsb[:, :], PS[0][:, 0:128], pb.sb["M01IU"][:, :]),
                    reads=["PS0", "M01IU"], writes=["ATSB"])
                for half in range(2):
                    pb.mm(PS[3][:, 0:256], qfm[:, half * 128:(half + 1) * 128],
                          sbf[half][:, :], start=(half == 0), stop=False,
                          reads=["QFM", f"SBF{half}"], writes=["PS3"])
                pb.mm(PS[3][:, 0:256], atsb[:, :], uadj[:, :], start=False, stop=True,
                      reads=["ATSB", "UADJ"], writes=["PS3"])
                pb.op("vector", lambda em: em.tensor_copy(otok[:, :], PS[3][:, 0:256]),
                      reads=["PS3"], writes=["OTOK"])
                pb.q_dma(od_scr[csl, hr], otok[:, :], reads=["OTOK"],
                         writes=[f"l1_od@{seq}"])
                for half in range(2):
                    pb.transp(PT[:, 0:128], otok[:, half * 128:(half + 1) * 128],
                              reads=["OTOK"], writes=["PTB0"])
                    pb.op("vector", lambda em, half=half: em.tensor_copy(
                        kfm[:, half * 128:(half + 1) * 128], PT[:, 0:128]),
                        reads=["PTB0"], writes=["KFM"])
                for half in range(2):
                    pb.q_dma(odf_scr[h * 256 + half * 128:h * 256 + (half + 1) * 128, csl],
                             kfm[:, half * 128:(half + 1) * 128],
                             reads=["KFM"], writes=[f"l1_odf@{seq}"])
                for half in range(2):
                    pb.mm(PS[4][:, 0:256], ktok[:, half * 128:(half + 1) * 128],
                          uadj[:, :], start=True, stop=True,
                          reads=["KTOK", "UADJ"], writes=["PS4"])
                    pb.op("vector", lambda em, half=half: em.tensor_add(
                        ssb[half][:, :], ssb[half][:, :], PS[4][:, 0:256]),
                        reads=[f"SSB{half}", "PS4"], writes=[f"SSB{half}"])
                    pb.op("vector", lambda em, half=half: em.tensor_copy(
                        sbf[half][:, :], ssb[half][:, :]),
                        reads=[f"SSB{half}"], writes=[f"SBF{half}"])

        ones128 = pb.sbuf("ONES128", [128, 1], BF16)
        pb.op("vector", lambda em: em.memset(ones128[:, :], 1.0), writes=["ONES128"])
        pfm = pb.sbuf("PBUF", [128, 2048], BF16)
        pfm2 = pb.sbuf("PBUF2", [128, 2048], BF16)
        nts = S // 512
        for pi, (scr, rkp) in enumerate(((ps_scr, f"l1_ps@{seq}"),
                                         (plg_scr, f"l1_pl@{seq}"),
                                         (odf_scr, f"l1_odf@{seq}"),
                                         (vT, f"l1_vT@{seq}"))):
            for h in range(H):
                sc_ = pb.sbuf("SCORES", [128, 2048], F32)
                for kind in range(2):
                    for ch in range(2):
                        row0 = h * 256 + ch * 128
                        pb.q_dma(pfm[:, 0:S], scr[row0:row0 + 128, seq * S:(seq + 1) * S],
                                 reads=[rkp], writes=["PBUF"])
                        if kind == 0:
                            til, tkk = pfm, "PBUF"
                        else:
                            fn = AF.Square if kind == 1 else AF.Abs
                            pb.op("scalar", lambda em, fn=fn: em.activation(
                                pfm2[:, 0:S], pfm[:, 0:S], fn),
                                reads=["PBUF"], writes=["PBUF2"])
                            til, tkk = pfm2, "PBUF2"
                        for ts_ in range(nts):
                            pb.mm(PS[ts_][0:1, :], ones128[:, :],
                                  til[:, ts_ * 512:(ts_ + 1) * 512],
                                  start=(ch == 0), stop=(ch == 1),
                                  reads=["ONES128", tkk], writes=[f"PS{ts_}"])
                    if kind == 0:
                        for ts_ in range(nts):
                            pb.op("vector", lambda em, ts_=ts_: em.tensor_scalar_mul(
                                conva[0:1, ts_ * 512:(ts_ + 1) * 512],
                                PS[ts_][0:1, :], 1.0 / 256.0),
                                reads=[f"PS{ts_}"], writes=["CONVA"])
                    else:
                        for ts_ in range(nts):
                            pb.op("vector", lambda em, ts_=ts_: em.tensor_copy(
                                sc_[0:1, ts_ * 512:(ts_ + 1) * 512], PS[ts_][0:1, :]),
                                reads=[f"PS{ts_}"], writes=["SCORES"])
                r0 = pi * 16 + h * 4
                pb.q_dma(gin_scr[r0:r0 + 1, seq * S:(seq + 1) * S], conva[0:1, 0:S],
                         reads=["CONVA"], writes=["l1_gin"])
                pb.op("scalar", lambda em: em.activation(pfm[0:1, 0:S], sc_[0:1, 0:S],
                                                         AF.Sqrt),
                      reads=["SCORES"], writes=["PBUF"])
                pb.g_dma(gin_scr[r0 + 3:r0 + 4, seq * S:(seq + 1) * S], pfm[0:1, 0:S],
                         reads=["PBUF"], writes=["l1_gin"])
                pb.op("vector", lambda em: em.tensor_mul(conva[0:1, 0:S],
                                                         conva[0:1, 0:S],
                                                         conva[0:1, 0:S]),
                      reads=["CONVA"], writes=["CONVA"])
                pb.op("vector", lambda em: em.scalar_tensor_tensor(
                    sc_[0:1, 0:S], sc_[0:1, 0:S], 1.0 / 256.0, conva[0:1, 0:S],
                    ALU.mult, ALU.subtract), reads=["SCORES", "CONVA"], writes=["SCORES"])
                pb.q_dma(gin_scr[r0 + 1:r0 + 2, seq * S:(seq + 1) * S], sc_[0:1, 0:S],
                         reads=["SCORES"], writes=["l1_gin"])
                for ch in range(2):
                    row0_ = h * 256 + ch * 128
                    pb.q_dma(pfm[:, 0:S], scr[row0_:row0_ + 128, seq * S:(seq + 1) * S],
                             reads=[rkp], writes=["PBUF"])
                    pb.op("scalar", lambda em: em.activation(pfm2[:, 0:S], pfm[:, 0:S],
                                                             AF.Abs),
                          reads=["PBUF"], writes=["PBUF2"])
                    for ts_ in range(nts):
                        pb.mm(PS[ts_][0:1, :], ones128[:, :],
                              pfm2[:, ts_ * 512:(ts_ + 1) * 512],
                              start=(ch == 0), stop=(ch == 1),
                              reads=["ONES128", "PBUF2"], writes=[f"PS{ts_}"])
                for ts_ in range(nts):
                    pb.op("vector", lambda em, ts_=ts_: em.tensor_scalar_mul(
                        conva[0:1, ts_ * 512:(ts_ + 1) * 512], PS[ts_][0:1, :],
                        1.0 / 256.0), reads=[f"PS{ts_}"], writes=["CONVA"])
                pb.q_dma(gin_scr[r0 + 2:r0 + 3, seq * S:(seq + 1) * S], conva[0:1, 0:S],
                         reads=["CONVA"], writes=["l1_gin"])

        tcon = pb.sbuf("TCON", [1, 64], F32)
        tinvb = pb.sbuf("TINVB", [128, 16], F32)
        nfeb = pb.sbuf("NFEB", [128, 16], F32)
        ltr = tcon[0:1, 0:16]
        fpr = tcon[0:1, 16:32]
        t1r = tcon[0:1, 32:48]
        t2r = tcon[0:1, 48:64]
        pb.q_dma(ltr, inp["l1_ltemp"][:, :], reads=["in_ltemp"], writes=["TCON"])
        pb.q_dma(fpr, inp["l1_fpar"][:, :], reads=["in_fpar"], writes=["TCON"])
        pb.op("scalar", lambda em: em.activation(t1r, ltr, AF.Abs),
              reads=["TCON"], writes=["TCON"])
        pb.op("scalar", lambda em: em.activation(t1r, t1r, AF.Exp, scale=-1.0),
              reads=["TCON"], writes=["TCON"])
        pb.op("scalar", lambda em: em.activation(t1r, t1r, AF.Ln, bias=1.0),
              reads=["TCON"], writes=["TCON"])
        pb.op("vector", lambda em: em.tensor_single_scalar(t2r, ltr, 0.0, ALU.max),
              reads=["TCON"], writes=["TCON"])
        pb.op("vector", lambda em: em.tensor_add(t1r, t1r, t2r),
              reads=["TCON"], writes=["TCON"])
        pb.op("vector", lambda em: em.tensor_single_scalar(t1r, t1r, 0.25, ALU.add),
              reads=["TCON"], writes=["TCON"])
        pb.op("vector", lambda em: em.reciprocal(t1r, t1r), reads=["TCON"], writes=["TCON"])
        pb.op("scalar", lambda em: em.activation(t2r, fpr, AF.Exp, scale=-1.0),
              reads=["TCON"], writes=["TCON"])
        pb.op("vector", lambda em: em.tensor_single_scalar(t2r, t2r, 1.0, ALU.add),
              reads=["TCON"], writes=["TCON"])
        pb.op("vector", lambda em: em.reciprocal(t2r, t2r),
              reads=["TCON"], writes=["TCON"])
        pb.op("vector", lambda em: em.tensor_scalar(t2r, t2r, 0.07, 0.005,
                                                    ALU.mult, ALU.add),
              reads=["TCON"], writes=["TCON"])
        pb.op("vector", lambda em: em.tensor_single_scalar(t2r, t2r, 0.25, ALU.min),
              reads=["TCON"], writes=["TCON"])
        for h in range(4):
            pb.op("vector", lambda em, h=h: em.reduce_sum(
                fpr[0:1, h:h + 1], tcon[0:1, 48 + h * 4:48 + (h + 1) * 4], AX.X),
                reads=["TCON"], writes=["TCON"])
        pb.op("vector", lambda em: em.tensor_single_scalar(
            fpr[0:1, 0:4], fpr[0:1, 0:4], 0.99, ALU.min), reads=["TCON"], writes=["TCON"])
        pb.op("vector", lambda em: em.reciprocal(fpr[0:1, 0:4], fpr[0:1, 0:4]),
              reads=["TCON"], writes=["TCON"])
        for h in range(4):
            pb.op("vector", lambda em, h=h: em.tensor_scalar_mul(
                tcon[0:1, 48 + h * 4:48 + (h + 1) * 4],
                tcon[0:1, 48 + h * 4:48 + (h + 1) * 4],
                fpr[0:1, h:h + 1]), reads=["TCON"], writes=["TCON"])
        pb.op("vector", lambda em: em.tensor_scalar(
            t2r, t2r, 0.99, 1e-9, ALU.mult, ALU.add), reads=["TCON"], writes=["TCON"])
        evb = pb.sb["E0"]
        for (rsrc, dst2, dk2) in ((t1r, tinvb, "TINVB"), (t2r, nfeb, "NFEB")):
            pb.op("vector", lambda em, rsrc=rsrc: em.tensor_copy(evb[0:1, 0:16], rsrc),
                  reads=["TCON"], writes=["E0"])
            pb.mm(PS[0][:, 0:16], pb.sb["onescol"][:, :], evb[0:1, 0:16],
                  start=True, stop=True, reads=["onescol", "E0"], writes=["PS0"])
            pb.op("vector", lambda em, dst2=dst2: em.tensor_copy(dst2[:, :], PS[0][:, 0:16]),
                  reads=["PS0"], writes=[dk2])

        for d in range(8):
            g = pb.sbuf(f"G{d}", [128, D], BF16)
            pb.g_dma(g[:, :], inp["l1_w1T"][d * 128:(d + 1) * 128, :],
                     reads=["in_w1T"], writes=[f"G{d}"])
        g8t = pb.sbuf("G8", [128, D], BF16)
        pb.g_dma(g8t[0:64, :], inp["l1_w1T"][1024:1088, :], reads=["in_w1T"],
                 writes=["G8"])
        gins = pb.sbuf("LT", [64, 2048], BF16)
        pb.g_dma(gins[:, 0:S], gin_scr[:, seq * S:(seq + 1) * S], reads=["l1_gin"],
                 writes=["LT"])
        gb1 = pb.sbuf("GB1", [128, 8], F32)
        pb.q_dma(gb1[:, :], inp["l1_gb1T"][:, :], reads=["in_gb1"], writes=["GB1"])
        w2ts = pb.sbuf("W2TS", [128, 128], BF16)
        for mc in range(8):
            pb.g_dma(w2ts[:, mc * 16:(mc + 1) * 16],
                     inp["l1_w2T"][mc * 128:(mc + 1) * 128, :],
                     reads=["in_w2T"], writes=["W2TS"])
        gb2 = pb.sbuf("GB2", [16, 1], F32)
        pb.q_dma(gb2[:, :], inp["l1_gb2c"][:, :], reads=["in_gb2"], writes=["GB2"])
        wgts = pb.sbuf("EG", [128, 512], F32)
        logt = pb.sbuf("LOGT", [128, 16], F32)
        ncol = pb.sbuf("DC0", [128, 1], F32)
        scol = pb.sbuf("DC1", [128, 1], F32)
        rcol = pb.sbuf("DC2", [128, 1], F32)

        for ts_ in range(nts):
            tsl = slice(ts_ * 512, (ts_ + 1) * 512)
            for mc in range(8):
                ps = PS[4 + mc % 2]
                psk = f"PS{4 + mc % 2}"
                for d in range(8):
                    pb.mm(ps[:, :], pb.sb[f"G{d}"][:, mc * 128:(mc + 1) * 128],
                          xp[d][:, tsl], start=(d == 0), stop=False,
                          reads=[f"G{d}", xpk[d]], writes=[psk])
                pb.mm(ps[:, :], g8t[0:64, mc * 128:(mc + 1) * 128], gins[:, tsl],
                      start=False, stop=True, reads=["G8", "LT"], writes=[psk])
                tgt = pfm if mc < 4 else pfm2
                tgtk = "PBUF" if mc < 4 else "PBUF2"
                pb.op("scalar", lambda em, ps=ps, mc=mc, tgt=tgt: em.activation(
                    tgt[:, (mc % 4) * 512:(mc % 4 + 1) * 512], ps[:, :], AF.Gelu,
                    bias=gb1[:, mc:mc + 1]),
                    reads=[psk, "GB1"], writes=[tgtk])
            for mc in range(8):
                tgt = pfm if mc < 4 else pfm2
                tgtk = "PBUF" if mc < 4 else "PBUF2"
                pb.mm(PS[0][0:16, :], w2ts[:, mc * 16:(mc + 1) * 16],
                      tgt[:, (mc % 4) * 512:(mc % 4 + 1) * 512],
                      start=(mc == 0), stop=(mc == 7),
                      reads=["W2TS", tgtk], writes=["PS0"])
            pb.op("scalar", lambda em: em.activation(
                pb.sb["STILE"][0:16, 0:512], PS[0][0:16, :], AF.Identity,
                bias=gb2[:, 0:1]), reads=["PS0", "GB2"], writes=["STILE"])
            for sub in range(4):
                tb = ts_ * 4 + sub
                pb.transp(PS[1][:, 0:16],
                          pb.sb["STILE"][0:16, sub * 128:(sub + 1) * 128],
                          reads=["STILE"], writes=["PS1"], np_=16, f32=True)
                pb.op("vector", lambda em: em.tensor_copy(logt[:, :], PS[1][:, 0:16]),
                      reads=["PS1"], writes=["LOGT"])
                pb.op("vector", lambda em: em.tensor_mul(logt[:, :], logt[:, :],
                                                         tinvb[:, :]),
                      reads=["LOGT", "TINVB"], writes=["LOGT"])
                for h in range(4):
                    hs = slice(h * 4, (h + 1) * 4)
                    pb.op("vector", lambda em, hs=hs: em.reduce_max(
                        ncol[:, :], logt[:, hs], AX.X, negate=True),
                        reads=["LOGT"], writes=["DC0"])
                    pb.op("scalar", lambda em, hs=hs: em.activation(
                        logt[:, hs], logt[:, hs], AF.Exp, bias=ncol[:, 0:1],
                        accum_out=scol[:, 0:1]),
                        reads=["LOGT", "DC0"], writes=["LOGT", "DC1"])
                    pb.op("vector", lambda em: em.reciprocal(rcol[:, :], scol[:, :]),
                          reads=["DC1"], writes=["DC2"])
                    pb.op("vector", lambda em, hs=hs: em.tensor_scalar_mul(
                        logt[:, hs], logt[:, hs], rcol[:, 0:1]),
                        reads=["LOGT", "DC2"], writes=["LOGT"])
                pb.op("vector", lambda em: em.tensor_max(logt[:, :], logt[:, :],
                                                         nfeb[:, :]),
                      reads=["LOGT", "NFEB"], writes=["LOGT"])
                for h in range(4):
                    hs = slice(h * 4, (h + 1) * 4)
                    pb.op("vector", lambda em, hs=hs: em.reduce_sum(
                        scol[:, :], logt[:, hs], AX.X), reads=["LOGT"], writes=["DC1"])
                    pb.op("vector", lambda em: em.tensor_single_scalar(
                        scol[:, :], scol[:, :], 1e-8, ALU.add),
                        reads=["DC1"], writes=["DC1"])
                    pb.op("vector", lambda em: em.reciprocal(rcol[:, :], scol[:, :]),
                          reads=["DC1"], writes=["DC2"])
                    pb.op("vector", lambda em, hs=hs, tb=tb, h=h: em.tensor_single_scalar(
                        wgts[:, tb * 16 + h * 4:tb * 16 + (h + 1) * 4],
                        logt[:, hs], rcol[:, 0:1], ALU.mult),
                        reads=["LOGT", "DC2"], writes=["EG"])

        gamb = pb.sbuf("CONVR", [128, 2112], BF16)[:, 0:1024]
        cp = [pb.sb["PBUF"][:, 0:1024], pb.sb["PBUF"][:, 1024:2048],
              pb.sb["PBUF2"][:, 0:1024], pb.sb["PBUF2"][:, 1024:2048]]
        cpk = ["PBUF", "PBUF", "PBUF2", "PBUF2"]
        pb.g_dma(cp[0][0:1, :], inp["l1_onw"][0:1, :], reads=["in_onw"], writes=["PBUF"])
        pb.mm(PS[0][:, 0:512], pb.sb["onescol"][:, :], cp[0][0:1, 0:512],
              start=True, stop=True, reads=["onescol", "PBUF"], writes=["PS0"])
        pb.mm(PS[1][:, 0:512], pb.sb["onescol"][:, :], cp[0][0:1, 512:1024],
              start=True, stop=True, reads=["onescol", "PBUF"], writes=["PS1"])
        pb.op("vector", lambda em: em.tensor_copy(gamb[:, 0:512], PS[0][:, 0:512]),
              reads=["PS0"], writes=["CONVR"])
        pb.op("vector", lambda em: em.tensor_copy(gamb[:, 512:1024], PS[1][:, 0:512]),
              reads=["PS1"], writes=["CONVR"])
        accm = pb.sb["STILE"]
        ob16 = pb.sb["CONVR"][:, 1024:2048]
        for tb in range(nib):
            t0 = seq * S + tb * 128
            for pi2, (scr, rkp) in enumerate(((ps_scr, f"l1_ps@{seq}"),
                                              (plg_scr, f"l1_pl@{seq}"),
                                              (od_scr, f"l1_od@{seq}"),
                                              (vT, f"l1_vT@{seq}"))):
                if pi2 == 2:
                    pb.q_dma(cp[pi2][:, :], scr[t0:t0 + 128, :], reads=[rkp],
                             writes=[cpk[pi2]])
                else:
                    for c in range(8):
                        pb.q_dma(cp[pi2][:, c * 128:(c + 1) * 128],
                                 scr[c * 128:(c + 1) * 128, t0:t0 + 128],
                                 reads=[rkp], writes=[cpk[pi2]], transpose=True)
            for h in range(4):
                hs = slice(h * 256, (h + 1) * 256)
                for pi2 in range(4):
                    wcol = wgts[:, (tb + 0) * 16 + h * 4 + pi2:
                                (tb + 0) * 16 + h * 4 + pi2 + 1]
                    if pi2 == 0:
                        pb.op("vector", lambda em, hs=hs, wcol=wcol: em.tensor_scalar_mul(
                            accm[:, hs], cp[0][:, hs], wcol),
                            reads=[cpk[0], "EG"], writes=["STILE"])
                    else:
                        pb.op("vector", lambda em, hs=hs, wcol=wcol, pi2=pi2:
                              em.scalar_tensor_tensor(
                                  accm[:, hs], cp[pi2][:, hs], wcol, accm[:, hs],
                                  ALU.mult, ALU.add),
                              reads=[cpk[pi2], "EG", "STILE"], writes=["STILE"])
                pb.op("scalar", lambda em, hs=hs: em.activation(
                    pb.sb["HF32"][:, 0:256], accm[:, hs], AF.Square,
                    accum_out=ncol[:, 0:1]),
                    reads=["STILE"], writes=["HF32", "DC0"])
                pb.op("vector", lambda em: em.tensor_scalar(
                    ncol[:, :], ncol[:, :], 1.0 / 256.0, 1e-5, ALU.mult, ALU.add),
                    reads=["DC0"], writes=["DC0"])
                pb.op("vector", lambda em: em.reciprocal(scol[:, :], ncol[:, :]),
                      reads=["DC0"], writes=["DC1"])
                pb.op("scalar", lambda em: em.activation(scol[:, :], scol[:, :], AF.Sqrt),
                      reads=["DC1"], writes=["DC1"])
                pb.op("vector", lambda em, hs=hs: em.tensor_scalar_mul(
                    accm[:, hs], accm[:, hs], scol[:, 0:1]),
                    reads=["STILE", "DC1"], writes=["STILE"])
            pb.op("vector", lambda em: em.tensor_mul(ob16[:, :], accm[:, 0:1024],
                                                     gamb[:, :]),
                  reads=["STILE", "CONVR"], writes=["CONVR"])
            pb.q_dma(o_scr[t0:t0 + 128, :], ob16[:, :], reads=["CONVR"],
                     writes=[f"l1_o@{seq}"])

        for d in range(8):
            g = pb.sbuf(f"G{16+d}", [128, D], BF16)
            pb.g_dma(g[:, :], inp["l1_woT"][d * 128:(d + 1) * 128, :],
                     reads=["in_wo1"], writes=[f"G{16+d}"])
        qi = [pb.sbuf(f"QI{d}", [128, 128], BF16) for d in range(8)]
        for tb in range(nib):
            t0 = seq * S + tb * 128
            for c in range(8):
                pb.q_dma(qi[c][:, :], o_scr[t0:t0 + 128, c * 128:(c + 1) * 128],
                         reads=[f"l1_o@{seq}"], writes=[f"QI{c}"], transpose=True)
            for oh in range(2):
                ps = PS[oh]
                psk = f"PS{oh}"
                for c in range(8):
                    pb.mm(ps[:, :], qi[c][:, :],
                          pb.sb[f"G{16+c}"][:, oh * 512:(oh + 1) * 512],
                          start=(c == 0), stop=(c == 7),
                          reads=[f"QI{c}", f"G{16+c}"], writes=[psk])
                eng = "vector" if oh == 0 else "scalar"
                if eng == "vector":
                    pb.op(eng, lambda em, ps=ps, oh=oh: em.tensor_copy(
                        pb.sb["HF32"][:, 0:512], ps[:, :]), reads=[psk], writes=["HF32"])
                else:
                    pb.op(eng, lambda em, ps=ps, oh=oh: em.copy(
                        pb.sb["HF32"][:, 0:512], ps[:, :]), reads=[psk], writes=["HF32"])
                pb.q_dma(s_scr[t0:t0 + 128, oh * 512:(oh + 1) * 512],
                         pb.sb["HF32"][:, 0:512], reads=["HF32"],
                         writes=[f"s_l1@{t0 // 128}"])

    gt32 = pb.sbuf("GT32", [128, 288], F32)
    xp = [pb.sb[f"XP{d}"] for d in range(8)]
    xpk = [f"XP{d}" for d in range(8)]
    ehalf = pb.sb["EG"] if "EG" in pb.sb else pb.sbuf("EG", [128, 512], F32)
    for e in range(8):
        for fs in range(2):
            for c in range(8):
                for j in range(2):
                    g = pb.sbuf(f"G{2*c+j}", [128, D], BF16)
                    pb.g_dma(g[:, :],
                             inp["L1_ffn_w1"][e, c * 128:(c + 1) * 128,
                                              fs * 2048 + j * 1024:fs * 2048 + (j + 1) * 1024],
                             reads=["in_moe1"], writes=[f"G{2*c+j}"])
            for fc in range(16):
                g = pb.sbuf(f"G{16+fc}", [128, D], BF16)
                pb.g_dma(g[:, :],
                         inp["L1_ffn_w2"][e, fs * 2048 + fc * 128:fs * 2048 + (fc + 1) * 128, :],
                         reads=["in_moe2"], writes=[f"G{16+fc}"])
            for tg in range(T // 512):
                for d in range(8):
                    xsl = xp[4 + d // 4][:, (d % 4) * 512:(d % 4 + 1) * 512]
                    pb.q_dma(xsl, xnT_scr[d * 128:(d + 1) * 128,
                                          tg * 512:(tg + 1) * 512],
                             reads=["l1_xnT"], writes=[xpk[4 + d // 4]])
                for fc in range(16):
                    ps = PS[4 + fc % 2]
                    psk = f"PS{4 + fc % 2}"
                    j = fc // 8
                    off = (fc % 8) * 128
                    for d in range(8):
                        xsl = xp[4 + d // 4][:, (d % 4) * 512:(d % 4 + 1) * 512]
                        pb.mm(ps[:, :], pb.sb[f"G{2*d+j}"][:, off:off + 128],
                              xsl, start=(d == 0), stop=(d == 7),
                              reads=[f"G{2*d+j}", xpk[4 + d // 4]], writes=[psk])
                    hsl = xp[fc // 4][:, (fc % 4) * 512:(fc % 4 + 1) * 512]
                    pb.op("scalar", lambda em, ps=ps, hsl=hsl: em.activation(
                        hsl, ps[:, :], AF.Gelu), reads=[psk],
                        writes=[xpk[fc // 4]])
                for tsub in range(4):
                    tbg = tg * 4 + tsub
                    for oh in range(2):
                        ps = PS[oh]
                        psk = f"PS{oh}"
                        for fc in range(16):
                            hsl = xp[fc // 4][:, (fc % 4) * 512 + tsub * 128:
                                              (fc % 4) * 512 + (tsub + 1) * 128]
                            pb.mm(ps[:, :], hsl,
                                  pb.sb[f"G{16+fc}"][:, oh * 512:(oh + 1) * 512],
                                  start=(fc == 0), stop=(fc == 15),
                                  reads=[xpk[fc // 4], f"G{16+fc}"], writes=[psk])
                        gcol = gt32[:, tbg * 9 + e:tbg * 9 + e + 1]
                        pb.op("vector", lambda em, ps=ps, gcol=gcol: em.tensor_scalar_mul(
                            ehalf[:, :], ps[:, :], gcol),
                            reads=[psk, "GT32"], writes=["EG"])
                        pb.g_dma(acc_scr[tbg * 128:(tbg + 1) * 128,
                                         oh * 512:(oh + 1) * 512],
                                 ehalf[:, :], reads=["EG"], writes=[f"g_l1@{tbg}"],
                                 accum=True)

    emit_combine(pb, "l1", xres, s_scr, pb.dr["l1_acc"], T)



def prep_weights_L1(w):
    out = {}
    out["l1_wqT"] = np.ascontiguousarray(np.asarray(w["L1_attn_Wq"]).T)
    out["l1_wkT"] = np.ascontiguousarray(np.asarray(w["L1_attn_Wk"]).T)
    out["l1_wvT"] = np.ascontiguousarray(np.asarray(w["L1_attn_Wv"]).T)
    out["l1_wbT"] = np.ascontiguousarray(np.asarray(w["L1_attn_Wb"]).T)
    for nm, key in (("l1_cqT", "L1_attn_conv_q_w"), ("l1_ckT", "L1_attn_conv_k_w"),
                    ("l1_cvT", "L1_attn_conv_v_w")):
        cw = np.asarray(w[key])
        out[nm] = np.ascontiguousarray(cw.reshape(8, 128, 4).transpose(1, 0, 2).reshape(128, 32))
    fs = np.asarray(w["L1_attn_fir_short_w"]).reshape(1024, 5)
    fl = np.asarray(w["L1_attn_fir_long_w"]).reshape(1024, 63)
    out["l1_firS"] = np.ascontiguousarray(fs.reshape(8, 128, 5).transpose(1, 0, 2).reshape(128, 40))
    out["l1_firL"] = np.ascontiguousarray(fl.reshape(8, 128, 63).transpose(1, 0, 2).reshape(128, 504))
    out["l1_ltemp"] = np.ascontiguousarray(np.asarray(w["L1_attn_log_temp"]).reshape(1, 16))
    out["l1_fpar"] = np.ascontiguousarray(np.asarray(w["L1_attn_floor_param"]).reshape(1, 16))
    out["l1_w1T"] = np.ascontiguousarray(np.asarray(w["L1_attn_gate_W1"]).T)
    out["l1_gb1T"] = np.ascontiguousarray(np.asarray(w["L1_attn_gate_b1"]).reshape(8, 128).T)
    out["l1_w2T"] = np.ascontiguousarray(np.asarray(w["L1_attn_gate_W2"]).T)
    out["l1_gb2c"] = np.ascontiguousarray(np.asarray(w["L1_attn_gate_b2"]).reshape(16, 1))
    out["l1_onw"] = np.ascontiguousarray(np.tile(np.asarray(w["L1_attn_o_norm_w"]), 4).reshape(1, 1024))
    out["l1_woT"] = np.ascontiguousarray(np.asarray(w["L1_attn_Wo"]).T)
    out["L1_ffn_router_w"] = np.ascontiguousarray(np.asarray(w["L1_ffn_router_w"]))
    out["l1_rb"] = np.ascontiguousarray(np.asarray(w["L1_ffn_router_b"]).reshape(1, 9))
    out["L1_ffn_w1"] = np.ascontiguousarray(np.asarray(w["L1_ffn_w1"]))
    out["L1_ffn_w2"] = np.ascontiguousarray(np.asarray(w["L1_ffn_w2"]))
    return out


def declare_inputs_L1(pb):
    inp = {}
    inp["l1_wqT"] = pb.dram("l1_wqT", [D, D], F32, kind="ExternalInput")
    inp["l1_wkT"] = pb.dram("l1_wkT", [D, D], F32, kind="ExternalInput")
    inp["l1_wvT"] = pb.dram("l1_wvT", [D, D], F32, kind="ExternalInput")
    inp["l1_wbT"] = pb.dram("l1_wbT", [D, 4], F32, kind="ExternalInput")
    inp["l1_cqT"] = pb.dram("l1_cqT", [128, 32], F32, kind="ExternalInput")
    inp["l1_ckT"] = pb.dram("l1_ckT", [128, 32], F32, kind="ExternalInput")
    inp["l1_cvT"] = pb.dram("l1_cvT", [128, 32], F32, kind="ExternalInput")
    inp["l1_firS"] = pb.dram("l1_firS", [128, 40], F32, kind="ExternalInput")
    inp["l1_firL"] = pb.dram("l1_firL", [128, 504], F32, kind="ExternalInput")
    inp["l1_ltemp"] = pb.dram("l1_ltemp", [1, 16], F32, kind="ExternalInput")
    inp["l1_fpar"] = pb.dram("l1_fpar", [1, 16], F32, kind="ExternalInput")
    inp["l1_w1T"] = pb.dram("l1_w1T", [1088, D], F32, kind="ExternalInput")
    inp["l1_gb1T"] = pb.dram("l1_gb1T", [128, 8], F32, kind="ExternalInput")
    inp["l1_w2T"] = pb.dram("l1_w2T", [D, 16], F32, kind="ExternalInput")
    inp["l1_gb2c"] = pb.dram("l1_gb2c", [16, 1], F32, kind="ExternalInput")
    inp["l1_onw"] = pb.dram("l1_onw", [1, D], F32, kind="ExternalInput")
    inp["l1_woT"] = pb.dram("l1_woT", [D, D], F32, kind="ExternalInput")
    inp["L1_ffn_router_w"] = pb.dram("L1_ffn_router_w", [D, 9], F32, kind="ExternalInput")
    inp["l1_rb"] = pb.dram("l1_rb", [1, 9], F32, kind="ExternalInput")
    inp["L1_ffn_w1"] = pb.dram("L1_ffn_w1", [8, D, 4 * D], F32, kind="ExternalInput")
    inp["L1_ffn_w2"] = pb.dram("L1_ffn_w2", [8, 4 * D, D], F32, kind="ExternalInput")
    return inp


def prep_weights_L0(w):
    out = {}
    out["L0_attn_w_dec_attn"] = np.ascontiguousarray(w["L0_attn_w_dec_attn"])
    wo = np.asarray(w["L0_attn_w_out"])
    out["l0_w1t"] = np.ascontiguousarray(wo[:, :D].T)
    out["l0_w2t"] = np.ascontiguousarray(wo[:, D:].T)
    out["L0_attn_b_out"] = np.ascontiguousarray(np.asarray(w["L0_attn_b_out"]).reshape(1, D))
    out["l0_wfct"] = np.ascontiguousarray(np.asarray(w["L0_ffn_W_fc"]).T)
    out["l0_wprojt"] = np.ascontiguousarray(np.asarray(w["L0_ffn_W_proj"]).T)
    return out


def declare_inputs_L0(pb):
    inp = {}
    inp["L0_attn_w_dec_attn"] = pb.dram("L0_attn_w_dec_attn", [D, D], F32, kind="ExternalInput")
    inp["l0_w1t"] = pb.dram("l0_w1t", [D, D], F32, kind="ExternalInput")
    inp["l0_w2t"] = pb.dram("l0_w2t", [D, D], F32, kind="ExternalInput")
    inp["L0_attn_b_out"] = pb.dram("L0_attn_b_out", [1, D], F32, kind="ExternalInput")
    inp["l0_wfct"] = pb.dram("l0_wfct", [D, 4 * D], F32, kind="ExternalInput")
    inp["l0_wprojt"] = pb.dram("l0_wprojt", [4 * D, D], F32, kind="ExternalInput")
    return inp


def build_program(n_seq=2, S=2048, layers=(0, 1, 2, 3), taps=()):
    nc = bass.Bass("TRN2", target_bir_lowering=False)
    es = contextlib.ExitStack()
    pb = PB(nc, es)
    T = n_seq * S

    x0 = pb.dram("x0", [T, D], F32, kind="ExternalInput")
    y = pb.dram("y", [T, D], F32, kind="ExternalOutput")

    emit_prelude(pb)
    emit_causal_masks(pb)
    emit_incl_masks(pb)
    xres = pb.dram("xres", [T, D], F32)
    pb.q_dma(xres[:, :], x0[:, :], reads=["in_x0"], writes=["xres"])

    if 0 in layers:
        inp = declare_inputs_L0(pb)
        emit_L0(pb, inp, n_seq, S)
    if 1 in layers:
        inp1 = declare_inputs_L1(pb)
        emit_L1(pb, inp1, n_seq, S)
    if 2 in layers:
        inp2 = declare_inputs_L2(pb)
        emit_L2(pb, inp2, n_seq, S)
    if 3 in layers:
        inp3 = declare_inputs_L3(pb)
        emit_L3(pb, inp3, n_seq, S)

    pb.q_dma(y[:, :], xres[:, :], reads=["xres"], writes=["out_y"])
    for tname in taps:
        tap = pb.dram(f"tap_{tname}", list(pb.dr[tname].shape), pb.dr[tname].dtype,
                      kind="ExternalOutput")
        nt = T // 128
        pb.q_dma(tap[:], pb.dr[tname][:],
                 reads=[tname] + rd_all(tname, nt) + [f"{tname}@{s}" for s in range(n_seq)],
                 writes=[f"tap_{tname}"])
    pb.drain()
    pb.finalize()
    es.close()
    return nc, pb


def kernel(**inputs):
    n_seq, S = 2, 2048
    T = n_seq * S
    nc, pb = build_program(n_seq=n_seq, S=S)
    w = prep_weights_L0(inputs)
    w.update(prep_weights_L1(inputs))
    w.update(prep_weights_L2(inputs))
    w.update(prep_weights_L3(inputs))
    x = np.ascontiguousarray(np.asarray(inputs["L0_attn_hidden_states"], dtype=np.float32))
    in_maps = []
    for c in range(NCORES):
        m = dict(w)
        m["x0"] = np.ascontiguousarray(x[c * n_seq:(c + 1) * n_seq].reshape(T, D))
        in_maps.append(m)
    res = run_bass_kernel_spmd(nc, in_maps, core_ids=list(range(NCORES)))
    out = np.concatenate([np.asarray(r["y"]).reshape(n_seq, S, D) for r in res.results], axis=0)
    return out


# version 05 (final)
# speedup vs baseline: 1.1308x; 1.0863x over previous
import contextlib
import math

import numpy as np
import ml_dtypes

import concourse.bass as bass
import concourse.mybir as mybir
from concourse.bass_utils import run_bass_kernel_spmd

F32 = mybir.dt.float32
BF16 = mybir.dt.bfloat16
AF = mybir.ActivationFunctionType
ALU = mybir.AluOpType
AX = mybir.AxisListType

D = 1024
NCORES = 8


class PB:
    STREAMS = ("tensor", "vector", "scalar", "gpsimd", "sync")
    DEFSEM = {"tensor": "t", "vector": "v", "scalar": "s", "gpsimd": "p", "sync": "qH"}

    def __init__(self, nc, es):
        self.nc = nc
        self.es = es
        self.streams = {e: [] for e in self.STREAMS}
        self.NROT = 16
        semkeys = ["t", "v", "s", "p"]
        for q in ("qH", "cH", "pG"):
            semkeys += [f"{q}{i}" for i in range(self.NROT)]
        self.sems = {k: es.enter_context(nc.semaphore("S" + k)) for k in semkeys}
        self.cnt = {k: 0 for k in semkeys}
        self.dma_i = {"qH": 0, "cH": 0, "pG": 0}
        self.waited = {}
        self.writer = {}
        self.readers = {}
        self.sb = {}
        self.dr = {}
        self.sbuf_bytes = 0

    def sbuf(self, name, shape, dtype):
        if name not in self.sb:
            self.sb[name] = self.es.enter_context(self.nc.sbuf_tensor(name, shape, dtype))
            self.sbuf_bytes += shape[1] * mybir.dt.size(dtype)
        return self.sb[name]

    def psum(self, name, shape, dtype):
        if name not in self.sb:
            self.sb[name] = self.es.enter_context(self.nc.psum_tensor(name, shape, dtype))
        return self.sb[name]

    def dram(self, name, shape, dtype, kind=None):
        if name not in self.dr:
            if kind is None:
                self.dr[name] = self.nc.dram_tensor(name, shape, dtype).ap()
            else:
                self.dr[name] = self.nc.dram_tensor(name, shape, dtype, kind=kind).ap()
        return self.dr[name]

    def _wait(self, stream, sk, val):
        if val <= 0:
            return
        if sk[-1].isdigit() and sk[:-1] in ("qH", "pG", "cH"):
            val = self.cnt[sk]
        key = (stream, sk)
        if self.waited.get(key, 0) >= val:
            return
        self.waited[key] = val
        sem = self.sems[sk]
        self.streams[stream].append(("w", sem, val))

    def op(self, stream, fn, reads=(), writes=(), sem=None, dma=False):
        if dma:
            q = sem
            sk = f"{q}{self.dma_i[q] % self.NROT}"
            self.dma_i[q] += 1
            self._wait(stream, sk, self.cnt[sk])
        else:
            sk = sem if sem is not None else self.DEFSEM[stream]
        for b in reads:
            w = self.writer.get(b)
            if w:
                self._wait(stream, w[0], w[1])
        for b in writes:
            w = self.writer.get(b)
            if w:
                self._wait(stream, w[0], w[1])
            for skr, c in self.readers.get(b, {}).items():
                self._wait(stream, skr, c)
        amt = 16 if dma else 1
        self.cnt[sk] += amt
        self.streams[stream].append(("i", fn, self.sems[sk], amt))
        pos = (sk, self.cnt[sk])
        for b in reads:
            d = self.readers.setdefault(b, {})
            d[sk] = max(d.get(sk, 0), pos[1])
        for b in writes:
            self.writer[b] = pos
            self.readers[b] = {}

    def q_dma(self, out, in_, reads, writes, transpose=False):
        if transpose:
            self.op("sync", lambda em: em.dma_start_transpose(out, in_),
                    reads=reads, writes=writes, sem="qH", dma=True)
        else:
            self.op("sync", lambda em: em.dma_start(out=out, in_=in_),
                    reads=reads, writes=writes, sem="qH", dma=True)

    def c_dma(self, out, in_, reads, writes):
        self.op("scalar", lambda em: em.dma_start(out=out, in_=in_),
                reads=reads, writes=writes, sem="cH", dma=True)

    def g_dma(self, out, in_, reads, writes, accum=False):
        if accum:
            self.op("gpsimd", lambda em: em.dma_start(out=out, in_=in_, accum_op=ALU.add),
                    reads=reads, writes=writes, sem="pG", dma=True)
        else:
            self.op("gpsimd", lambda em: em.dma_start(out=out, in_=in_),
                    reads=reads, writes=writes, sem="pG", dma=True)

    def mm(self, out, lhsT, rhs, start, stop, reads, writes):
        self.op("tensor",
                lambda em: em.matmul(out, lhsT, rhs, start=start, stop=stop),
                reads=reads, writes=writes)

    def transp(self, out, in_, reads, writes, np_=128, f32=False):
        nm = "identf" if f32 else "ident"
        ident = self.sb[nm]
        self.op("tensor", lambda em: em.transpose(out, in_, ident[0:np_, 0:np_]),
                reads=list(reads) + [nm], writes=writes)

    def finalize(self):
        block = self.es.enter_context(self.nc.Block())
        for stream in self.STREAMS:
            items = self.streams[stream]
            if not items:
                continue

            def make(items):
                def emit(em):
                    for it in items:
                        if it[0] == "w":
                            em.wait_ge(it[1], it[2])
                        else:
                            it[1](em).then_inc(it[2], it[3])
                return emit

            getattr(block, stream)(make(items))

    def drain(self):
        for sk, c in self.cnt.items():
            if c > 0:
                sem = self.sems[sk]
                self._wait("sync", sk, c)


def emit_prelude(pb):
    nc = pb.nc
    pb.sbuf("ident", [128, 128], BF16)
    pb.sbuf("identf", [128, 128], F32)
    pb.sbuf("onescol", [1, 128], BF16)
    pb.op("gpsimd", lambda em: em.iota(pb.sb["identf"][:, :], [[1, 128]], base=0,
                                       channel_multiplier=-1,
                                       allow_small_or_imprecise_dtypes=True),
          writes=["identf"])
    pb.op("vector", lambda em: em.tensor_single_scalar(
        pb.sb["identf"][:, :], pb.sb["identf"][:, :], 0.0, ALU.is_equal),
        reads=["identf"], writes=["identf"])
    pb.op("vector", lambda em: em.tensor_copy(pb.sb["ident"][:, :], pb.sb["identf"][:, :]),
          reads=["identf"], writes=["ident"])
    pb.op("vector", lambda em: em.memset(pb.sb["onescol"][:, :], 1.0), writes=["onescol"])


def emit_causal_masks(pb):
    for ds in range(4):
        t = pb.sbuf(f"mask{ds}", [128, 512], BF16)
        pb.op("gpsimd", lambda em, t=t, ds=ds: em.iota(
            t[:, :], [[1, 512]], base=-128 * ds, channel_multiplier=-1,
            allow_small_or_imprecise_dtypes=True), writes=[f"mask{ds}"])
        pb.op("vector", lambda em, t=t: em.tensor_single_scalar(t[:, :], t[:, :], 0.0, ALU.is_ge),
              reads=[f"mask{ds}"], writes=[f"mask{ds}"])
        pb.op("vector", lambda em, t=t: em.tensor_single_scalar(t[:, :], t[:, :], -1e30, ALU.mult),
              reads=[f"mask{ds}"], writes=[f"mask{ds}"])



def emit_ln(pb, tag, src_name, src_ap, T, eps=1e-5):
    xn_f = pb.dram(f"xnf_{tag}", [T, D], F32)
    xn_b = pb.dram(f"xnb_{tag}", [T, D], BF16)
    nt = T // 128
    for sl in range(2):
        pb.sbuf(f"FP{sl}", [128, D], F32)
        pb.sbuf(f"FP{2+sl}", [128, D], F32)
        pb.sbuf(f"FP{4+sl}", [128, D], F32)
        for cn in ("csum", "css", "cm", "cv", "cr"):
            pb.sbuf(f"{cn}{sl}", [128, 1], F32)
    for i in range(nt):
        sl = i % 2
        xt, sq, xo = pb.sb[f"FP{sl}"], pb.sb[f"FP{2+sl}"], pb.sb[f"FP{4+sl}"]
        csum, css = pb.sb[f"csum{sl}"], pb.sb[f"css{sl}"]
        cm, cv, cr = pb.sb[f"cm{sl}"], pb.sb[f"cv{sl}"], pb.sb[f"cr{sl}"]
        kx, ks, ko = f"FP{sl}", f"FP{2+sl}", f"FP{4+sl}"
        kc = [f"csum{sl}", f"css{sl}", f"cm{sl}", f"cv{sl}", f"cr{sl}"]
        pb.q_dma(xt[:, :], src_ap[i * 128:(i + 1) * 128, :], reads=[src_name], writes=[kx])
        pb.op("vector", lambda em, xt=xt, c=csum: em.reduce_sum(c[:, :], xt[:, :], AX.X),
              reads=[kx], writes=[kc[0]])
        pb.op("scalar", lambda em, xt=xt, sq=sq, c=css: em.activation(
            sq[:, :], xt[:, :], AF.Square, accum_out=c[:, 0:1]),
            reads=[kx], writes=[ks, kc[1]])
        pb.op("vector", lambda em, c=csum, m=cm: em.tensor_scalar_mul(m[:, :], c[:, :], 1.0 / D),
              reads=[kc[0]], writes=[kc[2]])
        pb.op("vector", lambda em, m=cm, v=cv: em.tensor_mul(v[:, :], m[:, :], m[:, :]),
              reads=[kc[2]], writes=[kc[3]])
        pb.op("vector", lambda em, c=css, v=cv: em.scalar_tensor_tensor(
            v[:, :], c[:, :], 1.0 / D, v[:, :], ALU.mult, ALU.subtract),
            reads=[kc[1], kc[3]], writes=[kc[3]])
        pb.op("vector", lambda em, v=cv: em.tensor_single_scalar(v[:, :], v[:, :], eps, ALU.add),
              reads=[kc[3]], writes=[kc[3]])
        pb.op("vector", lambda em, v=cv, r=cr: em.reciprocal(r[:, :], v[:, :]),
              reads=[kc[3]], writes=[kc[4]])
        pb.op("scalar", lambda em, r=cr: em.activation(r[:, :], r[:, :], AF.Sqrt),
              reads=[kc[4]], writes=[kc[4]])
        pb.op("vector", lambda em, xt=xt, xo=xo, m=cm, r=cr: em.tensor_scalar(
            xo[:, :], xt[:, :], m[:, 0:1], r[:, 0:1], ALU.subtract, ALU.mult),
            reads=[kx, kc[2], kc[4]], writes=[ko])
        pb.q_dma(xn_f[i * 128:(i + 1) * 128, :], xo[:, :], reads=[ko], writes=[f"xnf_{tag}@{i}"])
        pb.g_dma(xn_b[i * 128:(i + 1) * 128, :], xo[:, :], reads=[ko], writes=[f"xnb_{tag}@{i}"])
    return xn_f, xn_b


def rd_all(name, nt):
    return [f"{name}@{i}" for i in range(nt)]


def emit_load_xnT(pb, tag, xn_b, seq, S, T):
    nt = T // 128
    for d in range(8):
        xp = pb.sbuf(f"XP{d}", [128, 2048], BF16)
        pb.q_dma(xp[:, 0:S], xn_b[seq * S:(seq + 1) * S, d * 128:(d + 1) * 128],
                 reads=rd_all(f"xnb_{tag}", nt), writes=[f"XP{d}"], transpose=True)


def emit_combine(pb, tag, xres, s_scr, g_scr, T):
    nt = T // 128
    for i in range(nt):
        sl = i % 2
        xt = pb.sb[f"FP{sl}"]
        st = pb.sb[f"FP{2+sl}"]
        gt = pb.sb[f"FP{4+sl}"]
        kx, ks, kg = f"FP{sl}", f"FP{2+sl}", f"FP{4+sl}"
        r = slice(i * 128, (i + 1) * 128)
        pb.q_dma(xt[:, :], xres[r, :], reads=["xres"], writes=[kx])
        pb.q_dma(st[:, :], s_scr[r, :], reads=rd_all(f"s_{tag}", nt), writes=[ks])
        pb.q_dma(gt[:, :], g_scr[r, :], reads=rd_all(f"g_{tag}", nt), writes=[kg])
        pb.op("scalar", lambda em, g=gt: em.activation(g[:, :], g[:, :], AF.Silu),
              reads=[kg], writes=[kg])
        pb.op("vector", lambda em, s=st, g=gt: em.tensor_mul(s[:, :], s[:, :], g[:, :]),
              reads=[ks, kg], writes=[ks])
        pb.op("vector", lambda em, x=xt, s=st: em.tensor_add(x[:, :], x[:, :], s[:, :]),
              reads=[kx, ks], writes=[kx])
        pb.q_dma(xres[r, :], xt[:, :], reads=[kx], writes=["xres"])


def emit_L0(pb, inp, n_seq, S):
    T = n_seq * S
    nt = T // 128
    nib = S // 128
    xres = pb.dram("xres", [T, D], F32)
    xn_f, xn_b = emit_ln(pb, "l0", "xres", xres, T)

    qT = pb.dram("l0_qT", [D, T], BF16)
    Y2 = pb.dram("l0_Y2", [T, D], BF16)
    hT = pb.dram("l0_hT", [32, 128, T], BF16)
    s_scr = pb.dram("s_l0", [T, D], F32)
    g_scr = pb.dram("g_l0", [T, D], F32)

    PS = [pb.psum(f"PS{i}", [128, 512], F32) for i in range(6)]
    PT = pb.psum("PTB0", [128, 1024], BF16)

    for seq in range(n_seq):
        emit_load_xnT(pb, "l0", xn_b, seq, S, T)
        xp = [pb.sb[f"XP{d}"] for d in range(8)]
        xpk = [f"XP{d}" for d in range(8)]

        for d in range(8):
            g0 = pb.sbuf(f"G{d}", [128, D], BF16)
            pb.g_dma(g0[:, :], inp["L0_attn_w_dec_attn"][d * 128:(d + 1) * 128, :],
                     reads=["in_wdec"], writes=[f"G{d}"])
            g1 = pb.sbuf(f"G{8+d}", [128, D], BF16)
            pb.g_dma(g1[:, :], inp["l0_w2t"][d * 128:(d + 1) * 128, :],
                     reads=["in_w2t"], writes=[f"G{8+d}"])

        e0 = pb.sbuf("E0", [128, 512], BF16)
        e1 = pb.sbuf("QOUT", [128, 512], BF16)
        for tg in range(S // 512):
            tsl = slice(tg * 512, (tg + 1) * 512)
            for do in range(8):
                ps = PS[do % 2]
                psk = f"PS{do % 2}"
                for d in range(8):
                    pb.mm(ps[:, :], pb.sb[f"G{d}"][:, do * 128:(do + 1) * 128],
                          xp[d][:, tsl], start=(d == 0), stop=(d == 7),
                          reads=[f"G{d}", xpk[d]], writes=[psk])
                ev = (e0, "E0") if do % 2 == 0 else (e1, "QOUT")
                pb.op("vector", lambda em, ev=ev[0], ps=ps: em.tensor_copy(ev[:, :], ps[:, :]),
                      reads=[psk], writes=[ev[1]])
                pb.q_dma(qT[do * 128:(do + 1) * 128, seq * S + tg * 512: seq * S + (tg + 1) * 512],
                         ev[0][:, :], reads=[ev[1]], writes=[f"l0_qT@{seq}"])

        for tb in range(S // 128):
            t0 = tb * 128
            for oh in range(2):
                ps = PS[2 + oh]
                psk = f"PS{2+oh}"
                for d in range(8):
                    pb.mm(ps[:, :], xp[d][:, t0:t0 + 128],
                          pb.sb[f"G{8+d}"][:, oh * 512:(oh + 1) * 512],
                          start=(d == 0), stop=(d == 7),
                          reads=[xpk[d], f"G{8+d}"], writes=[psk])
            y2t = pb.sbuf("Y2T", [128, D], BF16)
            pb.op("vector", lambda em, ps=PS[2]: em.tensor_copy(y2t[:, 0:512], ps[:, :]),
                  reads=["PS2"], writes=["Y2T"])
            pb.op("vector", lambda em, ps=PS[3]: em.tensor_copy(y2t[:, 512:1024], ps[:, :]),
                  reads=["PS3"], writes=["Y2T"])
            pb.q_dma(Y2[seq * S + t0: seq * S + t0 + 128, :], y2t[:, :],
                     reads=["Y2T"], writes=[f"l0_Y2@{seq}"])

        for jb in range(nib):
            gi = pb.sbuf(f"G{16 + jb}", [128, D], BF16)
            pb.q_dma(gi[:, :], Y2[seq * S + jb * 128: seq * S + (jb + 1) * 128, :],
                     reads=[f"l0_Y2@{seq}"], writes=[f"G{16+jb}"])
        for d in range(8):
            pb.g_dma(pb.sb[f"G{8+d}"][:, :], inp["l0_w1t"][d * 128:(d + 1) * 128, :],
                     reads=["in_w1t"], writes=[f"G{8+d}"])
        brow = pb.sbuf("BROW", [1, D], BF16)
        pb.g_dma(brow[:, :], inp["L0_attn_b_out"][0:1, :], reads=["in_bout"], writes=["BROW"])

        scores = pb.sbuf("SCORES", [128, 2048], F32)
        pbuf = pb.sbuf("PBUF", [128, 2048], BF16)
        nmx = pb.sbuf("NMX", [128, 1], F32)
        ssum = pb.sbuf("SSUM", [128, 1], F32)
        srcp = pb.sbuf("SRCP", [128, 1], F32)
        qi = [pb.sbuf(f"QI{d}", [128, 128], BF16) for d in range(8)]
        ptile = [pb.sbuf(f"PTS{k}", [128, 128], BF16) for k in range(2)]
        stile = pb.sbuf("STILE", [128, 1024], F32)

        for ib in range(nib):
            Lj = (ib + 1) * 128
            for d in range(8):
                pb.q_dma(qi[d][:, :],
                         qT[d * 128:(d + 1) * 128, seq * S + ib * 128: seq * S + (ib + 1) * 128],
                         reads=[f"l0_qT@{seq}"], writes=[f"QI{d}"])
            ngroups = (Lj + 511) // 512
            for jg in range(ngroups):
                w = min(512, Lj - jg * 512)
                ps = PS[4]
                for d in range(8):
                    pb.mm(ps[:, 0:w], qi[d][:, :], xp[d][:, jg * 512: jg * 512 + w],
                          start=(d == 0), stop=(d == 7),
                          reads=[f"QI{d}", xpk[d]], writes=["PS4"])
                last = (jg == ngroups - 1)
                if last:
                    ds = ib % 4
                    mk = f"mask{ds}"
                    pb.op("vector", lambda em, w=w, jg=jg, mk=mk, ps=ps: em.tensor_add(
                        scores[:, jg * 512: jg * 512 + w], ps[:, 0:w], pb.sb[mk][:, 0:w]),
                        reads=["PS4", mk], writes=["SCORES"])
                    if ib == 0:
                        pb.op("vector", lambda em, ps=ps: em.tensor_copy(
                            scores[0:1, 0:1], ps[0:1, 0:1]),
                            reads=["PS4"], writes=["SCORES"])
                else:
                    pb.op("vector", lambda em, w=w, jg=jg, ps=ps: em.tensor_copy(
                        scores[:, jg * 512: jg * 512 + w], ps[:, 0:w]),
                        reads=["PS4"], writes=["SCORES"])
            pb.op("vector", lambda em, Lj=Lj: em.reduce_max(nmx[:, :], scores[:, 0:Lj], AX.X,
                                                            negate=True),
                  reads=["SCORES"], writes=["NMX"])
            pb.op("scalar", lambda em, Lj=Lj: em.activation(
                pbuf[:, 0:Lj], scores[:, 0:Lj], AF.Exp, bias=nmx[:, 0:1],
                accum_out=ssum[:, 0:1]),
                reads=["SCORES", "NMX"], writes=["PBUF", "SSUM"])
            pb.op("vector", lambda em: em.reciprocal(srcp[:, :], ssum[:, :]),
                  reads=["SSUM"], writes=["SRCP"])
            pb.op("vector", lambda em, Lj=Lj: em.tensor_scalar_mul(
                pbuf[:, 0:Lj], pbuf[:, 0:Lj], srcp[:, 0:1]),
                reads=["PBUF", "SRCP"], writes=["PBUF"])

            for oh in range(2):
                ps = PS[oh]
                psk = f"PS{oh}"
                osl = slice(oh * 512, (oh + 1) * 512)
                pb.mm(ps[:, :], pb.sb["onescol"][:, :], brow[:, osl], start=True, stop=False,
                      reads=["onescol", "BROW"], writes=[psk])
                for d in range(8):
                    pb.mm(ps[:, :], xp[d][:, ib * 128:(ib + 1) * 128],
                          pb.sb[f"G{8+d}"][:, osl], start=False, stop=False,
                          reads=[xpk[d], f"G{8+d}"], writes=[psk])
            for jb in range(ib + 1):
                pt = ptile[jb % 2]
                ptk = f"PTS{jb % 2}"
                pb.transp(PT[:, 0:128], pbuf[:, jb * 128:(jb + 1) * 128],
                          reads=["PBUF"], writes=["PTB0"])
                pb.op("vector", lambda em, pt=pt: em.tensor_copy(pt[:, :], PT[:, 0:128]),
                      reads=["PTB0"], writes=[ptk])
                for oh in range(2):
                    ps = PS[oh]
                    psk = f"PS{oh}"
                    osl = slice(oh * 512, (oh + 1) * 512)
                    lastmm = (jb == ib)
                    pb.mm(ps[:, :], pt[:, :], pb.sb[f"G{16+jb}"][:, osl],
                          start=False, stop=lastmm,
                          reads=[ptk, f"G{16+jb}"], writes=[psk])
            for oh in range(2):
                ps = PS[oh]
                psk = f"PS{oh}"
                ssl = slice(oh * 512, (oh + 1) * 512)
                pb.op("scalar", lambda em, ps=ps, ssl=ssl: em.activation(
                    stile[:, ssl], ps[:, :], AF.Exp, scale=2.0),
                    reads=[psk], writes=["STILE"])
                pb.op("vector", lambda em, ssl=ssl: em.tensor_single_scalar(
                    stile[:, ssl], stile[:, ssl], 1.0, ALU.add),
                    reads=["STILE"], writes=["STILE"])
                pb.op("vector", lambda em, ssl=ssl: em.reciprocal(
                    stile[:, ssl], stile[:, ssl]),
                    reads=["STILE"], writes=["STILE"])
                pb.op("vector", lambda em, ssl=ssl: em.tensor_scalar(
                    stile[:, ssl], stile[:, ssl], -2.0, 1.0, ALU.mult, ALU.add),
                    reads=["STILE"], writes=["STILE"])
            pb.q_dma(s_scr[seq * S + ib * 128: seq * S + (ib + 1) * 128, :], stile[:, :],
                     reads=["STILE"], writes=[f"s_l0@{seq * nib + ib}"])

        hf32 = pb.sbuf("HF32", [128, 512], F32)
        hbf = pb.sbuf("HBF", [128, 512], BF16)
        for fc in range(32):
            for d in range(8):
                pb.g_dma(pb.sb[f"G{d}"][:, 0:128],
                         inp["l0_wfct"][d * 128:(d + 1) * 128, fc * 128:(fc + 1) * 128],
                         reads=["in_wfct"], writes=[f"G{d}"])
            for tg in range(S // 512):
                tsl = slice(tg * 512, (tg + 1) * 512)
                ps = PS[4 + tg % 2]
                psk = f"PS{4 + tg % 2}"
                for d in range(8):
                    pb.mm(ps[:, :], pb.sb[f"G{d}"][:, 0:128], xp[d][:, tsl],
                          start=(d == 0), stop=(d == 7),
                          reads=[f"G{d}", xpk[d]], writes=[psk])
                pb.op("scalar", lambda em, ps=ps: em.activation(hf32[:, :], ps[:, :], AF.Relu),
                      reads=[psk], writes=["HF32"])
                pb.op("scalar", lambda em: em.activation(hbf[:, :], hf32[:, :], AF.Square),
                      reads=["HF32"], writes=["HBF"])
                pb.q_dma(hT[fc, :, seq * S + tg * 512: seq * S + (tg + 1) * 512],
                         hbf[:, :], reads=["HBF"], writes=[f"l0_hT@{fc}"])

        ybuf = pb.dram("l0_y", [T, D], F32)
        for fc in range(32):
            g = pb.sbuf(f"G{fc}", [128, D], BF16)
            pb.g_dma(g[:, :], inp["l0_wprojt"][fc * 128:(fc + 1) * 128, :],
                     reads=["in_wprojt"], writes=[f"G{fc}"])
        ht0 = pb.sbuf("HT0", [128, 128], BF16)
        ht1 = pb.sbuf("HT1", [128, 128], BF16)
        yev = pb.sbuf("STILE", [128, 1024], F32)
        for tb in range(S // 128):
            t0 = seq * S + tb * 128
            for fc in range(32):
                ht = (ht0, "HT0") if fc % 2 == 0 else (ht1, "HT1")
                pb.q_dma(ht[0][:, :], hT[fc, :, t0:t0 + 128], reads=[f"l0_hT@{fc}"],
                         writes=[ht[1]])
                for oh in range(2):
                    ps = PS[oh]
                    psk = f"PS{oh}"
                    pb.mm(ps[:, :], ht[0][:, :], pb.sb[f"G{fc}"][:, oh * 512:(oh + 1) * 512],
                          start=(fc == 0), stop=(fc == 31),
                          reads=[ht[1], f"G{fc}"], writes=[psk])
            pb.op("vector", lambda em: em.tensor_copy(yev[:, 0:512], PS[0][:, :]),
                  reads=["PS0"], writes=["STILE"])
            pb.op("vector", lambda em: em.tensor_copy(yev[:, 512:1024], PS[1][:, :]),
                  reads=["PS1"], writes=["STILE"])
            pb.q_dma(ybuf[t0:t0 + 128, :], yev[:, :], reads=["STILE"], writes=[f"l0_y@{seq}"])

    ybuf = pb.dr["l0_y"]
    cn1 = pb.sbuf("CN1", [128, 1], F32)
    pb.op("gpsimd", lambda em: em.memset(cn1[:, :], -1.0), writes=["CN1"])

    for i in range(nt):
        sl = i % 2
        yt = pb.sb[f"FP{sl}"]
        mem = pb.sb[f"FP{2+sl}"]
        acc = pb.sb[f"FP{4+sl}"]
        if sl == 0:
            spk = pb.sbuf("STILE", [128, 1024], F32)
            kp = "STILE"
        else:
            spk = pb.sbuf("CONVA", [128, 2048], F32)[:, 0:1024]
            kp = "CONVA"
        ky, km, ka = f"FP{sl}", f"FP{2+sl}", f"FP{4+sl}"
        r = slice(i * 128, (i + 1) * 128)
        eng = "vector" if i % 2 == 0 else "gpsimd"
        pb.q_dma(yt[:, :], ybuf[r, :], reads=[f"l0_y@{0}", f"l0_y@{n_seq-1}"], writes=[ky])

        def sgt(dst, src, kd, ks_):
            if eng == "vector":
                pb.op(eng, lambda em, d=dst, s=src: em.tensor_single_scalar(
                    d[:, :], s[:, :], 1.0, ALU.is_gt), reads=[ks_], writes=[kd])
            else:
                pb.op("scalar", lambda em, d=dst, s=src: em.sign(
                    d[:, :], s[:, :], bias=cn1[:, 0:1]), reads=[ks_, "CN1"], writes=[kd])
                pb.op("scalar", lambda em, d=dst: em.activation(
                    d[:, :], d[:, :], AF.Relu), reads=[kd], writes=[kd])

        def scl(dst_t, src_t, kd, ks_, cval):
            pb.op("scalar", lambda em, d=dst_t, s=src_t, cval=cval: em.activation(
                d[:, :], s[:, :], AF.Identity, scale=cval), reads=[ks_], writes=[kd])

        sgt(spk, yt, kp, ky)
        pb.op(eng, lambda em, a=acc, s=spk: em.tensor_copy(a[:, :], s[:, :]),
              reads=[kp], writes=[ka])
        pb.op(eng, lambda em, m=mem, y=yt: em.tensor_copy(m[:, :], y[:, :]),
              reads=[ky], writes=[km])
        for step in range(4):
            if eng == "vector":
                pb.op(eng, lambda em, m=mem, y=yt: em.scalar_tensor_tensor(
                    m[:, :], m[:, :], 0.9, y[:, :], ALU.mult, ALU.add),
                    reads=[km, ky], writes=[km])
            else:
                scl(mem, mem, km, km, 0.9)
                pb.op(eng, lambda em, m=mem, y=yt: em.tensor_add(m[:, :], m[:, :], y[:, :]),
                      reads=[km, ky], writes=[km])
            pb.op(eng, lambda em, m=mem, s=spk: em.tensor_sub(m[:, :], m[:, :], s[:, :]),
                  reads=[km, kp], writes=[km])
            sgt(spk, mem, kp, km)
            pb.op(eng, lambda em, a=acc, s=spk: em.tensor_add(a[:, :], a[:, :], s[:, :]),
                  reads=[ka, kp], writes=[ka])
        if eng == "vector":
            pb.op(eng, lambda em, a=acc: em.tensor_single_scalar(
                a[:, :], a[:, :], 0.2, ALU.mult), reads=[ka], writes=[ka])
        else:
            scl(acc, acc, ka, ka, 0.2)
        pb.q_dma(pb.dr["g_l0"][r, :], acc[:, :], reads=[ka], writes=[f"g_l0@{i}"])

    emit_combine(pb, "l0", xres, pb.dr["s_l0"], pb.dr["g_l0"], T)


QMIDS = [(0.75, 0.5), (1.25, 0.5), (1.75, 0.5), (2.5, 1.0), (3.5, 1.0), (5.0, 2.0)]


def emit_quant(pb, src, dst, W, rk, wk, uniq):
    nb = W // 16
    mx = pb.sbuf(f"QMX{uniq}", [128, 32], F32)
    _cvr = pb.sbuf("CONVR", [128, 2112], BF16)
    ub = _cvr[:, 0:512]
    ab = _cvr[:, 512:1024]
    ac = pb.sbuf("E0", [128, 512], BF16)
    cb = pb.sbuf("HBF", [128, 512], BF16)
    kmx, ku, ka, kac, kc = (f"QMX{uniq}", "CONVR", "CONVR", "E0", "HBF")
    assert nb <= 32 and W <= 512
    pb.op("vector", lambda em: em.tensor_reduce(
        mx[:, 0:nb], src[:, 0:W].rearrange("p (b c) -> p b c", c=16), AX.X, ALU.max,
        apply_absolute_value=True), reads=[rk], writes=[kmx])
    pb.op("vector", lambda em: em.tensor_scalar(
        mx[:, 0:nb], mx[:, 0:nb], 1.0 / 6.0, 1e-12, ALU.mult, ALU.max),
        reads=[kmx], writes=[kmx])
    sb_ = lambda t: bass.AP(t, 0, [[32, 128], [1, nb], [0, 16]])
    r3 = lambda t: t[:, 0:W].rearrange("p (b c) -> p b c", c=16)
    pb.op("vector", lambda em: em.reciprocal(mx[:, 0:nb], mx[:, 0:nb]),
          reads=[kmx], writes=[kmx])
    pb.op("vector", lambda em: em.tensor_tensor(
        r3(ub), r3(src), sb_(mx), ALU.mult), reads=[rk, kmx], writes=[ku])
    pb.op("scalar", lambda em: em.activation(ab[:, 0:W], ub[:, 0:W], AF.Abs),
          reads=[ku], writes=[ka])
    pb.op("vector", lambda em: em.tensor_scalar(
        ac[:, 0:W], ab[:, 0:W], 0.25, 0.5, ALU.is_gt, ALU.mult),
        reads=[ka], writes=[kac])
    for m, dl in QMIDS:
        pb.op("vector", lambda em, m=m, dl=dl: em.tensor_scalar(
            cb[:, 0:W], ab[:, 0:W], m, dl, ALU.is_gt, ALU.mult),
            reads=[ka], writes=[kc])
        pb.op("vector", lambda em: em.tensor_add(ac[:, 0:W], ac[:, 0:W], cb[:, 0:W]),
              reads=[kac, kc], writes=[kac])
    pb.op("scalar", lambda em: em.activation(ab[:, 0:W], ub[:, 0:W], AF.Sign),
          reads=[ku], writes=[ka])
    pb.op("vector", lambda em: em.tensor_mul(ac[:, 0:W], ac[:, 0:W], ab[:, 0:W]),
          reads=[kac, ka], writes=[kac])
    pb.op("vector", lambda em: em.reciprocal(mx[:, 0:nb], mx[:, 0:nb]),
          reads=[kmx], writes=[kmx])
    pb.op("vector", lambda em: em.tensor_tensor(
        r3(dst), r3(ac), sb_(mx), ALU.mult),
        reads=[kac, kmx], writes=[wk])


def emit_quant_dram(pb, src_dram, dst_dram, rows, W, rk, wk):
    qin = pb.sbuf("HF32", [128, 512], F32)
    qout = pb.sbuf("QOUT", [128, 512], BF16)
    for i in range(rows // 128):
        r = slice(i * 128, (i + 1) * 128)
        for c0 in range(0, W, 512):
            cw = min(512, W - c0)
            pb.q_dma(qin[:, 0:cw], src_dram[r, c0:c0 + cw], reads=[rk], writes=["HF32"])
            emit_quant(pb, qin, qout, cw, "HF32", "QOUT", "q")
            pb.q_dma(dst_dram[r, c0:c0 + cw], qout[:, 0:cw], reads=["QOUT"], writes=[wk])


def emit_incl_masks(pb):
    for ds in range(4):
        t = pb.sbuf(f"mi{ds}", [128, 512], BF16)
        pb.op("gpsimd", lambda em, t=t, ds=ds: em.iota(
            t[:, :], [[1, 512]], base=-128 * ds, channel_multiplier=-1,
            allow_small_or_imprecise_dtypes=True), writes=[f"mi{ds}"])
        pb.op("vector", lambda em, t=t: em.tensor_single_scalar(t[:, :], t[:, :], 0.0, ALU.is_gt),
              reads=[f"mi{ds}"], writes=[f"mi{ds}"])
        pb.op("vector", lambda em, t=t: em.tensor_single_scalar(t[:, :], t[:, :], -1e30, ALU.mult),
              reads=[f"mi{ds}"], writes=[f"mi{ds}"])


def emit_L3(pb, inp, n_seq, S):
    T = n_seq * S
    nt = T // 128
    nib = S // 128
    xres = pb.dr["xres"]
    xn_f, xn_b = emit_ln(pb, "l3", "xres", xres, T)

    vT = pb.dram("l3_vT", [D, T], BF16)
    xq = pb.dram("l3_xq", [T, D], BF16)
    hq = pb.dram("l3_hq", [T, 2816], BF16)
    wgq = pb.dram("l3_wgq", [2816, D], BF16)
    wuq = pb.dram("l3_wuq", [2816, D], BF16)
    wdq = pb.dram("l3_wdq", [D, 2816], BF16)
    s_scr = pb.dram("s_l3", [T, D], F32)
    g_scr = pb.dram("g_l3", [T, D], F32)
    PS = [pb.psum(f"PS{i}", [128, 512], F32) for i in range(6)]
    PT = pb.psum("PTB0", [128, 1024], BF16)

    emit_quant_dram(pb, inp["L3_ffn_w_gate"], wgq, 2816, D, "in_wg", "l3_wgq")
    emit_quant_dram(pb, inp["L3_ffn_w_up"], wuq, 2816, D, "in_wu", "l3_wuq")
    emit_quant_dram(pb, inp["L3_ffn_w_down"], wdq, D, 2816, "in_wd", "l3_wdq")

    qin = pb.sbuf("HF32", [128, 512], F32)
    qout = pb.sbuf("QOUT", [128, 512], BF16)
    for i in range(nt):
        r = slice(i * 128, (i + 1) * 128)
        for c0 in range(0, D, 512):
            pb.q_dma(qin[:, :], xn_f[r, c0:c0 + 512], reads=[f"xnf_l3@{i}"], writes=["HF32"])
            emit_quant(pb, qin, qout, 512, "HF32", "QOUT", "q")
            pb.q_dma(xq[r, c0:c0 + 512], qout[:, :], reads=["QOUT"], writes=[f"l3_xq@{i}"])

    LT = pb.sbuf("LT", [64, 2048], BF16)
    brow = pb.sbuf("BROW", [1, D], BF16)
    bvT = pb.sbuf("BVT", [128, 8], F32)
    pb.q_dma(bvT[:, :], inp["l3_bvT"][:, :], reads=["in_bvT"], writes=["BVT"])

    for seq in range(n_seq):
        emit_load_xnT(pb, "l3", xn_b, seq, S, T)
        xp = [pb.sb[f"XP{d}"] for d in range(8)]
        xpk = [f"XP{d}" for d in range(8)]

        wb_t = pb.sbuf("CONVR", [128, 2112], BF16)[:, 1024:1536]
        for d in range(8):
            pb.g_dma(wb_t[:, d * 64:(d + 1) * 64],
                     inp["l3_wbasisT"][d * 128:(d + 1) * 128, :],
                     reads=["in_wbasisT"], writes=["CONVR"])
            g = pb.sbuf(f"G{24+d}", [128, D], BF16)
            pb.g_dma(g[:, :], inp["l3_wvT"][d * 128:(d + 1) * 128, :],
                     reads=["in_wvT"], writes=[f"G{24+d}"])

        ev = pb.sbuf("E0", [128, 512], BF16)
        for tg in range(S // 512):
            tsl = slice(tg * 512, (tg + 1) * 512)
            ps = PS[4]
            for d in range(8):
                pb.mm(ps[0:64, :], wb_t[:, d * 64:(d + 1) * 64], xp[d][:, tsl],
                      start=(d == 0), stop=(d == 7), reads=["CONVR", xpk[d]], writes=["PS4"])
            pb.op("vector", lambda em, tsl=tsl, ps=ps: em.tensor_copy(LT[:, tsl], ps[0:64, :]),
                  reads=["PS4"], writes=["LT"])

        for do in range(8):
            for tg in range(S // 512):
                tsl = slice(tg * 512, (tg + 1) * 512)
                ps = PS[4 + tg % 2]
                psk = f"PS{4 + tg % 2}"
                for d in range(8):
                    pb.mm(ps[:, :], pb.sb[f"G{24+d}"][:, do * 128:(do + 1) * 128],
                          xp[d][:, tsl], start=(d == 0), stop=(d == 7),
                          reads=[f"G{24+d}", xpk[d]], writes=[psk])
                pb.op("scalar", lambda em, ps=ps, do=do: em.activation(
                    ev[:, :], ps[:, :], AF.Identity, bias=bvT[:, do:do + 1]),
                    reads=[psk, "BVT"], writes=["E0"])
                pb.q_dma(vT[do * 128:(do + 1) * 128, seq * S + tg * 512:seq * S + (tg + 1) * 512],
                         ev[:, :], reads=["E0"], writes=[f"l3_vT@{seq}"])

        for hp in range(8):
            for jb in range(nib):
                pb.q_dma(xp[hp][:, jb * 128:(jb + 1) * 128],
                         vT[hp * 128:(hp + 1) * 128,
                            seq * S + jb * 128:seq * S + (jb + 1) * 128],
                         reads=[f"l3_vT@{seq}"], writes=[xpk[hp]], transpose=True)

        for d in range(8):
            pb.g_dma(pb.sb[f"G{24+d}"][:, :], inp["l3_woT"][d * 128:(d + 1) * 128, :],
                     reads=["in_woT"], writes=[f"G{24+d}"])
        pb.g_dma(brow[:, :], inp["L3_attn_b_o"][0:1, :], reads=["in_bo"], writes=["BROW"])

        scores = pb.sbuf("SCORES", [128, 2048], F32)
        pbuf = pb.sbuf("PBUF", [128, 2048], BF16)
        nmx = pb.sbuf("NMX", [128, 1], F32)
        ssum = pb.sbuf("SSUM", [128, 1], F32)
        srcp = pb.sbuf("SRCP", [128, 1], F32)
        pts = [pb.sbuf(f"QI{k}", [128, 128], BF16) for k in range(8)] + \
              [pb.sbuf(f"PTS{k}", [128, 128], BF16) for k in range(8, 16)]
        ptsk = [f"QI{k}" if k < 8 else f"PTS{k}" for k in range(16)]
        _e0t = pb.sbuf("E0", [128, 512], BF16)
        ots = [pb.sbuf(f"OTS{c}", [128, 128], BF16) for c in range(6)] + \
              [_e0t[:, 0:128], _e0t[:, 128:256]]
        otsk = [f"OTS{c}" for c in range(6)] + ["E0", "E0"]
        stile = pb.sbuf("STILE", [128, 1024], F32)

        for ib in range(nib):
            Lj = (ib + 1) * 128
            isl = slice(ib * 128, (ib + 1) * 128)
            ngroups = (Lj + 511) // 512
            for jg in range(ngroups):
                w = min(512, Lj - jg * 512)
                ps = PS[4]
                pb.mm(ps[:, 0:w], LT[:, isl], LT[:, jg * 512:jg * 512 + w],
                      start=True, stop=True, reads=["LT"], writes=["PS4"])
                last = (jg == ngroups - 1)
                mk = f"mi{ib % 4}" if last else None
                if last:
                    pb.op("vector", lambda em, w=w, jg=jg, mk=mk, ps=ps: em.scalar_tensor_tensor(
                        scores[:, jg * 512:jg * 512 + w], ps[:, 0:w], 0.125,
                        pb.sb[mk][:, 0:w], ALU.mult, ALU.add),
                        reads=["PS4", mk], writes=["SCORES"])
                else:
                    pb.op("vector", lambda em, w=w, jg=jg, ps=ps: em.tensor_scalar_mul(
                        scores[:, jg * 512:jg * 512 + w], ps[:, 0:w], 0.125),
                        reads=["PS4"], writes=["SCORES"])
            pb.op("vector", lambda em, Lj=Lj: em.reduce_max(nmx[:, :], scores[:, 0:Lj], AX.X,
                                                            negate=True),
                  reads=["SCORES"], writes=["NMX"])
            pb.op("scalar", lambda em, Lj=Lj: em.activation(
                pbuf[:, 0:Lj], scores[:, 0:Lj], AF.Exp, bias=nmx[:, 0:1],
                accum_out=ssum[:, 0:1]),
                reads=["SCORES", "NMX"], writes=["PBUF", "SSUM"])
            pb.op("vector", lambda em: em.reciprocal(srcp[:, :], ssum[:, :]),
                  reads=["SSUM"], writes=["SRCP"])
            pb.op("vector", lambda em, Lj=Lj: em.tensor_scalar_mul(
                pbuf[:, 0:Lj], pbuf[:, 0:Lj], srcp[:, 0:1]),
                reads=["PBUF", "SRCP"], writes=["PBUF"])
            for jb in range(ib + 1):
                pb.transp(PT[:, 0:128], pbuf[:, jb * 128:(jb + 1) * 128],
                          reads=["PBUF"], writes=["PTB0"])
                pb.op("vector", lambda em, jb=jb: em.tensor_copy(pts[jb][:, :], PT[:, 0:128]),
                      reads=["PTB0"], writes=[ptsk[jb]])
            for h in range(16):
                hp, hh = h // 2, h % 2
                ps = PS[5]
                for jb in range(ib + 1):
                    pb.mm(ps[0:64, 0:128],
                          xp[hp][:, jb * 128 + hh * 64:jb * 128 + hh * 64 + 64],
                          pts[jb][:, :], start=(jb == 0), stop=(jb == ib),
                          reads=[xpk[hp], ptsk[jb]], writes=["PS5"])
                pb.op("vector", lambda em, h=h, ps=ps: em.tensor_copy(
                    ots[h // 2][(h % 2) * 64:(h % 2) * 64 + 64, :], ps[0:64, 0:128]),
                    reads=["PS5"], writes=[otsk[h // 2]])
            for oh in range(2):
                ps = PS[oh]
                psk = f"PS{oh}"
                osl = slice(oh * 512, (oh + 1) * 512)
                pb.mm(ps[:, :], pb.sb["onescol"][:, :], brow[:, osl], start=True, stop=False,
                      reads=["onescol", "BROW"], writes=[psk])
                for c in range(8):
                    pb.mm(ps[:, :], ots[c][:, :], pb.sb[f"G{24+c}"][:, osl],
                          start=False, stop=(c == 7),
                          reads=[otsk[c], f"G{24+c}"], writes=[psk])
                pb.op("scalar", lambda em, ps=ps, oh=oh: em.copy(
                    stile[:, oh * 512:(oh + 1) * 512], ps[:, :]),
                    reads=[psk], writes=["STILE"])
            pb.q_dma(s_scr[seq * S + ib * 128:seq * S + (ib + 1) * 128, :], stile[:, :],
                     reads=["STILE"], writes=[f"s_l3@{seq * nib + ib}"])

        for d in range(8):
            pb.q_dma(xp[d][:, 0:S], xq[seq * S:(seq + 1) * S, d * 128:(d + 1) * 128],
                     reads=rd_all("l3_xq", nt), writes=[xpk[d]], transpose=True)
        eg = pb.sbuf("EG", [128, 512], F32)
        eu = pb.sbuf("HF32", [128, 512], F32)
        hf = pb.sbuf("CONVA", [128, 2048], F32)[:, 0:512]
        hqb = pb.sbuf("CONVR", [128, 2112], BF16)[:, 1536:2048]
        fslices = [(k * 512, min(512, 2816 - k * 512)) for k in range(6)]
        for f0, fw in fslices:
            for d in range(8):
                g = pb.sbuf(f"G{d}", [128, D], BF16)
                pb.q_dma(g[:, 0:fw], wgq[f0:f0 + fw, d * 128:(d + 1) * 128],
                         reads=["l3_wgq"], writes=[f"G{d}"], transpose=True)
                g2 = pb.sbuf(f"G{8+d}", [128, D], BF16)
                pb.q_dma(g2[:, 0:fw], wuq[f0:f0 + fw, d * 128:(d + 1) * 128],
                         reads=["l3_wuq"], writes=[f"G{8+d}"], transpose=True)
            for tb in range(nib):
                t0 = seq * S + tb * 128
                tloc = slice(tb * 128, (tb + 1) * 128)
                for d in range(8):
                    pb.mm(PS[0][:, 0:fw], xp[d][:, tloc], pb.sb[f"G{d}"][:, 0:fw],
                          start=(d == 0), stop=(d == 7), reads=[xpk[d], f"G{d}"],
                          writes=["PS0"])
                    pb.mm(PS[1][:, 0:fw], xp[d][:, tloc], pb.sb[f"G{8+d}"][:, 0:fw],
                          start=(d == 0), stop=(d == 7), reads=[xpk[d], f"G{8+d}"],
                          writes=["PS1"])
                pb.op("scalar", lambda em, fw=fw: em.activation(eg[:, 0:fw], PS[0][:, 0:fw],
                                                                AF.Silu),
                      reads=["PS0"], writes=["EG"])
                pb.op("vector", lambda em, fw=fw: em.tensor_copy(eu[:, 0:fw], PS[1][:, 0:fw]),
                      reads=["PS1"], writes=["HF32"])
                pb.op("vector", lambda em, fw=fw: em.tensor_mul(hf[:, 0:fw], eg[:, 0:fw],
                                                                eu[:, 0:fw]),
                      reads=["EG", "HF32"], writes=["CONVA"])
                emit_quant(pb, hf, hqb, fw, "CONVA", "CONVR", "q")
                pb.q_dma(hq[t0:t0 + 128, f0:f0 + fw], hqb[:, 0:fw],
                         reads=["CONVR"], writes=[f"l3_hq@{seq}"])
        for fc in range(22):
            g = pb.sbuf(f"G{fc}", [128, D], BF16)
            pb.q_dma(g[:, :], wdq[0:D, fc * 128:(fc + 1) * 128],
                     reads=["l3_wdq"], writes=[f"G{fc}"], transpose=True)
        _e0b = pb.sbuf("E0", [128, 512], BF16)
        hti = [_e0b[:, 256:384], _e0b[:, 384:512]]
        gev = pb.sbuf("STILE", [128, 1024], F32)
        for tb in range(nib):
            t0 = seq * S + tb * 128
            for fc in range(22):
                ht = hti[fc % 2]
                htk = "E0"
                pb.q_dma(ht[:, :], hq[t0:t0 + 128, fc * 128:(fc + 1) * 128],
                         reads=[f"l3_hq@{seq}"], writes=[htk], transpose=True)
                for oh in range(2):
                    pb.mm(PS[oh][:, :], ht[:, :], pb.sb[f"G{fc}"][:, oh * 512:(oh + 1) * 512],
                          start=(fc == 0), stop=(fc == 21),
                          reads=[htk, f"G{fc}"], writes=[f"PS{oh}"])
            pb.op("vector", lambda em: em.tensor_copy(gev[:, 0:512], PS[0][:, :]),
                  reads=["PS0"], writes=["STILE"])
            pb.op("vector", lambda em: em.tensor_copy(gev[:, 512:1024], PS[1][:, :]),
                  reads=["PS1"], writes=["STILE"])
            pb.q_dma(g_scr[t0:t0 + 128, :], gev[:, :], reads=["STILE"],
                     writes=[f"g_l3@{(t0) // 128}"])

    emit_combine(pb, "l3", xres, s_scr, g_scr, T)


def prep_weights_L3(w):
    out = {}
    out["l3_wbasisT"] = np.ascontiguousarray(np.asarray(w["L3_attn_W_basis"]).T)
    out["l3_wvT"] = np.ascontiguousarray(np.asarray(w["L3_attn_W_v"]).T)
    out["l3_bvT"] = np.ascontiguousarray(np.asarray(w["L3_attn_b_v"]).reshape(8, 128).T)
    out["l3_woT"] = np.ascontiguousarray(np.asarray(w["L3_attn_W_o"]).T)
    out["L3_attn_b_o"] = np.ascontiguousarray(np.asarray(w["L3_attn_b_o"]).reshape(1, D))
    out["L3_ffn_w_gate"] = np.ascontiguousarray(np.asarray(w["L3_ffn_w_gate"]))
    out["L3_ffn_w_up"] = np.ascontiguousarray(np.asarray(w["L3_ffn_w_up"]))
    out["L3_ffn_w_down"] = np.ascontiguousarray(np.asarray(w["L3_ffn_w_down"]))
    return out


def declare_inputs_L3(pb):
    inp = {}
    inp["l3_wbasisT"] = pb.dram("l3_wbasisT", [D, 64], F32, kind="ExternalInput")
    inp["l3_wvT"] = pb.dram("l3_wvT", [D, D], F32, kind="ExternalInput")
    inp["l3_bvT"] = pb.dram("l3_bvT", [128, 8], F32, kind="ExternalInput")
    inp["l3_woT"] = pb.dram("l3_woT", [D, D], F32, kind="ExternalInput")
    inp["L3_attn_b_o"] = pb.dram("L3_attn_b_o", [1, D], F32, kind="ExternalInput")
    inp["L3_ffn_w_gate"] = pb.dram("L3_ffn_w_gate", [2816, D], F32, kind="ExternalInput")
    inp["L3_ffn_w_up"] = pb.dram("L3_ffn_w_up", [2816, D], F32, kind="ExternalInput")
    inp["L3_ffn_w_down"] = pb.dram("L3_ffn_w_down", [D, 2816], F32, kind="ExternalInput")
    return inp




SHIFT_ID_MAP = {}


def emit_shift_idents(pb):
    hosts = ([f"PTS{k}" for k in range(8, 16)] + [f"OTS{c}" for c in range(6)])
    for c in range(6):
        pb.sbuf(f"OTS{c}", [128, 128], BF16)
    for k in range(8, 16):
        pb.sbuf(f"PTS{k}", [128, 128], BF16)
    hi = 0
    for sz in [1, 2, 4, 8, 16, 32, 64]:
        for kind in ("IU", "ID"):
            SHIFT_ID_MAP[f"{kind}{sz}"] = hosts[hi]
            hi += 1
    for sz in [1, 2, 4, 8, 16, 32, 64]:
        up = pb.sb[SHIFT_ID_MAP[f"IU{sz}"]]
        dn = pb.sb[SHIFT_ID_MAP[f"ID{sz}"]]
        ku, kd = SHIFT_ID_MAP[f"IU{sz}"], SHIFT_ID_MAP[f"ID{sz}"]
        pb.op("vector", lambda em, up=up: em.memset(up[:, :], 0.0), writes=[ku])
        pb.op("vector", lambda em, dn=dn: em.memset(dn[:, :], 0.0), writes=[kd])
        pb.op("vector", lambda em, up=up, sz=sz: em.tensor_copy(
            up[:, sz:128], pb.sb["ident"][:, 0:128 - sz]),
            reads=["ident", ku], writes=[ku])
        pb.op("vector", lambda em, dn=dn, sz=sz: em.tensor_copy(
            dn[:, 0:128 - sz], pb.sb["ident"][:, sz:128]),
            reads=["ident", kd], writes=[kd])


def emit_butterfly_fold(pb, tag, loc_cols, cross_ap, w_src_ap, w_dst, transposed,
                        rk_loc, rk_cross, rk_w, wk_dst, bias_srcs=None,
                        bias_dst=None, bias_rks=None):
    xpn = [f"XP{d}" for d in range(8)]
    xp = [pb.sbuf(n, [128, 2048], BF16) for n in xpn]

    def blk(pair, b):
        return xp[pair * 2 + b // 16][:, (b % 16) * 128:(b % 16 + 1) * 128]

    def blkk(pair, b):
        return xpn[pair * 2 + b // 16]

    PT = pb.psum("PTB0", [128, 1024], BF16)
    PS0 = pb.psum("PS0", [128, 512], F32)
    PS1 = pb.psum("PS1", [128, 512], F32)
    lcbuf = pb.sbuf("SCORES", [128, 2048], F32)
    lc = lcbuf[:, 0:1120]
    pb.q_dma(lc, loc_cols[:, :], reads=[rk_loc], writes=["SCORES"])

    def build_stage(s, into_pair, transp):
        sz = 2 ** s
        kiu, kid = SHIFT_ID_MAP[f"IU{sz}"], SHIFT_ID_MAP[f"ID{sz}"]
        iu, idn = pb.sb[kiu], pb.sb[kid]
        kd, ku, kn = (0, 3, 4) if transp else (0, 1, 2)
        for b in range(32):
            t = blk(into_pair, b)
            tk = blkk(into_pair, b)
            cd = lc[:, s * 160 + kd * 32 + b:s * 160 + kd * 32 + b + 1]
            cu = lc[:, s * 160 + ku * 32 + b:s * 160 + ku * 32 + b + 1]
            cn = lc[:, s * 160 + kn * 32 + b:s * 160 + kn * 32 + b + 1]
            pb.op("vector", lambda em, t=t, cd=cd: em.tensor_scalar_mul(
                t, pb.sb["ident"][:, :], cd), reads=["ident", "SCORES"], writes=[tk])
            pb.op("vector", lambda em, t=t, cu=cu, iu=iu: em.scalar_tensor_tensor(
                t, iu[:, :], cu, t, ALU.mult, ALU.add),
                reads=[kiu, "SCORES", tk], writes=[tk])
            pb.op("vector", lambda em, t=t, cn=cn, idn=idn: em.scalar_tensor_tensor(
                t, idn[:, :], cn, t, ALU.mult, ALU.add),
                reads=[kid, "SCORES", tk], writes=[tk])

    build_stage(0, 1, transp=False)
    for k in range(1, 7):
        build_stage(k, 0, transp=True)
        for b in range(32):
            pb.mm(PS0[:, 0:128], blk(0, b), blk(1, b), start=True, stop=True,
                  reads=[blkk(0, b), blkk(1, b)], writes=["PS0"])
            pb.op("vector", lambda em, b=b: em.tensor_copy(blk(2, b), PS0[:, 0:128]),
                  reads=["PS0"], writes=[blkk(2, b)])
        for t in range(2):
            pb.op("vector", lambda em, t=t: em.tensor_copy(xp[2 + t][:, :], xp[4 + t][:, :]),
                  reads=[xpn[4 + t]], writes=[xpn[2 + t]])
    if not transposed:
        for b in range(32):
            pb.transp(PT[:, 0:128], blk(1, b), reads=[blkk(1, b)], writes=["PTB0"])
            pb.op("vector", lambda em, b=b: em.tensor_copy(blk(3, b), PT[:, 0:128]),
                  reads=["PTB0"], writes=[blkk(3, b)])
        lhs_pair = 3
    else:
        lhs_pair = 1

    G = [pb.sbuf(f"G{i}", [128, D], BF16) for i in range(32)]
    wtmp = pb.sbuf("WTMP", [128, D], BF16)
    for b in range(32):
        pb.g_dma(wtmp[:, :], w_src_ap[b * 128:(b + 1) * 128, :], reads=[rk_w],
                 writes=["WTMP"])
        if transposed:
            pb.op("vector", lambda em, b=b: em.tensor_copy(G[b][:, :], wtmp[:, :]),
                  reads=["WTMP"], writes=[f"G{b}"])
        else:
            for oh in range(2):
                ps = (PS0, "PS0") if oh == 0 else (PS1, "PS1")
                pb.mm(ps[0][:, :], blk(lhs_pair, b), wtmp[:, oh * 512:(oh + 1) * 512],
                      start=True, stop=True,
                      reads=[blkk(lhs_pair, b), "WTMP"], writes=[ps[1]])
                pb.op("vector", lambda em, b=b, oh=oh, ps=ps: em.tensor_copy(
                    G[b][:, oh * 512:(oh + 1) * 512], ps[0][:, :]),
                    reads=[ps[1]], writes=[f"G{b}"])

    cc = lcbuf[:, 1120:1440]
    pb.q_dma(cc, cross_ap[:, :], reads=[rk_cross], writes=["SCORES"])
    ctmp1 = pb.sbuf("CT1", [128, D], BF16)
    ctmp2 = pb.sbuf("CT2", [128, D], BF16)
    stages = [11, 10, 9, 8, 7] if transposed else [7, 8, 9, 10, 11]
    for s in stages:
        cs = 2 ** (s - 7)
        pairs = [c for c in range(32) if not (c & cs)]
        for pi, cA in enumerate(pairs):
            cB = cA | cs
            col = (s - 7) * 64 + pi * 4
            i01, i10 = (2, 1) if transposed else (1, 2)
            w00 = cc[:, col + 0:col + 1]
            w01 = cc[:, col + i01:col + i01 + 1]
            w10 = cc[:, col + i10:col + i10 + 1]
            w11 = cc[:, col + 3:col + 4]
            pb.op("vector", lambda em, cB=cB, w01=w01: em.tensor_scalar_mul(
                ctmp1[:, :], G[cB][:, :], w01), reads=[f"G{cB}", "SCORES"], writes=["CT1"])
            pb.op("vector", lambda em, cA=cA, w10=w10: em.tensor_scalar_mul(
                ctmp2[:, :], G[cA][:, :], w10), reads=[f"G{cA}", "SCORES"], writes=["CT2"])
            pb.op("vector", lambda em, cA=cA, w00=w00: em.scalar_tensor_tensor(
                G[cA][:, :], G[cA][:, :], w00, ctmp1[:, :], ALU.mult, ALU.add),
                reads=[f"G{cA}", "SCORES", "CT1"], writes=[f"G{cA}"])
            pb.op("vector", lambda em, cB=cB, w11=w11: em.scalar_tensor_tensor(
                G[cB][:, :], G[cB][:, :], w11, ctmp2[:, :], ALU.mult, ALU.add),
                reads=[f"G{cB}", "SCORES", "CT2"], writes=[f"G{cB}"])

    if transposed:
        for b in range(32):
            for oh in range(2):
                ps = (PS0, "PS0") if oh == 0 else (PS1, "PS1")
                pb.mm(ps[0][:, :], blk(lhs_pair, b), G[b][:, oh * 512:(oh + 1) * 512],
                      start=True, stop=True,
                      reads=[blkk(lhs_pair, b), f"G{b}"], writes=[ps[1]])
                pb.op("vector", lambda em, b=b, oh=oh, ps=ps: em.tensor_copy(
                    G[b][:, oh * 512:(oh + 1) * 512], ps[0][:, :]),
                    reads=[ps[1]], writes=[f"G{b}"])

    for b in range(32):
        pb.q_dma(w_dst[b * 128:(b + 1) * 128, :], G[b][:, :], reads=[f"G{b}"],
                 writes=[wk_dst])

    if bias_srcs is not None:
        b0, bup = bias_srcs
        rb0, rbup = bias_rks
        bcol = pb.sbuf("BCOL", [128, 32], F32)
        bcu = pb.sbuf("BCU", [128, 32], F32)
        pb.q_dma(bcol[:, :], b0[:, :], reads=[rb0], writes=["BCOL"])
        pb.q_dma(bcu[:, :], bup[:, :], reads=[rbup], writes=["BCU"])
        bb = pb.sbuf("BB", [128, 32], BF16)
        pb.op("vector", lambda em: em.tensor_copy(bb[:, :], bcol[:, :]),
              reads=["BCOL"], writes=["BB"])
        for b in range(32):
            pb.mm(PS0[:, 0:1], blk(3, b), bb[:, b:b + 1], start=True, stop=True,
                  reads=[blkk(3, b), "BB"], writes=["PS0"])
            pb.op("vector", lambda em, b=b: em.tensor_copy(bcol[:, b:b + 1], PS0[:, 0:1]),
                  reads=["PS0"], writes=["BCOL"])
        for s in [7, 8, 9, 10, 11]:
            cs = 2 ** (s - 7)
            pairs = [c for c in range(32) if not (c & cs)]
            for pi, cA in enumerate(pairs):
                cB = cA | cs
                col = (s - 7) * 64 + pi * 4
                pb.op("vector", lambda em, cB=cB, col=col: em.tensor_scalar_mul(
                    bb[:, 0:1], bcol[:, cB:cB + 1], cc[:, col + 1:col + 2]),
                    reads=["BCOL", "SCORES"], writes=["BB"])
                pb.op("vector", lambda em, cA=cA, col=col: em.tensor_scalar_mul(
                    bb[:, 1:2], bcol[:, cA:cA + 1], cc[:, col + 2:col + 3]),
                    reads=["BCOL", "SCORES"], writes=["BB"])
                pb.op("vector", lambda em, cA=cA, col=col: em.scalar_tensor_tensor(
                    bcol[:, cA:cA + 1], bcol[:, cA:cA + 1], cc[:, col:col + 1],
                    bb[:, 0:1], ALU.mult, ALU.add),
                    reads=["BCOL", "SCORES", "BB"], writes=["BCOL"])
                pb.op("vector", lambda em, cB=cB, col=col: em.scalar_tensor_tensor(
                    bcol[:, cB:cB + 1], bcol[:, cB:cB + 1], cc[:, col + 3:col + 4],
                    bb[:, 1:2], ALU.mult, ALU.add),
                    reads=["BCOL", "SCORES", "BB"], writes=["BCOL"])
        pb.op("vector", lambda em: em.tensor_add(bias_dst[:, :], bcol[:, :], bcu[:, :]),
              reads=["BCOL", "BCU"], writes=["BT32"])


def emit_L2(pb, inp, n_seq, S):
    T = n_seq * S
    nt = T // 128
    nib = S // 128
    xres = pb.dr["xres"]

    wexpF = pb.dram("l2_wexpF", [4 * D, D], BF16)
    wconF = pb.dram("l2_wconF", [4 * D, D], BF16)
    bt32 = pb.sbuf("BT32", [128, 32], F32)
    emit_shift_idents(pb)
    emit_butterfly_fold(pb, "u", inp["l2_bfu_loc"], inp["l2_bfu_cross"],
                        inp["l2_wexpT"], wexpF, False,
                        "in_bfu_loc", "in_bfu_cross", "in_wexpT", "l2_wexpF",
                        bias_srcs=(inp["l2_bexpT"], inp["l2_bupT"]),
                        bias_dst=bt32, bias_rks=("in_bexpT", "in_bupT"))
    emit_butterfly_fold(pb, "d", inp["l2_bfd_loc"], inp["l2_bfd_cross"],
                        inp["L2_ffn_W_con"], wconF, True,
                        "in_bfd_loc", "in_bfd_cross", "in_wcon", "l2_wconF")

    wpre = pb.sbuf("WPRE", [128, 128], BF16)
    wpost = pb.sbuf("WPOST", [128, 128], BF16)
    for nm, src_ap, rk in (("WPRE", inp["l2_wpreT"], "in_wpreT"),
                           ("WPOST", inp["l2_wpostT"], "in_wpostT")):
        t = pb.sb[nm]
        pb.op("vector", lambda em, t=t: em.memset(t[:, :], 0.0), writes=[nm])
        for b in range(8):
            pb.g_dma(t[b * 16:(b + 1) * 16, b * 16:(b + 1) * 16],
                     src_ap[:, :], reads=[rk], writes=[nm])

    xn_f, xn_b = emit_ln(pb, "l2", "xres", xres, T)

    qhm = pb.dram("l2_qhm", [D, T], BF16)
    khm = pb.dram("l2_khm", [D, T], BF16)
    vT = pb.dram("l2_vT", [D, T], BF16)
    qraw = pb.dram("l2_qraw", [D, T], BF16)
    kraw = pb.dram("l2_kraw", [D, T], BF16)
    Sm = pb.dram("l2_sm", [T // 8 * 128, 2048], BF16)
    S3 = pb.dram("l2_s3", [16, 2048, 2048], BF16)
    hT = pb.dram("l2_hT", [32, 128, T], BF16)
    s_scr = pb.dram("s_l2", [T, D], F32)
    g_scr = pb.dram("g_l2", [T, D], F32)
    PS = [pb.psum(f"PS{i}", [128, 512], F32) for i in range(6)]
    PT = pb.sb["PTB0"]

    invc = pb.sbuf("INVC", [128, 1], F32)
    pb.op("gpsimd", lambda em: em.iota(invc[0:32, :], [[1, 1]], base=0,
                                       channel_multiplier=1,
                                       allow_small_or_imprecise_dtypes=True),
          writes=["INVC"])
    pb.op("scalar", lambda em: em.activation(invc[0:32, :], invc[0:32, :], AF.Exp,
                                             scale=-math.log(10000.0) / 32.0),
          reads=["INVC"], writes=["INVC"])
    for rep in range(1, 4):
        pb.g_dma(invc[rep * 32:(rep + 1) * 32, :], invc[0:32, :], reads=["INVC"],
                 writes=["INVC"])
    TWO_PI = 2.0 * math.pi

    def emit_rope_tables(cosb, sinb):
        fr = pb.sbuf("FP0", [128, D], F32)
        ftmp = pb.sbuf("FP2", [128, D], F32)

        def wrap_pi():
            pb.op("vector", lambda em: em.tensor_scalar(
                ftmp[:, :], fr[:, :], math.pi, -TWO_PI, ALU.is_ge, ALU.mult),
                reads=["FP0"], writes=["FP2"])
            pb.op("vector", lambda em: em.tensor_add(fr[:, :], fr[:, :], ftmp[:, :]),
                  reads=["FP0", "FP2"], writes=["FP0"])

        for hh_ in range(max(1, S // 512)):
            w_ = min(S, 512)
            csl = slice(hh_ * 512, hh_ * 512 + w_)
            pb.op("gpsimd", lambda em, hh_=hh_, w_=w_: em.iota(
                fr[:, 0:w_], [[1, w_]], base=hh_ * 512, channel_multiplier=0,
                allow_small_or_imprecise_dtypes=True), writes=["FP0"])
            pb.op("vector", lambda em, w_=w_: em.tensor_scalar_mul(
                fr[:, 0:w_], fr[:, 0:w_], invc[:, 0:1]),
                reads=["FP0", "INVC"], writes=["FP0"])
            pb.op("vector", lambda em, w_=w_: em.tensor_scalar(
                ftmp[:, 0:w_], fr[:, 0:w_], 1.0 / TWO_PI, 8388608.0 - 0.5,
                ALU.mult, ALU.add), reads=["FP0"], writes=["FP2"])
            pb.op("vector", lambda em, w_=w_: em.tensor_single_scalar(
                ftmp[:, 0:w_], ftmp[:, 0:w_], -8388608.0, ALU.add),
                reads=["FP2"], writes=["FP2"])
            pb.op("vector", lambda em, w_=w_: em.scalar_tensor_tensor(
                fr[:, 0:w_], ftmp[:, 0:w_], -TWO_PI, fr[:, 0:w_], ALU.mult, ALU.add),
                reads=["FP2", "FP0"], writes=["FP0"])
            wrap_pi()
            pb.op("scalar", lambda em, csl=csl, w_=w_: em.activation(
                sinb[:, csl], fr[:, 0:w_], AF.Sin), reads=["FP0"], writes=["XP1"])
            pb.op("vector", lambda em, w_=w_: em.tensor_single_scalar(
                fr[:, 0:w_], fr[:, 0:w_], math.pi / 2.0, ALU.add),
                reads=["FP0"], writes=["FP0"])
            wrap_pi()
            pb.op("scalar", lambda em, csl=csl, w_=w_: em.activation(
                cosb[:, csl], fr[:, 0:w_], AF.Sin), reads=["FP0"], writes=["XP0"])

    bqkvT = pb.sbuf("BQKVT", [128, 24], F32)
    pb.q_dma(bqkvT[:, :], inp["l2_bqkvT"][:, :], reads=["in_bqkvT"], writes=["BQKVT"])

    for seq in range(n_seq):
        emit_load_xnT(pb, "l2", xn_b, seq, S, T)
        xp = [pb.sb[f"XP{d}"] for d in range(8)]
        xpk = [f"XP{d}" for d in range(8)]
        ssl = slice(seq * S, (seq + 1) * S)

        wq = [pb.sbuf(f"G{24+i}", [128, D], BF16) for i in range(2)]
        ev = pb.sbuf("E0", [128, 512], BF16)
        for m in range(24):
            wqt = wq[m % 2]
            wqk = f"G{24 + m % 2}"
            for d in range(8):
                pb.g_dma(wqt[:, d * 128:(d + 1) * 128],
                         inp["l2_wqkvT"][d * 128:(d + 1) * 128, m * 128:(m + 1) * 128],
                         reads=["in_wqkvT"], writes=[wqk])
            for tg in range(S // 512):
                tsl = slice(tg * 512, (tg + 1) * 512)
                ps = PS[4 + tg % 2]
                psk = f"PS{4 + tg % 2}"
                for d in range(8):
                    pb.mm(ps[:, :], wqt[:, d * 128:(d + 1) * 128], xp[d][:, tsl],
                          start=(d == 0), stop=(d == 7), reads=[wqk, xpk[d]],
                          writes=[psk])
                pb.op("scalar", lambda em, ps=ps, m=m: em.activation(
                    ev[:, :], ps[:, :], AF.Identity, bias=bqkvT[:, m:m + 1]),
                    reads=[psk, "BQKVT"], writes=["E0"])
                gcol = seq * S + tg * 512
                if m < 16:
                    dst = qraw if m < 8 else kraw
                    dk = "l2_qraw" if m < 8 else "l2_kraw"
                    mm = m % 8
                    for half in range(2):
                        for hh in range(2):
                            rbase = hh * 64 + half * 32
                            drow = half * 512 + 64 * mm + hh * 32
                            pb.q_dma(dst[drow:drow + 32, gcol:gcol + 512],
                                     ev[rbase:rbase + 32, :],
                                     reads=["E0"], writes=[f"{dk}@{seq}"])
                else:
                    pb.q_dma(vT[(m - 16) * 128:(m - 15) * 128, gcol:gcol + 512],
                             ev[:, :], reads=["E0"], writes=[f"l2_vT@{seq}"])

        cosb = pb.sb["XP0"]
        sinb = pb.sb["XP1"]
        emit_rope_tables(cosb, sinb)
        rw = [pb.sbuf(f"XP{4+i}", [128, 2048], BF16) for i in range(4)]
        rwk = [f"XP{4+i}" for i in range(4)]
        for (raw, hm, rkk, wkk, scaled) in ((qraw, qhm, "l2_qraw", "l2_qhm", True),
                                            (kraw, khm, "l2_kraw", "l2_khm", False)):
            for rc in range(4):
                pb.q_dma(rw[0][:, 0:S], raw[rc * 128:(rc + 1) * 128, ssl],
                         reads=[f"{rkk}@{seq}"], writes=[rwk[0]])
                pb.q_dma(rw[1][:, 0:S], raw[512 + rc * 128:512 + (rc + 1) * 128, ssl],
                         reads=[f"{rkk}@{seq}"], writes=[rwk[1]])
                pb.op("vector", lambda em: em.tensor_mul(rw[2][:, 0:S], rw[0][:, 0:S],
                                                         cosb[:, 0:S]),
                      reads=[rwk[0], "XP0"], writes=[rwk[2]])
                pb.op("vector", lambda em: em.tensor_mul(rw[3][:, 0:S], rw[1][:, 0:S],
                                                         sinb[:, 0:S]),
                      reads=[rwk[1], "XP1"], writes=[rwk[3]])
                pb.op("vector", lambda em: em.tensor_sub(rw[2][:, 0:S], rw[2][:, 0:S],
                                                         rw[3][:, 0:S]),
                      reads=[rwk[2], rwk[3]], writes=[rwk[2]])
                if scaled:
                    pb.op("vector", lambda em: em.tensor_single_scalar(
                        rw[2][:, 0:S], rw[2][:, 0:S], 0.125, ALU.mult),
                        reads=[rwk[2]], writes=[rwk[2]])
                pb.op("vector", lambda em: em.tensor_mul(rw[3][:, 0:S], rw[0][:, 0:S],
                                                         sinb[:, 0:S]),
                      reads=[rwk[0], "XP1"], writes=[rwk[3]])
                pb.op("vector", lambda em: em.tensor_mul(rw[0][:, 0:S], rw[1][:, 0:S],
                                                         cosb[:, 0:S]),
                      reads=[rwk[1], "XP0"], writes=[rwk[0]])
                pb.op("vector", lambda em: em.tensor_add(rw[3][:, 0:S], rw[3][:, 0:S],
                                                         rw[0][:, 0:S]),
                      reads=[rwk[3], rwk[0]], writes=[rwk[3]])
                if scaled:
                    pb.op("vector", lambda em: em.tensor_single_scalar(
                        rw[3][:, 0:S], rw[3][:, 0:S], 0.125, ALU.mult),
                        reads=[rwk[3]], writes=[rwk[3]])
                for half, til, tk in ((0, rw[2], rwk[2]), (1, rw[3], rwk[3])):
                    for a4 in range(4):
                        base = (256 * rc + 32 * half + 64 * a4) * T + seq * S
                        pb.q_dma(bass.AP(hm.tensor, base, [[T, 32], [1, S]]),
                                 til[a4 * 32:(a4 + 1) * 32, 0:S],
                                 reads=[tk], writes=[f"{wkk}@{seq}"])

        khm_sb = [pb.sbuf(f"G{i}", [128, D], BF16) for i in range(16)]
        nhalf = max(1, S // 1024)
        hw = min(S, 1024)
        for c in range(8):
            for jh in range(nhalf):
                pb.q_dma(khm_sb[2 * c + jh][:, 0:hw],
                         khm[c * 128:(c + 1) * 128,
                             seq * S + jh * 1024:seq * S + jh * 1024 + hw],
                         reads=[f"l2_khm@{seq}"], writes=[f"G{2*c+jh}"])
        qi = [pb.sbuf(f"QI{d}", [128, 128], BF16) for d in range(8)]
        lb = [pb.sbuf("E0", [128, 512], BF16), pb.sbuf("QOUT", [128, 512], BF16)]
        for ib in range(nib):
            for d in range(8):
                pb.q_dma(qi[d][:, :],
                         qhm[d * 128:(d + 1) * 128,
                             seq * S + ib * 128:seq * S + (ib + 1) * 128],
                         reads=[f"l2_qhm@{seq}"], writes=[f"QI{d}"])
            for h in range(16):
                lhs = qi[h // 2][(h % 2) * 64:(h % 2) * 64 + 64, :]
                for jg in range(S // 512):
                    ps = PS[4 + jg % 2]
                    psk = f"PS{4 + jg % 2}"
                    jh = (jg * 512) // 1024
                    rhs = khm_sb[2 * (h // 2) + jh][
                        (h % 2) * 64:(h % 2) * 64 + 64,
                        jg * 512 - jh * 1024:jg * 512 - jh * 1024 + 512]
                    pb.mm(ps[:, :], lhs, rhs, start=True, stop=True,
                          reads=[f"QI{h//2}", f"G{2*(h//2)+jh}"], writes=[psk])
                    e = lb[jg % 2]
                    ek = "E0" if jg % 2 == 0 else "QOUT"
                    eng = "vector" if jg % 2 == 0 else "scalar"
                    if eng == "vector":
                        pb.op(eng, lambda em, e=e, ps=ps: em.tensor_copy(e[:, :], ps[:, :]),
                              reads=[psk], writes=[ek])
                    else:
                        pb.op(eng, lambda em, e=e, ps=ps: em.copy(e[:, :], ps[:, :]),
                              reads=[psk], writes=[ek])
                    base = seq * 16 * S * S + h * S * S + (ib * 128) * S + jg * 512
                    pb.q_dma(bass.AP(Sm.tensor, base, [[S, 128], [1, 512]]),
                             e[:, :], reads=[ek], writes=[f"l2_sm@{seq}"])

        rot = [
            (pb.sbuf("PBUF", [128, 2048], BF16), "PBUF",
             pb.sbuf("PBUF2", [128, 2048], BF16), "PBUF2"),
            (pb.sbuf("XP0", [128, 2048], BF16), "XP0",
             pb.sbuf("XP1", [128, 2048], BF16), "XP1"),
            (pb.sbuf("XP2", [128, 2048], BF16), "XP2",
             pb.sbuf("XP3", [128, 2048], BF16), "XP3"),
        ]
        stats = [(pb.sbuf(f"SCOL{r}", [128, 4], F32),
                  pb.sbuf(f"SSUM{r}", [128, 1], F32),
                  pb.sbuf(f"SRCP{r}", [128, 1], F32)) for r in range(3)]
        NROTB = 3
        for gi in range(S // 8):
            strip, bufk, pstr, pstk = rot[gi % NROTB]
            scol, ssum, srcp = stats[gi % NROTB]
            sck, ssk, srk = f"SCOL{gi % NROTB}", f"SSUM{gi % NROTB}", f"SRCP{gi % NROTB}"
            sbase = seq * 16 * S * S
            for is_ in range(8):
                pb.q_dma(strip[is_ * 16:(is_ + 1) * 16, 0:S],
                         bass.AP(Sm.tensor, sbase + (gi * 8 + is_) * S,
                                 [[S * S, 16], [1, S]]),
                         reads=[f"l2_sm@{seq}"], writes=[bufk])
            for jg in range(S // 512):
                ps = PS[jg % 2]
                psk = f"PS{jg % 2}"
                pb.mm(ps[:, :], wpre[:, :], strip[:, jg * 512:(jg + 1) * 512],
                      start=True, stop=True, reads=["WPRE", bufk], writes=[psk])
                pb.op("scalar", lambda em, jg=jg, ps=ps, pstr=pstr, scol=scol:
                      em.activation(
                          pstr[:, jg * 512:(jg + 1) * 512], ps[:, :], AF.Exp,
                          accum_out=scol[:, jg:jg + 1]),
                      reads=[psk], writes=[pstk, sck])
            pb.op("vector", lambda em, ssum=ssum, scol=scol: em.reduce_sum(
                ssum[:, :], scol[:, 0:S // 512], AX.X),
                reads=[sck], writes=[ssk])
            pb.op("vector", lambda em, srcp=srcp, ssum=ssum: em.reciprocal(
                srcp[:, :], ssum[:, :]),
                reads=[ssk], writes=[srk])
            pb.op("vector", lambda em, pstr=pstr, srcp=srcp: em.tensor_scalar_mul(
                pstr[:, 0:S], pstr[:, 0:S], srcp[:, 0:1]),
                reads=[pstk, srk], writes=[pstk])
            for jg in range(S // 512):
                ps = PS[2 + jg % 2]
                psk = f"PS{2 + jg % 2}"
                pb.mm(ps[:, :], wpost[:, :], pstr[:, jg * 512:(jg + 1) * 512],
                      start=True, stop=True, reads=["WPOST", pstk], writes=[psk])
                eng = "vector" if jg % 2 == 0 else "scalar"
                if eng == "vector":
                    pb.op(eng, lambda em, jg=jg, ps=ps, strip=strip: em.tensor_copy(
                        strip[:, jg * 512:(jg + 1) * 512], ps[:, :]),
                        reads=[psk], writes=[bufk])
                else:
                    pb.op(eng, lambda em, jg=jg, ps=ps, strip=strip: em.copy(
                        strip[:, jg * 512:(jg + 1) * 512], ps[:, :]),
                        reads=[psk], writes=[bufk])
            for is_ in range(8):
                base = (gi * 8 + is_) * 2048
                pb.q_dma(bass.AP(S3.tensor, base, [[2048 * 2048, 16], [1, S]]),
                         strip[is_ * 16:(is_ + 1) * 16, 0:S],
                         reads=[bufk], writes=[f"l2_s3@{seq}"])

        for hp in range(8):
            for jb in range(nib):
                pb.q_dma(xp[hp][:, jb * 128:(jb + 1) * 128],
                         vT[hp * 128:(hp + 1) * 128,
                            seq * S + jb * 128:seq * S + (jb + 1) * 128],
                         reads=[f"l2_vT@{seq}"], writes=[xpk[hp]], transpose=True)
        for d in range(8):
            g = pb.sbuf(f"G{16+d}", [128, D], BF16)
            pb.g_dma(g[:, :], inp["l2_woutT"][d * 128:(d + 1) * 128, :],
                     reads=["in_woutT"], writes=[f"G{16+d}"])
        brow = pb.sbuf("BROW", [1, D], BF16)
        pb.g_dma(brow[:, :], inp["L2_attn_bout"][0:1, :], reads=["in_bout2"],
                 writes=["BROW"])

        rt = [pb.sbuf("E0", [128, 512], BF16), pb.sbuf("QOUT", [128, 512], BF16)]
        _pb1 = pb.sbuf("PBUF", [128, 2048], BF16)
        _pb2 = pb.sbuf("PBUF2", [128, 2048], BF16)
        ot = [(_pb1 if c < 4 else _pb2)[:, (c % 4) * 512:(c % 4 + 1) * 512]
              for c in range(8)]
        otk = ["PBUF" if c < 4 else "PBUF2" for c in range(8)]
        stile = pb.sbuf("STILE", [128, 1024], F32)
        for ig in range(S // 512):
            for g in range(16):
                ps = PS[5]
                for jb in range(nib):
                    r = rt[jb % 2]
                    rk_ = "E0" if jb % 2 == 0 else "QOUT"
                    pb.q_dma(r[:, :],
                             S3[g, ig * 512:(ig + 1) * 512, jb * 128:(jb + 1) * 128],
                             reads=[f"l2_s3@{seq}"], writes=[rk_], transpose=True)
                    pb.mm(ps[0:64, :],
                          xp[g // 2][:, jb * 128 + (g % 2) * 64:jb * 128 + (g % 2) * 64 + 64],
                          r[:, :], start=(jb == 0), stop=(jb == nib - 1),
                          reads=[xpk[g // 2], rk_], writes=["PS5"])
                pb.op("vector", lambda em, g=g, ps=ps: em.tensor_copy(
                    ot[g // 2][(g % 2) * 64:(g % 2) * 64 + 64, :], ps[0:64, :]),
                    reads=["PS5"], writes=[otk[g // 2]])
            for isub in range(4):
                for oh in range(2):
                    ps = PS[oh]
                    psk = f"PS{oh}"
                    osl = slice(oh * 512, (oh + 1) * 512)
                    pb.mm(ps[:, :], pb.sb["onescol"][:, :], brow[:, osl],
                          start=True, stop=False, reads=["onescol", "BROW"], writes=[psk])
                    for c in range(8):
                        pb.mm(ps[:, :], ot[c][:, isub * 128:(isub + 1) * 128],
                              pb.sb[f"G{16+c}"][:, osl], start=False, stop=(c == 7),
                              reads=[otk[c], f"G{16+c}"], writes=[psk])
                    if oh == 0:
                        pb.op("vector", lambda em, ps=ps: em.tensor_copy(
                            stile[:, 0:512], ps[:, :]), reads=[psk], writes=["STILE"])
                    else:
                        pb.op("scalar", lambda em, ps=ps: em.copy(
                            stile[:, 512:1024], ps[:, :]), reads=[psk], writes=["STILE"])
                t0 = seq * S + ig * 512 + isub * 128
                pb.q_dma(s_scr[t0:t0 + 128, :], stile[:, :], reads=["STILE"],
                         writes=[f"s_l2@{t0 // 128}"])

        emit_load_xnT(pb, "l2", xn_b, seq, S, T)
        hf32 = pb.sbuf("HF32", [128, 512], F32)
        hbf = pb.sbuf("HBF", [128, 512], BF16)
        for fc in range(32):
            for d in range(8):
                pb.q_dma(qi[d][:, :],
                         wexpF[fc * 128:(fc + 1) * 128, d * 128:(d + 1) * 128],
                         reads=["l2_wexpF"], writes=[f"QI{d}"], transpose=True)
            for tg in range(S // 512):
                tsl = slice(tg * 512, (tg + 1) * 512)
                ps = PS[4 + tg % 2]
                psk = f"PS{4 + tg % 2}"
                for d in range(8):
                    pb.mm(ps[:, :], qi[d][:, :], xp[d][:, tsl],
                          start=(d == 0), stop=(d == 7), reads=[f"QI{d}", xpk[d]],
                          writes=[psk])
                pb.op("scalar", lambda em, ps=ps, fc=fc: em.activation(
                    hbf[:, :], ps[:, :], AF.Gelu, bias=bt32[:, fc:fc + 1]),
                    reads=[psk, "BT32"], writes=["HBF"])
                pb.q_dma(hT[fc, :, seq * S + tg * 512:seq * S + (tg + 1) * 512],
                         hbf[:, :], reads=["HBF"], writes=[f"l2_hT@{fc}"])
        for fc in range(32):
            g = pb.sbuf(f"G{fc}", [128, D], BF16)
            pb.q_dma(g[:, :], wconF[fc * 128:(fc + 1) * 128, :], reads=["l2_wconF"],
                     writes=[f"G{fc}"])
        brow2 = pb.sbuf("PBUF2", [128, 2048], BF16)[0:1, 0:D]
        brow3 = pb.sb["PBUF2"][0:1, D:2 * D]
        pb.g_dma(brow2, inp["l2_bcon"][0:1, :], reads=["in_bcon"], writes=["PBUF2"])
        pb.g_dma(brow3, inp["l2_dbias"][0:1, :], reads=["in_dbias"], writes=["PBUF2"])
        ht0 = pb.sbuf("HT0", [128, 128], BF16)
        ht1 = pb.sbuf("HT1", [128, 128], BF16)
        yev = pb.sbuf("STILE", [128, 1024], F32)
        for tb in range(nib):
            t0 = seq * S + tb * 128
            for oh in range(2):
                osl = slice(oh * 512, (oh + 1) * 512)
                pb.mm(PS[oh][:, :], pb.sb["onescol"][:, :], brow2[:, osl],
                      start=True, stop=False, reads=["onescol", "PBUF2"], writes=[f"PS{oh}"])
                pb.mm(PS[oh][:, :], pb.sb["onescol"][:, :], brow3[:, osl],
                      start=False, stop=False, reads=["onescol", "PBUF2"], writes=[f"PS{oh}"])
            for fc in range(32):
                ht = (ht0, "HT0") if fc % 2 == 0 else (ht1, "HT1")
                pb.q_dma(ht[0][:, :], hT[fc, :, t0:t0 + 128], reads=[f"l2_hT@{fc}"],
                         writes=[ht[1]])
                for oh in range(2):
                    pb.mm(PS[oh][:, :], ht[0][:, :],
                          pb.sb[f"G{fc}"][:, oh * 512:(oh + 1) * 512],
                          start=False, stop=(fc == 31),
                          reads=[ht[1], f"G{fc}"], writes=[f"PS{oh}"])
            pb.op("vector", lambda em: em.tensor_copy(yev[:, 0:512], PS[0][:, :]),
                  reads=["PS0"], writes=["STILE"])
            pb.op("vector", lambda em: em.tensor_copy(yev[:, 512:1024], PS[1][:, :]),
                  reads=["PS1"], writes=["STILE"])
            pb.q_dma(g_scr[t0:t0 + 128, :], yev[:, :], reads=["STILE"],
                     writes=[f"g_l2@{t0 // 128}"])

    emit_combine(pb, "l2", xres, s_scr, g_scr, T)


def prep_weights_L2(w):
    out = {}
    out["l2_wqkvT"] = np.ascontiguousarray(np.asarray(w["L2_attn_Wqkv"]).T)
    out["l2_bqkvT"] = np.ascontiguousarray(np.asarray(w["L2_attn_bqkv"]).reshape(24, 128).T)
    out["l2_wpreT"] = np.ascontiguousarray(np.asarray(w["L2_attn_Wpre"]).T)
    out["l2_wpostT"] = np.ascontiguousarray(np.asarray(w["L2_attn_Wpost"]).T)
    out["l2_woutT"] = np.ascontiguousarray(np.asarray(w["L2_attn_Wout"]).T)
    out["L2_attn_bout"] = np.ascontiguousarray(np.asarray(w["L2_attn_bout"]).reshape(1, D))
    out["l2_wexpT"] = np.ascontiguousarray(np.asarray(w["L2_ffn_W_exp"]).T)
    out["L2_ffn_W_con"] = np.ascontiguousarray(np.asarray(w["L2_ffn_W_con"]))
    out["l2_bexpT"] = np.ascontiguousarray(np.asarray(w["L2_ffn_b_exp"]).reshape(32, 128).T)
    out["l2_bupT"] = np.ascontiguousarray(np.asarray(w["L2_ffn_up_bias"]).reshape(32, 128).T)
    out["l2_bcon"] = np.ascontiguousarray(np.asarray(w["L2_ffn_b_con"]).reshape(1, D))
    out["l2_dbias"] = np.ascontiguousarray(np.asarray(w["L2_ffn_down_bias"]).reshape(1, D))
    for tag, wname in (("u", "L2_ffn_up_weights"), ("d", "L2_ffn_down_weights")):
        bw = np.asarray(w[wname])
        loc = np.zeros((128, 1120), np.float32)
        for s in range(7):
            sz = 2 ** s
            ww = bw[s]
            for blk in range(32):
                cd = np.zeros(128, np.float32)
                cu = np.zeros(128, np.float32)
                cn = np.zeros(128, np.float32)
                cuT = np.zeros(128, np.float32)
                cnT = np.zeros(128, np.float32)
                for p in range(128):
                    f = blk * 128 + p
                    q_, rr = f // (2 * sz), f % (2 * sz)
                    if rr < sz:
                        k = q_ * sz + rr
                        cd[p] = ww[k, 0, 0]
                        cu[p] = ww[k, 0, 1]
                        cuT[p] = ww[k, 1, 0]
                    else:
                        k = q_ * sz + (rr - sz)
                        cd[p] = ww[k, 1, 1]
                        cn[p] = ww[k, 1, 0]
                        cnT[p] = ww[k, 0, 1]
                loc[:, s * 160 + 0 * 32 + blk] = cd
                loc[:, s * 160 + 1 * 32 + blk] = cu
                loc[:, s * 160 + 2 * 32 + blk] = cn
                loc[:, s * 160 + 3 * 32 + blk] = cuT
                loc[:, s * 160 + 4 * 32 + blk] = cnT
        out[f"l2_bf{tag}_loc"] = loc
        cross = np.zeros((128, 320), np.float32)
        for s in range(7, 12):
            sz = 2 ** s
            cs = 2 ** (s - 7)
            ww = bw[s]
            pairs = [c for c in range(32) if not (c & cs)]
            for pi, cA in enumerate(pairs):
                for e in range(128):
                    a = cA * 128 + e
                    k = (a // (2 * sz)) * sz + a % sz
                    col = (s - 7) * 64 + pi * 4
                    cross[e, col + 0] = ww[k, 0, 0]
                    cross[e, col + 1] = ww[k, 0, 1]
                    cross[e, col + 2] = ww[k, 1, 0]
                    cross[e, col + 3] = ww[k, 1, 1]
        out[f"l2_bf{tag}_cross"] = cross
    return out


def declare_inputs_L2(pb):
    inp = {}
    inp["l2_wqkvT"] = pb.dram("l2_wqkvT", [D, 3 * D], F32, kind="ExternalInput")
    inp["l2_bqkvT"] = pb.dram("l2_bqkvT", [128, 24], F32, kind="ExternalInput")
    inp["l2_wpreT"] = pb.dram("l2_wpreT", [16, 16], F32, kind="ExternalInput")
    inp["l2_wpostT"] = pb.dram("l2_wpostT", [16, 16], F32, kind="ExternalInput")
    inp["l2_woutT"] = pb.dram("l2_woutT", [D, D], F32, kind="ExternalInput")
    inp["L2_attn_bout"] = pb.dram("L2_attn_bout", [1, D], F32, kind="ExternalInput")
    inp["l2_wexpT"] = pb.dram("l2_wexpT", [4 * D, D], F32, kind="ExternalInput")
    inp["L2_ffn_W_con"] = pb.dram("L2_ffn_W_con", [4 * D, D], F32, kind="ExternalInput")
    inp["l2_bexpT"] = pb.dram("l2_bexpT", [128, 32], F32, kind="ExternalInput")
    inp["l2_bupT"] = pb.dram("l2_bupT", [128, 32], F32, kind="ExternalInput")
    inp["l2_bcon"] = pb.dram("l2_bcon", [1, D], F32, kind="ExternalInput")
    inp["l2_dbias"] = pb.dram("l2_dbias", [1, D], F32, kind="ExternalInput")
    inp["l2_bfu_loc"] = pb.dram("l2_bfu_loc", [128, 1120], F32, kind="ExternalInput")
    inp["l2_bfd_loc"] = pb.dram("l2_bfd_loc", [128, 1120], F32, kind="ExternalInput")
    inp["l2_bfu_cross"] = pb.dram("l2_bfu_cross", [128, 320], F32, kind="ExternalInput")
    inp["l2_bfd_cross"] = pb.dram("l2_bfd_cross", [128, 320], F32, kind="ExternalInput")
    return inp




def emit_L1(pb, inp, n_seq, S):
    T = n_seq * S
    nt = T // 128
    nib = S // 128
    H, DK = 4, 256
    xres = pb.dr["xres"]
    xn_f, xn_b = emit_ln(pb, "l1", "xres", xres, T)

    qT = pb.dram("l1_qT", [D, T], BF16)
    kT = pb.dram("l1_kT", [D, T], BF16)
    vT = pb.dram("l1_vT", [D, T], BF16)
    beta_scr = pb.dram("l1_beta", [T, 4], F32)
    od_scr = pb.dram("l1_od", [T, D], BF16)
    odf_scr = pb.dram("l1_odf", [D, T], BF16)
    ps_scr = pb.dram("l1_ps", [D, T], BF16)
    plg_scr = pb.dram("l1_pl", [D, T], BF16)
    gin_scr = pb.dram("l1_gin", [64, T], F32)
    o_scr = pb.dram("l1_o", [T, D], BF16)
    xnT_scr = pb.dram("l1_xnT", [D, T], BF16)
    acc_scr = pb.dram("l1_acc", [T, D], F32)
    s_scr = pb.dram("s_l1", [T, D], F32)
    g_scr = pb.dram("g_l1", [T, D], F32)
    PS = [pb.psum(f"PS{i}", [128, 512], F32) for i in range(6)]
    PT = pb.psum("PTB0", [128, 1024], BF16)
    ident = pb.sb["ident"]
    pb.sbuf("STILE", [128, 1024], F32)
    pb.sbuf("HF32", [128, 512], F32)

    for nm, cm, pat, thr in (("M01SL", 1, -1, 1.0),
                             ("M01SU", -1, 1, 1.0),
                             ("M01IU", -1, 1, 0.0)):
        t = pb.sbuf(nm, [128, 128], BF16)
        tf = pb.sbuf("identf", [128, 128], F32)
        pb.op("gpsimd", lambda em, tf=tf, cm=cm, pat=pat: em.iota(
            tf[:, :], [[pat, 128]], base=0, channel_multiplier=cm,
            allow_small_or_imprecise_dtypes=True), writes=["identf"])
        pb.op("vector", lambda em, t=t, tf=tf, thr=thr: em.tensor_single_scalar(
            t[:, :], tf[:, :], thr, ALU.is_ge), reads=["identf"], writes=[nm])
    pb.op("gpsimd", lambda em: em.iota(pb.sb["identf"][:, :], [[1, 128]], base=0,
                                       channel_multiplier=-1,
                                       allow_small_or_imprecise_dtypes=True),
          writes=["identf"])
    pb.op("vector", lambda em: em.tensor_single_scalar(
        pb.sb["identf"][:, :], pb.sb["identf"][:, :], 0.0, ALU.is_equal),
        reads=["identf"], writes=["identf"])

    cq = pb.sbuf("CQW", [128, 96], F32)
    pb.q_dma(cq[:, 0:32], inp["l1_cqT"][:, :], reads=["in_cq"], writes=["CQW"])
    pb.q_dma(cq[:, 32:64], inp["l1_ckT"][:, :], reads=["in_ck"], writes=["CQW"])
    pb.q_dma(cq[:, 64:96], inp["l1_cvT"][:, :], reads=["in_cv"], writes=["CQW"])
    firs = pb.sbuf("FIRS", [128, 40], F32)
    firl = pb.sbuf("FIRL", [128, 504], F32)
    pb.q_dma(firs[:, :], inp["l1_firS"][:, :], reads=["in_firS"], writes=["FIRS"])
    pb.q_dma(firl[:, :], inp["l1_firL"][:, :], reads=["in_firL"], writes=["FIRL"])

    convr = pb.sbuf("CONVR", [128, 2112], BF16)
    conva = pb.sbuf("CONVA", [128, 2048], F32)
    ev = pb.sbuf("E0", [128, 512], BF16)

    for seq in range(n_seq):
        emit_load_xnT(pb, "l1", xn_b, seq, S, T)
        xp = [pb.sb[f"XP{d}"] for d in range(8)]
        xpk = [f"XP{d}" for d in range(8)]
        for d in range(8):
            pb.q_dma(xnT_scr[d * 128:(d + 1) * 128, seq * S:(seq + 1) * S],
                     xp[d][:, 0:S], reads=[xpk[d]], writes=["l1_xnT"])

        for pi, (wkey, rkw, dst, dk_, coff) in enumerate(
                (("l1_wqT", "in_wq", qT, "l1_qT", 0),
                 ("l1_wkT", "in_wk", kT, "l1_kT", 32),
                 ("l1_wvT", "in_wv", vT, "l1_vT", 64))):
            for d in range(8):
                g = pb.sbuf(f"G{d}", [128, D], BF16)
                pb.g_dma(g[:, :], inp[wkey][d * 128:(d + 1) * 128, :],
                         reads=[rkw], writes=[f"G{d}"])
            for c in range(8):
                pb.op("vector", lambda em: em.memset(convr[:, 0:3], 0.0), writes=["CONVR"])
                for tg in range(S // 512):
                    ps = PS[4 + tg % 2]
                    psk = f"PS{4 + tg % 2}"
                    for d in range(8):
                        pb.mm(ps[:, :], pb.sb[f"G{d}"][:, c * 128:(c + 1) * 128],
                              xp[d][:, tg * 512:(tg + 1) * 512],
                              start=(d == 0), stop=(d == 7),
                              reads=[f"G{d}", xpk[d]], writes=[psk])
                    pb.op("vector", lambda em, tg=tg, ps=ps: em.tensor_copy(
                        convr[:, 3 + tg * 512:3 + (tg + 1) * 512], ps[:, :]),
                        reads=[psk], writes=["CONVR"])
                pb.op("vector", lambda em, c=c, coff=coff: em.tensor_scalar_mul(
                    conva[:, 0:S], convr[:, 0:S], cq[:, coff + c * 4:coff + c * 4 + 1]),
                    reads=["CONVR", "CQW"], writes=["CONVA"])
                for k_ in range(1, 4):
                    pb.op("vector", lambda em, c=c, k_=k_, coff=coff: em.scalar_tensor_tensor(
                        conva[:, 0:S], convr[:, k_:k_ + S],
                        cq[:, coff + c * 4 + k_:coff + c * 4 + k_ + 1],
                        conva[:, 0:S], ALU.mult, ALU.add),
                        reads=["CONVR", "CQW", "CONVA"], writes=["CONVA"])
                for tg in range(S // 512):
                    pb.op("scalar", lambda em, tg=tg: em.activation(
                        ev[:, :], conva[:, tg * 512:(tg + 1) * 512], AF.Silu),
                        reads=["CONVA"], writes=["E0"])
                    pb.q_dma(dst[c * 128:(c + 1) * 128,
                                  seq * S + tg * 512:seq * S + (tg + 1) * 512],
                             ev[:, :], reads=["E0"], writes=[f"{dk_}@{seq}"])

        wb = pb.sbuf("WBT", [128, 32], BF16)
        for d in range(8):
            pb.g_dma(wb[:, d * 4:(d + 1) * 4], inp["l1_wbT"][d * 128:(d + 1) * 128, :],
                     reads=["in_wb"], writes=["WBT"])
        bt4 = pb.sbuf("BT4", [128, 8], F32)
        for tb in range(nib):
            ps = PS[0]
            for d in range(8):
                pb.mm(ps[:, 0:4], xp[d][:, tb * 128:(tb + 1) * 128],
                      wb[:, d * 4:(d + 1) * 4], start=(d == 0), stop=(d == 7),
                      reads=[xpk[d], "WBT"], writes=["PS0"])
            pb.op("scalar", lambda em: em.activation(bt4[:, 0:4], PS[0][:, 0:4],
                                                     AF.Exp, scale=-1.0),
                  reads=["PS0"], writes=["BT4"])
            pb.op("vector", lambda em: em.tensor_single_scalar(
                bt4[:, 0:4], bt4[:, 0:4], 1.0, ALU.add),
                reads=["BT4"], writes=["BT4"])
            pb.op("vector", lambda em: em.reciprocal(bt4[:, 0:4], bt4[:, 0:4]),
                  reads=["BT4"], writes=["BT4"])
            pb.q_dma(beta_scr[seq * S + tb * 128:seq * S + (tb + 1) * 128, 0:4],
                     bt4[:, 0:4], reads=["BT4"], writes=["l1_beta"])

        rws = pb.sbuf("RWS", [128, 96], BF16)
        for c in range(8):
            pb.g_dma(rws[:, c * 9:(c + 1) * 9],
                     inp["L1_ffn_router_w"][c * 128:(c + 1) * 128, :],
                     reads=["in_rw"], writes=["RWS"])
        pb.g_dma(rws[0:1, 72:81], inp["l1_rb"][0:1, :], reads=["in_rb"], writes=["RWS"])
        bb9s = pb.sbuf("BB9", [128, 9], F32)
        pb.mm(PS[3][:, 0:9], pb.sb["onescol"][:, :], rws[0:1, 72:81],
              start=True, stop=True, reads=["onescol", "RWS"], writes=["PS3"])
        pb.op("vector", lambda em: em.tensor_copy(bb9s[:, :], PS[3][:, 0:9]),
              reads=["PS3"], writes=["BB9"])
        gt32 = pb.sbuf("GT32", [128, 288], F32)
        pr9 = pb.sbuf("PR9", [128, 16], F32)
        ncol9 = pb.sbuf("DC0", [128, 1], F32)
        scol9 = pb.sbuf("DC1", [128, 1], F32)
        rcol9 = pb.sbuf("DC2", [128, 1], F32)
        logt9 = pb.sbuf("LOGT", [128, 16], F32)
        for tb in range(nib):
            tbg = seq * nib + tb
            ps = PS[2]
            for d in range(8):
                pb.mm(ps[:, 0:9], xp[d][:, tb * 128:(tb + 1) * 128],
                      rws[:, d * 9:(d + 1) * 9], start=(d == 0), stop=(d == 7),
                      reads=[xpk[d], "RWS"], writes=["PS2"])
            pb.op("vector", lambda em, ps=ps: em.tensor_add(pr9[:, 0:9], bb9s[:, :],
                                                            ps[:, 0:9]),
                  reads=["PS2", "BB9"], writes=["PR9"])
            pb.op("vector", lambda em: em.reduce_max(ncol9[:, :], pr9[:, 0:9], AX.X,
                                                     negate=True),
                  reads=["PR9"], writes=["DC0"])
            pb.op("scalar", lambda em: em.activation(pr9[:, 0:9], pr9[:, 0:9], AF.Exp,
                                                     bias=ncol9[:, 0:1],
                                                     accum_out=scol9[:, 0:1]),
                  reads=["PR9", "DC0"], writes=["PR9", "DC1"])
            pb.op("vector", lambda em: em.reciprocal(rcol9[:, :], scol9[:, :]),
                  reads=["DC1"], writes=["DC2"])
            pb.op("vector", lambda em: em.tensor_scalar_mul(pr9[:, 0:9], pr9[:, 0:9],
                                                            rcol9[:, 0:1]),
                  reads=["PR9", "DC2"], writes=["PR9"])
            pb.op("vector", lambda em: em.reduce_max(ncol9[:, :], pr9[:, 0:9], AX.X),
                  reads=["PR9"], writes=["DC0"])
            pb.op("vector", lambda em: em.tensor_scalar(
                logt9[:, 0:9], pr9[:, 0:9], ncol9[:, 0:1], None, ALU.is_equal),
                reads=["PR9", "DC0"], writes=["LOGT"])
            pb.op("vector", lambda em: em.scalar_tensor_tensor(
                logt9[:, 0:9], logt9[:, 0:9], -10.0, pr9[:, 0:9], ALU.mult, ALU.add),
                reads=["LOGT", "PR9"], writes=["LOGT"])
            pb.op("vector", lambda em: em.reduce_max(scol9[:, :], logt9[:, 0:9], AX.X),
                  reads=["LOGT"], writes=["DC1"])
            pb.op("vector", lambda em: em.tensor_scalar(
                logt9[:, 0:9], pr9[:, 0:9], scol9[:, 0:1], None, ALU.is_ge),
                reads=["PR9", "DC1"], writes=["LOGT"])
            pb.op("vector", lambda em, tbg=tbg: em.tensor_mul(
                gt32[:, tbg * 9:tbg * 9 + 9], pr9[:, 0:9], logt9[:, 0:9]),
                reads=["PR9", "LOGT"], writes=["GT32"])
            xt9 = pb.sb["FP0"]
            t0g = tbg * 128
            pb.q_dma(xt9[:, :], pb.dr["xnf_l1"][t0g:t0g + 128, :],
                     reads=[f"xnf_l1@{tbg}"], writes=["FP0"])
            pb.op("vector", lambda em, tbg=tbg: em.tensor_scalar_mul(
                xt9[:, :], xt9[:, :], gt32[:, tbg * 9 + 8:tbg * 9 + 9]),
                reads=["FP0", "GT32"], writes=["FP0"])
            pb.q_dma(acc_scr[t0g:t0g + 128, :], xt9[:, :], reads=["FP0"],
                     writes=[f"g_l1@{tbg}"])

        firc = pb.sbuf("CONVR", [128, 2112], BF16)
        for c in range(8):
            pb.op("vector", lambda em: em.memset(firc[:, 0:62], 0.0), writes=["CONVR"])
            pb.q_dma(firc[:, 62:62 + S],
                     vT[c * 128:(c + 1) * 128, seq * S:(seq + 1) * S],
                     reads=[f"l1_vT@{seq}"], writes=["CONVR"])
            for (pk, ntap, wof, wt, dst, dk_) in ((4, 5, 58, firs, ps_scr, "l1_ps"),
                                                  (62, 63, 0, firl, plg_scr, "l1_pl")):
                pb.op("vector", lambda em, c=c, wt=wt, ntap=ntap, pk=pk: em.tensor_scalar_mul(
                    conva[:, 0:S], firc[:, 62 - (ntap - 1):62 - (ntap - 1) + S],
                    wt[:, c * ntap:c * ntap + 1]),
                    reads=["CONVR", "FIRS", "FIRL"], writes=["CONVA"])
                for k_ in range(1, ntap):
                    pb.op("vector", lambda em, c=c, k_=k_, wt=wt, ntap=ntap: em.scalar_tensor_tensor(
                        conva[:, 0:S], firc[:, 62 - (ntap - 1) + k_:62 - (ntap - 1) + k_ + S],
                        wt[:, c * ntap + k_:c * ntap + k_ + 1],
                        conva[:, 0:S], ALU.mult, ALU.add),
                        reads=["CONVR", "FIRS", "FIRL", "CONVA"], writes=["CONVA"])
                for tg in range(S // 512):
                    pb.op("scalar", lambda em, tg=tg: em.copy(
                        ev[:, :], conva[:, tg * 512:(tg + 1) * 512]),
                        reads=["CONVA"], writes=["E0"])
                    pb.q_dma(dst[c * 128:(c + 1) * 128,
                                  seq * S + tg * 512:seq * S + (tg + 1) * 512],
                             ev[:, :], reads=["E0"], writes=[f"{dk_}@{seq}"])

        qtok = pb.sbuf("QTOK", [128, 256], BF16)
        ktok = pb.sbuf("KTOK", [128, 256], BF16)
        vtok = pb.sbuf("VTOK", [128, 256], BF16)
        kbt = pb.sbuf("KBT", [128, 256], BF16)
        vbt = pb.sbuf("VBT", [128, 256], BF16)
        sqt = pb.sbuf("HF32", [128, 512], F32)[:, 0:256]
        kfm = pb.sbuf("KFM", [128, 256], BF16)
        kbfm = pb.sbuf("KBFM", [128, 256], BF16)
        qfm = pb.sbuf("QFM", [128, 256], BF16)
        wfm = pb.sbuf("WFM", [128, 256], BF16)
        asb = pb.sbuf("ASB", [128, 128], BF16)
        bsb = pb.sbuf("BSB", [128, 128], BF16)
        a2sb = pb.sbuf("A2SB", [128, 128], BF16)
        b2sb = pb.sbuf("B2SB", [128, 128], BF16)
        mtsb = pb.sbuf("MTSB", [128, 128], BF16)
        atsb = pb.sbuf("ATSB", [128, 128], BF16)
        usb = pb.sbuf("USB", [128, 256], BF16)
        wsb = pb.sbuf("WSB", [128, 256], BF16)
        uadj = pb.sbuf("UADJ", [128, 256], BF16)
        otok = pb.sbuf("OTOK", [128, 256], BF16)
        ssb = [pb.sbuf(f"SSB{i}", [128, 256], F32) for i in range(2)]
        sbf = [pb.sbuf(f"SBF{i}", [128, 256], BF16) for i in range(2)]
        c1, c2, c3 = (pb.sbuf(f"DC{i}", [128, 1], F32) for i in range(3))

        for h in range(H):
            hr = slice(h * 256, (h + 1) * 256)
            for i in range(2):
                pb.op("vector", lambda em, i=i: em.memset(ssb[i][:, :], 0.0),
                      writes=[f"SSB{i}"])
                pb.op("vector", lambda em, i=i: em.memset(sbf[i][:, :], 0.0),
                      writes=[f"SBF{i}"])
            for ci in range(nib):
                t0 = seq * S + ci * 128
                csl = slice(t0, t0 + 128)
                pb.q_dma(qtok[:, :], qT[hr, csl], reads=[f"l1_qT@{seq}"],
                         writes=["QTOK"], transpose=True)
                pb.q_dma(ktok[:, :], kT[hr, csl], reads=[f"l1_kT@{seq}"],
                         writes=["KTOK"], transpose=True)
                pb.q_dma(vtok[:, :], vT[hr, csl], reads=[f"l1_vT@{seq}"],
                         writes=["VTOK"], transpose=True)
                pb.q_dma(bt4[:, 0:4], beta_scr[csl, 0:4], reads=["l1_beta"],
                         writes=["BT4"])
                bcol = bt4[:, h:h + 1]
                for (tsrc, tk, dst2, dk2) in ((qtok, "QTOK", qfm, None),
                                              (ktok, "KTOK", None, None)):
                    pb.op("scalar", lambda em, t=tsrc: em.activation(
                        sqt[:, :], t[:, :], AF.Square, accum_out=c1[:, 0:1]),
                        reads=[tk], writes=["HF32", "DC0"])
                    pb.op("vector", lambda em: em.tensor_single_scalar(
                        c1[:, :], c1[:, :], 1e-6, ALU.add), reads=["DC0"], writes=["DC0"])
                    pb.op("vector", lambda em: em.reciprocal(c2[:, :], c1[:, :]),
                          reads=["DC0"], writes=["DC1"])
                    pb.op("scalar", lambda em: em.activation(c2[:, :], c2[:, :], AF.Sqrt),
                          reads=["DC1"], writes=["DC1"])
                    pb.op("vector", lambda em, t=tsrc: em.tensor_scalar_mul(
                        t[:, :], t[:, :], c2[:, 0:1]), reads=[tk, "DC1"], writes=[tk])
                pb.op("vector", lambda em, bcol=bcol: em.tensor_scalar_mul(
                    kbt[:, :], ktok[:, :], bcol), reads=["KTOK", "BT4"], writes=["KBT"])
                pb.op("vector", lambda em, bcol=bcol: em.tensor_scalar_mul(
                    vbt[:, :], vtok[:, :], bcol), reads=["VTOK", "BT4"], writes=["VBT"])
                for (src2, sk2, dfm, dfk) in ((ktok, "KTOK", kfm, "KFM"),
                                              (kbt, "KBT", kbfm, "KBFM"),
                                              (qtok, "QTOK", qfm, "QFM")):
                    for half in range(2):
                        pb.transp(PT[:, 0:128], src2[:, half * 128:(half + 1) * 128],
                                  reads=[sk2], writes=["PTB0"])
                        pb.op("vector", lambda em, dfm=dfm, half=half: em.tensor_copy(
                            dfm[:, half * 128:(half + 1) * 128], PT[:, 0:128]),
                            reads=["PTB0"], writes=[dfk])
                for half in range(2):
                    pb.mm(PS[0][:, 0:128], kbfm[:, half * 128:(half + 1) * 128],
                          kfm[:, half * 128:(half + 1) * 128],
                          start=(half == 0), stop=(half == 1),
                          reads=["KBFM", "KFM"], writes=["PS0"])
                pb.op("vector", lambda em: em.scalar_tensor_tensor(
                    asb[:, :], PS[0][:, 0:128], -1.0, pb.sb["M01SL"][:, :],
                    ALU.mult, ALU.mult), reads=["PS0", "M01SL"], writes=["ASB"])
                for half in range(2):
                    pb.mm(PS[1][:, 0:128], kfm[:, half * 128:(half + 1) * 128],
                          kbfm[:, half * 128:(half + 1) * 128],
                          start=(half == 0), stop=(half == 1),
                          reads=["KFM", "KBFM"], writes=["PS1"])
                pb.op("vector", lambda em: em.scalar_tensor_tensor(
                    bsb[:, :], PS[1][:, 0:128], -1.0, pb.sb["M01SU"][:, :],
                    ALU.mult, ALU.mult), reads=["PS1", "M01SU"], writes=["BSB"])
                pb.op("vector", lambda em: em.tensor_add(mtsb[:, :], bsb[:, :],
                                                         ident[:, :]),
                      reads=["BSB", "ident"], writes=["MTSB"])
                srcA, srcAk, srcB, srcBk = asb, "ASB", bsb, "BSB"
                for lev in range(6):
                    pb.mm(PS[0][:, 0:128], srcB[:, :], srcA[:, :], start=True, stop=True,
                          reads=[srcBk, srcAk], writes=["PS0"])
                    pb.mm(PS[1][:, 0:128], srcA[:, :], srcB[:, :], start=True, stop=True,
                          reads=[srcAk, srcBk], writes=["PS1"])
                    dstA, dstAk = (a2sb, "A2SB") if srcA is asb else (asb, "ASB")
                    dstB, dstBk = (b2sb, "B2SB") if srcB is bsb else (bsb, "BSB")
                    pb.op("vector", lambda em, dstA=dstA: em.tensor_copy(dstA[:, :],
                                                                         PS[0][:, 0:128]),
                          reads=["PS0"], writes=[dstAk])
                    pb.op("vector", lambda em, dstB=dstB: em.tensor_copy(dstB[:, :],
                                                                         PS[1][:, 0:128]),
                          reads=["PS1"], writes=[dstBk])
                    pb.mm(PS[2][:, 0:128], dstA[:, :], mtsb[:, :], start=True, stop=True,
                          reads=[dstAk, "MTSB"], writes=["PS2"])
                    pb.op("vector", lambda em: em.scalar_tensor_tensor(
                        mtsb[:, :], PS[2][:, 0:128], 1.0, mtsb[:, :], ALU.mult, ALU.add),
                        reads=["PS2", "MTSB"], writes=["MTSB"])
                    srcA, srcAk = dstA, dstAk
                    srcB, srcBk = dstB, dstBk
                pb.mm(PS[1][:, 0:256], mtsb[:, :], vbt[:, :], start=True, stop=True,
                      reads=["MTSB", "VBT"], writes=["PS1"])
                pb.op("vector", lambda em: em.tensor_copy(usb[:, :], PS[1][:, 0:256]),
                      reads=["PS1"], writes=["USB"])
                pb.mm(PS[1][:, 0:256], mtsb[:, :], kbt[:, :], start=True, stop=True,
                      reads=["MTSB", "KBT"], writes=["PS1"])
                pb.op("vector", lambda em: em.tensor_copy(wsb[:, :], PS[1][:, 0:256]),
                      reads=["PS1"], writes=["WSB"])
                for half in range(2):
                    pb.transp(PT[:, 0:128], wsb[:, half * 128:(half + 1) * 128],
                              reads=["WSB"], writes=["PTB0"])
                    pb.op("vector", lambda em, half=half: em.tensor_copy(
                        wfm[:, half * 128:(half + 1) * 128], PT[:, 0:128]),
                        reads=["PTB0"], writes=["WFM"])
                for half in range(2):
                    pb.mm(PS[1][:, 0:256], wfm[:, half * 128:(half + 1) * 128],
                          sbf[half][:, :], start=(half == 0), stop=(half == 1),
                          reads=["WFM", f"SBF{half}"], writes=["PS1"])
                pb.op("vector", lambda em: em.tensor_sub(uadj[:, :], usb[:, :],
                                                         PS[1][:, 0:256]),
                      reads=["USB", "PS1"], writes=["UADJ"])
                for half in range(2):
                    pb.mm(PS[0][:, 0:128],
                          kfm[:, half * 128:(half + 1) * 128],
                          qfm[:, half * 128:(half + 1) * 128],
                          start=(half == 0), stop=(half == 1),
                          reads=["KFM", "QFM"], writes=["PS0"])
                pb.op("vector", lambda em: em.tensor_mul(
                    atsb[:, :], PS[0][:, 0:128], pb.sb["M01IU"][:, :]),
                    reads=["PS0", "M01IU"], writes=["ATSB"])
                for half in range(2):
                    pb.mm(PS[3][:, 0:256], qfm[:, half * 128:(half + 1) * 128],
                          sbf[half][:, :], start=(half == 0), stop=False,
                          reads=["QFM", f"SBF{half}"], writes=["PS3"])
                pb.mm(PS[3][:, 0:256], atsb[:, :], uadj[:, :], start=False, stop=True,
                      reads=["ATSB", "UADJ"], writes=["PS3"])
                pb.op("vector", lambda em: em.tensor_copy(otok[:, :], PS[3][:, 0:256]),
                      reads=["PS3"], writes=["OTOK"])
                pb.q_dma(od_scr[csl, hr], otok[:, :], reads=["OTOK"],
                         writes=[f"l1_od@{seq}"])
                for half in range(2):
                    pb.transp(PT[:, 0:128], otok[:, half * 128:(half + 1) * 128],
                              reads=["OTOK"], writes=["PTB0"])
                    pb.op("vector", lambda em, half=half: em.tensor_copy(
                        kfm[:, half * 128:(half + 1) * 128], PT[:, 0:128]),
                        reads=["PTB0"], writes=["KFM"])
                for half in range(2):
                    pb.q_dma(odf_scr[h * 256 + half * 128:h * 256 + (half + 1) * 128, csl],
                             kfm[:, half * 128:(half + 1) * 128],
                             reads=["KFM"], writes=[f"l1_odf@{seq}"])
                for half in range(2):
                    pb.mm(PS[4][:, 0:256], ktok[:, half * 128:(half + 1) * 128],
                          uadj[:, :], start=True, stop=True,
                          reads=["KTOK", "UADJ"], writes=["PS4"])
                    pb.op("vector", lambda em, half=half: em.tensor_add(
                        ssb[half][:, :], ssb[half][:, :], PS[4][:, 0:256]),
                        reads=[f"SSB{half}", "PS4"], writes=[f"SSB{half}"])
                    pb.op("vector", lambda em, half=half: em.tensor_copy(
                        sbf[half][:, :], ssb[half][:, :]),
                        reads=[f"SSB{half}"], writes=[f"SBF{half}"])

        ones128 = pb.sbuf("ONES128", [128, 1], BF16)
        pb.op("vector", lambda em: em.memset(ones128[:, :], 1.0), writes=["ONES128"])
        pfm = pb.sbuf("PBUF", [128, 2048], BF16)
        pfm2 = pb.sbuf("PBUF2", [128, 2048], BF16)
        nts = S // 512
        for pi, (scr, rkp) in enumerate(((ps_scr, f"l1_ps@{seq}"),
                                         (plg_scr, f"l1_pl@{seq}"),
                                         (odf_scr, f"l1_odf@{seq}"),
                                         (vT, f"l1_vT@{seq}"))):
            for h in range(H):
                sc_ = pb.sbuf("SCORES", [128, 2048], F32)
                for kind in range(2):
                    for ch in range(2):
                        row0 = h * 256 + ch * 128
                        pb.q_dma(pfm[:, 0:S], scr[row0:row0 + 128, seq * S:(seq + 1) * S],
                                 reads=[rkp], writes=["PBUF"])
                        if kind == 0:
                            til, tkk = pfm, "PBUF"
                        else:
                            fn = AF.Square if kind == 1 else AF.Abs
                            pb.op("scalar", lambda em, fn=fn: em.activation(
                                pfm2[:, 0:S], pfm[:, 0:S], fn),
                                reads=["PBUF"], writes=["PBUF2"])
                            til, tkk = pfm2, "PBUF2"
                        for ts_ in range(nts):
                            pb.mm(PS[ts_][0:1, :], ones128[:, :],
                                  til[:, ts_ * 512:(ts_ + 1) * 512],
                                  start=(ch == 0), stop=(ch == 1),
                                  reads=["ONES128", tkk], writes=[f"PS{ts_}"])
                    if kind == 0:
                        for ts_ in range(nts):
                            pb.op("vector", lambda em, ts_=ts_: em.tensor_scalar_mul(
                                conva[0:1, ts_ * 512:(ts_ + 1) * 512],
                                PS[ts_][0:1, :], 1.0 / 256.0),
                                reads=[f"PS{ts_}"], writes=["CONVA"])
                    else:
                        for ts_ in range(nts):
                            pb.op("vector", lambda em, ts_=ts_: em.tensor_copy(
                                sc_[0:1, ts_ * 512:(ts_ + 1) * 512], PS[ts_][0:1, :]),
                                reads=[f"PS{ts_}"], writes=["SCORES"])
                r0 = pi * 16 + h * 4
                pb.q_dma(gin_scr[r0:r0 + 1, seq * S:(seq + 1) * S], conva[0:1, 0:S],
                         reads=["CONVA"], writes=["l1_gin"])
                pb.op("scalar", lambda em: em.activation(pfm[0:1, 0:S], sc_[0:1, 0:S],
                                                         AF.Sqrt),
                      reads=["SCORES"], writes=["PBUF"])
                pb.g_dma(gin_scr[r0 + 3:r0 + 4, seq * S:(seq + 1) * S], pfm[0:1, 0:S],
                         reads=["PBUF"], writes=["l1_gin"])
                pb.op("vector", lambda em: em.tensor_mul(conva[0:1, 0:S],
                                                         conva[0:1, 0:S],
                                                         conva[0:1, 0:S]),
                      reads=["CONVA"], writes=["CONVA"])
                pb.op("vector", lambda em: em.scalar_tensor_tensor(
                    sc_[0:1, 0:S], sc_[0:1, 0:S], 1.0 / 256.0, conva[0:1, 0:S],
                    ALU.mult, ALU.subtract), reads=["SCORES", "CONVA"], writes=["SCORES"])
                pb.q_dma(gin_scr[r0 + 1:r0 + 2, seq * S:(seq + 1) * S], sc_[0:1, 0:S],
                         reads=["SCORES"], writes=["l1_gin"])
                for ch in range(2):
                    row0_ = h * 256 + ch * 128
                    pb.q_dma(pfm[:, 0:S], scr[row0_:row0_ + 128, seq * S:(seq + 1) * S],
                             reads=[rkp], writes=["PBUF"])
                    pb.op("scalar", lambda em: em.activation(pfm2[:, 0:S], pfm[:, 0:S],
                                                             AF.Abs),
                          reads=["PBUF"], writes=["PBUF2"])
                    for ts_ in range(nts):
                        pb.mm(PS[ts_][0:1, :], ones128[:, :],
                              pfm2[:, ts_ * 512:(ts_ + 1) * 512],
                              start=(ch == 0), stop=(ch == 1),
                              reads=["ONES128", "PBUF2"], writes=[f"PS{ts_}"])
                for ts_ in range(nts):
                    pb.op("vector", lambda em, ts_=ts_: em.tensor_scalar_mul(
                        conva[0:1, ts_ * 512:(ts_ + 1) * 512], PS[ts_][0:1, :],
                        1.0 / 256.0), reads=[f"PS{ts_}"], writes=["CONVA"])
                pb.q_dma(gin_scr[r0 + 2:r0 + 3, seq * S:(seq + 1) * S], conva[0:1, 0:S],
                         reads=["CONVA"], writes=["l1_gin"])

        tcon = pb.sbuf("TCON", [1, 64], F32)
        tinvb = pb.sbuf("TINVB", [128, 16], F32)
        nfeb = pb.sbuf("NFEB", [128, 16], F32)
        ltr = tcon[0:1, 0:16]
        fpr = tcon[0:1, 16:32]
        t1r = tcon[0:1, 32:48]
        t2r = tcon[0:1, 48:64]
        pb.q_dma(ltr, inp["l1_ltemp"][:, :], reads=["in_ltemp"], writes=["TCON"])
        pb.q_dma(fpr, inp["l1_fpar"][:, :], reads=["in_fpar"], writes=["TCON"])
        pb.op("scalar", lambda em: em.activation(t1r, ltr, AF.Abs),
              reads=["TCON"], writes=["TCON"])
        pb.op("scalar", lambda em: em.activation(t1r, t1r, AF.Exp, scale=-1.0),
              reads=["TCON"], writes=["TCON"])
        pb.op("scalar", lambda em: em.activation(t1r, t1r, AF.Ln, bias=1.0),
              reads=["TCON"], writes=["TCON"])
        pb.op("vector", lambda em: em.tensor_single_scalar(t2r, ltr, 0.0, ALU.max),
              reads=["TCON"], writes=["TCON"])
        pb.op("vector", lambda em: em.tensor_add(t1r, t1r, t2r),
              reads=["TCON"], writes=["TCON"])
        pb.op("vector", lambda em: em.tensor_single_scalar(t1r, t1r, 0.25, ALU.add),
              reads=["TCON"], writes=["TCON"])
        pb.op("vector", lambda em: em.reciprocal(t1r, t1r), reads=["TCON"], writes=["TCON"])
        pb.op("scalar", lambda em: em.activation(t2r, fpr, AF.Exp, scale=-1.0),
              reads=["TCON"], writes=["TCON"])
        pb.op("vector", lambda em: em.tensor_single_scalar(t2r, t2r, 1.0, ALU.add),
              reads=["TCON"], writes=["TCON"])
        pb.op("vector", lambda em: em.reciprocal(t2r, t2r),
              reads=["TCON"], writes=["TCON"])
        pb.op("vector", lambda em: em.tensor_scalar(t2r, t2r, 0.07, 0.005,
                                                    ALU.mult, ALU.add),
              reads=["TCON"], writes=["TCON"])
        pb.op("vector", lambda em: em.tensor_single_scalar(t2r, t2r, 0.25, ALU.min),
              reads=["TCON"], writes=["TCON"])
        for h in range(4):
            pb.op("vector", lambda em, h=h: em.reduce_sum(
                fpr[0:1, h:h + 1], tcon[0:1, 48 + h * 4:48 + (h + 1) * 4], AX.X),
                reads=["TCON"], writes=["TCON"])
        pb.op("vector", lambda em: em.tensor_single_scalar(
            fpr[0:1, 0:4], fpr[0:1, 0:4], 0.99, ALU.min), reads=["TCON"], writes=["TCON"])
        pb.op("vector", lambda em: em.reciprocal(fpr[0:1, 0:4], fpr[0:1, 0:4]),
              reads=["TCON"], writes=["TCON"])
        for h in range(4):
            pb.op("vector", lambda em, h=h: em.tensor_scalar_mul(
                tcon[0:1, 48 + h * 4:48 + (h + 1) * 4],
                tcon[0:1, 48 + h * 4:48 + (h + 1) * 4],
                fpr[0:1, h:h + 1]), reads=["TCON"], writes=["TCON"])
        pb.op("vector", lambda em: em.tensor_scalar(
            t2r, t2r, 0.99, 1e-9, ALU.mult, ALU.add), reads=["TCON"], writes=["TCON"])
        evb = pb.sb["E0"]
        for (rsrc, dst2, dk2) in ((t1r, tinvb, "TINVB"), (t2r, nfeb, "NFEB")):
            pb.op("vector", lambda em, rsrc=rsrc: em.tensor_copy(evb[0:1, 0:16], rsrc),
                  reads=["TCON"], writes=["E0"])
            pb.mm(PS[0][:, 0:16], pb.sb["onescol"][:, :], evb[0:1, 0:16],
                  start=True, stop=True, reads=["onescol", "E0"], writes=["PS0"])
            pb.op("vector", lambda em, dst2=dst2: em.tensor_copy(dst2[:, :], PS[0][:, 0:16]),
                  reads=["PS0"], writes=[dk2])

        for d in range(8):
            g = pb.sbuf(f"G{d}", [128, D], BF16)
            pb.g_dma(g[:, :], inp["l1_w1T"][d * 128:(d + 1) * 128, :],
                     reads=["in_w1T"], writes=[f"G{d}"])
        g8t = pb.sbuf("G8", [128, D], BF16)
        pb.g_dma(g8t[0:64, :], inp["l1_w1T"][1024:1088, :], reads=["in_w1T"],
                 writes=["G8"])
        gins = pb.sbuf("LT", [64, 2048], BF16)
        pb.g_dma(gins[:, 0:S], gin_scr[:, seq * S:(seq + 1) * S], reads=["l1_gin"],
                 writes=["LT"])
        gb1 = pb.sbuf("GB1", [128, 8], F32)
        pb.q_dma(gb1[:, :], inp["l1_gb1T"][:, :], reads=["in_gb1"], writes=["GB1"])
        w2ts = pb.sbuf("W2TS", [128, 128], BF16)
        for mc in range(8):
            pb.g_dma(w2ts[:, mc * 16:(mc + 1) * 16],
                     inp["l1_w2T"][mc * 128:(mc + 1) * 128, :],
                     reads=["in_w2T"], writes=["W2TS"])
        gb2 = pb.sbuf("GB2", [16, 1], F32)
        pb.q_dma(gb2[:, :], inp["l1_gb2c"][:, :], reads=["in_gb2"], writes=["GB2"])
        wgts = pb.sbuf("EG", [128, 512], F32)
        logt = pb.sbuf("LOGT", [128, 16], F32)
        ncol = pb.sbuf("DC0", [128, 1], F32)
        scol = pb.sbuf("DC1", [128, 1], F32)
        rcol = pb.sbuf("DC2", [128, 1], F32)

        for ts_ in range(nts):
            tsl = slice(ts_ * 512, (ts_ + 1) * 512)
            for mc in range(8):
                ps = PS[4 + mc % 2]
                psk = f"PS{4 + mc % 2}"
                for d in range(8):
                    pb.mm(ps[:, :], pb.sb[f"G{d}"][:, mc * 128:(mc + 1) * 128],
                          xp[d][:, tsl], start=(d == 0), stop=False,
                          reads=[f"G{d}", xpk[d]], writes=[psk])
                pb.mm(ps[:, :], g8t[0:64, mc * 128:(mc + 1) * 128], gins[:, tsl],
                      start=False, stop=True, reads=["G8", "LT"], writes=[psk])
                tgt = pfm if mc < 4 else pfm2
                tgtk = "PBUF" if mc < 4 else "PBUF2"
                pb.op("scalar", lambda em, ps=ps, mc=mc, tgt=tgt: em.activation(
                    tgt[:, (mc % 4) * 512:(mc % 4 + 1) * 512], ps[:, :], AF.Gelu,
                    bias=gb1[:, mc:mc + 1]),
                    reads=[psk, "GB1"], writes=[tgtk])
            for mc in range(8):
                tgt = pfm if mc < 4 else pfm2
                tgtk = "PBUF" if mc < 4 else "PBUF2"
                pb.mm(PS[0][0:16, :], w2ts[:, mc * 16:(mc + 1) * 16],
                      tgt[:, (mc % 4) * 512:(mc % 4 + 1) * 512],
                      start=(mc == 0), stop=(mc == 7),
                      reads=["W2TS", tgtk], writes=["PS0"])
            pb.op("scalar", lambda em: em.activation(
                pb.sb["STILE"][0:16, 0:512], PS[0][0:16, :], AF.Identity,
                bias=gb2[:, 0:1]), reads=["PS0", "GB2"], writes=["STILE"])
            for sub in range(4):
                tb = ts_ * 4 + sub
                pb.transp(PS[1][:, 0:16],
                          pb.sb["STILE"][0:16, sub * 128:(sub + 1) * 128],
                          reads=["STILE"], writes=["PS1"], np_=16, f32=True)
                pb.op("vector", lambda em: em.tensor_copy(logt[:, :], PS[1][:, 0:16]),
                      reads=["PS1"], writes=["LOGT"])
                pb.op("vector", lambda em: em.tensor_mul(logt[:, :], logt[:, :],
                                                         tinvb[:, :]),
                      reads=["LOGT", "TINVB"], writes=["LOGT"])
                for h in range(4):
                    hs = slice(h * 4, (h + 1) * 4)
                    pb.op("vector", lambda em, hs=hs: em.reduce_max(
                        ncol[:, :], logt[:, hs], AX.X, negate=True),
                        reads=["LOGT"], writes=["DC0"])
                    pb.op("scalar", lambda em, hs=hs: em.activation(
                        logt[:, hs], logt[:, hs], AF.Exp, bias=ncol[:, 0:1],
                        accum_out=scol[:, 0:1]),
                        reads=["LOGT", "DC0"], writes=["LOGT", "DC1"])
                    pb.op("vector", lambda em: em.reciprocal(rcol[:, :], scol[:, :]),
                          reads=["DC1"], writes=["DC2"])
                    pb.op("vector", lambda em, hs=hs: em.tensor_scalar_mul(
                        logt[:, hs], logt[:, hs], rcol[:, 0:1]),
                        reads=["LOGT", "DC2"], writes=["LOGT"])
                pb.op("vector", lambda em: em.tensor_max(logt[:, :], logt[:, :],
                                                         nfeb[:, :]),
                      reads=["LOGT", "NFEB"], writes=["LOGT"])
                for h in range(4):
                    hs = slice(h * 4, (h + 1) * 4)
                    pb.op("vector", lambda em, hs=hs: em.reduce_sum(
                        scol[:, :], logt[:, hs], AX.X), reads=["LOGT"], writes=["DC1"])
                    pb.op("vector", lambda em: em.tensor_single_scalar(
                        scol[:, :], scol[:, :], 1e-8, ALU.add),
                        reads=["DC1"], writes=["DC1"])
                    pb.op("vector", lambda em: em.reciprocal(rcol[:, :], scol[:, :]),
                          reads=["DC1"], writes=["DC2"])
                    pb.op("vector", lambda em, hs=hs, tb=tb, h=h: em.tensor_single_scalar(
                        wgts[:, tb * 16 + h * 4:tb * 16 + (h + 1) * 4],
                        logt[:, hs], rcol[:, 0:1], ALU.mult),
                        reads=["LOGT", "DC2"], writes=["EG"])

        gamb = pb.sbuf("CONVR", [128, 2112], BF16)[:, 0:1024]
        cp = [pb.sb["PBUF"][:, 0:1024], pb.sb["PBUF"][:, 1024:2048],
              pb.sb["PBUF2"][:, 0:1024], pb.sb["PBUF2"][:, 1024:2048]]
        cpk = ["PBUF", "PBUF", "PBUF2", "PBUF2"]
        pb.g_dma(cp[0][0:1, :], inp["l1_onw"][0:1, :], reads=["in_onw"], writes=["PBUF"])
        pb.mm(PS[0][:, 0:512], pb.sb["onescol"][:, :], cp[0][0:1, 0:512],
              start=True, stop=True, reads=["onescol", "PBUF"], writes=["PS0"])
        pb.mm(PS[1][:, 0:512], pb.sb["onescol"][:, :], cp[0][0:1, 512:1024],
              start=True, stop=True, reads=["onescol", "PBUF"], writes=["PS1"])
        pb.op("vector", lambda em: em.tensor_copy(gamb[:, 0:512], PS[0][:, 0:512]),
              reads=["PS0"], writes=["CONVR"])
        pb.op("vector", lambda em: em.tensor_copy(gamb[:, 512:1024], PS[1][:, 0:512]),
              reads=["PS1"], writes=["CONVR"])
        accm = pb.sb["STILE"]
        ob16 = pb.sb["CONVR"][:, 1024:2048]
        for tb in range(nib):
            t0 = seq * S + tb * 128
            for pi2, (scr, rkp) in enumerate(((ps_scr, f"l1_ps@{seq}"),
                                              (plg_scr, f"l1_pl@{seq}"),
                                              (od_scr, f"l1_od@{seq}"),
                                              (vT, f"l1_vT@{seq}"))):
                if pi2 == 2:
                    pb.q_dma(cp[pi2][:, :], scr[t0:t0 + 128, :], reads=[rkp],
                             writes=[cpk[pi2]])
                else:
                    for c in range(8):
                        pb.q_dma(cp[pi2][:, c * 128:(c + 1) * 128],
                                 scr[c * 128:(c + 1) * 128, t0:t0 + 128],
                                 reads=[rkp], writes=[cpk[pi2]], transpose=True)
            for h in range(4):
                hs = slice(h * 256, (h + 1) * 256)
                for pi2 in range(4):
                    wcol = wgts[:, (tb + 0) * 16 + h * 4 + pi2:
                                (tb + 0) * 16 + h * 4 + pi2 + 1]
                    if pi2 == 0:
                        pb.op("vector", lambda em, hs=hs, wcol=wcol: em.tensor_scalar_mul(
                            accm[:, hs], cp[0][:, hs], wcol),
                            reads=[cpk[0], "EG"], writes=["STILE"])
                    else:
                        pb.op("vector", lambda em, hs=hs, wcol=wcol, pi2=pi2:
                              em.scalar_tensor_tensor(
                                  accm[:, hs], cp[pi2][:, hs], wcol, accm[:, hs],
                                  ALU.mult, ALU.add),
                              reads=[cpk[pi2], "EG", "STILE"], writes=["STILE"])
                pb.op("scalar", lambda em, hs=hs: em.activation(
                    pb.sb["HF32"][:, 0:256], accm[:, hs], AF.Square,
                    accum_out=ncol[:, 0:1]),
                    reads=["STILE"], writes=["HF32", "DC0"])
                pb.op("vector", lambda em: em.tensor_scalar(
                    ncol[:, :], ncol[:, :], 1.0 / 256.0, 1e-5, ALU.mult, ALU.add),
                    reads=["DC0"], writes=["DC0"])
                pb.op("vector", lambda em: em.reciprocal(scol[:, :], ncol[:, :]),
                      reads=["DC0"], writes=["DC1"])
                pb.op("scalar", lambda em: em.activation(scol[:, :], scol[:, :], AF.Sqrt),
                      reads=["DC1"], writes=["DC1"])
                pb.op("vector", lambda em, hs=hs: em.tensor_scalar_mul(
                    accm[:, hs], accm[:, hs], scol[:, 0:1]),
                    reads=["STILE", "DC1"], writes=["STILE"])
            pb.op("vector", lambda em: em.tensor_mul(ob16[:, :], accm[:, 0:1024],
                                                     gamb[:, :]),
                  reads=["STILE", "CONVR"], writes=["CONVR"])
            pb.q_dma(o_scr[t0:t0 + 128, :], ob16[:, :], reads=["CONVR"],
                     writes=[f"l1_o@{seq}"])

        for d in range(8):
            g = pb.sbuf(f"G{16+d}", [128, D], BF16)
            pb.g_dma(g[:, :], inp["l1_woT"][d * 128:(d + 1) * 128, :],
                     reads=["in_wo1"], writes=[f"G{16+d}"])
        qi = [pb.sbuf(f"QI{d}", [128, 128], BF16) for d in range(8)]
        for tb in range(nib):
            t0 = seq * S + tb * 128
            for c in range(8):
                pb.q_dma(qi[c][:, :], o_scr[t0:t0 + 128, c * 128:(c + 1) * 128],
                         reads=[f"l1_o@{seq}"], writes=[f"QI{c}"], transpose=True)
            for oh in range(2):
                ps = PS[oh]
                psk = f"PS{oh}"
                for c in range(8):
                    pb.mm(ps[:, :], qi[c][:, :],
                          pb.sb[f"G{16+c}"][:, oh * 512:(oh + 1) * 512],
                          start=(c == 0), stop=(c == 7),
                          reads=[f"QI{c}", f"G{16+c}"], writes=[psk])
                eng = "vector" if oh == 0 else "scalar"
                if eng == "vector":
                    pb.op(eng, lambda em, ps=ps, oh=oh: em.tensor_copy(
                        pb.sb["HF32"][:, 0:512], ps[:, :]), reads=[psk], writes=["HF32"])
                else:
                    pb.op(eng, lambda em, ps=ps, oh=oh: em.copy(
                        pb.sb["HF32"][:, 0:512], ps[:, :]), reads=[psk], writes=["HF32"])
                pb.q_dma(s_scr[t0:t0 + 128, oh * 512:(oh + 1) * 512],
                         pb.sb["HF32"][:, 0:512], reads=["HF32"],
                         writes=[f"s_l1@{t0 // 128}"])

    gt32 = pb.sbuf("GT32", [128, 288], F32)
    xp = [pb.sb[f"XP{d}"] for d in range(8)]
    xpk = [f"XP{d}" for d in range(8)]
    ehalf = pb.sb["EG"] if "EG" in pb.sb else pb.sbuf("EG", [128, 512], F32)
    for e in range(8):
        for fs in range(2):
            for c in range(8):
                for j in range(2):
                    g = pb.sbuf(f"G{2*c+j}", [128, D], BF16)
                    pb.g_dma(g[:, :],
                             inp["L1_ffn_w1"][e, c * 128:(c + 1) * 128,
                                              fs * 2048 + j * 1024:fs * 2048 + (j + 1) * 1024],
                             reads=["in_moe1"], writes=[f"G{2*c+j}"])
            for fc in range(16):
                g = pb.sbuf(f"G{16+fc}", [128, D], BF16)
                pb.g_dma(g[:, :],
                         inp["L1_ffn_w2"][e, fs * 2048 + fc * 128:fs * 2048 + (fc + 1) * 128, :],
                         reads=["in_moe2"], writes=[f"G{16+fc}"])
            for tg in range(T // 512):
                for d in range(8):
                    xsl = xp[4 + d // 4][:, (d % 4) * 512:(d % 4 + 1) * 512]
                    pb.q_dma(xsl, xnT_scr[d * 128:(d + 1) * 128,
                                          tg * 512:(tg + 1) * 512],
                             reads=["l1_xnT"], writes=[xpk[4 + d // 4]])
                for fc in range(16):
                    ps = PS[4 + fc % 2]
                    psk = f"PS{4 + fc % 2}"
                    j = fc // 8
                    off = (fc % 8) * 128
                    for d in range(8):
                        xsl = xp[4 + d // 4][:, (d % 4) * 512:(d % 4 + 1) * 512]
                        pb.mm(ps[:, :], pb.sb[f"G{2*d+j}"][:, off:off + 128],
                              xsl, start=(d == 0), stop=(d == 7),
                              reads=[f"G{2*d+j}", xpk[4 + d // 4]], writes=[psk])
                    hsl = xp[fc // 4][:, (fc % 4) * 512:(fc % 4 + 1) * 512]
                    pb.op("scalar", lambda em, ps=ps, hsl=hsl: em.activation(
                        hsl, ps[:, :], AF.Gelu), reads=[psk],
                        writes=[xpk[fc // 4]])
                for tsub in range(4):
                    tbg = tg * 4 + tsub
                    for oh in range(2):
                        ps = PS[oh]
                        psk = f"PS{oh}"
                        for fc in range(16):
                            hsl = xp[fc // 4][:, (fc % 4) * 512 + tsub * 128:
                                              (fc % 4) * 512 + (tsub + 1) * 128]
                            pb.mm(ps[:, :], hsl,
                                  pb.sb[f"G{16+fc}"][:, oh * 512:(oh + 1) * 512],
                                  start=(fc == 0), stop=(fc == 15),
                                  reads=[xpk[fc // 4], f"G{16+fc}"], writes=[psk])
                        gcol = gt32[:, tbg * 9 + e:tbg * 9 + e + 1]
                        pb.op("vector", lambda em, ps=ps, gcol=gcol: em.tensor_scalar_mul(
                            ehalf[:, :], ps[:, :], gcol),
                            reads=[psk, "GT32"], writes=["EG"])
                        pb.g_dma(acc_scr[tbg * 128:(tbg + 1) * 128,
                                         oh * 512:(oh + 1) * 512],
                                 ehalf[:, :], reads=["EG"], writes=[f"g_l1@{tbg}"],
                                 accum=True)

    emit_combine(pb, "l1", xres, s_scr, pb.dr["l1_acc"], T)



def prep_weights_L1(w):
    out = {}
    out["l1_wqT"] = np.ascontiguousarray(np.asarray(w["L1_attn_Wq"]).T)
    out["l1_wkT"] = np.ascontiguousarray(np.asarray(w["L1_attn_Wk"]).T)
    out["l1_wvT"] = np.ascontiguousarray(np.asarray(w["L1_attn_Wv"]).T)
    out["l1_wbT"] = np.ascontiguousarray(np.asarray(w["L1_attn_Wb"]).T)
    for nm, key in (("l1_cqT", "L1_attn_conv_q_w"), ("l1_ckT", "L1_attn_conv_k_w"),
                    ("l1_cvT", "L1_attn_conv_v_w")):
        cw = np.asarray(w[key])
        out[nm] = np.ascontiguousarray(cw.reshape(8, 128, 4).transpose(1, 0, 2).reshape(128, 32))
    fs = np.asarray(w["L1_attn_fir_short_w"]).reshape(1024, 5)
    fl = np.asarray(w["L1_attn_fir_long_w"]).reshape(1024, 63)
    out["l1_firS"] = np.ascontiguousarray(fs.reshape(8, 128, 5).transpose(1, 0, 2).reshape(128, 40))
    out["l1_firL"] = np.ascontiguousarray(fl.reshape(8, 128, 63).transpose(1, 0, 2).reshape(128, 504))
    out["l1_ltemp"] = np.ascontiguousarray(np.asarray(w["L1_attn_log_temp"]).reshape(1, 16))
    out["l1_fpar"] = np.ascontiguousarray(np.asarray(w["L1_attn_floor_param"]).reshape(1, 16))
    out["l1_w1T"] = np.ascontiguousarray(np.asarray(w["L1_attn_gate_W1"]).T)
    out["l1_gb1T"] = np.ascontiguousarray(np.asarray(w["L1_attn_gate_b1"]).reshape(8, 128).T)
    out["l1_w2T"] = np.ascontiguousarray(np.asarray(w["L1_attn_gate_W2"]).T)
    out["l1_gb2c"] = np.ascontiguousarray(np.asarray(w["L1_attn_gate_b2"]).reshape(16, 1))
    out["l1_onw"] = np.ascontiguousarray(np.tile(np.asarray(w["L1_attn_o_norm_w"]), 4).reshape(1, 1024))
    out["l1_woT"] = np.ascontiguousarray(np.asarray(w["L1_attn_Wo"]).T)
    out["L1_ffn_router_w"] = np.ascontiguousarray(np.asarray(w["L1_ffn_router_w"]))
    out["l1_rb"] = np.ascontiguousarray(np.asarray(w["L1_ffn_router_b"]).reshape(1, 9))
    out["L1_ffn_w1"] = np.ascontiguousarray(np.asarray(w["L1_ffn_w1"]))
    out["L1_ffn_w2"] = np.ascontiguousarray(np.asarray(w["L1_ffn_w2"]))
    return out


def declare_inputs_L1(pb):
    inp = {}
    inp["l1_wqT"] = pb.dram("l1_wqT", [D, D], F32, kind="ExternalInput")
    inp["l1_wkT"] = pb.dram("l1_wkT", [D, D], F32, kind="ExternalInput")
    inp["l1_wvT"] = pb.dram("l1_wvT", [D, D], F32, kind="ExternalInput")
    inp["l1_wbT"] = pb.dram("l1_wbT", [D, 4], F32, kind="ExternalInput")
    inp["l1_cqT"] = pb.dram("l1_cqT", [128, 32], F32, kind="ExternalInput")
    inp["l1_ckT"] = pb.dram("l1_ckT", [128, 32], F32, kind="ExternalInput")
    inp["l1_cvT"] = pb.dram("l1_cvT", [128, 32], F32, kind="ExternalInput")
    inp["l1_firS"] = pb.dram("l1_firS", [128, 40], F32, kind="ExternalInput")
    inp["l1_firL"] = pb.dram("l1_firL", [128, 504], F32, kind="ExternalInput")
    inp["l1_ltemp"] = pb.dram("l1_ltemp", [1, 16], F32, kind="ExternalInput")
    inp["l1_fpar"] = pb.dram("l1_fpar", [1, 16], F32, kind="ExternalInput")
    inp["l1_w1T"] = pb.dram("l1_w1T", [1088, D], F32, kind="ExternalInput")
    inp["l1_gb1T"] = pb.dram("l1_gb1T", [128, 8], F32, kind="ExternalInput")
    inp["l1_w2T"] = pb.dram("l1_w2T", [D, 16], F32, kind="ExternalInput")
    inp["l1_gb2c"] = pb.dram("l1_gb2c", [16, 1], F32, kind="ExternalInput")
    inp["l1_onw"] = pb.dram("l1_onw", [1, D], F32, kind="ExternalInput")
    inp["l1_woT"] = pb.dram("l1_woT", [D, D], F32, kind="ExternalInput")
    inp["L1_ffn_router_w"] = pb.dram("L1_ffn_router_w", [D, 9], F32, kind="ExternalInput")
    inp["l1_rb"] = pb.dram("l1_rb", [1, 9], F32, kind="ExternalInput")
    inp["L1_ffn_w1"] = pb.dram("L1_ffn_w1", [8, D, 4 * D], F32, kind="ExternalInput")
    inp["L1_ffn_w2"] = pb.dram("L1_ffn_w2", [8, 4 * D, D], F32, kind="ExternalInput")
    return inp


def prep_weights_L0(w):
    out = {}
    out["L0_attn_w_dec_attn"] = np.ascontiguousarray(w["L0_attn_w_dec_attn"])
    wo = np.asarray(w["L0_attn_w_out"])
    out["l0_w1t"] = np.ascontiguousarray(wo[:, :D].T)
    out["l0_w2t"] = np.ascontiguousarray(wo[:, D:].T)
    out["L0_attn_b_out"] = np.ascontiguousarray(np.asarray(w["L0_attn_b_out"]).reshape(1, D))
    out["l0_wfct"] = np.ascontiguousarray(np.asarray(w["L0_ffn_W_fc"]).T)
    out["l0_wprojt"] = np.ascontiguousarray(np.asarray(w["L0_ffn_W_proj"]).T)
    return out


def declare_inputs_L0(pb):
    inp = {}
    inp["L0_attn_w_dec_attn"] = pb.dram("L0_attn_w_dec_attn", [D, D], F32, kind="ExternalInput")
    inp["l0_w1t"] = pb.dram("l0_w1t", [D, D], F32, kind="ExternalInput")
    inp["l0_w2t"] = pb.dram("l0_w2t", [D, D], F32, kind="ExternalInput")
    inp["L0_attn_b_out"] = pb.dram("L0_attn_b_out", [1, D], F32, kind="ExternalInput")
    inp["l0_wfct"] = pb.dram("l0_wfct", [D, 4 * D], F32, kind="ExternalInput")
    inp["l0_wprojt"] = pb.dram("l0_wprojt", [4 * D, D], F32, kind="ExternalInput")
    return inp


def build_program(n_seq=2, S=2048, layers=(0, 1, 2, 3), taps=()):
    nc = bass.Bass("TRN2", target_bir_lowering=False)
    es = contextlib.ExitStack()
    pb = PB(nc, es)
    T = n_seq * S

    x0 = pb.dram("x0", [T, D], F32, kind="ExternalInput")
    y = pb.dram("y", [T, D], F32, kind="ExternalOutput")

    emit_prelude(pb)
    emit_causal_masks(pb)
    emit_incl_masks(pb)
    xres = pb.dram("xres", [T, D], F32)
    pb.q_dma(xres[:, :], x0[:, :], reads=["in_x0"], writes=["xres"])

    if 0 in layers:
        inp = declare_inputs_L0(pb)
        emit_L0(pb, inp, n_seq, S)
    if 1 in layers:
        inp1 = declare_inputs_L1(pb)
        emit_L1(pb, inp1, n_seq, S)
    if 2 in layers:
        inp2 = declare_inputs_L2(pb)
        emit_L2(pb, inp2, n_seq, S)
    if 3 in layers:
        inp3 = declare_inputs_L3(pb)
        emit_L3(pb, inp3, n_seq, S)

    pb.q_dma(y[:, :], xres[:, :], reads=["xres"], writes=["out_y"])
    for tname in taps:
        tap = pb.dram(f"tap_{tname}", list(pb.dr[tname].shape), pb.dr[tname].dtype,
                      kind="ExternalOutput")
        nt = T // 128
        pb.q_dma(tap[:], pb.dr[tname][:],
                 reads=[tname] + rd_all(tname, nt) + [f"{tname}@{s}" for s in range(n_seq)],
                 writes=[f"tap_{tname}"])
    pb.drain()
    pb.finalize()
    es.close()
    return nc, pb


def kernel(**inputs):
    n_seq, S = 2, 2048
    T = n_seq * S
    nc, pb = build_program(n_seq=n_seq, S=S)
    w = prep_weights_L0(inputs)
    w.update(prep_weights_L1(inputs))
    w.update(prep_weights_L2(inputs))
    w.update(prep_weights_L3(inputs))
    x = np.ascontiguousarray(np.asarray(inputs["L0_attn_hidden_states"], dtype=np.float32))
    in_maps = []
    for c in range(NCORES):
        m = dict(w)
        m["x0"] = np.ascontiguousarray(x[c * n_seq:(c + 1) * n_seq].reshape(T, D))
        in_maps.append(m)
    res = run_bass_kernel_spmd(nc, in_maps, core_ids=list(range(NCORES)))
    out = np.concatenate([np.asarray(r["y"]).reshape(n_seq, S, D) for r in res.results], axis=0)
    return out
